# version 04
# baseline (speedup 1.0000x reference)
import math
from contextlib import ExitStack

import os
import numpy as np
import ml_dtypes

import concourse.bass as bass
import concourse.mybir as mybir
from concourse.bass_types import AP
from concourse.bass_utils import run_bass_kernel_spmd
from concourse.aot_env import allow_ici_all_to_all

F32 = mybir.dt.float32
BF16 = mybir.dt.bfloat16
AF = mybir.ActivationFunctionType
ALU = mybir.AluOpType

NCORE = 8
B = 2
L = 8192
D = 1024
TC = 2048
DI = 2048
NS = 16
GROUP = 4
EPS = 1e-5
bf = ml_dtypes.bfloat16



class Sem:
    def __init__(self, stack, nc, name):
        self.h = stack.enter_context(nc.semaphore(name))
        self.n = 0

    def inc(self, k=1):
        self.n += k
        return self.n

    @property
    def tok(self):
        return (self, self.n)


class Arena:
    def __init__(self, t):
        self.t = t
        self.cols = t.shape[1]
        self.off = 0
        self.hwm = 0

    def alloc(self, w, rows=128):
        assert self.off + w <= self.cols, f"arena overflow {self.off}+{w}>{self.cols}"
        ap = self.t[0:rows, self.off:self.off + w]
        self.off += w
        self.hwm = max(self.hwm, self.off)
        return ap

    def mark(self):
        return self.off

    def release(self, m):
        self.off = m


class Prog:

    def __init__(self, name="prog"):
        self.nc = bass.Bass("TRN2", target_bir_lowering=False)
        self.stack = ExitStack()
        self.scopes = []
        self.prefix = ""
        self.block = None
        self.sems = {}
        self.name = name
        self.ext_cache = {}
        self.prep = {}
        self._z1 = 0

    def sem(self, name):
        if name not in self.sems:
            self.sems[name] = Sem(self.stack, self.nc, name)
        return self.sems[name]

    def _st(self):
        return self.scopes[-1] if self.scopes else self.stack

    def sbuf(self, name, shape, dt):
        return self._st().enter_context(
            self.nc.sbuf_tensor(self.prefix + name, shape, dt))

    def psum(self, name, shape, dt=F32):
        return self._st().enter_context(
            self.nc.psum_tensor(self.prefix + name, shape, dt))

    def begin_program(self, consts):
        self.block = self.stack.enter_context(self.nc.Block())
        self.cmap = {}
        self.cpool = self.stack.enter_context(
            self.nc.sbuf_tensor("cpool", [128, max(1, len(consts))], F32))
        init = self.sem("init")
        @self.block.vector
        def _(v):
            last = None
            for i, val in enumerate(consts):
                self.cmap[val] = i
                last = v.memset(self.cpool[:, i:i + 1], val)
            last.then_inc(init.h, 1)
        init.inc(1)

    def begin(self, consts=(), zero_psums=()):
        init = self.sem("init")
        @self.block.vector
        def _(v):
            last = None
            for ps in zero_psums:
                last = v.memset(ps[:], 0.0)
            if last is None:
                last = v.memset(self.cpool[:, 0:1], 1.0)
            last.then_inc(init.h, 1)
        init.inc(1)
        self._z1 = init.n

    def barrier(self):
        snap = [(s.h, s.n) for s in self.sems.values() if s.n > 0]
        blk = self.block
        for engf in (blk.sync, blk.scalar, blk.vector, blk.tensor, blk.gpsimd):
            @engf
            def _(e, snap=snap):
                for h, v in snap:
                    e.wait_ge(h, v)

    def scope_open(self, tag):
        self.barrier()
        self.scopes.append(ExitStack())
        self.prefix = tag + "_"
        self._dbg_barrier = None
        self._dbg_proj = None
        self._dbg_revchunk = False

    def scope_close(self):
        self.scopes.pop().close()
        self.prefix = ""

    def constf(self, val, p0=0, nrow=128):
        c = self.cmap[val]
        return self.cpool[p0:p0 + nrow, c:c + 1]

    def ext(self, name, shape, dt, fn):
        if name not in self.ext_cache:
            self.ext_cache[name] = self.nc.dram_tensor(
                name, list(shape), dt, kind="ExternalInput").ap()
            self.prep[name] = fn
        return self.ext_cache[name]

    def dram(self, name, shape, dt):
        h = self.nc.dram_tensor(name, list(shape), dt)
        return h.ap()

    def dout(self, name, shape, dt=F32):
        return self.nc.dram_tensor(name, list(shape), dt, kind="ExternalOutput").ap()

    def finish(self):
        self.stack.close()


RG4 = [[0, 1, 2, 3], [4, 5, 6, 7]]
RG8 = [[0, 1, 2, 3, 4, 5, 6, 7]]


def emit_exchange(p, colls):
    cc = p.sem("cc")
    @p.block.gpsimd
    def _(g):
        for kind, i_ap, o_ap, rg in colls:
            if kind == "AllToAll":
                with allow_ici_all_to_all():
                    ins_ = g.collective_compute(
                        kind, ALU.bypass, replica_groups=rg,
                        ins=[i_ap.opt()], outs=[o_ap.opt()])
            else:
                ins_ = g.collective_compute(
                    kind, ALU.bypass, replica_groups=rg,
                    ins=[i_ap.opt()], outs=[o_ap.opt()])
            ins_.then_inc(cc.h, 16)
            cc.inc(16)
            g.wait_ge(cc.h, cc.n)


def rep_row(eng, dst, src_row, n_rep=128):
    rep_ap = AP(src_row.tensor, src_row.offset,
                [[src_row.ap[0][0], 1], [0, n_rep], list(src_row.ap[-1])])
    return eng.dma_start(out=dst, in_=rep_ap)


def tchunks_of(T, step=512):
    return [(s, min(s + step, T)) for s in range(0, T, step)]



def emit_ln_stats(p, x_tiles, x2_pp, psA, psB, ones_col, ready, mu, var, r1s, init_tok=None):
    block = p.block
    pe, act, dve = p.sem("pe"), p.sem("act"), p.sem("dve")
    n_k = len(x_tiles)
    act_base, pe_base = act.n, pe.n

    @block.scalar
    def _(s):
        s.wait_ge(ready[0].h, ready[1])
        for i in range(n_k):
            if i >= 2:
                s.wait_ge(pe.h, pe_base + i - 1)
            s.activation(x2_pp[i % 2], x_tiles[i], AF.Square).then_inc(act.h, 1)
    act.n += n_k

    T = x_tiles[0].shape[-1]
    tch = tchunks_of(T)

    @block.tensor
    def _(t):
        for i in range(n_k):
            if i == 0:
                t.wait_ge(ready[0].h, ready[1])
                if init_tok is not None:
                    t.wait_ge(init_tok[0].h, init_tok[1])
            for (t0, t1) in tch:
                t.matmul(psA[:, t0:t1], ones_col, x_tiles[i][:, t0:t1],
                         start=(i == 0), stop=(i == n_k - 1))
            t.wait_ge(act.h, act_base + i + 1)
            last = None
            for (t0, t1) in tch:
                last = t.matmul(psB[:, t0:t1], ones_col, x2_pp[i % 2][:, t0:t1],
                                start=(i == 0), stop=(i == n_k - 1))
            last.then_inc(pe.h, 1)
    pe.n += n_k
    pe_done = pe.n
    inv_n = 1.0 / (128 * n_k)

    @block.vector
    def _(v):
        v.wait_ge(pe.h, pe_done)
        v.tensor_scalar(mu, psA[0:1, :], inv_n, None, ALU.mult)
        v.tensor_scalar(var, psB[0:1, :], inv_n, None, ALU.mult)
        v.tensor_mul(r1s, mu, mu)
        v.tensor_sub(var, var, r1s).then_inc(dve.h, 1)
    dve.inc(1)
    return dve.tok


def emit_rsqrt_row(p, out_row, in_row, eps_ap, after):
    block = p.block
    act = p.sem("act")

    @block.scalar
    def _(s):
        s.wait_ge(after[0].h, after[1])
        s.activation(out_row, in_row, AF.Ln, bias=eps_ap)
        s.activation(out_row, out_row, AF.Exp, scale=-0.5).then_inc(act.h, 1)
    act.inc(1)
    return act.tok


def emit_proj(p, w_dram, in_tiles, n_do, wbufs, psums, epilogue, in_ready,
              w_rows0=0, extra_in=None, prev_tail=None):
    block = p.block
    pe, dC = p.sem("pe"), p.sem("dC")
    n_k = len(in_tiles)
    T = in_tiles[0].shape[-1]
    tch = tchunks_of(T)

    def w_ap(do):
        return w_dram[:, (w_rows0 + do) * 128:(w_rows0 + do + 1) * 128] \
            .rearrange("(ki p) c -> p ki c", p=128)

    consumed = {}
    load_tok = {}
    wdve = os.environ.get("WDVE", "0") == "1"
    wstage = getattr(p, "_w_f32_stage", None)
    dve_s = p.sem("dve")

    gp_scr = getattr(p, "_gp_scr", None)
    pool_s = p.sem("pool")
    gptouch = os.environ.get("GPTOUCH", "1") == "1"

    def emit_load(do, buf_idx, gate_tok=None):
        if not wdve:
            @block.gpsimd
            def _(g, do=do, buf_idx=buf_idx):
                if gate_tok is not None:
                    g.wait_ge(gate_tok[0].h, gate_tok[1])
                if do < 2 and prev_tail:
                    for tok in prev_tail:
                        g.wait_ge(tok[0].h, tok[1])
                g.dma_start(out=wbufs[buf_idx][:].rearrange("p (ki c) -> p ki c", c=128),
                            in_=w_ap(do)).then_inc(dC.h, 16)
            dC.inc(16)
            if gptouch:
                _ld = dC.n
                @block.gpsimd
                def _(g, buf_idx=buf_idx, _ld=_ld):
                    g.wait_ge(dC.h, _ld)
                    g.tensor_copy(gp_scr, wbufs[buf_idx][:, 1020:1024]).then_inc(pool_s.h, 1)
                pool_s.inc(1)
                return ("pool", pool_s.n)
            return ("dC", dC.n)
        else:
            @block.gpsimd
            def _(g, do=do, buf_idx=buf_idx):
                if gate_tok is not None:
                    g.wait_ge(gate_tok[0].h, gate_tok[1])
                g.dma_start(out=wstage[:].rearrange("p (ki c) -> p ki c", c=128),
                            in_=w_ap(do).bitcast(F32) if False else w_ap(do)).then_inc(dC.h, 16)
            dC.inc(16)
            _ld = dC.n
            @block.vector
            def _(v, buf_idx=buf_idx, _ld=_ld):
                v.wait_ge(dC.h, _ld)
                v.tensor_copy(wbufs[buf_idx][:], wstage[:]).then_inc(dve_s.h, 1)
            dve_s.inc(1)
            return ("dve", dve_s.n)

    for do in range(min(2, n_do)):
        gate = None
        if wdve and do > 0:
            gate = (dve_s, load_tok[do - 1][1])
        load_tok[do] = emit_load(do, do, gate_tok=gate)

    barrier = getattr(p, "_dbg_barrier", None)
    for do in range(n_do):
        ps = psums[do % 2]
        @block.tensor
        def _(t, do=do, ps=ps):
            lt = load_tok[do]
            t.wait_ge({"dC": dC, "dve": dve_s, "pool": p.sem("pool")}[lt[0]].h, lt[1])
            if do == 0:
                if "init" in p.sems:
                    t.wait_ge(p.sems["init"].h, p._z1)
                t.wait_ge(in_ready[0].h, in_ready[1])
                if extra_in is not None:
                    t.wait_ge(extra_in[0].h, extra_in[1])
                if prev_tail:
                    for tok in prev_tail:
                        t.wait_ge(tok[0].h, tok[1])
                if barrier:
                    for (sh, sv) in barrier:
                        t.wait_ge(sh, sv)
            if os.environ.get("SERIAL", "0") == "1" and do >= 1:
                tok = consumed[do - 1]
                t.wait_ge(tok[0].h, tok[1])
            if do >= 2:
                tok = consumed[do - 2]
                t.wait_ge(tok[0].h, tok[1])
            last = None
            for (t0, t1) in (tch[::-1] if getattr(p, "_dbg_revchunk", False) else tch):
                for ki in range(n_k):
                    last = t.matmul(ps[:, t0:t1],
                                    wbufs[do % 2][:, ki * 128:(ki + 1) * 128],
                                    in_tiles[ki][:, t0:t1],
                                    start=(ki == 0), stop=(ki == n_k - 1))
            last.then_inc(pe.h, 1)
        pe.inc(1)
        if do == 0 and getattr(p, "_dbg_proj", None) is not None:
            o_ps, o_wb = p._dbg_proj
            tok = pe.tok
            stg = p._dbg_stage
            p.sem("act").inc(1)
            _t = p.sem("act").n
            @block.scalar
            def _(s, ps=ps, tok=tok, stg=stg, _t=_t):
                s.wait_ge(tok[0].h, tok[1])
                s.copy(stg, ps[:]).then_inc(p.sem("act").h, 1)
                s.wait_ge(p.sem("act").h, _t)
                s.dma_start(out=o_ps[:], in_=stg).then_inc(p.sem("dB").h, 16)
                s.dma_start(out=o_wb[:], in_=wbufs[0][:]).then_inc(p.sem("dB").h, 16)
            p.sem("dB").inc(32)
        consumed[do] = epilogue(do, ps, pe.tok)
        if do + 2 < n_do:
            gate = consumed[do]
            if wdve:
                gate2 = (dve_s, load_tok[do + 1][1])
                @block.gpsimd
                def _(g, gate2=gate2):
                    g.wait_ge(gate2[0].h, gate2[1])
            load_tok[do + 2] = emit_load(do + 2, do % 2, gate_tok=gate)
    return consumed


def emit_softplus_from_psum(p, out_bf, ps, bias_col, s1, s2, mm_tok):
    block = p.block
    act, dve = p.sem("act"), p.sem("dve")

    @block.scalar
    def _(s):
        s.wait_ge(mm_tok[0].h, mm_tok[1])
        s.activation(s1[:], ps, AF.Relu, bias=bias_col)
        s.activation(s2[:], ps, AF.Abs, bias=bias_col).then_inc(act.h, 1)
        s.activation(s2[:], s2[:], AF.Exp, scale=-1.0)
        s.activation(s2[:], s2[:], AF.Ln, bias=1.0).then_inc(act.h, 1)
    act.inc(2)
    ps_free = (act, act.n - 1)
    ln_done = act.n

    @block.vector
    def _(v):
        v.wait_ge(act.h, ln_done)
        v.tensor_add(out_bf[:], s1[:], s2[:]).then_inc(dve.h, 1)
    dve.inc(1)
    return ps_free, dve.tok



def phase_tokA(p, io, inner_ln=True, with_gb=True, dbg=False):
    xT = io["xT"]
    ln_gb = io["ln_gb"]
    W_in = io["W_in"]
    o_xbr = io["o_xbr"]
    o_sz = io["o_sz"]
    o_edge = io["o_edge"]
    if dbg:
        o_xn = p.dout("o_xn", (D, TC), BF16)
        o_rows = p.dout("o_rows", (1, 6 * TC), F32)
        p._dbg_proj = (p.dout("o_ps", (128, TC), F32), p.dout("o_wb", (128, 1024), BF16))

    n_kD = D // 128
    fa = Arena(p.sbuf("fa", [128, 37 * 1024], F32))
    ba = Arena(p.sbuf("ba", [128, 25 * 1024], BF16))
    psA = p.psum("psA", [128, TC], F32)
    psB = p.psum("psB", [128, TC], F32)

    x_t = [fa.alloc(TC) for _ in range(n_kD)]
    x2_pp = [fa.alloc(TC) for _ in range(2)]
    reps = fa.alloc(2 * TC)
    rows = fa.alloc(6 * TC, rows=1)
    gbt = p.sbuf("gbt", [128, 2 * n_kD], F32)
    xn_t = [ba.alloc(TC) for _ in range(n_kD)]
    wbufs = [p.sbuf(f"wb{i}", [128, 128 * n_kD], BF16) for i in range(2)]
    ones128 = p.sbuf("ones128", [128, 128], F32)
    wstage_t = p.sbuf("wstage", [128, 1024], F32)
    gp_scr_t = p.sbuf("gp_scr", [128, 4], BF16)
    stage = [ba.alloc(TC) for _ in range(4)]

    p.begin(consts=(1.0, EPS), zero_psums=(psA, psB))
    ones_c = ones128[:]
    block = p.block
    dA, dB, dC = p.sem("dA"), p.sem("dB"), p.sem("dC")
    dve, act = p.sem("dve"), p.sem("act")

    @block.sync
    def _(s):
        for i in range(n_kD):
            s.dma_start(out=x_t[i], in_=xT[i * 128:(i + 1) * 128, :]).then_inc(dA.h, 16)
        s.dma_start(out=gbt[:], in_=ln_gb[:]).then_inc(dA.h, 16)
    dA.n += 16 * (n_kD + 1)
    x_loaded = (dA, dA.n)

    @block.vector
    def _(v):
        v.wait_ge(p.sem("init").h, p._z1)
        v.memset(ones128[:], 1.0).then_inc(p.sem("init").h, 1)
    p.sem("init").inc(1)

    R = lambda i: rows[0:1, i * TC:(i + 1) * TC]
    mu, var, r0, r1, rr, mr = R(0), R(1), R(2), R(3), R(4), R(5)
    scr = R(3)
    init_tok = (p.sem("init"), p._z1 + 1)
    import os as _os
    if _os.environ.get("STATS", "1") == "0":
        @block.vector
        def _(v):
            v.wait_ge(x_loaded[0].h, x_loaded[1])
            v.wait_ge(init_tok[0].h, init_tok[1])
            v.memset(mu, 0.0)
            v.memset(var, 1.0).then_inc(p.sem("dve").h, 1)
        p.sem("dve").inc(1)
    elif _os.environ.get("VIRGIN", "0") == "1":
        emit_ln_stats(p, x_t, x2_pp, psB[:, :], psB[:, :], ones_c,
                      x_loaded, mu, var, scr, init_tok)
    else:
        emit_ln_stats(p, x_t, x2_pp, psA[:, :], psB[:, :], ones_c,
                      x_loaded, mu, var, scr, init_tok)
    t0 = emit_rsqrt_row(p, r0, var, p.constf(EPS, 0, 1), p.sem("dve").tok)
    if inner_ln:
        @block.vector
        def _(v):
            v.wait_ge(t0[0].h, t0[1])
            v.tensor_mul(r1, r0, r0)
            v.tensor_mul(r1, r1, var).then_inc(dve.h, 1)
        dve.inc(1)
        t1 = emit_rsqrt_row(p, r1, r1, p.constf(EPS, 0, 1), dve.tok)
        @block.vector
        def _(v):
            v.wait_ge(t1[0].h, t1[1])
            v.tensor_mul(rr, r0, r1)
            v.tensor_mul(mr, mu, rr).then_inc(dve.h, 1)
        dve.inc(1)
    else:
        @block.vector
        def _(v):
            v.wait_ge(t0[0].h, t0[1])
            v.tensor_copy(rr, r0)
            v.tensor_mul(mr, mu, r0).then_inc(dve.h, 1)
        dve.inc(1)
    rows_tok = dve.tok

    @block.gpsimd
    def _(g):
        g.wait_ge(rows_tok[0].h, rows_tok[1])
        rep_row(g, reps[:, 0:TC], rr).then_inc(dC.h, 16)
        rep_row(g, reps[:, TC:2 * TC], mr).then_inc(dC.h, 16)
    dC.inc(32)
    reps_tok = (dC, dC.n)

    tmp = x2_pp[0]
    @block.vector
    def _(v):
        v.wait_ge(reps_tok[0].h, reps_tok[1])
        last = None
        for i in range(n_kD):
            v.tensor_mul(tmp, x_t[i], reps[:, 0:TC])
            v.tensor_sub(tmp, tmp, reps[:, TC:2 * TC])
            if with_gb:
                last = v.tensor_scalar(xn_t[i], tmp, gbt[:, i:i + 1],
                                       gbt[:, n_kD + i:n_kD + i + 1], ALU.mult, ALU.add)
            else:
                last = v.tensor_copy(xn_t[i], tmp)
        last.then_inc(dve.h, 1)
    dve.inc(1)
    if os.environ.get("DELAY", "0") == "1":
        @block.vector
        def _(v):
            last2 = None
            for _i in range(10):
                last2 = v.tensor_copy(tmp[:, 0:512], x_t[0][:, 0:512])
            last2.then_inc(dve.h, 1)
        dve.inc(1)
    xn_tok = dve.tok

    p._dbg_stage = x2_pp[1]
    p._w_f32_stage = wstage_t[:]
    p._gp_scr = gp_scr_t[:]
    if dbg:
        o_pre = p.dout("o_pre", (128, TC), F32)
        _do_pre = os.environ.get("PRE", "1") == "1"
        if _do_pre:
            act.inc(1)
            _pt = act.n
            @block.scalar
            def _(s, _pt=_pt):
                s.wait_ge(p.sem("dve").h, p.sem("dve").n)
                s.copy(x2_pp[1], psA[:]).then_inc(act.h, 1)
                s.wait_ge(act.h, _pt)
                s.dma_start(out=o_pre[:], in_=x2_pp[1]).then_inc(dB.h, 16)
            dB.inc(16)
        else:
            @block.gpsimd
            def _(g):
                g.dma_start(out=o_pre[:], in_=x2_pp[1]).then_inc(dB.h, 16)
            dB.inc(16)
    if dbg:
        p._dbg_barrier = [(p.sem(n).h, p.sem(n).n) for n in ("dA", "dB", "dC", "dve", "act")]
        import os as _os
        p._dbg_revchunk = _os.environ.get("REV", "0") == "1"
    store_tok = {}

    def epi(do, ps, mm_tok):
        st = stage[do % 4]
        act.inc(1)
        ps_free = (act, act.n)
        @block.scalar
        def _(s, do=do, ps=ps, mm_tok=mm_tok, st=st, psf=ps_free[1]):
            s.wait_ge(mm_tok[0].h, mm_tok[1])
            if do >= 4:
                s.wait_ge(dB.h, 16 * store_tok[do - 4])
            if do < 16:
                if os.environ.get("VEVAC", "0") == "1":
                    pass
                s.copy(st, ps[:]).then_inc(act.h, 1)
            else:
                s.activation(st, ps[:], AF.Silu).then_inc(act.h, 1)
            s.wait_ge(act.h, psf)
            dst = o_xbr if do < 16 else o_sz
            r0_ = (do % 16) * 128
            s.dma_start(out=dst[r0_:r0_ + 128, :], in_=st).then_inc(dB.h, 16)
            if do < 16:
                s.dma_start(out=o_edge[r0_:r0_ + 128, 0:3], in_=st[:, 0:3]) \
                    .then_inc(dB.h, 16)
                s.dma_start(out=o_edge[r0_:r0_ + 128, 3:6], in_=st[:, TC - 3:TC]) \
                    .then_inc(dB.h, 16)
        dB.inc(48 if do < 16 else 16)
        store_tok[do] = dB.n // 16
        return ps_free

    if dbg:
        @block.gpsimd
        def _(g):
            g.wait_ge(xn_tok[0].h, xn_tok[1])
            for i in range(n_kD):
                g.dma_start(out=o_xn[i * 128:(i + 1) * 128, :], in_=xn_t[i]).then_inc(dC.h, 16)
            g.dma_start(out=o_rows[:], in_=rows).then_inc(dC.h, 16)
        dC.inc(16 * (n_kD + 1))

    emit_proj(p, W_in, xn_t, 2 * DI // 128, wbufs, [psA, psB], epi, xn_tok)

    @block.gpsimd
    def _(g):
        g.wait_ge(dB.h, dB.n)
    assert fa.hwm <= fa.cols and ba.hwm <= ba.cols




def emit_conv_silu(p, xbr_tiles, xb_tiles, convw, convb, cacc, xbr_ready, o_xb=None,
                   halo=3):
    block = p.block
    dve, act, dB = p.sem("dve"), p.sem("act"), p.sem("dB")
    toks = []
    n_dt = len(xbr_tiles)
    T = xbr_tiles[0].shape[-1] - halo
    prev_silu = None
    for dt in range(n_dt):
        rd = xbr_ready[dt]
        @block.vector
        def _(v, dt=dt, rd=rd, prev_silu=prev_silu):
            v.wait_ge(rd[0].h, rd[1])
            if prev_silu is not None:
                v.wait_ge(prev_silu[0].h, prev_silu[1])
            sh = halo - 3
            v.tensor_scalar(cacc[:, 0:T], xbr_tiles[dt][:, sh:sh + T],
                            convw[:, dt * 4:dt * 4 + 1], None, ALU.mult)
            last = None
            for j in range(1, 4):
                last = v.scalar_tensor_tensor(cacc[:, 0:T],
                                              xbr_tiles[dt][:, sh + j:sh + j + T],
                                              convw[:, dt * 4 + j:dt * 4 + j + 1],
                                              cacc[:, 0:T], ALU.mult, ALU.add)
            last.then_inc(dve.h, 1)
        dve.inc(1)
        cv = dve.n
        act.inc(1)
        silu_tok = (act, act.n)
        if o_xb is not None:
            dB.inc(16)
            st_tok = dB.n
        @block.scalar
        def _(s, dt=dt, cv=cv, silu_tok=silu_tok):
            s.wait_ge(dve.h, cv)
            s.activation(xb_tiles[dt], cacc[:, 0:T], AF.Silu,
                         bias=convb[:, dt:dt + 1]).then_inc(act.h, 1)
            if o_xb is not None:
                s.wait_ge(act.h, silu_tok[1])
                s.dma_start(out=o_xb[dt * 128:(dt + 1) * 128, :], in_=xb_tiles[dt]) \
                    .then_inc(dB.h, 16)
        prev_silu = silu_tok
        toks.append(silu_tok)
    return toks


def phase_tokB_a(p, io):
    n_dt = DI // 128
    i_body = io["i_xbr_body"]
    edgeAG = io["edgeAG"]
    hmask_in = io["hmask"]
    convw_in = io["convw"]
    convb_in = io["convb_bdel"]
    W_delta = io["W_delta"]
    W_bc = io["W_bc"]
    o_xb = io["o_xb"]
    o_delta = io["o_delta"]
    o_bc = io["o_bc"]

    fa = Arena(p.sbuf("fa", [128, 8 * 1024], F32))
    ba = Arena(p.sbuf("ba", [128, 71 * 1024], BF16))
    psA = p.psum("psA", [128, TC], F32)
    psB = p.psum("psB", [128, TC], F32)

    xbr_t = [ba.alloc(3 + TC) for _ in range(n_dt)]
    xb_t = [ba.alloc(TC) for _ in range(n_dt)]
    stage = [ba.alloc(TC) for _ in range(2)]
    cacc = fa.alloc(TC)
    sp1, sp2 = fa.alloc(TC), fa.alloc(TC)
    bc_stage = ba.alloc(TC, rows=32)
    cw = p.sbuf("cw", [128, n_dt * 4], F32)
    cb = p.sbuf("cb", [128, 2 * n_dt], F32)
    wbufs = [p.sbuf(f"wb{i}", [128, 128 * n_dt], BF16) for i in range(2)]
    gp_scr_t = p.sbuf("gp_scr", [128, 4], BF16)
    ehall = p.sbuf("ehall", [128, n_dt * 12], BF16)
    hm = p.sbuf("hm", [128, 4], F32)

    p.begin(consts=(1.0,), zero_psums=(psA, psB))
    block = p.block
    dA, dB, dC = p.sem("dA"), p.sem("dB"), p.sem("dC")
    dve, act = p.sem("dve"), p.sem("act")
    p._gp_scr = gp_scr_t[:]

    @block.sync
    def _(s):
        for i in range(n_dt):
            s.dma_start(out=xbr_t[i][:, 3:3 + TC],
                        in_=i_body[i * 128:(i + 1) * 128, :]).then_inc(dA.h, 16)
            for j in range(4):
                s.dma_start(out=ehall[:, i * 12 + j * 3:i * 12 + (j + 1) * 3],
                            in_=edgeAG[j * DI + i * 128:j * DI + (i + 1) * 128, 3:6]) \
                    .then_inc(dA.h, 16)
        s.dma_start(out=cw[:], in_=convw_in[:]).then_inc(dA.h, 16)
        s.dma_start(out=cb[:], in_=convb_in[:]).then_inc(dA.h, 16)
        s.dma_start(out=hm[:], in_=hmask_in[:]).then_inc(dA.h, 16)
    dA.n += 16 * (5 * n_dt + 3)
    all_ld = dA.n
    @block.vector
    def _(v):
        v.wait_ge(dA.h, all_ld)
        last = None
        for i in range(n_dt):
            v.tensor_scalar(cacc[:, 0:3], ehall[:, i * 12:i * 12 + 3],
                            hm[:, 0:1], None, ALU.mult)
            for j in range(1, 4):
                v.scalar_tensor_tensor(cacc[:, 0:3],
                                       ehall[:, i * 12 + j * 3:i * 12 + (j + 1) * 3],
                                       hm[:, j:j + 1], cacc[:, 0:3],
                                       ALU.mult, ALU.add)
            last = v.tensor_copy(xbr_t[i][:, 0:3], cacc[:, 0:3])
        last.then_inc(dve.h, 1)
    dve.inc(1)
    xbr_ready = [(dve, dve.n)] * n_dt

    xb_toks = emit_conv_silu(p, xbr_t, xb_t, cw[:], cb[:, 0:n_dt], cacc,
                             xbr_ready, o_xb=o_xb)
    xb_all = xb_toks[-1]

    store_tok = {}
    sp_guard = {}

    def epi_delta(do, ps, mm_tok):
        st = stage[do % 2]
        act_l, dve_l, dB_l = p.sem("act"), p.sem("dve"), p.sem("dB")
        prev_sum = sp_guard.get(do - 1)
        @block.scalar
        def _(s, do=do, ps=ps, mm_tok=mm_tok, st=st, prev_sum=prev_sum):
            s.wait_ge(mm_tok[0].h, mm_tok[1])
            if prev_sum is not None:
                s.wait_ge(dve_l.h, prev_sum)
            bcol = cb[:, n_dt + do:n_dt + do + 1]
            s.activation(sp1, ps[:], AF.Relu, bias=bcol)
            s.activation(sp2, ps[:], AF.Abs, bias=bcol).then_inc(act_l.h, 1)
            s.activation(sp2, sp2, AF.Exp, scale=-1.0)
            s.activation(sp2, sp2, AF.Ln, bias=1.0).then_inc(act_l.h, 1)
        act_l.inc(2)
        ps_free = (act_l, act_l.n - 1)
        ln_done = act_l.n
        @block.vector
        def _(v, do=do, ln_done=ln_done, st=st):
            v.wait_ge(act_l.h, ln_done)
            if do >= 2:
                v.wait_ge(dB_l.h, 16 * store_tok[do - 2])
            v.tensor_add(st, sp1, sp2).then_inc(dve_l.h, 1)
        dve_l.inc(1)
        sp_guard[do] = dve_l.n
        sum_done = dve_l.n
        @block.scalar
        def _(s, do=do, st=st, sum_done=sum_done):
            s.wait_ge(dve_l.h, sum_done)
            s.dma_start(out=o_delta[do * 128:(do + 1) * 128, :], in_=st).then_inc(dB_l.h, 16)
        dB_l.inc(16)
        store_tok[do] = dB_l.n // 16
        return ps_free

    emit_proj(p, W_delta, [xb_t[i] for i in range(n_dt)], n_dt, wbufs,
              [psA, psB], epi_delta, xb_all)
    wdel_done = p.sem("act").tok
    pe = p.sem("pe")
    wdel_mm_done = pe.tok

    last_ps_free = None
    dCl = p.sem("dC")
    @block.gpsimd
    def _(g):
        g.wait_ge(pe.h, wdel_mm_done[1])
        g.dma_start(out=wbufs[0][:, 0:2 * NS].rearrange("p (ki c) -> p ki c", c=2 * NS)
                    if False else wbufs[0][:].rearrange("p (ki c) -> p ki c", c=128)[:, 0:n_dt, 0:2 * NS],
                    in_=W_bc.rearrange("(ki p) c -> p ki c", p=128)).then_inc(dCl.h, 16)
        g.wait_ge(dCl.h, dCl.n + 16)
        g.tensor_copy(gp_scr_t[:, 0:4],
                      wbufs[0][:, 15 * 128 + 2 * NS - 4:15 * 128 + 2 * NS]) \
            .then_inc(p.sem("pool").h, 1)
    dCl.inc(16)
    p.sem("pool").inc(1)
    bc_w_tok = (p.sem("pool"), p.sem("pool").n)

    act_all = p.sem("act").n
    @block.tensor
    def _(t):
        t.wait_ge(bc_w_tok[0].h, bc_w_tok[1])
        t.wait_ge(p.sem("act").h, act_all)
        last = None
        for (t0, t1) in tchunks_of(TC):
            for ki in range(n_dt):
                last = t.matmul(psA[0:32, t0:t1],
                                wbufs[0][:].rearrange("p (ki c) -> p ki c", c=128)[:, ki, 0:2 * NS],
                                xb_t[ki][:, t0:t1],
                                start=(ki == 0), stop=(ki == n_dt - 1))
        last.then_inc(p.sem("pe").h, 1)
    p.sem("pe").inc(1)
    bc_done = p.sem("pe").tok

    @block.vector
    def _(v):
        v.wait_ge(bc_done[0].h, bc_done[1])
        v.tensor_copy(bc_stage, psA[0:32, :]).then_inc(dve.h, 1)
    dve.inc(1)
    bcc = dve.n
    @block.scalar
    def _(s):
        s.wait_ge(dve.h, bcc)
        s.dma_start(out=o_bc[:], in_=bc_stage).then_inc(dB.h, 16)
    dB.inc(16)

    @block.gpsimd
    def _(g):
        g.wait_ge(dB.h, p.sem("dB").n)




def phase_scan(p, io, reverse=False):
    n_dt = 2
    NQ = 8
    TQ = 2048
    i_delta = io["dOut"]
    i_xb = io["xbOut"]
    i_bc = io["bcAG"]
    i_logA = io["i_logA"]
    ident = io["ident"]
    o_y = io["yIn"]

    ba = Arena(p.sbuf("ba", [128, 45 * 2048], BF16))
    psA = p.psum("psA", [128, TQ], F32)
    psB = p.psum("psB", [128, TQ], F32)
    psums = [psA, psB]

    del_t = [ba.alloc(TQ) for _ in range(n_dt)]
    u_t = [ba.alloc(TQ) for _ in range(n_dt)]
    brep = [ba.alloc(TQ) for _ in range(NS)]
    crep = [ba.alloc(TQ) for _ in range(NS)]
    faa = Arena(p.sbuf("faa", [128, 4 * 1024], F32))
    a_pp = [faa.alloc(TQ) for _ in range(2)]
    b_pp = [ba.alloc(TQ) for _ in range(2)]
    h_pp = [ba.alloc(TQ) for _ in range(2)]
    hc_pp = b_pp
    ystage = [ba.alloc(TQ)]
    idt = p.sbuf("idt", [128, 128], BF16)
    At = p.sbuf("At", [128, 32], F32)
    carr = p.sbuf("carr", [128, 64], F32)
    gp_scr_t = p.sbuf("gp_scr", [128, 4], BF16)
    gp_scr = gp_scr_t[:]
    scan_done_list = []
    prev_evac = {}
    ystore = {}

    p.begin(consts=(1.0,), zero_psums=(psA, psB))
    block = p.block
    dA, dB, dC = p.sem("dA"), p.sem("dB"), p.sem("dC")
    dve, act, pe, pool = p.sem("dve"), p.sem("act"), p.sem("pe"), p.sem("pool")

    @block.sync
    def _(s):
        s.dma_start(out=At[:], in_=i_logA[:]).then_inc(dA.h, 16)
        s.dma_start(out=idt[:], in_=ident[:]).then_inc(dA.h, 16)
    dA.inc(32)
    consts_loaded = dA.n

    @block.scalar
    def _(s):
        s.wait_ge(dA.h, consts_loaded)
        s.activation(At[:], At[:], AF.Exp).then_inc(act.h, 1)
    act.inc(1)
    @block.vector
    def _(v):
        v.wait_ge(act.h, act.n)
        v.tensor_scalar(At[:], At[:], -1.0, None, ALU.mult).then_inc(dve.h, 1)
    dve.inc(1)
    A_ready = dve.n

    Hs = list(range(NQ))
    if reverse:
        Hs = Hs[::-1]
    units = []
    for hi, H in enumerate(Hs):
        for dt in range(n_dt):
            for n in range(NS):
                units.append((hi, H, dt, n))

    scan_done = {}
    yma = {}
    aexp = {}
    evac_tok = {}
    h_load = {}
    rep_tok = {}

    for hi, H in enumerate(Hs):
        @block.sync
        def _(s, H=H, hi=hi):
            if hi >= 1:
                kprev = hi * 32 - 1
                s.wait_ge(dve.h, scan_done_list[kprev])
            for dt in range(n_dt):
                r0 = H * 256 + dt * 128
                s.dma_start(out=del_t[dt], in_=i_delta[r0:r0 + 128, 0:TQ]) \
                    .then_inc(dA.h, 16)
                s.dma_start(out=u_t[dt], in_=i_xb[r0:r0 + 128, 0:TQ]) \
                    .then_inc(dA.h, 16)
        dA.inc(16 * 2 * n_dt)
        q_loaded = dA.n
        @block.vector
        def _(v, q_loaded=q_loaded):
            v.wait_ge(dA.h, q_loaded)
            last = None
            for dt in range(n_dt):
                last = v.tensor_mul(u_t[dt], u_t[dt], del_t[dt])
            last.then_inc(dve.h, 1)
        dve.inc(1)
        h_load[hi] = (q_loaded, dve.n)

        @block.gpsimd
        def _(g, H=H, hi=hi):
            g.wait_ge(dA.h, consts_loaded)
            if hi >= 1:
                kprev = hi * 32 - 1
                g.wait_ge(dve.h, scan_done_list[kprev])
            for n in range(NS):
                rb = H * 32 + n
                rep_row(g, brep[n], i_bc[rb:rb + 1, 0:TQ]).then_inc(dC.h, 16)
                rep_row(g, crep[n], i_bc[rb + NS:rb + NS + 1, 0:TQ]) \
                    .then_inc(dC.h, 16)
            g.wait_ge(dC.h, dC.n + 32 * NS)
            last = None
            for n in range(NS):
                last = g.tensor_copy(gp_scr[:, 0:2], brep[n][:, TQ - 2:TQ])
                last = g.tensor_copy(gp_scr[:, 2:4], crep[n][:, TQ - 2:TQ])
            last.then_inc(pool.h, 1)
        dC.inc(32 * NS)
        pool.inc(1)
        reps_ready = pool.n

        for dt in range(n_dt):
            for n in range(NS):
                k = hi * 32 + dt * 16 + n
                u = dt * 16 + n
                psY = psums[dt % 2]
                @block.scalar
                def _(s, k=k, dt=dt, u=u, hi=hi):
                    if k == 0:
                        s.wait_ge(dve.h, A_ready)
                    s.wait_ge(dA.h, h_load[hi][0])
                    if k >= 2:
                        s.wait_ge(dve.h, scan_done.get(k - 2, 0))
                    s.activation(a_pp[k % 2], del_t[dt], AF.Exp,
                                 scale=At[:, u:u + 1]).then_inc(act.h, 1)
                act.inc(1)
                aexp[k] = act.n
                @block.vector
                def _(v, k=k, dt=dt, n=n, u=u, hi=hi, psY=psY):
                    v.wait_ge(dve.h, h_load[hi][1])
                    v.wait_ge(pool.h, reps_ready)
                    if k >= 2:
                        v.wait_ge(pe.h, yma.get(k - 2, 0))
                    v.tensor_mul(b_pp[k % 2], u_t[dt], brep[n])
                    v.wait_ge(act.h, aexp[k])
                    init = 0.0 if hi % 4 == 0 else carr[:, u:u + 1]
                    if reverse:
                        v.tensor_tensor_scan(h_pp[k % 2][:, ::-1], a_pp[k % 2][:, ::-1],
                                             b_pp[k % 2][:, ::-1], init, ALU.mult,
                                             ALU.add).then_inc(dve.h, 1)
                        v.wait_ge(dve.h, dve.n + 1)
                        v.tensor_copy(carr[:, u:u + 1], h_pp[k % 2][:, 0:1])
                    else:
                        v.tensor_tensor_scan(h_pp[k % 2], a_pp[k % 2], b_pp[k % 2],
                                             init, ALU.mult, ALU.add).then_inc(dve.h, 1)
                        v.wait_ge(dve.h, dve.n + 1)
                        v.tensor_copy(carr[:, u:u + 1], h_pp[k % 2][:, TQ - 1:TQ])
                    v.tensor_mul(hc_pp[k % 2], h_pp[k % 2], crep[n]).then_inc(dve.h, 1)
                dve.inc(2)
                scan_done[k] = dve.n
                @block.tensor
                def _(t, k=k, dt=dt, n=n, hi=hi, psY=psY):
                    t.wait_ge(dve.h, scan_done[k])
                    if n == 0:
                        pv = prev_evac.get(dt % 2)
                        if pv is not None:
                            t.wait_ge(act.h, pv)
                        if k == 0:
                            t.wait_ge(p.sem("init").h, p._z1)
                            t.wait_ge(dA.h, consts_loaded)
                    last = None
                    for tc in range(TQ // 512):
                        last = t.matmul(psY[:, tc * 512:(tc + 1) * 512], idt[:],
                                        hc_pp[k % 2][:, tc * 512:(tc + 1) * 512],
                                        start=(n == 0), stop=(n == NS - 1))
                    last.then_inc(pe.h, 1)
                pe.inc(1)
                yma[k] = pe.n

            klast = hi * 32 + dt * 16 + NS - 1
            act.inc(1)
            ev = act.n
            dB.inc(16)
            stq = dB.n
            @block.scalar
            def _(s, hi=hi, H=H, dt=dt, klast=klast, ev=ev, psY=psums[dt % 2]):
                s.wait_ge(pe.h, yma[klast])
                if (hi * 2 + dt) >= 1:
                    s.wait_ge(dB.h, 16 * ystore[(hi * 2 + dt) - 1])
                s.copy(ystage[0], psY[:]).then_inc(act.h, 1)
                s.wait_ge(act.h, ev)
                s.dma_start(out=o_y[H * 256 + dt * 128:H * 256 + (dt + 1) * 128, 0:TQ],
                            in_=ystage[0]).then_inc(dB.h, 16)
            prev_evac[dt % 2] = ev
            evac_tok[(hi, dt)] = ev
            ystore[hi * 2 + dt] = dB.n // 16
        scan_done_list.extend([scan_done[hi * 32 + i] for i in range(32)])

    @block.gpsimd
    def _(g):
        g.wait_ge(dB.h, dB.n)




def phase_l0a_out(p, io):
    n_dt = DI // 128
    n_do = D // 128
    xT = io["xT"]
    i_y = io["i_y"]
    i_xb = io["i_xb"]
    i_sz = io["i_sz"]
    dp128 = io["dp128"]
    W_out = io["W_out"]
    o_x = io["o_x"]

    fa = Arena(p.sbuf("fa", [128, 30 * 1024], F32))
    ba = Arena(p.sbuf("ba", [128, 39 * 1024], BF16))
    psA = p.psum("psA", [128, TC], F32)
    psB = p.psum("psB", [128, TC], F32)

    x_t = [fa.alloc(TC) for _ in range(n_do)]
    x2_pp = [fa.alloc(TC) for _ in range(2)]
    reps = fa.alloc(2 * TC)
    rows = fa.alloc(3 * TC, rows=1)
    ostage = x2_pp[1]
    g_t = [ba.alloc(TC) for _ in range(n_dt)]
    in_pp = [ba.alloc(TC) for _ in range(3)]
    wbufs = [p.sbuf(f"wb{i}", [128, 128 * n_dt], BF16) for i in range(2)]
    dpt = p.sbuf("dpt", [128, n_dt], F32)
    ones128 = p.sbuf("ones128", [128, 128], F32)
    gp_scr_t = p.sbuf("gp_scr", [128, 4], BF16)

    p.begin(consts=(1.0, EPS), zero_psums=(psA, psB))
    block = p.block
    dA, dB, dC = p.sem("dA"), p.sem("dB"), p.sem("dC")
    dve, act, pe = p.sem("dve"), p.sem("act"), p.sem("pe")
    p._gp_scr = gp_scr_t[:]

    @block.sync
    def _(s):
        for i in range(n_do):
            s.dma_start(out=x_t[i], in_=xT[i * 128:(i + 1) * 128, :]).then_inc(dA.h, 16)
        s.dma_start(out=dpt[:], in_=dp128[:]).then_inc(dA.h, 16)
    dA.n += 16 * (n_do + 1)
    x_loaded = (dA, dA.n)

    @block.vector
    def _(v):
        v.wait_ge(p.sem("init").h, p._z1)
        v.memset(ones128[:], 1.0).then_inc(p.sem("init").h, 1)
    p.sem("init").inc(1)
    init_tok = (p.sem("init"), p._z1 + 1)

    R = lambda i: rows[0:1, i * TC:(i + 1) * TC]
    mu, var, r0 = R(0), R(1), R(2)
    m0 = R(1)
    emit_ln_stats(p, x_t, x2_pp, psA[:, :], psB[:, :], ones128[:],
                  x_loaded, mu, var, R(2), init_tok)
    t0 = emit_rsqrt_row(p, r0, var, p.constf(EPS, 0, 1), p.sem("dve").tok)
    @block.vector
    def _(v):
        v.wait_ge(t0[0].h, t0[1])
        v.tensor_mul(m0, mu, r0).then_inc(dve.h, 1)
    dve.inc(1)
    rows_tok = dve.tok

    @block.gpsimd
    def _(g):
        g.wait_ge(rows_tok[0].h, rows_tok[1])
        rep_row(g, reps[:, 0:TC], r0).then_inc(dC.h, 16)
        rep_row(g, reps[:, TC:2 * TC], m0).then_inc(dC.h, 16)
    dC.inc(32)
    reps_tok = (dC, dC.n)

    @block.vector
    def _(v):
        v.wait_ge(reps_tok[0].h, reps_tok[1])
        last = None
        for i in range(n_do):
            v.tensor_mul(x2_pp[0], x_t[i], reps[:, 0:TC])
            v.tensor_add(x_t[i], x_t[i], x2_pp[0])
            last = v.tensor_sub(x_t[i], x_t[i], reps[:, TC:2 * TC])
        last.then_inc(dve.h, 1)
    dve.inc(1)
    xp_tok = dve.tok

    gtoks = {}
    ld_tok = {}
    for dt in range(n_dt):
        buf = [in_pp[j] for j in range(3)]
        @block.sync
        def _(s, dt=dt, buf=buf):
            if dt >= 1:
                s.wait_ge(dve.h, gtoks[dt - 1])
            s.dma_start(out=buf[0], in_=i_y[dt * 128:(dt + 1) * 128, :]).then_inc(dA.h, 16)
            s.dma_start(out=buf[1], in_=i_xb[dt * 128:(dt + 1) * 128, :]).then_inc(dA.h, 16)
            s.dma_start(out=buf[2], in_=i_sz[dt * 128:(dt + 1) * 128, :]).then_inc(dA.h, 16)
        dA.inc(48)
        ld_tok[dt] = dA.n
        @block.vector
        def _(v, dt=dt, buf=buf):
            v.wait_ge(dA.h, ld_tok[dt])
            v.scalar_tensor_tensor(x2_pp[1], buf[1], dpt[:, dt:dt + 1], buf[0],
                                   ALU.mult, ALU.add)
            v.tensor_mul(g_t[dt], x2_pp[1], buf[2]).then_inc(dve.h, 1)
        dve.inc(1)
        gtoks[dt] = dve.n
    g_all = (dve, dve.n)

    store_tok = {}

    def epi_out(do, ps, mm_tok):
        dve_l, dB_l = p.sem("dve"), p.sem("dB")
        @block.vector
        def _(v, do=do, ps=ps, mm_tok=mm_tok):
            v.wait_ge(mm_tok[0].h, mm_tok[1])
            if do >= 1:
                v.wait_ge(dB_l.h, 16 * store_tok[do - 1])
            v.tensor_add(ostage, ps[:], x_t[do]).then_inc(dve_l.h, 1)
        dve_l.inc(1)
        ps_free = (dve_l, dve_l.n)
        @block.scalar
        def _(s, do=do, psf=ps_free[1]):
            s.wait_ge(dve_l.h, psf)
            s.dma_start(out=o_x[do * 128:(do + 1) * 128, :], in_=ostage).then_inc(dB_l.h, 16)
        dB_l.inc(16)
        store_tok[do] = dB_l.n // 16
        return ps_free

    emit_proj(p, W_out, [g_t[i] for i in range(n_dt)], n_do, wbufs,
              [psA, psB], epi_out, g_all, extra_in=xp_tok)

    @block.gpsimd
    def _(g):
        g.wait_ge(dB.h, p.sem("dB").n)




def emit_ln_streamed(p, xT, n_k, T, xs_pp, sq_pp, psA_l, psB_l, ones_col, init_tok,
                     xn_t, reps, rows_mu, rows_var, rows_r0, rows_m0, eps_ap,
                     gb_cols=None, extra_scale_row=None):
    block = p.block
    dA, dC = p.sem("dA"), p.sem("dC")
    dve, act, pe = p.sem("dve"), p.sem("act"), p.sem("pe")
    tch = tchunks_of(T)
    sq_done = {}
    mm_done = {}
    ld = {}
    for i in range(n_k):
        @block.sync
        def _(s, i=i):
            if i >= 2:
                s.wait_ge(pe.h, mm_done[i - 2])
            s.dma_start(out=xs_pp[i % 2], in_=xT[i * 128:(i + 1) * 128, :]).then_inc(dA.h, 16)
        dA.inc(16)
        ld[i] = dA.n
        @block.scalar
        def _(s, i=i):
            s.wait_ge(dA.h, ld[i])
            if i >= 2:
                s.wait_ge(pe.h, mm_done[i - 2])
            s.activation(sq_pp[i % 2], xs_pp[i % 2], AF.Square).then_inc(act.h, 1)
        act.inc(1)
        sq_done[i] = act.n
        @block.tensor
        def _(t, i=i):
            if i == 0:
                t.wait_ge(init_tok[0].h, init_tok[1])
            t.wait_ge(act.h, sq_done[i])
            last = None
            for ci, (t0, t1) in enumerate(tch):
                t.matmul(psA_l[ci][:, 0:t1 - t0], ones_col, xs_pp[i % 2][:, t0:t1],
                         start=(i == 0), stop=(i == n_k - 1))
                last = t.matmul(psB_l[ci][:, 0:t1 - t0], ones_col, sq_pp[i % 2][:, t0:t1],
                                start=(i == 0), stop=(i == n_k - 1))
            last.then_inc(pe.h, 1)
        pe.inc(1)
        mm_done[i] = pe.n
    stats_done = pe.n
    inv_n = 1.0 / (128 * n_k)
    @block.vector
    def _(v):
        v.wait_ge(pe.h, stats_done)
        for ci, (t0, t1) in enumerate(tch):
            v.tensor_scalar(rows_mu[:, t0:t1], psA_l[ci][0:1, 0:t1 - t0], inv_n, None, ALU.mult)
            v.tensor_scalar(rows_var[:, t0:t1], psB_l[ci][0:1, 0:t1 - t0], inv_n, None, ALU.mult)
        v.tensor_mul(rows_r0, rows_mu, rows_mu)
        v.tensor_sub(rows_var, rows_var, rows_r0).then_inc(dve.h, 1)
    dve.inc(1)
    t0 = emit_rsqrt_row(p, rows_r0, rows_var, eps_ap, p.sem("dve").tok)
    @block.vector
    def _(v):
        v.wait_ge(t0[0].h, t0[1])
        v.tensor_mul(rows_m0, rows_mu, rows_r0).then_inc(dve.h, 1)
    dve.inc(1)
    rows_tok = dve.tok
    @block.gpsimd
    def _(g):
        g.wait_ge(rows_tok[0].h, rows_tok[1])
        rep_row(g, reps[:, 0:T], rows_r0).then_inc(dC.h, 16)
        rep_row(g, reps[:, T:2 * T], rows_m0).then_inc(dC.h, 16)
    dC.inc(32)
    reps_tok = dC.n
    ap_done = {}
    for i in range(n_k):
        @block.sync
        def _(s, i=i):
            if i == 0:
                s.wait_ge(pe.h, stats_done)
            if i >= 2:
                s.wait_ge(dve.h, ap_done[i - 2])
            s.dma_start(out=xs_pp[i % 2], in_=xT[i * 128:(i + 1) * 128, :]).then_inc(dA.h, 16)
        dA.inc(16)
        l2 = dA.n
        @block.vector
        def _(v, i=i, l2=l2):
            v.wait_ge(dA.h, l2)
            v.wait_ge(dC.h, reps_tok)
            v.tensor_mul(sq_pp[i % 2], xs_pp[i % 2], reps[:, 0:T])
            last = v.tensor_sub(xn_t[i], sq_pp[i % 2], reps[:, T:2 * T])
            last.then_inc(dve.h, 1)
        dve.inc(1)
        ap_done[i] = dve.n
    return (p.sem("dve"), p.sem("dve").n)


def phase_ffn(p, io):
    DH = 4096
    n_kD, n_dh = D // 128, DH // 128
    TH = 512
    NTH = TC // TH
    xT = io["xT"]
    w12 = io["w12"]
    s12 = io["s12"]
    h1T = io["h1T"]
    h1s = io["h1s"]
    h2T = io["h2T"]
    h2s = io["h2s"]
    w3T = io["w3T"]
    w3s = io["w3s"]
    o_x = io["o_x"]

    fa = Arena(p.sbuf("fa", [128, 21 * 1024], F32))
    ba = Arena(p.sbuf("ba", [128, 49 * 1024], BF16))
    psall = [p.psum(f"ps{i}", [128, 512], F32) for i in range(8)]

    xs_pp = [fa.alloc(TC) for _ in range(2)]
    sq_pp = [fa.alloc(TC) for _ in range(2)]
    reps = fa.alloc(2 * TC)
    rows = fa.alloc(3 * TC, rows=1)
    silu_scr = fa.alloc(TH)
    ostage = fa.alloc(TH)
    xn_t = [ba.alloc(TC) for _ in range(n_kD)]
    h_t = [ba.alloc(TH) for _ in range(n_dh)]
    g_t = [ba.alloc(TH) for _ in range(n_dh)]
    wb12 = [p.sbuf(f"wb12_{i}", [128, 2 * 128 * n_kD], BF16) for i in range(2)]
    wbh = [p.sbuf(f"wbh{i}", [128, 128 * n_dh], BF16) for i in range(2)]
    sc = p.sbuf("sc", [128, 4 * n_dh + n_kD], F32)
    ones128 = p.sbuf("ones128", [128, 128], F32)
    gp_scr_t = p.sbuf("gp_scr", [128, 4], BF16)

    p.begin(consts=(1.0, EPS), zero_psums=psall)
    block = p.block
    dA, dB, dC = p.sem("dA"), p.sem("dB"), p.sem("dC")
    dve, act, pe = p.sem("dve"), p.sem("act"), p.sem("pe")
    p._gp_scr = gp_scr_t[:]

    @block.sync
    def _(s):
        s.dma_start(out=sc[:, 0:2 * n_dh], in_=s12[:]).then_inc(dA.h, 16)
        s.dma_start(out=sc[:, 2 * n_dh:3 * n_dh], in_=h1s[:]).then_inc(dA.h, 16)
        s.dma_start(out=sc[:, 3 * n_dh:4 * n_dh], in_=h2s[:]).then_inc(dA.h, 16)
        s.dma_start(out=sc[:, 4 * n_dh:4 * n_dh + n_kD], in_=w3s[:]).then_inc(dA.h, 16)
    dA.inc(64)
    sc_loaded = (dA, dA.n)

    @block.vector
    def _(v):
        v.wait_ge(p.sem("init").h, p._z1)
        v.memset(ones128[:], 1.0).then_inc(p.sem("init").h, 1)
    p.sem("init").inc(1)
    init_tok = (p.sem("init"), p._z1 + 1)

    R = lambda i: rows[0:1, i * TC:(i + 1) * TC]
    xn_tok = emit_ln_streamed(p, xT, n_kD, TC, xs_pp, sq_pp, psall[0:4], psall[4:8],
                              ones128[:], init_tok, xn_t, reps,
                              R(0), R(1), R(2), R(1), p.constf(EPS, 0, 1))

    prev_tail = [xn_tok, sc_loaded]
    add_done = {}
    store_tok = {}
    for th in range(NTH):
        tsl = slice(th * TH, (th + 1) * TH)
        xn_h = [t[:, tsl] for t in xn_t]
        h_toks = {}

        def epi12(do, psa, psb, mm_tok, th=th):
            act_l, dve_l = p.sem("act"), p.sem("dve")
            @block.scalar
            def _(s, do=do, psa=psa, mm_tok=mm_tok):
                s.wait_ge(mm_tok[0].h, mm_tok[1])
                if do >= 2:
                    s.wait_ge(dve_l.h, h_toks[do - 2])
                s.activation(silu_scr, psa[:], AF.Silu,
                             scale=sc[:, do:do + 1]).then_inc(act_l.h, 1)
            act_l.inc(1)
            silu_done = act_l.n
            @block.vector
            def _(v, do=do, psb=psb, silu_done=silu_done):
                v.wait_ge(act_l.h, silu_done)
                v.scalar_tensor_tensor(h_t[do][:], psb[:], sc[:, n_dh + do:n_dh + do + 1],
                                       silu_scr, ALU.mult, ALU.mult).then_inc(dve_l.h, 1)
            dve_l.inc(1)
            h_toks[do] = dve_l.n
            return (dve_l, dve_l.n)

        cons = emit_proj12(p, w12, xn_h, n_dh, wb12, [psall[0], psall[2]],
                           [psall[1], psall[3]], epi12, prev_tail[0],
                           prev_tail=prev_tail)
        h_all = (p.sem("dve"), p.sem("dve").n)
        p12_tail = [cons[n_dh - 2], cons[n_dh - 1]]

        def mk_epi_gelu(dst_tiles, scol0, toks):
            def epi(do, ps, mm_tok):
                act_l = p.sem("act")
                @block.scalar
                def _(s, do=do, ps=ps, mm_tok=mm_tok):
                    s.wait_ge(mm_tok[0].h, mm_tok[1])
                    s.activation(dst_tiles[do][:], ps[:], AF.Gelu,
                                 scale=sc[:, scol0 + do:scol0 + do + 1]).then_inc(act_l.h, 1)
                act_l.inc(1)
                toks[do] = act_l.n
                return (act_l, act_l.n)
            return epi

        g1_toks = {}
        cons = emit_proj(p, h1T, [t[:] for t in h_t], n_dh, wbh, [psall[4], psall[5]],
                         mk_epi_gelu(g_t, 2 * n_dh, g1_toks), h_all,
                         prev_tail=p12_tail)
        g1_all = (p.sem("act"), p.sem("act").n)
        g1_tail = [cons[n_dh - 2], cons[n_dh - 1]]
        g2_toks = {}
        cons = emit_proj(p, h2T, [t[:] for t in g_t], n_dh, wbh, [psall[6], psall[7]],
                         mk_epi_gelu(h_t, 3 * n_dh, g2_toks), g1_all,
                         prev_tail=g1_tail)
        g2_all = (p.sem("act"), p.sem("act").n)
        g2_tail = [cons[n_dh - 2], cons[n_dh - 1]]

        def epi_out(do, ps, mm_tok, th=th):
            dve_l, dB_l, dA_l = p.sem("dve"), p.sem("dB"), p.sem("dA")
            xr = xs_pp[do % 2][:, 0:TH]
            gi = th * n_kD + do
            @block.sync
            def _(s, do=do, th=th, xr=xr, gi=gi):
                if gi >= 2:
                    s.wait_ge(dve_l.h, add_done[gi - 2])
                else:
                    s.wait_ge(xn_tok[0].h, xn_tok[1])
                s.dma_start(out=xr, in_=xT[do * 128:(do + 1) * 128, th * TH:(th + 1) * TH]) \
                    .then_inc(dA_l.h, 16)
            dA_l.inc(16)
            xl = dA_l.n
            @block.vector
            def _(v, do=do, ps=ps, mm_tok=mm_tok, xl=xl, xr=xr, gi=gi):
                v.wait_ge(mm_tok[0].h, mm_tok[1])
                v.wait_ge(dA_l.h, xl)
                if gi >= 1:
                    v.wait_ge(dB_l.h, 16 * store_tok[gi - 1])
                v.scalar_tensor_tensor(ostage, ps[:], sc[:, 4 * n_dh + do:4 * n_dh + do + 1],
                                       xr, ALU.mult, ALU.add).then_inc(dve_l.h, 1)
            dve_l.inc(1)
            add_done[gi] = dve_l.n
            ps_free = (dve_l, dve_l.n)
            @block.scalar
            def _(s, do=do, th=th, psf=ps_free[1], gi=gi):
                s.wait_ge(dve_l.h, psf)
                s.dma_start(out=o_x[do * 128:(do + 1) * 128, th * TH:(th + 1) * TH],
                            in_=ostage).then_inc(dB_l.h, 16)
            dB_l.inc(16)
            store_tok[gi] = dB_l.n // 16
            return ps_free

        cons = emit_proj(p, w3T, [t[:] for t in h_t], n_kD, wbh, [psall[4], psall[5]],
                         epi_out, g2_all, prev_tail=g2_tail)
        prev_tail = [cons[n_kD - 2], cons[n_kD - 1]]

    @block.gpsimd
    def _(g):
        g.wait_ge(p.sem("dB").h, p.sem("dB").n)


def emit_proj12(p, w_dram, in_tiles, n_do, wbufs, psumsA, psumsB, epilogue, in_ready,
                 prev_tail=None):
    block = p.block
    pe, dC, pool = p.sem("pe"), p.sem("dC"), p.sem("pool")
    n_k = len(in_tiles)
    T = in_tiles[0].shape[-1]
    tch = tchunks_of(T)
    DHcols = w_dram.shape[-1] // 2
    gp_scr = p._gp_scr

    def w_ap(do, half):
        c0 = half * DHcols + do * 128
        return w_dram[:, c0:c0 + 128].rearrange("(ki p) c -> p ki c", p=128)

    consumed = {}
    load_tok = {}

    def emit_load(do, buf_idx):
        @block.gpsimd
        def _(g, do=do, buf_idx=buf_idx):
            if do >= 2:
                tok = consumed[do - 2]
                g.wait_ge(tok[0].h, tok[1])
            elif prev_tail:
                for tok in prev_tail:
                    g.wait_ge(tok[0].h, tok[1])
            g.dma_start(out=wbufs[buf_idx][:, 0:128 * n_k]
                        .rearrange("p (ki c) -> p ki c", c=128),
                        in_=w_ap(do, 0)).then_inc(dC.h, 16)
            g.dma_start(out=wbufs[buf_idx][:, 128 * n_k:256 * n_k]
                        .rearrange("p (ki c) -> p ki c", c=128),
                        in_=w_ap(do, 1)).then_inc(dC.h, 16)
            g.wait_ge(dC.h, dC.n + 32)
            g.tensor_copy(gp_scr[:, 0:4],
                          wbufs[buf_idx][:, 256 * n_k - 4:256 * n_k]).then_inc(pool.h, 1)
        dC.inc(32)
        pool.inc(1)
        load_tok[do] = pool.n

    for do in range(min(2, n_do)):
        emit_load(do, do)
    for do in range(n_do):
        psa, psb = psumsA[do % 2], psumsB[do % 2]
        @block.tensor
        def _(t, do=do, psa=psa, psb=psb):
            t.wait_ge(pool.h, load_tok[do])
            if do == 0:
                if "init" in p.sems:
                    t.wait_ge(p.sems["init"].h, p._z1)
                t.wait_ge(in_ready[0].h, in_ready[1])
                if prev_tail:
                    for tok in prev_tail:
                        t.wait_ge(tok[0].h, tok[1])
            if do >= 2:
                tok = consumed[do - 2]
                t.wait_ge(tok[0].h, tok[1])
            last = None
            for (t0, t1) in tch:
                for ki in range(n_k):
                    t.matmul(psa[:, t0:t1], wbufs[do % 2][:, ki * 128:(ki + 1) * 128],
                             in_tiles[ki][:, t0:t1], start=(ki == 0), stop=(ki == n_k - 1))
                for ki in range(n_k):
                    last = t.matmul(psb[:, t0:t1],
                                    wbufs[do % 2][:, 128 * n_k + ki * 128:128 * n_k + (ki + 1) * 128],
                                    in_tiles[ki][:, t0:t1], start=(ki == 0), stop=(ki == n_k - 1))
            last.then_inc(pe.h, 1)
        pe.inc(1)
        consumed[do] = epilogue(do, psa, psb, pe.tok)
        if do + 2 < n_do:
            emit_load(do + 2, do % 2)
    return consumed




def phase_tokB_b(p, io):
    n_dt = DI // 128
    i_body = io["i_xbr_body"]
    edgeAG = io["edgeAG"]
    hmaskL_in = io["hmaskL"]
    hmaskR_in = io["hmaskR"]
    convw_f = io["convw_f"]
    convw_r = io["convw_r"]
    cb4 = io["cb4"]
    xprojT_f = io["xprojT_f"]
    xprojT_r = io["xprojT_r"]
    dtwT_f = io["dtwT_f"]
    dtwT_r = io["dtwT_r"]
    outs = {}
    for br in ("f", "r"):
        outs[f"xb_{br}"] = io[f"o_xb_{br}"]
        outs[f"delta_{br}"] = io[f"o_delta_{br}"]
        outs[f"bc_{br}"] = io[f"o_bc_{br}"]

    fa = Arena(p.sbuf("fa", [128, 12 * 1024], F32))
    ba = Arena(p.sbuf("ba", [128, 65 * 1024], BF16))
    psA = p.psum("psA", [128, TC], F32)
    psB = p.psum("psB", [128, TC], F32)

    xbr_t = [ba.alloc(6 + TC) for _ in range(n_dt)]
    bmark = ba.mark()
    cacc = fa.alloc(TC)
    sp1, sp2 = fa.alloc(TC), fa.alloc(TC)
    xdbl = p.sbuf("xdbl", [96, TC], BF16)[:, :]
    cwf = p.sbuf("cwf", [128, n_dt * 4], F32)
    cwr = p.sbuf("cwr", [128, n_dt * 4], F32)
    cbt = p.sbuf("cbt", [128, 4 * n_dt], F32)
    ehL = p.sbuf("ehL", [128, n_dt * 12], BF16)
    ehR = p.sbuf("ehR", [128, n_dt * 12], BF16)
    hmL = p.sbuf("hmL", [128, 4], F32)
    hmR = p.sbuf("hmR", [128, 4], F32)
    wpj = p.sbuf("wpj", [128, 16 * 96], BF16)
    wdt = p.sbuf("wdt", [64, DI], BF16)
    gp_scr_t = p.sbuf("gp_scr", [128, 4], BF16)

    p.begin(consts=(1.0,), zero_psums=(psA, psB))
    block = p.block
    dA, dB, dC = p.sem("dA"), p.sem("dB"), p.sem("dC")
    dve, act, pe, pool = p.sem("dve"), p.sem("act"), p.sem("pe"), p.sem("pool")
    p._gp_scr = gp_scr_t[:]

    @block.sync
    def _(s):
        for i in range(n_dt):
            s.dma_start(out=xbr_t[i][:, 3:3 + TC],
                        in_=i_body[i * 128:(i + 1) * 128, :]).then_inc(dA.h, 16)
            for j in range(4):
                s.dma_start(out=ehL[:, i * 12 + j * 3:i * 12 + (j + 1) * 3],
                            in_=edgeAG[j * DI + i * 128:j * DI + (i + 1) * 128, 3:6]) \
                    .then_inc(dA.h, 16)
                s.dma_start(out=ehR[:, i * 12 + j * 3:i * 12 + (j + 1) * 3],
                            in_=edgeAG[j * DI + i * 128:j * DI + (i + 1) * 128, 0:3]) \
                    .then_inc(dA.h, 16)
        s.dma_start(out=cwf[:], in_=convw_f[:]).then_inc(dA.h, 16)
        s.dma_start(out=cwr[:], in_=convw_r[:]).then_inc(dA.h, 16)
        s.dma_start(out=cbt[:], in_=cb4[:]).then_inc(dA.h, 16)
        s.dma_start(out=hmL[:], in_=hmaskL_in[:]).then_inc(dA.h, 16)
        s.dma_start(out=hmR[:], in_=hmaskR_in[:]).then_inc(dA.h, 16)
    dA.n += 16 * (9 * n_dt + 5)
    all_ld = dA.n
    @block.vector
    def _(v):
        v.wait_ge(dA.h, all_ld)
        last = None
        for i in range(n_dt):
            for (eh, hm_, c0) in ((ehL, hmL, 0), (ehR, hmR, 3 + TC)):
                v.tensor_scalar(cacc[:, 0:3], eh[:, i * 12:i * 12 + 3],
                                hm_[:, 0:1], None, ALU.mult)
                for j in range(1, 4):
                    v.scalar_tensor_tensor(cacc[:, 0:3],
                                           eh[:, i * 12 + j * 3:i * 12 + (j + 1) * 3],
                                           hm_[:, j:j + 1], cacc[:, 0:3],
                                           ALU.mult, ALU.add)
                last = v.tensor_copy(xbr_t[i][:, c0:c0 + 3], cacc[:, 0:3])
        last.then_inc(dve.h, 1)
    dve.inc(1)
    loaded = (dve, dve.n)

    for bi, br in enumerate(("f", "r")):
        mark = ba.mark()
        xb_t = [ba.alloc(TC) for _ in range(n_dt)]
        cw = cwf if br == "f" else cwr
        halo_off = 0 if br == "f" else 3
        xb_toks = emit_conv_silu(p, [t[:, halo_off:halo_off + 3 + TC] for t in xbr_t],
                                 xb_t, cw[:], cbt[:, bi * n_dt:(bi + 1) * n_dt], cacc,
                                 [loaded] * n_dt, o_xb=outs[f"xb_{br}"])
        xb_all = xb_toks[-1]

        @block.gpsimd
        def _(g, br=br):
            if br == "r":
                g.wait_ge(pe.h, pe.n)
            g.dma_start(out=wpj[:].rearrange("p (ki c) -> p ki c", c=96),
                        in_=(xprojT_f if br == "f" else xprojT_r)
                        .rearrange("(ki p) c -> p ki c", p=128)).then_inc(dC.h, 16)
            g.dma_start(out=wdt[:], in_=(dtwT_f if br == "f" else dtwT_r)) \
                .then_inc(dC.h, 16)
            g.wait_ge(dC.h, dC.n + 32)
            g.tensor_copy(gp_scr_t[:, 0:4], wpj[:, 15 * 96 + 92:15 * 96 + 96]) \
                .then_inc(pool.h, 1)
            g.tensor_copy(gp_scr_t[0:64, 0:4], wdt[:, DI - 4:DI]).then_inc(pool.h, 1)
        dC.inc(32)
        pool.inc(2)
        w_tok = pool.n

        @block.tensor
        def _(t, br=br, w_tok=w_tok, xb_all=xb_all, xb_t=xb_t):
            t.wait_ge(pool.h, w_tok)
            t.wait_ge(xb_all[0].h, xb_all[1])
            if br == "f":
                t.wait_ge(p.sem("init").h, p._z1)
            last = None
            for (t0, t1) in tchunks_of(TC):
                for ki in range(n_dt):
                    last = t.matmul(psA[0:96, t0:t1], wpj[:, ki * 96:(ki + 1) * 96],
                                    xb_t[ki][:, t0:t1], start=(ki == 0), stop=(ki == n_dt - 1))
            last.then_inc(pe.h, 1)
        pe.inc(1)
        xp_done = pe.n

        @block.vector
        def _(v, xp_done=xp_done):
            v.wait_ge(pe.h, xp_done)
            v.tensor_copy(xdbl, psA[0:96, :]).then_inc(dve.h, 1)
        dve.inc(1)
        xd_tok = dve.n
        @block.scalar
        def _(s, br=br, xd_tok=xd_tok):
            s.wait_ge(dve.h, xd_tok)
            s.dma_start(out=outs[f"bc_{br}"][:], in_=xdbl[64:96, :]).then_inc(dB.h, 16)
        dB.inc(16)

        store_tok = {}
        sp_guard = {}
        cons_d = {}
        for do in range(n_dt):
            @block.tensor
            def _(t, do=do, xd_tok=xd_tok):
                t.wait_ge(dve.h, xd_tok)
                if do >= 2:
                    tok = cons_d[do - 2]
                    t.wait_ge(tok[0].h, tok[1])
                last = None
                for (t0, t1) in tchunks_of(TC):
                    last = t.matmul((psB if do % 2 == 0 else psA)[:, t0:t1],
                                    wdt[:, do * 128:(do + 1) * 128],
                                    xdbl[0:64, t0:t1], start=True, stop=True)
                last.then_inc(pe.h, 1)
            pe.inc(1)
            mm_tok = pe.tok
            ps = psB if do % 2 == 0 else psA
            st = None
            act_l, dve_l, dB_l = act, dve, dB
            prev_sum = sp_guard.get(do - 1)
            @block.scalar
            def _(s, do=do, ps=ps, mm_tok=mm_tok, prev_sum=prev_sum, bi=bi):
                s.wait_ge(mm_tok[0].h, mm_tok[1])
                if prev_sum is not None:
                    s.wait_ge(dve_l.h, prev_sum)
                bcol = cbt[:, (2 + bi) * n_dt + do:(2 + bi) * n_dt + do + 1]
                s.activation(sp1, ps[:], AF.Relu, bias=bcol)
                s.activation(sp2, ps[:], AF.Abs, bias=bcol).then_inc(act_l.h, 1)
                s.activation(sp2, sp2, AF.Exp, scale=-1.0)
                s.activation(sp2, sp2, AF.Ln, bias=1.0).then_inc(act_l.h, 1)
            act_l.inc(2)
            ps_free = (act_l, act_l.n - 1)
            ln_done = act_l.n
            dstage = xb_t[do]
            @block.vector
            def _(v, do=do, ln_done=ln_done, dstage=dstage):
                v.wait_ge(act_l.h, ln_done)
                v.tensor_add(dstage[:], sp1, sp2).then_inc(dve_l.h, 1)
            dve_l.inc(1)
            sp_guard[do] = dve_l.n
            sum_done = dve_l.n
            @block.scalar
            def _(s, do=do, br=br, sum_done=sum_done, dstage=dstage):
                s.wait_ge(dve_l.h, sum_done)
                s.dma_start(out=outs[f"delta_{br}"][do * 128:(do + 1) * 128, :],
                            in_=dstage[:]).then_inc(dB_l.h, 16)
            dB_l.inc(16)
            store_tok[do] = dB_l.n
            cons_d[do] = ps_free
        ba.release(mark)

    @block.gpsimd
    def _(g):
        g.wait_ge(dB.h, dB.n)




def phase_l0b_out(p, io):
    n_dt = DI // 128
    n_do = D // 128
    xT = io["xT"]
    i_y_f = io["i_y_f"]
    i_y_r = io["i_y_r"]
    i_xb_f = io["i_xb_f"]
    i_xb_r = io["i_xb_r"]
    i_sz = io["i_sz"]
    d2 = io["d2"]
    W_outT = io["W_outT"]
    o_x = io["o_x"]

    fa = Arena(p.sbuf("fa", [128, 22 * 1024], F32))
    ba = Arena(p.sbuf("ba", [128, 43 * 1024], BF16))
    psA = p.psum("psA", [128, TC], F32)
    psB = p.psum("psB", [128, TC], F32)

    xs_pp = [fa.alloc(TC) for _ in range(2)]
    sq_pp = [fa.alloc(TC) for _ in range(2)]
    reps = fa.alloc(2 * TC)
    rows = fa.alloc(3 * TC, rows=1)
    vscr = fa.alloc(TC)
    ostage = fa.alloc(TC)
    g_t = [ba.alloc(TC) for _ in range(n_dt)]
    in_pp = [ba.alloc(TC) for _ in range(5)]
    wbufs = [p.sbuf(f"wb{i}", [128, 128 * n_dt], BF16) for i in range(2)]
    dpt = p.sbuf("dpt", [128, 2 * n_dt], F32)
    ones128 = p.sbuf("ones128", [128, 128], F32)
    gp_scr_t = p.sbuf("gp_scr", [128, 4], BF16)

    p.begin(consts=(1.0, EPS), zero_psums=(psA, psB))
    block = p.block
    dA, dB, dC = p.sem("dA"), p.sem("dB"), p.sem("dC")
    dve, act, pe = p.sem("dve"), p.sem("act"), p.sem("pe")
    p._gp_scr = gp_scr_t[:]

    @block.sync
    def _(s):
        s.dma_start(out=dpt[:], in_=d2[:]).then_inc(dA.h, 16)
    dA.inc(16)

    @block.vector
    def _(v):
        v.wait_ge(p.sem("init").h, p._z1)
        v.memset(ones128[:], 1.0).then_inc(p.sem("init").h, 1)
    p.sem("init").inc(1)
    init_tok = (p.sem("init"), p._z1 + 1)

    R = lambda i: rows[0:1, i * TC:(i + 1) * TC]
    tch = tchunks_of(TC)
    sq_done, mm_done, ld = {}, {}, {}
    for i in range(n_do):
        @block.sync
        def _(s, i=i):
            if i >= 2:
                s.wait_ge(pe.h, mm_done[i - 2])
            s.dma_start(out=xs_pp[i % 2], in_=xT[i * 128:(i + 1) * 128, :]).then_inc(dA.h, 16)
        dA.inc(16)
        ld[i] = dA.n
        @block.scalar
        def _(s, i=i):
            s.wait_ge(dA.h, ld[i])
            if i >= 2:
                s.wait_ge(pe.h, mm_done[i - 2])
            s.activation(sq_pp[i % 2], xs_pp[i % 2], AF.Square).then_inc(act.h, 1)
        act.inc(1)
        sq_done[i] = act.n
        @block.tensor
        def _(t, i=i):
            if i == 0:
                t.wait_ge(init_tok[0].h, init_tok[1])
            t.wait_ge(act.h, sq_done[i])
            last = None
            for ci, (t0, t1) in enumerate(tch):
                t.matmul(psA[:, t0:t1], ones128[:], xs_pp[i % 2][:, t0:t1],
                         start=(i == 0), stop=(i == n_do - 1))
                last = t.matmul(psB[:, t0:t1], ones128[:], sq_pp[i % 2][:, t0:t1],
                                start=(i == 0), stop=(i == n_do - 1))
            last.then_inc(pe.h, 1)
        pe.inc(1)
        mm_done[i] = pe.n
    stats_done = pe.n
    mu, var, r0 = R(0), R(1), R(2)
    m0 = R(1)
    @block.vector
    def _(v):
        v.wait_ge(pe.h, stats_done)
        v.tensor_scalar(mu, psA[0:1, :], 1.0 / D, None, ALU.mult)
        v.tensor_scalar(var, psB[0:1, :], 1.0 / D, None, ALU.mult)
        v.tensor_mul(r0, mu, mu)
        v.tensor_sub(var, var, r0).then_inc(dve.h, 1)
    dve.inc(1)
    t0tok = emit_rsqrt_row(p, r0, var, p.constf(EPS, 0, 1), p.sem("dve").tok)
    @block.vector
    def _(v):
        v.wait_ge(t0tok[0].h, t0tok[1])
        v.tensor_mul(m0, mu, r0)
        v.tensor_scalar(r0, r0, 1.0, None, ALU.add).then_inc(dve.h, 1)
    dve.inc(1)
    rows_tok = dve.tok
    @block.gpsimd
    def _(g):
        g.wait_ge(rows_tok[0].h, rows_tok[1])
        rep_row(g, reps[:, 0:TC], r0).then_inc(dC.h, 16)
        rep_row(g, reps[:, TC:2 * TC], m0).then_inc(dC.h, 16)
    dC.inc(32)
    reps_tok = (dC, dC.n)

    gtoks, gld = {}, {}
    for dt in range(n_dt):
        buf = in_pp
        @block.sync
        def _(s, dt=dt, buf=buf):
            if dt >= 1:
                s.wait_ge(dve.h, gtoks[dt - 1])
            s.dma_start(out=buf[0], in_=i_y_f[dt * 128:(dt + 1) * 128, :]).then_inc(dA.h, 16)
            s.dma_start(out=buf[1], in_=i_xb_f[dt * 128:(dt + 1) * 128, :]).then_inc(dA.h, 16)
            s.dma_start(out=buf[2], in_=i_y_r[dt * 128:(dt + 1) * 128, :]).then_inc(dA.h, 16)
            s.dma_start(out=buf[3], in_=i_xb_r[dt * 128:(dt + 1) * 128, :]).then_inc(dA.h, 16)
            s.dma_start(out=buf[4], in_=i_sz[dt * 128:(dt + 1) * 128, :]).then_inc(dA.h, 16)
        dA.inc(80)
        gld[dt] = dA.n
        @block.vector
        def _(v, dt=dt, buf=buf):
            v.wait_ge(dA.h, gld[dt])
            v.scalar_tensor_tensor(vscr, buf[1], dpt[:, dt:dt + 1], buf[0],
                                   ALU.mult, ALU.add)
            v.scalar_tensor_tensor(sq_pp[0], buf[3], dpt[:, n_dt + dt:n_dt + dt + 1],
                                   buf[2], ALU.mult, ALU.add)
            v.tensor_add(vscr, vscr, sq_pp[0])
            v.tensor_mul(g_t[dt], vscr, buf[4]).then_inc(dve.h, 1)
        dve.inc(1)
        gtoks[dt] = dve.n
    g_all = (dve, dve.n)

    store_tok, add_done = {}, {}

    def epi_out(do, ps, mm_tok):
        dve_l, dB_l, dA_l = p.sem("dve"), p.sem("dB"), p.sem("dA")
        xr = xs_pp[do % 2]
        @block.sync
        def _(s, do=do, xr=xr):
            if do >= 2:
                s.wait_ge(dve_l.h, add_done[do - 2])
            s.dma_start(out=xr, in_=xT[do * 128:(do + 1) * 128, :]).then_inc(dA_l.h, 16)
        dA_l.inc(16)
        xl = dA_l.n
        @block.vector
        def _(v, do=do, ps=ps, mm_tok=mm_tok, xl=xl, xr=xr):
            v.wait_ge(mm_tok[0].h, mm_tok[1])
            v.wait_ge(dA_l.h, xl)
            v.wait_ge(dC.h, reps_tok[1])
            if do >= 1:
                v.wait_ge(dB_l.h, 16 * store_tok[do - 1])
            v.tensor_mul(ostage, xr, reps[:, 0:TC])
            v.tensor_sub(ostage, ostage, reps[:, TC:2 * TC])
            v.tensor_add(ostage, ostage, ps[:]).then_inc(dve_l.h, 1)
        dve_l.inc(1)
        add_done[do] = dve_l.n
        ps_free = (dve_l, dve_l.n)
        @block.scalar
        def _(s, do=do, psf=ps_free[1]):
            s.wait_ge(dve_l.h, psf)
            s.dma_start(out=o_x[do * 128:(do + 1) * 128, :], in_=ostage).then_inc(dB_l.h, 16)
        dB_l.inc(16)
        store_tok[do] = dB_l.n // 16
        return ps_free

    emit_proj(p, W_outT, [g_t[i] for i in range(n_dt)], n_do, wbufs,
              [psA, psB], epi_out, g_all)

    @block.gpsimd
    def _(g):
        g.wait_ge(dB.h, p.sem("dB").n)



def phase_ln_dump(p, io):
    n_kD = D // 128
    xT = io["xT"]
    o_xn = io["o_xn"]
    o_edge = io["o_edge"]
    fa = Arena(p.sbuf("fa", [128, 19 * 1024], F32))
    ba = Arena(p.sbuf("ba", [128, 17 * 1024], BF16))
    psA = p.psum("psA", [128, TC], F32)
    psB = p.psum("psB", [128, TC], F32)
    xs_pp = [fa.alloc(TC) for _ in range(2)]
    sq_pp = [fa.alloc(TC) for _ in range(2)]
    reps = fa.alloc(2 * TC)
    rows = fa.alloc(3 * TC, rows=1)
    xn_t = [ba.alloc(TC) for _ in range(n_kD)]
    ones128 = p.sbuf("ones128", [128, 128], F32)
    p.begin(consts=(1.0, EPS), zero_psums=(psA, psB))
    block = p.block
    dve, dB = p.sem("dve"), p.sem("dB")
    @block.vector
    def _(v):
        v.wait_ge(p.sem("init").h, p._z1)
        v.memset(ones128[:], 1.0).then_inc(p.sem("init").h, 1)
    p.sem("init").inc(1)
    init_tok = (p.sem("init"), p._z1 + 1)
    R = lambda i: rows[0:1, i * TC:(i + 1) * TC]
    psA_l = [psA[:, i * 512:(i + 1) * 512] for i in range(4)]
    psB_l = [psB[:, i * 512:(i + 1) * 512] for i in range(4)]
    xn_tok = emit_ln_streamed(p, xT, n_kD, TC, xs_pp, sq_pp, psA_l, psB_l,
                              ones128[:], init_tok, xn_t, reps,
                              R(0), R(1), R(2), R(1), p.constf(EPS, 0, 1))
    st = {}
    @block.scalar
    def _(s):
        s.wait_ge(xn_tok[0].h, xn_tok[1])
        for i in range(n_kD):
            s.dma_start(out=o_xn[i * 128:(i + 1) * 128, :], in_=xn_t[i]).then_inc(dB.h, 16)
            s.dma_start(out=o_edge[i * 128:(i + 1) * 128, 0:1],
                        in_=xn_t[i][:, TC - 1:TC]).then_inc(dB.h, 16)
    dB.inc(32 * n_kD)
    @block.gpsimd
    def _(g):
        g.wait_ge(dB.h, dB.n)


def phase_l1f(p, io):
    n_kD = D // 128
    xT = io["xT"]
    i_xnb = io["i_xn_body"]
    edgeAG = io["edgeAG"]
    hmask_in = io["hmask"]
    W_rT = io["W_rT"]
    W_kT = io["W_kT"]
    W_vT = io["W_vT"]
    mu2 = io["mu2"]
    o_x = io["o_x"]

    fa = Arena(p.sbuf("fa", [128, 10 * 1024], F32))
    ba = Arena(p.sbuf("ba", [128, 64 * 1024], BF16))
    psA = p.psum("psA", [128, TC], F32)
    psB = p.psum("psB", [128, TC], F32)

    xs_pp = [fa.alloc(TC) for _ in range(2)]
    ostage = fa.alloc(TC)
    vscr = fa.alloc(TC)
    xr_t = [ba.alloc(TC) for _ in range(n_kD)]
    xk_t = [ba.alloc(TC) for _ in range(n_kD)]
    _xn_mark = ba.mark()
    xn_t = [ba.alloc(1 + TC) for _ in range(n_kD)]
    ba.release(_xn_mark)
    r_t = [ba.alloc(TC) for _ in range(n_kD)]
    k2_t = [ba.alloc(TC) for _ in range(n_kD)]
    assert r_t[0].offset // 1 >= 0
    wbufs = [p.sbuf(f"wb{i}", [128, 128 * n_kD], BF16) for i in range(2)]
    mut = p.sbuf("mut", [128, 4 * n_kD], F32)
    gp_scr_t = p.sbuf("gp_scr", [128, 4], BF16)
    eh1 = p.sbuf("eh1", [128, 4 * n_kD], BF16)
    hm1 = p.sbuf("hm1", [128, 4], F32)

    p.begin(consts=(1.0,), zero_psums=(psA, psB))
    block = p.block
    dA, dB, dC = p.sem("dA"), p.sem("dB"), p.sem("dC")
    dve, act, pe = p.sem("dve"), p.sem("act"), p.sem("pe")
    p._gp_scr = gp_scr_t[:]

    @block.sync
    def _(s):
        for i in range(n_kD):
            s.dma_start(out=xn_t[i][:, 1:1 + TC],
                        in_=i_xnb[i * 128:(i + 1) * 128, :]).then_inc(dA.h, 16)
            for j in range(4):
                s.dma_start(out=eh1[:, i * 4 + j:i * 4 + j + 1],
                            in_=edgeAG[j * D + i * 128:j * D + (i + 1) * 128, 0:1]) \
                    .then_inc(dA.h, 16)
        s.dma_start(out=mut[:, 0:2 * n_kD], in_=mu2[:]).then_inc(dA.h, 16)
        s.dma_start(out=hm1[:], in_=hmask_in[:]).then_inc(dA.h, 16)
    dA.n += 16 * (5 * n_kD + 2)
    loaded = (dA, dA.n)

    @block.vector
    def _(v):
        v.wait_ge(loaded[0].h, loaded[1])
        v.wait_ge(p.sem("init").h, p._z1)
        for i in range(n_kD):
            v.tensor_scalar(vscr[:, 0:1], eh1[:, i * 4:i * 4 + 1],
                            hm1[:, 0:1], None, ALU.mult)
            for j in range(1, 4):
                v.scalar_tensor_tensor(vscr[:, 0:1], eh1[:, i * 4 + j:i * 4 + j + 1],
                                       hm1[:, j:j + 1], vscr[:, 0:1],
                                       ALU.mult, ALU.add)
            v.tensor_copy(xn_t[i][:, 0:1], vscr[:, 0:1])
        v.tensor_scalar(mut[:, 2 * n_kD:4 * n_kD], mut[:, 0:2 * n_kD], -1.0, 1.0,
                        ALU.mult, ALU.add)
        last = None
        for i in range(n_kD):
            cur = xn_t[i][:, 1:1 + TC]
            prv = xn_t[i][:, 0:TC]
            v.tensor_scalar(vscr, cur, mut[:, i:i + 1], None, ALU.mult)
            v.scalar_tensor_tensor(xr_t[i], prv, mut[:, 2 * n_kD + i:2 * n_kD + i + 1],
                                   vscr, ALU.mult, ALU.add)
            v.tensor_scalar(vscr, cur, mut[:, n_kD + i:n_kD + i + 1], None, ALU.mult)
            last = v.scalar_tensor_tensor(xk_t[i], prv,
                                          mut[:, 3 * n_kD + i:3 * n_kD + i + 1],
                                          vscr, ALU.mult, ALU.add)
        last.then_inc(dve.h, 1)
    dve.inc(1)
    mix_tok = (dve, dve.n)

    def epi_r(do, ps, mm_tok):
        act_l = p.sem("act")
        @block.scalar
        def _(s, do=do, ps=ps, mm_tok=mm_tok):
            s.wait_ge(mm_tok[0].h, mm_tok[1])
            s.activation(r_t[do][:], ps[:], AF.Sigmoid).then_inc(act_l.h, 1)
        act_l.inc(1)
        return (act_l, act_l.n)

    cons = emit_proj(p, W_rT, [t[:] for t in xr_t], n_kD, wbufs, [psA, psB],
                     epi_r, mix_tok)
    r_tail = [cons[n_kD - 2], cons[n_kD - 1]]

    def epi_k(do, ps, mm_tok):
        act_l = p.sem("act")
        @block.scalar
        def _(s, do=do, ps=ps, mm_tok=mm_tok):
            s.wait_ge(mm_tok[0].h, mm_tok[1])
            s.activation(vscr, ps[:], AF.Relu).then_inc(act_l.h, 1)
            s.wait_ge(act_l.h, act_l.n + 1)
            s.activation(k2_t[do][:], vscr, AF.Square).then_inc(act_l.h, 1)
        act_l.inc(2)
        return (act_l, act_l.n)

    cons = emit_proj(p, W_kT, [t[:] for t in xk_t], n_kD, wbufs, [psA, psB],
                     epi_k, mix_tok, prev_tail=r_tail)
    k_tail = [cons[n_kD - 2], cons[n_kD - 1]]
    k_all = (p.sem("act"), p.sem("act").n)

    store_tok, add_done = {}, {}

    def epi_v(do, ps, mm_tok):
        dve_l, dB_l, dA_l = p.sem("dve"), p.sem("dB"), p.sem("dA")
        xr = xs_pp[do % 2]
        @block.sync
        def _(s, do=do, xr=xr):
            if do >= 2:
                s.wait_ge(dve_l.h, add_done[do - 2])
            s.dma_start(out=xr, in_=xT[do * 128:(do + 1) * 128, :]).then_inc(dA_l.h, 16)
        dA_l.inc(16)
        xl = dA_l.n
        @block.vector
        def _(v, do=do, ps=ps, mm_tok=mm_tok, xl=xl, xr=xr):
            v.wait_ge(mm_tok[0].h, mm_tok[1])
            v.wait_ge(dA_l.h, xl)
            if do >= 1:
                v.wait_ge(dB_l.h, 16 * store_tok[do - 1])
            v.tensor_mul(ostage, ps[:], r_t[do][:])
            v.tensor_add(ostage, ostage, xr).then_inc(dve_l.h, 1)
        dve_l.inc(1)
        add_done[do] = dve_l.n
        ps_free = (dve_l, dve_l.n)
        @block.scalar
        def _(s, do=do, psf=ps_free[1]):
            s.wait_ge(dve_l.h, psf)
            s.dma_start(out=o_x[do * 128:(do + 1) * 128, :], in_=ostage).then_inc(dB_l.h, 16)
        dB_l.inc(16)
        store_tok[do] = dB_l.n // 16
        return ps_free

    emit_proj(p, W_vT, [t[:] for t in k2_t], n_kD, wbufs, [psA, psB],
              epi_v, k_all, prev_tail=k_tail)

    @block.gpsimd
    def _(g):
        g.wait_ge(dB.h, p.sem("dB").n)




def phase_l1a_xd(p, io):
    n_kD = D // 128
    xT = io["xT"]
    W_down = io["W_down"]
    o_xd = io["o_xd"]
    fa = Arena(p.sbuf("fa", [128, 19 * 1024], F32))
    ba = Arena(p.sbuf("ba", [128, 21 * 1024], BF16))
    psA = p.psum("psA", [128, TC], F32)
    psB = p.psum("psB", [128, TC], F32)
    xs_pp = [fa.alloc(TC) for _ in range(2)]
    sq_pp = [fa.alloc(TC) for _ in range(2)]
    reps = fa.alloc(2 * TC)
    rows = fa.alloc(3 * TC, rows=1)
    xn_t = [ba.alloc(TC) for _ in range(n_kD)]
    stage = [ba.alloc(TC) for _ in range(2)]
    wbufs = [p.sbuf(f"wb{i}", [128, 128 * n_kD], BF16) for i in range(2)]
    ones128 = p.sbuf("ones128", [128, 128], F32)
    gp_scr_t = p.sbuf("gp_scr", [128, 4], BF16)
    p.begin(consts=(1.0, EPS), zero_psums=(psA, psB))
    block = p.block
    dB = p.sem("dB")
    p._gp_scr = gp_scr_t[:]
    @block.vector
    def _(v):
        v.wait_ge(p.sem("init").h, p._z1)
        v.memset(ones128[:], 1.0).then_inc(p.sem("init").h, 1)
    p.sem("init").inc(1)
    init_tok = (p.sem("init"), p._z1 + 1)
    R = lambda i: rows[0:1, i * TC:(i + 1) * TC]
    psA_l = [psA[:, i * 512:(i + 1) * 512] for i in range(4)]
    psB_l = [psB[:, i * 512:(i + 1) * 512] for i in range(4)]
    xn_tok = emit_ln_streamed(p, xT, n_kD, TC, xs_pp, sq_pp, psA_l, psB_l,
                              ones128[:], init_tok, xn_t, reps,
                              R(0), R(1), R(2), R(1), p.constf(EPS, 0, 1))
    store_tok = {}

    def epi(do, ps, mm_tok):
        act_l, dB_l = p.sem("act"), p.sem("dB")
        st = stage[do % 2]
        @block.scalar
        def _(s, do=do, ps=ps, mm_tok=mm_tok, st=st):
            s.wait_ge(mm_tok[0].h, mm_tok[1])
            if do >= 2:
                s.wait_ge(dB_l.h, 16 * store_tok[do - 2])
            s.copy(st, ps[:]).then_inc(act_l.h, 1)
            s.wait_ge(act_l.h, act_l.n + 1)
            s.dma_start(out=o_xd[do * 128:(do + 1) * 128, :], in_=st).then_inc(dB_l.h, 16)
        act_l.inc(1)
        dB_l.inc(16)
        store_tok[do] = dB_l.n // 16
        return (act_l, act_l.n)

    emit_proj(p, W_down, [t[:] for t in xn_t], n_kD, wbufs, [psA, psB], epi, xn_tok)
    @block.gpsimd
    def _(g):
        g.wait_ge(dB.h, p.sem("dB").n)




def phase_l1a_att(p, io):
    HEADS, S = 16, 64
    KC = 2048
    NKC = L // KC
    QG = 1024
    NQG = TC // QG
    SSC = float(S) ** (-0.25)
    SS2 = float(S) ** (-0.5)
    i_q = io["i_q"]
    kvAG = io["kvAG"]
    tkT = io["tkT"]
    tqT = io["tqT"]
    tvT = io["tvT"]
    W_unify = io["W_unify"]
    bu = io["bu"]
    xT = io["xT"]
    o_x = io["o_x"]
    import os as _os
    _dbg = _os.environ.get("ATTDBG", "0") == "1"
    if _dbg:
        o_hs = p.dout("o_hs", (1, 68), F32)
        o_q0 = p.dout("o_q0", (65, 1024), BF16)
        o_k0 = p.dout("o_k0", (65, 2048), BF16)
        o_acc0 = p.dout("o_acc0", (65, 1024), F32)

    n_kD = D // 128
    fa = Arena(p.sbuf("fa", [128, 7 * 1024], F32))
    ba = Arena(p.sbuf("ba", [128, 22 * 1024], BF16))
    psA = p.psum("psA", [128, TC], F32)
    psB = p.psum("psB", [128, TC], F32)

    Kh = [p.sbuf(f"Kh{i}", [65, KC], BF16) for i in range(2)]
    Vt = [p.sbuf(f"Vt{i}", [128, (KC // 128) * 65], BF16) for i in range(2)]
    Qh = [p.sbuf(f"Qh{i}", [65, QG], BF16) for i in range(HEADS)]
    acch = [p.sbuf(f"acc{i}", [65, QG], F32) for i in range(HEADS)]
    P_t = ba.alloc(2048)
    xdc_pp = [ba.alloc(KC) for _ in range(2)]
    qsrc = [ba.alloc(QG) for _ in range(n_kD)]
    ocat = [ba.alloc(QG) for _ in range(n_kD)]
    ostage = fa.alloc(QG)
    orep = fa.alloc(QG)
    xs_pp = [fa.alloc(QG) for _ in range(2)]
    recr = fa.alloc(1024, rows=1)
    tkbuf = p.sbuf("tkbuf", [128, 3 * HEADS * S], BF16)
    hs = p.sbuf("hs", [1, 4 * HEADS + 4], F32)
    wbufs = [p.sbuf(f"wb{i}", [128, 128 * n_kD], BF16) for i in range(2)]
    but = p.sbuf("but", [128, n_kD], F32)
    ones128 = p.sbuf("ones128", [128, 128], F32)
    onesb = p.sbuf("onesb", [128, 128], BF16)
    qsq = p.sbuf("qsq", [64, 1024], BF16)
    sqscr = p.sbuf("sqscr", [64, HEADS * S], F32)
    gp_scr_t = p.sbuf("gp_scr", [128, 4], BF16)

    p.begin(consts=(1.0, EPS, SSC), zero_psums=(psA, psB))
    block = p.block
    dA, dB, dC = p.sem("dA"), p.sem("dB"), p.sem("dC")
    dve, act, pe, pool = p.sem("dve"), p.sem("act"), p.sem("pe"), p.sem("pool")
    p._gp_scr = gp_scr_t[:]

    @block.vector
    def _(v):
        v.wait_ge(p.sem("init").h, p._z1)
        v.memset(ones128[:], 1.0)
        v.memset(onesb[:], 1.0)
        for i in range(2):
            v.memset(Kh[i][64:65, :], -1.0)
            v.memset(Vt[i][:], 0.0)
        last = None
        for i in range(2):
            last = v.memset(
                Vt[i][:].rearrange("p (kt c) -> p kt c", c=65)[:, :, 64:65], 1.0)
        last.then_inc(p.sem("init").h, 1)
    p.sem("init").inc(1)
    init_tok = (p.sem("init"), p._z1 + 1)

    @block.gpsimd
    def _(g):
        for j, w in enumerate((tkT, tqT, tvT)):
            g.dma_start(out=tkbuf[0:64, j * 1024:(j + 1) * 1024], in_=w[:]).then_inc(dA.h, 16)
            g.dma_start(out=tkbuf[64:128, j * 1024:(j + 1) * 1024], in_=w[:]).then_inc(dA.h, 16)
        g.wait_ge(dA.h, dA.n + 96)
        g.tensor_copy(gp_scr_t[:, 0:4], tkbuf[:, 3 * 1024 - 4:3 * 1024]).then_inc(pool.h, 1)
    dA.inc(96)
    pool.inc(1)
    tk_pool_tok = pool.n
    @block.sync
    def _(s):
        s.dma_start(out=but[:], in_=bu[:]).then_inc(dA.h, 16)
    dA.inc(16)
    tk_tok = dA.n

    mm_tok0 = {}
    cp_tok0 = {}
    for q4 in range(4):
        ps = psA if q4 % 2 == 0 else psB
        for i in range(n_kD):
            part = q4 * n_kD + i
            @block.sync
            def _(s, q4=q4, i=i, part=part):
                if part >= 2:
                    s.wait_ge(pe.h, mm_tok0[part - 2])
                s.dma_start(out=xdc_pp[part % 2][:, 0:KC],
                            in_=kvAG[q4 * D + i * 128:q4 * D + (i + 1) * 128,
                                     0:KC]).then_inc(dA.h, 16)
            dA.inc(16)
            ld1 = dA.n
            @block.scalar
            def _(s, part=part, ld1=ld1):
                s.wait_ge(dA.h, ld1)
                if part >= 2:
                    s.wait_ge(pe.h, mm_tok0[part - 2])
                s.activation(xdc_pp[part % 2][:, 0:KC], xdc_pp[part % 2][:, 0:KC],
                             AF.Square).then_inc(act.h, 1)
            act.inc(1)
            sq1 = act.n
            @block.tensor
            def _(t, q4=q4, i=i, part=part, sq1=sq1, ps=ps):
                if part == 0:
                    t.wait_ge(init_tok[0].h, init_tok[1])
                if i == 0 and q4 >= 2:
                    t.wait_ge(dve.h, cp_tok0[q4 - 2])
                t.wait_ge(act.h, sq1)
                last = None
                for tc in range(4):
                    last = t.matmul(ps[:, tc * 512:(tc + 1) * 512],
                                    onesb[:],
                                    xdc_pp[part % 2][:, tc * 512:(tc + 1) * 512],
                                    start=(i == 0), stop=(i == n_kD - 1))
                last.then_inc(pe.h, 1)
            pe.inc(1)
            mm_tok0[part] = pe.n
        done = pe.n
        @block.vector
        def _(v, q4=q4, done=done, ps=ps):
            v.wait_ge(pe.h, done)
            v.tensor_reduce(hs[0:1, 2 * HEADS + q4:2 * HEADS + q4 + 1], ps[0:1, :],
                            mybir.AxisListType.X, ALU.max).then_inc(dve.h, 1)
        dve.inc(1)
        cp_tok0[q4] = dve.n

    @block.vector
    def _(v):
        v.wait_ge(dve.h, cp_tok0[3])
        v.tensor_reduce(hs[0:1, 4 * HEADS:4 * HEADS + 1], hs[0:1, 2 * HEADS:2 * HEADS + 4],
                        mybir.AxisListType.X, ALU.max).then_inc(dve.h, 1)
    dve.inc(1)
    xdmax_tok = dve.n

    @block.scalar
    def _(s):
        s.wait_ge(pool.h, tk_pool_tok)
        s.wait_ge(dA.h, tk_tok)
        s.activation(sqscr[:], tkbuf[0:64, 0:1024], AF.Square).then_inc(act.h, 1)
    act.inc(1)
    sqk = act.n
    @block.tensor
    def _(t):
        t.wait_ge(act.h, sqk)
        t.wait_ge(dve.h, xdmax_tok)
        last = None
        for tc in range(2):
            last = t.matmul(psB[:, 1024 + tc * 512:1024 + (tc + 1) * 512],
                            ones128[0:64, :], sqscr[:, tc * 512:(tc + 1) * 512],
                            start=True, stop=True)
        last.then_inc(pe.h, 1)
    pe.inc(1)
    frok = pe.n
    @block.vector
    def _(v):
        v.wait_ge(pe.h, frok)
        for h in range(HEADS):
            v.tensor_reduce(hs[0:1, h:h + 1], psB[0:1, 1024 + h * 64:1024 + (h + 1) * 64],
                            mybir.AxisListType.X, ALU.add)
        v.tensor_scalar(hs[0:1, HEADS:2 * HEADS], hs[0:1, 0:HEADS],
                        hs[0:1, 4 * HEADS:4 * HEADS + 1], SS2, ALU.mult, ALU.mult) \
            .then_inc(dve.h, 1)
    dve.inc(1)
    mf_tok = dve.n

    ld_gate = {(0, 0, 0): frok}
    psA_half_gate = {}
    P_half_gate = {}
    accslot_gate = {}
    last_exp_d = {"v": None}
    unify_done = None
    unify_tail = None
    for qg in range(NQG):
        q0 = qg * QG
        @block.sync
        def _(s, qg=qg):
            if qg == 1:
                s.wait_ge(unify_done[0].h, unify_done[1])
            for j in range(n_kD):
                s.dma_start(out=qsrc[j], in_=i_q[j * 128:(j + 1) * 128, q0:q0 + QG]) \
                    .then_inc(dA.h, 16)
        dA.inc(16 * n_kD)
        q_ld = dA.n

        qb = {}
        for h in range(HEADS):
            e0 = 64 * (h % 2)
            @block.tensor
            def _(t, h=h, qg=qg, q_ld=q_ld, e0=e0):
                t.wait_ge(dA.h, q_ld)
                t.wait_ge(dve.h, mf_tok)
                if qg == 1:
                    t.wait_ge(unify_done[0].h, unify_done[1])
                if h >= 1:
                    t.wait_ge(act.h, qb[h - 1])
                last = None
                for tc in range(2):
                    last = t.matmul(psB[0:64, 1024 + tc * 512:1024 + tc * 512 + 512],
                                    tkbuf[e0:e0 + 64, 1024 + h * 64:1024 + (h + 1) * 64],
                                    qsrc[h // 2][e0:e0 + 64, tc * 512:(tc + 1) * 512],
                                    start=True, stop=True)
                last.then_inc(pe.h, 1)
            pe.inc(1)
            qmm = pe.n
            @block.scalar
            def _(s, h=h, qmm=qmm):
                s.wait_ge(pe.h, qmm)
                s.activation(Qh[h][0:64, :], psB[0:64, 1024:2048], AF.Copy,
                             scale=p.constf(SSC, 0, 64)).then_inc(act.h, 1)
                s.wait_ge(act.h, act.n + 1)
                s.activation(qsq[0:64, :], Qh[h][0:64, :], AF.Square).then_inc(act.h, 1)
            act.inc(2)
            qevac = act.n
            @block.tensor
            def _(t, h=h, qevac=qevac):
                t.wait_ge(act.h, qevac)
                last = None
                for tc in range(2):
                    last = t.matmul(psB[0:64, 1024 + tc * 512:1024 + tc * 512 + 512],
                                    onesb[0:64, 0:64], qsq[0:64, tc * 512:(tc + 1) * 512],
                                    start=True, stop=True)
                last.then_inc(pe.h, 1)
            pe.inc(1)
            qn2m = pe.n
            @block.scalar
            def _(s, h=h, qn2m=qn2m):
                s.wait_ge(pe.h, qn2m)
                s.activation(Qh[h][64:65, :], psB[0:1, 1024:2048], AF.Sqrt,
                             scale=hs[0:1, HEADS + h:HEADS + h + 1]).then_inc(act.h, 1)
            act.inc(1)
            qb[h] = act.n
        q_all = (act, act.n)

        @block.vector
        def _(v, qg=qg):
            v.wait_ge(q_all[0].h, q_all[1])
            last = None
            for h in range(HEADS):
                last = v.memset(acch[h][:], 0.0)
            last.then_inc(dve.h, 1)
        dve.inc(1)
        acc0 = dve.n

        for kc in range(NKC):
            k0 = kc * KC
            for hp in range(HEADS // 2):
                @block.sync
                def _(s, kc=kc, hp=hp, qg=qg):
                    gate = ld_gate.get((qg, kc, hp))
                    if gate is not None:
                        s.wait_ge(pe.h, gate)
                    s.dma_start(out=xdc_pp[hp % 2][:, 0:KC],
                                in_=kvAG[kc * D + hp * 128:kc * D + (hp + 1) * 128,
                                         0:KC]).then_inc(dA.h, 16)
                dA.inc(16)
                xld = dA.n
                for i01 in range(2):
                    h = 2 * hp + i01
                    e0 = 64 * i01
                    le = last_exp_d["v"]
                    @block.tensor
                    def _(t, h=h, i01=i01, xld=xld, hp=hp, e0=e0, le=le):
                        t.wait_ge(dA.h, xld)
                        t.wait_ge(dve.h, acc0)
                        if le is not None:
                            t.wait_ge(act.h, le)
                        last = None
                        for tc in range(4):
                            last = t.matmul(psA[0:64, tc * 512:(tc + 1) * 512],
                                            tkbuf[e0:e0 + 64, h * 64:(h + 1) * 64],
                                            xdc_pp[hp % 2][e0:e0 + 64,
                                                           tc * 512:(tc + 1) * 512],
                                            start=True, stop=True)
                        last.then_inc(pe.h, 1)
                    pe.inc(1)
                    kmm = pe.n
                    @block.scalar
                    def _(s, i01=i01, kmm=kmm):
                        s.wait_ge(pe.h, kmm)
                        s.activation(Kh[i01][0:64, :], psA[0:64, :], AF.Copy,
                                     scale=p.constf(SSC, 0, 64)).then_inc(act.h, 1)
                    act.inc(1)
                    kevac = act.n
                    @block.tensor
                    def _(t, h=h, i01=i01, hp=hp, kevac=kevac, e0=e0):
                        t.wait_ge(act.h, kevac)
                        last = None
                        for kt in range(16):
                            last = t.matmul(psA[:, kt * 64:(kt + 1) * 64],
                                            xdc_pp[hp % 2][e0:e0 + 64,
                                                           kt * 128:(kt + 1) * 128],
                                            tkbuf[e0:e0 + 64,
                                                  2048 + h * 64:2048 + (h + 1) * 64],
                                            start=True, stop=True)
                        last.then_inc(pe.h, 1)
                    pe.inc(1)
                    vmm = pe.n
                    @block.scalar
                    def _(s, i01=i01, vmm=vmm):
                        s.wait_ge(pe.h, vmm)
                        s.activation(
                            Vt[i01][:].rearrange("p (kt c) -> p kt c", c=65)[:, :, 0:64],
                            psA[:, 0:1024].rearrange("p (kt c) -> p kt c", c=64),
                            AF.Copy).then_inc(act.h, 1)
                    act.inc(1)
                    kv_ready = act.n

                    for qc in range(QG // 512):
                        slot = qc % 2
                        for ktg in range(KC // 256):
                            halfsel = ktg % 2
                            ghalf = psA_half_gate.get(halfsel)
                            @block.tensor
                            def _(t, h=h, i01=i01, qc=qc, ktg=ktg, halfsel=halfsel,
                                  kv_ready=kv_ready, slot=slot, ghalf=ghalf):
                                t.wait_ge(act.h, kv_ready)
                                if ghalf is not None:
                                    t.wait_ge(act.h, ghalf)
                                if ktg == 0:
                                    ga = accslot_gate.get(slot)
                                    if ga is not None:
                                        t.wait_ge(dve.h, ga)
                                last = None
                                for j in range(2):
                                    kt = ktg * 2 + j
                                    last = t.matmul(
                                        psA[:, halfsel * 1024 + j * 512:
                                            halfsel * 1024 + (j + 1) * 512],
                                        Kh[i01][:, kt * 128:(kt + 1) * 128],
                                        Qh[h][:, qc * 512:(qc + 1) * 512],
                                        start=True, stop=True)
                                last.then_inc(pe.h, 1)
                            pe.inc(1)
                            smm = pe.n
                            gp_ = P_half_gate.get(halfsel)
                            @block.scalar
                            def _(s, halfsel=halfsel, smm=smm, gp_=gp_):
                                s.wait_ge(pe.h, smm)
                                if gp_ is not None:
                                    s.wait_ge(pe.h, gp_)
                                s.activation(P_t[:, halfsel * 1024:(halfsel + 1) * 1024],
                                             psA[:, halfsel * 1024:(halfsel + 1) * 1024],
                                             AF.Exp).then_inc(act.h, 1)
                            act.inc(1)
                            last_exp_d["v"] = act.n
                            psA_half_gate[halfsel] = act.n
                            @block.tensor
                            def _(t, h=h, i01=i01, qc=qc, ktg=ktg, halfsel=halfsel,
                                  slot=slot, myexp=act.n):
                                t.wait_ge(act.h, myexp)
                                last = None
                                for j in range(2):
                                    kt = ktg * 2 + j
                                    last = t.matmul(
                                        psB[0:65, slot * 512:slot * 512 + 512],
                                        Vt[i01][:, kt * 65:(kt + 1) * 65],
                                        P_t[:, halfsel * 1024 + j * 512:
                                            halfsel * 1024 + (j + 1) * 512],
                                        start=(ktg == 0 and j == 0),
                                        stop=(ktg == KC // 256 - 1 and j == 1))
                                last.then_inc(pe.h, 1)
                            pe.inc(1)
                            P_half_gate[halfsel] = pe.n
                        avl = pe.n
                        @block.vector
                        def _(v, h=h, qc=qc, slot=slot, avl=avl):
                            v.wait_ge(pe.h, avl)
                            v.tensor_add(acch[h][:, qc * 512:(qc + 1) * 512],
                                         acch[h][:, qc * 512:(qc + 1) * 512],
                                         psB[0:65, slot * 512:slot * 512 + 512]) \
                                .then_inc(dve.h, 1)
                        dve.inc(1)
                        accslot_gate[slot] = dve.n
                ld_gate[(qg, kc, hp + 1)] = pe.n
                if hp + 1 == HEADS // 2:
                    if kc + 1 < NKC:
                        ld_gate[(qg, kc + 1, 0)] = pe.n
                    elif qg + 1 < NQG:
                        ld_gate[(qg + 1, 0, 0)] = pe.n

        if _dbg and qg == 0:
            accslot_snapshot = accslot_gate.get(1, 0)
            @block.scalar
            def _(s, accslot_snapshot=accslot_snapshot):
                s.wait_ge(dve.h, p.sems["dve"].n)
                s.dma_start(out=o_hs[:], in_=hs[:]).then_inc(dB.h, 16)
                s.dma_start(out=o_q0[:], in_=Qh[0][:]).then_inc(dB.h, 16)
                s.dma_start(out=o_k0[:], in_=Kh[0][:]).then_inc(dB.h, 16)
                s.dma_start(out=o_acc0[:], in_=acch[0][:]).then_inc(dB.h, 16)
            dB.inc(64)

        fin = {}
        for h in range(HEADS):
            @block.vector
            def _(v, h=h):
                v.reciprocal(recr, acch[h][64:65, :]).then_inc(dve.h, 1)
            dve.inc(1)
            rc = dve.n
            @block.gpsimd
            def _(g, h=h, rc=rc):
                g.wait_ge(dve.h, rc)
                if h >= 1:
                    g.wait_ge(dve.h, fin[h - 1])
                rep_row(g, orep[0:64, :], recr, n_rep=64).then_inc(dC.h, 16)
            dC.inc(16)
            rr = dC.n
            @block.vector
            def _(v, h=h, rr=rr):
                v.wait_ge(dC.h, rr)
                v.tensor_mul(ocat[h // 2][64 * (h % 2):64 * (h % 2) + 64, :],
                             acch[h][0:64, :], orep[0:64, :]).then_inc(dve.h, 1)
            dve.inc(1)
            fin[h] = dve.n
        ocat_tok = (p.sem("dve"), p.sem("dve").n)

        store_tok = {}
        add_done = {}

        def epi_u(do, ps, mm_tok, qg=qg, store_tok=store_tok, add_done=add_done):
            dve_l, dB_l, dA_l = p.sem("dve"), p.sem("dB"), p.sem("dA")
            xr = xs_pp[do % 2]
            @block.sync
            def _(s, do=do, xr=xr, qg=qg):
                if do >= 2:
                    s.wait_ge(dve_l.h, add_done[do - 2])
                s.dma_start(out=xr, in_=xT[do * 128:(do + 1) * 128,
                                           qg * QG:(qg + 1) * QG]).then_inc(dA_l.h, 16)
            dA_l.inc(16)
            xl = dA_l.n
            @block.vector
            def _(v, do=do, ps=ps, mm_tok=mm_tok, xl=xl, xr=xr):
                v.wait_ge(mm_tok[0].h, mm_tok[1])
                v.wait_ge(dA_l.h, xl)
                if do >= 1:
                    v.wait_ge(dB_l.h, 16 * store_tok[do - 1])
                v.scalar_tensor_tensor(ostage, ps[:, 0:QG], but[:, do:do + 1], xr,
                                       ALU.add, ALU.add).then_inc(dve_l.h, 1)
            dve_l.inc(1)
            add_done[do] = dve_l.n
            ps_free = (dve_l, dve_l.n)
            @block.scalar
            def _(s, do=do, qg=qg, psf=ps_free[1]):
                s.wait_ge(dve_l.h, psf)
                s.dma_start(out=o_x[do * 128:(do + 1) * 128, qg * QG:(qg + 1) * QG],
                            in_=ostage).then_inc(dB_l.h, 16)
            dB_l.inc(16)
            store_tok[do] = dB_l.n // 16
            return ps_free

        ptail = [ocat_tok] + (unify_tail if unify_tail else [])
        cons = emit_proj(p, W_unify, [t2[:] for t2 in ocat], n_kD, wbufs,
                         [psA, psB], epi_u, ocat_tok, prev_tail=ptail)
        unify_tail = [cons[n_kD - 2], cons[n_kD - 1]]
        unify_done = (p.sem("dve"), p.sem("dve").n)

    @block.gpsimd
    def _(g):
        g.wait_ge(dB.h, p.sem("dB").n)





def phase_l1b_tok(p, io):
    n_kD = D // 128
    S = 128
    xT = io["xT"]
    Wqkvg = io["Wqkvg"]
    bqkvg = io["bqkvg"]
    outs = [io[f"o_{n}"] for n in ("q", "k", "v", "g")]
    fa = Arena(p.sbuf("fa", [128, 41 * 1024], F32))
    ba = Arena(p.sbuf("ba", [128, 2 * 1024], BF16))
    psA = p.psum("psA", [128, TC], F32)
    psB = p.psum("psB", [128, TC], F32)
    xs_pp = [fa.alloc(TC) for _ in range(2)]
    sq_pp = [fa.alloc(TC) for _ in range(2)]
    reps = fa.alloc(2 * TC)
    rows = fa.alloc(3 * TC, rows=1)
    xn_t = [fa.alloc(TC) for _ in range(n_kD)]
    stage = [fa.alloc(TC) for _ in range(2)]
    gstage = sq_pp[1]
    wq = p.sbuf("wq", [128, 4 * n_kD * S], F32)
    bt = p.sbuf("bt", [128, 4], F32)
    ones128 = p.sbuf("ones128", [128, 128], F32)
    gp_scr_t = p.sbuf("gp_scr", [128, 4], BF16)
    p.begin(consts=(1.0, EPS), zero_psums=(psA, psB))
    block = p.block
    dA, dB, dC = p.sem("dA"), p.sem("dB"), p.sem("dC")
    dve, act, pe, pool = p.sem("dve"), p.sem("act"), p.sem("pe"), p.sem("pool")
    p._gp_scr = gp_scr_t[:]
    @block.vector
    def _(v):
        v.wait_ge(p.sem("init").h, p._z1)
        v.memset(ones128[:], 1.0).then_inc(p.sem("init").h, 1)
    p.sem("init").inc(1)
    init_tok = (p.sem("init"), p._z1 + 1)
    @block.sync
    def _(s):
        s.dma_start(out=bt[:], in_=bqkvg[:]).then_inc(dA.h, 16)
    dA.inc(16)
    @block.sync
    def _(s):
        s.dma_start(out=wq[:].rearrange("p (ki c) -> p ki c", c=4 * S),
                    in_=Wqkvg.rearrange("(ki p) c -> p ki c", p=128)).then_inc(dC.h, 16)
    dC.inc(16)
    w_tok_dma = dC.n
    @block.gpsimd
    def _(g):
        g.wait_ge(dC.h, w_tok_dma)
        g.tensor_copy(gp_scr_t[:, 0:2], wq[:, 4 * n_kD * S - 1:4 * n_kD * S].bitcast(BF16)) \
            .then_inc(pool.h, 1)
    pool.inc(1)
    w_tok = pool.n
    R = lambda i: rows[0:1, i * TC:(i + 1) * TC]
    psA_l = [psA[:, i * 512:(i + 1) * 512] for i in range(4)]
    psB_l = [psB[:, i * 512:(i + 1) * 512] for i in range(4)]
    xn_tok = emit_ln_streamed(p, xT, n_kD, TC, xs_pp, sq_pp, psA_l, psB_l,
                              ones128[:], init_tok, xn_t, reps,
                              R(0), R(1), R(2), R(1), p.constf(EPS, 0, 1))
    prev = {}
    for j in range(4):
        ps = psA if j % 2 == 0 else psB
        @block.tensor
        def _(t, j=j, ps=ps):
            t.wait_ge(pool.h, w_tok)
            if j == 0:
                t.wait_ge(xn_tok[0].h, xn_tok[1])
            if j >= 2:
                t.wait_ge(p.sem("dB").h, 16 * prev[j - 2])
            last = None
            for (t0, t1) in tchunks_of(TC):
                for ki in range(n_kD):
                    last = t.matmul(ps[:, t0:t1],
                                    wq[:].rearrange("p (ki c) -> p ki c", c=4 * S)
                                    [:, ki, j * S:(j + 1) * S],
                                    xn_t[ki][:, t0:t1],
                                    start=(ki == 0), stop=(ki == n_kD - 1))
            last.then_inc(pe.h, 1)
        pe.inc(1)
        mm = pe.n
        if j == 3:
            @block.scalar
            def _(s, j=j, ps=ps, mm=mm):
                s.wait_ge(pe.h, mm)
                s.wait_ge(p.sem("dB").h, 16 * prev[j - 2])
                s.activation(gstage, ps[:], AF.Sigmoid, bias=bt[:, j:j + 1]) \
                    .then_inc(act.h, 1)
                s.wait_ge(act.h, act.n + 1)
                s.dma_start(out=outs[j][:], in_=gstage).then_inc(p.sem("dB").h, 16)
            act.inc(1)
            p.sem("dB").inc(16)
            prev[j] = p.sem("dB").n // 16
        else:
            @block.vector
            def _(v, j=j, ps=ps, mm=mm):
                v.wait_ge(pe.h, mm)
                if j >= 2:
                    v.wait_ge(p.sem("dB").h, 16 * prev[j - 2])
                v.tensor_scalar(stage[j % 2], ps[:], bt[:, j:j + 1], None, ALU.add) \
                    .then_inc(dve.h, 1)
            dve.inc(1)
            sd = dve.n
            @block.scalar
            def _(s, j=j, sd=sd):
                s.wait_ge(dve.h, sd)
                s.dma_start(out=outs[j][:], in_=stage[j % 2]).then_inc(p.sem("dB").h, 16)
            p.sem("dB").inc(16)
            prev[j] = p.sem("dB").n // 16
    @block.gpsimd
    def _(g):
        g.wait_ge(p.sem("dB").h, p.sem("dB").n)


def phase_l1b_scan(p, io):
    NV = 16
    TW = 2048
    NTW = 8
    qAG = io["qAG"]
    kAG = io["kAG"]
    gAG = io["gAG"]
    vOut = io["vOut"]
    o_y = io["yIn2"]
    _dbg = os.environ.get("GLADBG", "0") == "1"
    if _dbg:
        o_carr = p.dout("o_carr", (128, NV * NTW), F32)
    fga = Arena(p.sbuf("fga", [128, 32 * 1024], F32))
    fga2 = Arena(p.sbuf("fga2", [128, 14 * 1024], F32))
    psA = p.psum("psA", [128, TW], F32)
    psB = p.psum("psB", [128, TW], F32)
    qc = [fga.alloc(TW) for _ in range(2)]
    kc = [fga.alloc(TW) for _ in range(2)]
    gc = [fga.alloc(TW) for _ in range(2)]
    vc = [fga.alloc(TW, rows=NV) for _ in range(2)]
    vrep = [fga2.alloc(TW) for _ in range(2)]
    b_pp = [fga2.alloc(TW) for _ in range(2)]
    h_pp = [fga2.alloc(TW) for _ in range(2)]
    ystage = fga2.alloc(TW, rows=NV)
    sel = p.sbuf("sel", [128, NV * 128], F32)
    carr = p.sbuf("carr", [128, NV], F32)
    gp_scr_t = p.sbuf("gp_scr", [128, 4], BF16)
    p.begin(consts=(1.0,), zero_psums=(psA, psB))
    block = p.block
    dA, dB, dC = p.sem("dA"), p.sem("dB"), p.sem("dC")
    dve, act, pe, pool = p.sem("dve"), p.sem("act"), p.sem("pe"), p.sem("pool")
    p._gp_scr = gp_scr_t[:]
    @block.vector
    def _(v):
        v.wait_ge(p.sem("init").h, p._z1)
        v.memset(sel[:], 0.0)
        last = None
        for j in range(NV):
            last = v.memset(sel[:, j * 128 + j:j * 128 + j + 1], 1.0)
        last.then_inc(p.sem("init").h, 1)
    p.sem("init").inc(1)
    init_tok = (p.sem("init"), p._z1 + 1)
    rep_tok = {}
    ystore = {}
    prev_evac = {}
    yma = {}
    scan_done = {}
    ld = {}
    for tw in range(NTW):
        psY = psA if tw % 2 == 0 else psB
        @block.sync
        def _(s, tw=tw):
            if tw >= 2:
                s.wait_ge(dve.h, scan_done[(tw - 1) * NV - 1])
                s.wait_ge(pool.h, rep_tok[(tw - 1) * NV - 1])
            s.dma_start(out=qc[tw % 2], in_=qAG[tw * 128:(tw + 1) * 128, :]) \
                .then_inc(dA.h, 16)
            s.dma_start(out=kc[tw % 2], in_=kAG[tw * 128:(tw + 1) * 128, :]) \
                .then_inc(dA.h, 16)
            s.dma_start(out=gc[tw % 2], in_=gAG[tw * 128:(tw + 1) * 128, :]) \
                .then_inc(dA.h, 16)
            s.dma_start(out=vc[tw % 2], in_=vOut[tw * NV:(tw + 1) * NV, :]) \
                .then_inc(dA.h, 16)
        dA.inc(64)
        ld[tw] = dA.n
        for vv in range(NV):
            u = tw * NV + vv
            @block.gpsimd
            def _(g, tw=tw, vv=vv, u=u):
                g.wait_ge(dA.h, ld[tw])
                if u >= 2:
                    g.wait_ge(dve.h, scan_done[u - 2])
                rep_row(g, vrep[u % 2], vc[tw % 2][vv:vv + 1, :]) \
                    .then_inc(dC.h, 16)
                g.wait_ge(dC.h, dC.n + 16)
                g.tensor_copy(gp_scr_t[:, 0:2], vrep[u % 2][:, TW - 2:TW]) \
                    .then_inc(pool.h, 1)
            dC.inc(16)
            pool.inc(1)
            rep_tok[u] = pool.n
            @block.vector
            def _(v, tw=tw, vv=vv, u=u):
                v.wait_ge(pool.h, rep_tok[u])
                v.wait_ge(dA.h, ld[tw])
                if u >= 2:
                    v.wait_ge(pe.h, yma.get(u - 2, 0))
                v.tensor_mul(b_pp[u % 2], kc[tw % 2][:, :], vrep[u % 2])
                init = 0.0 if tw % 4 == 0 else carr[:, vv:vv + 1]
                v.tensor_tensor_scan(h_pp[u % 2], gc[tw % 2][:, :],
                                     b_pp[u % 2], init, ALU.mult, ALU.add) \
                    .then_inc(dve.h, 1)
                v.wait_ge(dve.h, dve.n + 1)
                v.tensor_copy(carr[:, vv:vv + 1], h_pp[u % 2][:, TW - 1:TW])
                v.tensor_mul(b_pp[u % 2], h_pp[u % 2], qc[tw % 2][:, :]) \
                    .then_inc(dve.h, 1)
            dve.inc(2)
            scan_done[u] = dve.n
            @block.tensor
            def _(t, tw=tw, vv=vv, u=u, psY=psY):
                t.wait_ge(dve.h, scan_done[u])
                if vv == 0:
                    pv = prev_evac.get(tw % 2)
                    if pv is not None:
                        t.wait_ge(act.h, pv)
                    t.wait_ge(init_tok[0].h, init_tok[1])
                last = None
                for tc in range(TW // 512):
                    last = t.matmul(psY[:, tc * 512:(tc + 1) * 512],
                                    sel[:, vv * 128:(vv + 1) * 128],
                                    b_pp[u % 2][:, tc * 512:(tc + 1) * 512],
                                    start=(vv == 0), stop=(vv == NV - 1))
                last.then_inc(pe.h, 1)
            pe.inc(1)
            yma[u] = pe.n
        if _dbg:
            lastu = tw * NV + NV - 1
            @block.scalar
            def _(s, tw=tw, lastu=lastu):
                s.wait_ge(dve.h, scan_done[lastu])
                s.dma_start(out=o_carr[:, tw * NV:(tw + 1) * NV], in_=carr[:]) \
                    .then_inc(dB.h, 16)
            dB.inc(16)
        klast = tw * NV + NV - 1
        act.inc(1)
        ev = act.n
        dB.inc(16)
        @block.scalar
        def _(s, tw=tw, klast=klast, ev=ev, psY=psY):
            s.wait_ge(pe.h, yma[klast])
            if tw >= 1:
                s.wait_ge(dB.h, 16 * ystore[tw - 1])
            s.copy(ystage[0:NV, :], psY[0:NV, :]).then_inc(act.h, 1)
            s.wait_ge(act.h, ev)
            s.dma_start(out=o_y[tw * NV:(tw + 1) * NV, 0:TW], in_=ystage[0:NV, :]) \
                .then_inc(dB.h, 16)
        ystore[tw] = dB.n // 16
        prev_evac[tw % 2] = ev
    @block.gpsimd
    def _(g):
        g.wait_ge(dB.h, dB.n)




def phase_l1b_out(p, io):
    n_kD = D // 128
    xT = io["xT"]
    i_y = io["i_y"]
    Wo = io["Wo"]
    bgb = io["bgb"]
    o_x = io["o_x"]
    fa = Arena(p.sbuf("fa", [128, 39 * 1024], F32))
    ba = Arena(p.sbuf("ba", [128, 1 * 1024], BF16))
    psA = p.psum("psA", [128, TC], F32)
    psB = p.psum("psB", [128, TC], F32)
    xs_pp = [fa.alloc(TC) for _ in range(2)]
    sq_pp = [fa.alloc(TC) for _ in range(2)]
    reps = fa.alloc(2 * TC)
    rows = fa.alloc(4 * TC, rows=1)
    y2_t = [fa.alloc(TC) for _ in range(n_kD)]
    ostage = sq_pp[1]
    y_sb = fa.alloc(TC)
    wo_sb = p.sbuf("wo_sb", [128, D], F32)
    bg = p.sbuf("bg", [128, 3 * n_kD], F32)
    ones128 = p.sbuf("ones128", [128, 128], F32)
    gp_scr_t = p.sbuf("gp_scr", [128, 4], BF16)
    p.begin(consts=(1.0, EPS), zero_psums=(psA, psB))
    block = p.block
    dA, dB, dC = p.sem("dA"), p.sem("dB"), p.sem("dC")
    dve, act, pe, pool = p.sem("dve"), p.sem("act"), p.sem("pe"), p.sem("pool")
    p._gp_scr = gp_scr_t[:]
    @block.vector
    def _(v):
        v.wait_ge(p.sem("init").h, p._z1)
        v.memset(ones128[:], 1.0).then_inc(p.sem("init").h, 1)
    p.sem("init").inc(1)
    init_tok = (p.sem("init"), p._z1 + 1)
    @block.sync
    def _(s):
        s.dma_start(out=bg[:], in_=bgb[:]).then_inc(dA.h, 16)
        s.dma_start(out=y_sb, in_=i_y[:]).then_inc(dA.h, 16)
    dA.inc(32)
    y_tok = dA.n
    @block.gpsimd
    def _(g):
        g.dma_start(out=wo_sb[:], in_=Wo[:]).then_inc(dC.h, 16)
        g.wait_ge(dC.h, dC.n + 16)
        g.tensor_copy(gp_scr_t[0:128, 0:1].bitcast(F32) if False else gp_scr_t[:, 0:2],
                      wo_sb[:, D - 1:D].bitcast(BF16) if False else wo_sb[:, D - 1:D]
                      .bitcast(BF16)).then_inc(pool.h, 1)
    dC.inc(16)
    pool.inc(1)
    wo_tok = pool.n

    R = lambda i: rows[0:1, i * TC:(i + 1) * TC]
    mu, var, r0 = R(0), R(1), R(2)
    m0 = R(1)
    psA_l = [psA[:, i * 512:(i + 1) * 512] for i in range(4)]
    psB_l = [psB[:, i * 512:(i + 1) * 512] for i in range(4)]
    tch = tchunks_of(TC)
    mm_done, ld, sqd = {}, {}, {}
    for i in range(n_kD):
        @block.sync
        def _(s, i=i):
            if i >= 2:
                s.wait_ge(pe.h, mm_done[i - 2])
            s.dma_start(out=xs_pp[i % 2], in_=xT[i * 128:(i + 1) * 128, :]).then_inc(dA.h, 16)
        dA.inc(16)
        ld[i] = dA.n
        @block.scalar
        def _(s, i=i):
            s.wait_ge(dA.h, ld[i])
            if i >= 2:
                s.wait_ge(pe.h, mm_done[i - 2])
            s.activation(sq_pp[i % 2], xs_pp[i % 2], AF.Square).then_inc(act.h, 1)
        act.inc(1)
        sqd[i] = act.n
        @block.tensor
        def _(t, i=i):
            if i == 0:
                t.wait_ge(init_tok[0].h, init_tok[1])
            t.wait_ge(act.h, sqd[i])
            last = None
            for ci, (t0, t1) in enumerate(tch):
                t.matmul(psA_l[ci], ones128[:], xs_pp[i % 2][:, t0:t1],
                         start=(i == 0), stop=(i == n_kD - 1))
                last = t.matmul(psB_l[ci], ones128[:], sq_pp[i % 2][:, t0:t1],
                                start=(i == 0), stop=(i == n_kD - 1))
            last.then_inc(pe.h, 1)
        pe.inc(1)
        mm_done[i] = pe.n
    stats_done = pe.n
    @block.vector
    def _(v):
        v.wait_ge(pe.h, stats_done)
        v.tensor_scalar(mu, psA[0:1, :], 1.0 / D, None, ALU.mult)
        v.tensor_scalar(var, psB[0:1, :], 1.0 / D, None, ALU.mult)
        v.tensor_mul(r0, mu, mu)
        v.tensor_sub(var, var, r0).then_inc(dve.h, 1)
    dve.inc(1)
    t0tok = emit_rsqrt_row(p, r0, var, p.constf(EPS, 0, 1), p.sem("dve").tok)
    @block.vector
    def _(v):
        v.wait_ge(t0tok[0].h, t0tok[1])
        v.tensor_mul(m0, mu, r0).then_inc(dve.h, 1)
    dve.inc(1)
    rows_tok = dve.tok
    @block.gpsimd
    def _(g):
        g.wait_ge(rows_tok[0].h, rows_tok[1])
        rep_row(g, reps[:, 0:TC], r0).then_inc(dC.h, 16)
        rep_row(g, reps[:, TC:2 * TC], m0).then_inc(dC.h, 16)
    dC.inc(32)
    reps_tok = dC.n

    y2d = {}
    for do in range(n_kD):
        ps = psA if do % 2 == 0 else psB
        @block.tensor
        def _(t, do=do, ps=ps):
            t.wait_ge(pool.h, wo_tok)
            t.wait_ge(dA.h, y_tok)
            if do < 2:
                t.wait_ge(pe.h, stats_done)
            if do < 2:
                t.wait_ge(dve.h, rows_tok[1])
            if do >= 2:
                t.wait_ge(dve.h, y2d[do - 2])
            last = None
            for (t0, t1) in tchunks_of(TC):
                last = t.matmul(ps[:, t0:t1], wo_sb[:, do * 128:(do + 1) * 128],
                                y_sb[:, t0:t1], start=True, stop=True)
            last.then_inc(pe.h, 1)
        pe.inc(1)
        mm = pe.n
        @block.sync
        def _(s, do=do, mm=mm):
            if do >= 2:
                s.wait_ge(dve.h, y2d[do - 2])
            else:
                s.wait_ge(pe.h, stats_done)
            s.dma_start(out=xs_pp[do % 2], in_=xT[do * 128:(do + 1) * 128, :]) \
                .then_inc(dA.h, 16)
        dA.inc(16)
        xl = dA.n
        @block.vector
        def _(v, do=do, ps=ps, mm=mm, xl=xl):
            v.wait_ge(pe.h, mm)
            v.wait_ge(dA.h, xl)
            v.wait_ge(dC.h, reps_tok)
            v.tensor_mul(y2_t[do], xs_pp[do % 2], reps[:, 0:TC])
            v.tensor_sub(y2_t[do], y2_t[do], reps[:, TC:2 * TC])
            v.scalar_tensor_tensor(y2_t[do], ps[:], bg[:, do:do + 1], y2_t[do],
                                   ALU.add, ALU.add).then_inc(dve.h, 1)
        dve.inc(1)
        y2d[do] = dve.n
    y2_all = dve.n

    mu2, var2, r2 = R(3), R(0), R(3)
    mu2, var2 = R(0), R(2)
    r2, m2 = R(3), R(2)
    sq2 = {}
    mmd2 = {}
    for i in range(n_kD):
        @block.scalar
        def _(s, i=i):
            s.wait_ge(dve.h, y2_all)
            if i >= 2:
                s.wait_ge(pe.h, mmd2.get(i - 2, 0))
            s.activation(sq_pp[i % 2], y2_t[i], AF.Square).then_inc(act.h, 1)
        act.inc(1)
        sq2[i] = act.n
        @block.tensor
        def _(t, i=i):
            t.wait_ge(act.h, sq2[i])
            if i == 0:
                t.wait_ge(dve.h, y2_all)
            last = None
            for ci, (t0, t1) in enumerate(tch):
                t.matmul(psA_l[ci], ones128[:], y2_t[i][:, t0:t1],
                         start=(i == 0), stop=(i == n_kD - 1))
                last = t.matmul(psB_l[ci], ones128[:], sq_pp[i % 2][:, t0:t1],
                                start=(i == 0), stop=(i == n_kD - 1))
            last.then_inc(pe.h, 1)
        pe.inc(1)
        mmd2[i] = pe.n
    st2 = pe.n
    @block.vector
    def _(v):
        v.wait_ge(pe.h, st2)
        v.tensor_scalar(mu2, psA[0:1, :], 1.0 / D, None, ALU.mult)
        v.tensor_scalar(var2, psB[0:1, :], 1.0 / D, None, ALU.mult)
        v.tensor_mul(r2, mu2, mu2)
        v.tensor_sub(var2, var2, r2).then_inc(dve.h, 1)
    dve.inc(1)
    t2tok = emit_rsqrt_row(p, r2, var2, p.constf(EPS, 0, 1), p.sem("dve").tok)
    @block.vector
    def _(v):
        v.wait_ge(t2tok[0].h, t2tok[1])
        v.tensor_mul(m2, mu2, r2).then_inc(dve.h, 1)
    dve.inc(1)
    rt2 = dve.tok
    @block.gpsimd
    def _(g):
        g.wait_ge(rt2[0].h, rt2[1])
        rep_row(g, reps[:, 0:TC], r2).then_inc(dC.h, 16)
        rep_row(g, reps[:, TC:2 * TC], m2).then_inc(dC.h, 16)
    dC.inc(32)
    reps2_tok = dC.n

    store_tok = {}
    app = {}
    for do in range(n_kD):
        @block.sync
        def _(s, do=do):
            if do >= 2:
                s.wait_ge(dve.h, app.get(do - 2, 0))
            else:
                s.wait_ge(dve.h, y2_all)
            s.dma_start(out=xs_pp[do % 2], in_=xT[do * 128:(do + 1) * 128, :]) \
                .then_inc(dA.h, 16)
        dA.inc(16)
        xl = dA.n
        @block.vector
        def _(v, do=do, xl=xl):
            v.wait_ge(dA.h, xl)
            v.wait_ge(dC.h, reps2_tok)
            if do >= 1:
                v.wait_ge(dB.h, 16 * store_tok[do - 1])
            v.tensor_mul(ostage, y2_t[do], reps[:, 0:TC])
            v.tensor_sub(ostage, ostage, reps[:, TC:2 * TC])
            v.tensor_scalar(ostage, ostage, bg[:, n_kD + do:n_kD + do + 1],
                            bg[:, 2 * n_kD + do:2 * n_kD + do + 1], ALU.mult, ALU.add)
            v.tensor_add(ostage, ostage, xs_pp[do % 2]).then_inc(dve.h, 1)
        dve.inc(1)
        app[do] = dve.n
        @block.scalar
        def _(s, do=do, ad=app[do]):
            s.wait_ge(dve.h, ad)
            s.dma_start(out=o_x[do * 128:(do + 1) * 128, :], in_=ostage).then_inc(dB.h, 16)
        dB.inc(16)
        store_tok[do] = dB.n // 16
    @block.gpsimd
    def _(g):
        g.wait_ge(dB.h, dB.n)



def col128(*vecs):
    cols = [np.ascontiguousarray(v.reshape(-1, 128).T) for v in vecs]
    return np.ascontiguousarray(np.concatenate(cols, axis=1))


def shard_tokens_T(x):
    out = []
    for c in range(NCORE):
        b = c // GROUP
        t0 = (c % GROUP) * TC
        out.append(np.ascontiguousarray(x[b, t0:t0 + TC, :].T))
    return out


def _conv128(w):
    w = np.ascontiguousarray(w.reshape(DI, 4))
    return np.ascontiguousarray(w.reshape(16, 128, 4).transpose(1, 0, 2).reshape(128, 64))


def _tt_heads(w):
    return np.ascontiguousarray(w.transpose(2, 0, 1).reshape(64, 1024))


def _cat(*a, axis=1):
    return np.ascontiguousarray(np.concatenate(a, axis=axis))


def mk_mask(shift):
    out = []
    for c in range(NCORE):
        j = c % GROUP + shift
        m = np.zeros((128, 4), np.float32)
        if 0 <= j < GROUP:
            m[:, j] = 1.0
        out.append(m)
    return out


def lA_prep(key):
    def fn(ins):
        logA = ins[key]
        out = []
        for c in range(NCORE):
            ch0 = c * 256
            out.append(np.ascontiguousarray(
                logA[ch0:ch0 + 256].reshape(2, 128, NS).transpose(1, 0, 2)
                .reshape(128, 32)))
        return out
    return fn


PCORES = list(range(NCORE))
_SSC = float(64) ** (-0.25)
_CONSTS = (1.0, EPS, _SSC)


def percore(key):
    return lambda env, _k=key: env[_k]


def ag4(arrs):
    return [np.ascontiguousarray(
        np.concatenate([arrs[(c // GROUP) * GROUP + j] for j in range(GROUP)],
                       axis=0)) for c in range(NCORE)]


def ag8(arrs):
    cat = np.ascontiguousarray(np.concatenate(arrs, axis=0))
    return [cat] * NCORE


def a2a8(arrs, rows):
    return [np.ascontiguousarray(
        np.concatenate([a[k * rows:(k + 1) * rows] for a in arrs], axis=0))
        for k in range(NCORE)]


def _new_prog(name):
    p = Prog(name)
    p.begin_program(consts=_CONSTS)
    return p


def _fin(p):
    p.barrier()
    p.finish()
    return p


def _ph(p, tag, fn, io, **kw):
    p.scope_open(tag)
    fn(p, io, **kw)
    p.scope_close()


def build_LA1():
    p = _new_prog("LA1")
    E = p.ext
    _ph(p, "a", phase_tokA, {
        "xT": E("xT", (D, TC), F32,
                lambda env: shard_tokens_T(env["L0_attn_a_x"])),
        "ln_gb": E("ln_gb", (128, 16), F32,
                   lambda env: [col128(env["L0_attn_a_ln_g"],
                                       env["L0_attn_a_ln_b"])] * NCORE),
        "W_in": E("W_in", (D, 2 * DI), F32,
                  lambda env: [env["L0_attn_a_W_in"]] * NCORE),
        "o_xbr": p.dout("o_xbr", (DI, TC), BF16),
        "o_sz": p.dout("o_sz", (DI, TC), BF16),
        "o_edge": p.dout("o_edge", (DI, 6), BF16)})
    return _fin(p)


def _tokB_a_io(p):
    E = p.ext
    return {
        "i_xbr_body": E("i_body", (DI, TC), BF16, percore("xbrA")),
        "edgeAG": E("edgeAG", (4 * DI, 6), BF16, percore("edgeA_AG")),
        "hmask": E("hmask", (128, 4), F32, lambda env: mk_mask(-1)),
        "convw": E("convw", (128, 64), F32,
                   lambda env: [_conv128(env["L0_attn_a_conv_w"])] * NCORE),
        "convb_bdel": E("cbb", (128, 32), F32,
                        lambda env: [col128(env["L0_attn_a_conv_b"],
                                            env["L0_attn_a_b_delta"])] * NCORE),
        "W_delta": E("W_delta", (DI, DI), F32,
                     lambda env: [env["L0_attn_a_W_delta"]] * NCORE),
        "W_bc": E("W_bc", (DI, 2 * NS), F32,
                  lambda env: [_cat(env["L0_attn_a_W_b"],
                                    env["L0_attn_a_W_c"])] * NCORE),
        "o_xb": p.dout("o_xb", (DI, TC), BF16),
        "o_delta": p.dout("o_delta", (DI, TC), BF16),
        "o_bc": p.dout("o_bc", (2 * NS, TC), BF16)}


def build_LA2():
    p = _new_prog("LA2")
    _ph(p, "a", phase_tokB_a, _tokB_a_io(p))
    return _fin(p)


def _scan_io(p, sfx, dkey, xkey, bckey, lakey):
    E = p.ext
    return {
        "dOut": E("dOut" + sfx, (DI, TC), BF16, percore(dkey)),
        "xbOut": E("xbOut" + sfx, (DI, TC), BF16, percore(xkey)),
        "bcAG": E("bcAG" + sfx, (16 * NS, TC), BF16, percore(bckey)),
        "i_logA": E("logA" + sfx, (128, 32), F32, lA_prep(lakey)),
        "ident": E("ident", (128, 128), BF16,
                   lambda env: [np.eye(128, dtype=bf)] * NCORE),
        "yIn": p.dout("o_y" + sfx, (DI, TC), BF16)}


def build_LA3():
    p = _new_prog("LA3")
    _ph(p, "s", phase_scan,
        _scan_io(p, "", "delA_o", "xbA_o", "bcA_AG", "L0_attn_a_log_A"),
        reverse=False)
    return _fin(p)


def build_LA4():
    p = _new_prog("LA4")
    E = p.ext
    x1 = p.dram("x1", (D, TC), F32)
    x2 = p.dout("o_x2", (D, TC), F32)
    _ph(p, "o", phase_l0a_out, {
        "xT": E("xT", (D, TC), F32,
                lambda env: shard_tokens_T(env["L0_attn_a_x"])),
        "i_y": E("i_y", (DI, TC), BF16, percore("yA_out")),
        "i_xb": E("i_xb", (DI, TC), BF16, percore("xbA")),
        "i_sz": E("i_sz", (DI, TC), BF16, percore("szA")),
        "dp128": E("dp128", (128, 16), F32,
                   lambda env: [col128(env["L0_attn_a_Dp"])] * NCORE),
        "W_out": E("W_out", (DI, D), F32,
                   lambda env: [env["L0_attn_a_W_out"]] * NCORE),
        "o_x": x1})
    _ph(p, "f", phase_ffn, _ffn_io(p, x1, x2))
    _ph(p, "t", phase_tokA, {
        "xT": x2,
        "ln_gb": E("ln_gb", (128, 16), F32,
                   lambda env: [col128(env["L0_attn_b_norm_w"],
                                       env["L0_attn_b_norm_b"])] * NCORE),
        "W_in": E("W_in", (D, 2 * DI), F32,
                  lambda env: [np.ascontiguousarray(
                      env["L0_attn_b_in_proj_w"].T)] * NCORE),
        "o_xbr": p.dout("o_xbr", (DI, TC), BF16),
        "o_sz": p.dout("o_sz", (DI, TC), BF16),
        "o_edge": p.dout("o_edge", (DI, 6), BF16)})
    return _fin(p)


def _ffn_io(p, xin, xout):
    E = p.ext
    return {
        "xT": xin,
        "w12": E("ffn_w12", (D, 8192), F32,
                 lambda env: [_cat(env["L0_ffn_w1_w"].T,
                                   env["L0_ffn_w2_w"].T)] * NCORE),
        "s12": E("ffn_s12", (128, 64), F32,
                 lambda env: [col128(env["L0_ffn_w1_s"],
                                     env["L0_ffn_w2_s"])] * NCORE),
        "h1T": E("ffn_h1T", (4096, 4096), F32,
                 lambda env: [np.ascontiguousarray(env["L0_ffn_h1_w"].T)] * NCORE),
        "h1s": E("ffn_h1s", (128, 32), F32,
                 lambda env: [col128(env["L0_ffn_h1_s"])] * NCORE),
        "h2T": E("ffn_h2T", (4096, 4096), F32,
                 lambda env: [np.ascontiguousarray(env["L0_ffn_h2_w"].T)] * NCORE),
        "h2s": E("ffn_h2s", (128, 32), F32,
                 lambda env: [col128(env["L0_ffn_h2_s"])] * NCORE),
        "w3T": E("ffn_w3T", (4096, D), F32,
                 lambda env: [np.ascontiguousarray(env["L0_ffn_w3_w"].T)] * NCORE),
        "w3s": E("ffn_w3s", (128, 8), F32,
                 lambda env: [col128(env["L0_ffn_w3_s"])] * NCORE),
        "o_x": xout}


def build_LA5():
    p = _new_prog("LA5")
    E = p.ext
    io = {
        "i_xbr_body": E("i_body", (DI, TC), BF16, percore("xbrB")),
        "edgeAG": E("edgeAG", (4 * DI, 6), BF16, percore("edgeB_AG")),
        "hmaskL": E("hmaskL", (128, 4), F32, lambda env: mk_mask(-1)),
        "hmaskR": E("hmaskR", (128, 4), F32, lambda env: mk_mask(+1)),
        "convw_f": E("cwf", (128, 64), F32,
                     lambda env: [_conv128(env["L0_attn_b_conv_w_f"])] * NCORE),
        "convw_r": E("cwr", (128, 64), F32,
                     lambda env: [_conv128(
                         env["L0_attn_b_conv_w_r"][:, :, ::-1])] * NCORE),
        "cb4": E("cb4", (128, 64), F32,
                 lambda env: [col128(env["L0_attn_b_conv_b_f"],
                                     env["L0_attn_b_conv_b_r"],
                                     env["L0_attn_b_dt_proj_b_f"],
                                     env["L0_attn_b_dt_proj_b_r"])] * NCORE),
        "xprojT_f": E("xpf", (DI, 96), F32,
                      lambda env: [np.ascontiguousarray(
                          env["L0_attn_b_x_proj_w_f"].T)] * NCORE),
        "xprojT_r": E("xpr", (DI, 96), F32,
                      lambda env: [np.ascontiguousarray(
                          env["L0_attn_b_x_proj_w_r"].T)] * NCORE),
        "dtwT_f": E("dtf", (64, DI), F32,
                    lambda env: [np.ascontiguousarray(
                        env["L0_attn_b_dt_proj_w_f"].T)] * NCORE),
        "dtwT_r": E("dtr", (64, DI), F32,
                    lambda env: [np.ascontiguousarray(
                        env["L0_attn_b_dt_proj_w_r"].T)] * NCORE)}
    for br in ("f", "r"):
        io[f"o_xb_{br}"] = p.dout(f"o_xb_{br}", (DI, TC), BF16)
        io[f"o_delta_{br}"] = p.dout(f"o_delta_{br}", (DI, TC), BF16)
        io[f"o_bc_{br}"] = p.dout(f"o_bc_{br}", (2 * NS, TC), BF16)
    _ph(p, "b", phase_tokB_b, io)
    return _fin(p)


def build_LA6():
    p = _new_prog("LA6")
    _ph(p, "f", phase_scan,
        _scan_io(p, "_f", "delF_o", "xbF_o", "bcF_AG", "L0_attn_b_A_log_f"),
        reverse=False)
    _ph(p, "r", phase_scan,
        _scan_io(p, "_r", "delR_o", "xbR_o", "bcR_AG", "L0_attn_b_A_log_r"),
        reverse=True)
    return _fin(p)


def build_LA7():
    p = _new_prog("LA7")
    E = p.ext
    x3 = p.dram("x3", (D, TC), F32)
    x4 = p.dout("o_x4", (D, TC), F32)
    _ph(p, "o", phase_l0b_out, {
        "xT": E("xT", (D, TC), F32, percore("x2")),
        "i_y_f": E("i_y_f", (DI, TC), BF16, percore("yF_out")),
        "i_y_r": E("i_y_r", (DI, TC), BF16, percore("yR_out")),
        "i_xb_f": E("i_xb_f", (DI, TC), BF16, percore("xbF")),
        "i_xb_r": E("i_xb_r", (DI, TC), BF16, percore("xbR")),
        "i_sz": E("i_sz", (DI, TC), BF16, percore("szB")),
        "d2": E("d2", (128, 32), F32,
                lambda env: [col128(env["L0_attn_b_D_f"],
                                    env["L0_attn_b_D_r"])] * NCORE),
        "W_outT": E("W_outT", (DI, D), F32,
                    lambda env: [np.ascontiguousarray(
                        env["L0_attn_b_out_proj_w"].T)] * NCORE),
        "o_x": x3})
    _ph(p, "f", phase_ffn, _ffn_io(p, x3, x4))
    _ph(p, "d", phase_l1a_xd, {
        "xT": x4,
        "W_down": E("W_down", (D, D), F32,
                    lambda env: [env["L1_attn_a_W_down"]] * NCORE),
        "o_xd": p.dout("o_xd", (D, TC), BF16)})
    return _fin(p)


def build_LA8():
    p = _new_prog("LA8")
    E = p.ext
    x4 = E("x4", (D, TC), F32, percore("x4"))
    x5 = p.dout("o_x5", (D, TC), F32)
    _ph(p, "a", phase_l1a_att, {
        "i_q": E("i_q", (D, TC), BF16, percore("xd")),
        "kvAG": E("kvAG", (4 * D, TC), BF16, percore("kvAG")),
        "tkT": E("tkT", (64, 1024), F32,
                 lambda env: [_tt_heads(env["L1_attn_a_tokeys"])] * NCORE),
        "tqT": E("tqT", (64, 1024), F32,
                 lambda env: [_tt_heads(env["L1_attn_a_toqueries"])] * NCORE),
        "tvT": E("tvT", (64, 1024), F32,
                 lambda env: [_tt_heads(env["L1_attn_a_tovalues"])] * NCORE),
        "W_unify": E("W_unify", (D, D), F32,
                     lambda env: [env["L1_attn_a_W_unify"]] * NCORE),
        "bu": E("bu", (128, 8), F32,
                lambda env: [col128(env["L1_attn_a_b_unify"])] * NCORE),
        "xT": x4, "o_x": x5})
    _ph(p, "n", phase_ln_dump, {
        "xT": x5,
        "o_xn": p.dout("o_xn", (D, TC), BF16),
        "o_edge": p.dout("o_edge", (D, 1), BF16)})
    return _fin(p)


def _l1f_io(p, xin, xnkey, edgekey, xout):
    E = p.ext
    return {
        "xT": xin,
        "i_xn_body": E("i_xnb", (D, TC), BF16, percore(xnkey)),
        "edgeAG": E("edgeAG1", (4 * D, 1), BF16, percore(edgekey)),
        "hmask": E("hmask", (128, 4), F32, lambda env: mk_mask(-1)),
        "W_rT": E("l1f_WrT", (D, D), F32,
                  lambda env: [np.ascontiguousarray(env["L1_ffn_W_r"].T)] * NCORE),
        "W_kT": E("l1f_WkT", (D, D), F32,
                  lambda env: [np.ascontiguousarray(env["L1_ffn_W_k"].T)] * NCORE),
        "W_vT": E("l1f_WvT", (D, D), F32,
                  lambda env: [np.ascontiguousarray(env["L1_ffn_W_v"].T)] * NCORE),
        "mu2": E("l1f_mu2", (128, 16), F32,
                 lambda env: [col128(env["L1_ffn_mu_r"],
                                     env["L1_ffn_mu_k"])] * NCORE),
        "o_x": xout}


def build_LA9():
    p = _new_prog("LA9")
    E = p.ext
    x5 = E("x5", (D, TC), F32, percore("x5"))
    x6 = p.dout("o_x6", (D, TC), F32)
    _ph(p, "f", phase_l1f, _l1f_io(p, x5, "xn1", "edge1_AG", x6))
    io = {
        "xT": x6,
        "Wqkvg": E("W4", (D, 512), F32,
                   lambda env: [_cat(env["L1_attn_b_Wq"], env["L1_attn_b_Wk"],
                                     env["L1_attn_b_Wv"],
                                     env["L1_attn_b_Wg"])] * NCORE),
        "bqkvg": E("b4", (128, 4), F32,
                   lambda env: [col128(env["L1_attn_b_bq"], env["L1_attn_b_bk"],
                                       env["L1_attn_b_bv"],
                                       env["L1_attn_b_bg"])] * NCORE)}
    for n in ("q", "k", "v", "g"):
        io[f"o_{n}"] = p.dout(f"o_{n}", (128, TC), F32)
    _ph(p, "t", phase_l1b_tok, io)
    return _fin(p)


def build_LA10():
    p = _new_prog("LA10")
    E = p.ext
    _ph(p, "s", phase_l1b_scan, {
        "qAG": E("qAG", (1024, TC), F32, percore("qAG")),
        "kAG": E("kAG", (1024, TC), F32, percore("kAG")),
        "gAG": E("gAG", (1024, TC), F32, percore("gAG")),
        "vOut": E("vOut", (128, TC), F32, percore("vOut")),
        "yIn2": p.dout("o_y2", (128, TC), F32)})
    return _fin(p)


def build_LA11():
    p = _new_prog("LA11")
    E = p.ext
    x6 = E("x6", (D, TC), F32, percore("x6"))
    x7 = p.dout("o_x7", (D, TC), F32)
    _ph(p, "o", phase_l1b_out, {
        "xT": x6,
        "i_y": E("i_y", (128, TC), F32, percore("yOut2")),
        "Wo": E("Wo", (128, D), F32,
                lambda env: [env["L1_attn_b_Wo"]] * NCORE),
        "bgb": E("bgb", (128, 24), F32,
                 lambda env: [col128(env["L1_attn_b_bo"],
                                     env["L1_attn_b_gamma"],
                                     env["L1_attn_b_beta"])] * NCORE),
        "o_x": x7})
    _ph(p, "n", phase_ln_dump, {
        "xT": x7,
        "o_xn": p.dout("o_xn", (D, TC), BF16),
        "o_edge": p.dout("o_edge", (D, 1), BF16)})
    return _fin(p)


def build_LA12():
    p = _new_prog("LA12")
    E = p.ext
    x7 = E("x7", (D, TC), F32, percore("x7"))
    _ph(p, "f", phase_l1f,
        _l1f_io(p, x7, "xn2", "edge2_AG", p.dout("o_x", (D, TC), F32)))
    return _fin(p)


_LP = {}


def _get(name, builder):
    if name not in _LP:
        _LP[name] = builder()
    return _LP[name]


def _launch(p, env):
    maps = [dict() for _ in range(NCORE)]
    for name, fn in p.prep.items():
        arrs = fn(env)
        for c in range(NCORE):
            maps[c][name] = arrs[c]
    res = run_bass_kernel_spmd(p.nc, maps, core_ids=PCORES)
    return res.results


def kernel(**ins):
    env = dict(ins)
    g = lambda r, k: [r[c][k] for c in range(NCORE)]

    r = _launch(_get("LA1", build_LA1), env)
    env["xbrA"], env["szA"] = g(r, "o_xbr"), g(r, "o_sz")
    env["edgeA_AG"] = ag4(g(r, "o_edge"))

    r = _launch(_get("LA2", build_LA2), env)
    env["xbA"] = g(r, "o_xb")
    env["delA_o"] = a2a8(g(r, "o_delta"), 256)
    env["xbA_o"] = a2a8(g(r, "o_xb"), 256)
    env["bcA_AG"] = ag8(g(r, "o_bc"))

    r = _launch(_get("LA3", build_LA3), env)
    env["yA_out"] = a2a8(g(r, "o_y"), 256)

    r = _launch(_get("LA4", build_LA4), env)
    env["x2"] = g(r, "o_x2")
    env["xbrB"], env["szB"] = g(r, "o_xbr"), g(r, "o_sz")
    env["edgeB_AG"] = ag4(g(r, "o_edge"))

    r = _launch(_get("LA5", build_LA5), env)
    env["xbF"], env["xbR"] = g(r, "o_xb_f"), g(r, "o_xb_r")
    env["delF_o"] = a2a8(g(r, "o_delta_f"), 256)
    env["xbF_o"] = a2a8(g(r, "o_xb_f"), 256)
    env["bcF_AG"] = ag8(g(r, "o_bc_f"))
    env["delR_o"] = a2a8(g(r, "o_delta_r"), 256)
    env["xbR_o"] = a2a8(g(r, "o_xb_r"), 256)
    env["bcR_AG"] = ag8(g(r, "o_bc_r"))

    r = _launch(_get("LA6", build_LA6), env)
    env["yF_out"] = a2a8(g(r, "o_y_f"), 256)
    env["yR_out"] = a2a8(g(r, "o_y_r"), 256)

    r = _launch(_get("LA7", build_LA7), env)
    env["x4"], env["xd"] = g(r, "o_x4"), g(r, "o_xd")
    env["kvAG"] = ag4(env["xd"])

    r = _launch(_get("LA8", build_LA8), env)
    env["x5"], env["xn1"] = g(r, "o_x5"), g(r, "o_xn")
    env["edge1_AG"] = ag4(g(r, "o_edge"))

    r = _launch(_get("LA9", build_LA9), env)
    env["x6"] = g(r, "o_x6")
    env["qAG"] = ag8(g(r, "o_q"))
    env["kAG"] = ag8(g(r, "o_k"))
    env["gAG"] = ag8(g(r, "o_g"))
    env["vOut"] = a2a8(g(r, "o_v"), 16)

    r = _launch(_get("LA10", build_LA10), env)
    env["yOut2"] = a2a8(g(r, "o_y2"), 16)

    r = _launch(_get("LA11", build_LA11), env)
    env["x7"], env["xn2"] = g(r, "o_x7"), g(r, "o_xn")
    env["edge2_AG"] = ag4(g(r, "o_edge"))

    r = _launch(_get("LA12", build_LA12), env)
    out = np.empty((B, L, D), np.float32)
    for c in range(NCORE):
        b_, t0 = c // GROUP, (c % GROUP) * TC
        out[b_, t0:t0 + TC, :] = r[c]["o_x"].T
    return out


# version 05
# speedup vs baseline: 1.0788x; 1.0788x over previous
import math
from contextlib import ExitStack

import os
import numpy as np
import ml_dtypes

import concourse.bass as bass
import concourse.mybir as mybir
from concourse.bass_types import AP
from concourse.bass_utils import run_bass_kernel_spmd
from concourse.aot_env import allow_ici_all_to_all

F32 = mybir.dt.float32
BF16 = mybir.dt.bfloat16
AF = mybir.ActivationFunctionType
ALU = mybir.AluOpType

NCORE = 8
B = 2
L = 8192
D = 1024
TC = 2048
DI = 2048
NS = 16
GROUP = 4
EPS = 1e-5
bf = ml_dtypes.bfloat16



class Sem:
    def __init__(self, stack, nc, name):
        self.h = stack.enter_context(nc.semaphore(name))
        self.n = 0

    def inc(self, k=1):
        self.n += k
        return self.n

    @property
    def tok(self):
        return (self, self.n)


class Arena:
    def __init__(self, t):
        self.t = t
        self.cols = t.shape[1]
        self.off = 0
        self.hwm = 0

    def alloc(self, w, rows=128):
        assert self.off + w <= self.cols, f"arena overflow {self.off}+{w}>{self.cols}"
        ap = self.t[0:rows, self.off:self.off + w]
        self.off += w
        self.hwm = max(self.hwm, self.off)
        return ap

    def mark(self):
        return self.off

    def release(self, m):
        self.off = m


class Prog:

    def __init__(self, name="prog"):
        self.nc = bass.Bass("TRN2", target_bir_lowering=False)
        self.stack = ExitStack()
        self.scopes = []
        self.prefix = ""
        self.block = None
        self.sems = {}
        self.name = name
        self.ext_cache = {}
        self.prep = {}
        self._z1 = 0

    def sem(self, name):
        if name not in self.sems:
            self.sems[name] = Sem(self.stack, self.nc, name)
        return self.sems[name]

    def _st(self):
        return self.scopes[-1] if self.scopes else self.stack

    def sbuf(self, name, shape, dt):
        return self._st().enter_context(
            self.nc.sbuf_tensor(self.prefix + name, shape, dt))

    def psum(self, name, shape, dt=F32):
        return self._st().enter_context(
            self.nc.psum_tensor(self.prefix + name, shape, dt))

    def begin_program(self, consts):
        self.block = self.stack.enter_context(self.nc.Block())
        self.cmap = {}
        self.cpool = self.stack.enter_context(
            self.nc.sbuf_tensor("cpool", [128, max(1, len(consts))], F32))
        init = self.sem("init")
        @self.block.vector
        def _(v):
            last = None
            for i, val in enumerate(consts):
                self.cmap[val] = i
                last = v.memset(self.cpool[:, i:i + 1], val)
            last.then_inc(init.h, 1)
        init.inc(1)

    def begin(self, consts=(), zero_psums=()):
        init = self.sem("init")
        @self.block.vector
        def _(v):
            last = None
            for ps in zero_psums:
                last = v.memset(ps[:], 0.0)
            if last is None:
                last = v.memset(self.cpool[:, 0:1], 1.0)
            last.then_inc(init.h, 1)
        init.inc(1)
        self._z1 = init.n

    def barrier(self):
        snap = [(s.h, s.n) for s in self.sems.values() if s.n > 0]
        blk = self.block
        for engf in (blk.sync, blk.scalar, blk.vector, blk.tensor, blk.gpsimd):
            @engf
            def _(e, snap=snap):
                for h, v in snap:
                    e.wait_ge(h, v)

    def scope_open(self, tag):
        self.barrier()
        self.scopes.append(ExitStack())
        self.prefix = tag + "_"
        self._dbg_barrier = None
        self._dbg_proj = None
        self._dbg_revchunk = False

    def scope_close(self):
        self.scopes.pop().close()
        self.prefix = ""

    def constf(self, val, p0=0, nrow=128):
        c = self.cmap[val]
        return self.cpool[p0:p0 + nrow, c:c + 1]

    def ext(self, name, shape, dt, fn):
        if name not in self.ext_cache:
            self.ext_cache[name] = self.nc.dram_tensor(
                name, list(shape), dt, kind="ExternalInput").ap()
            self.prep[name] = fn
        return self.ext_cache[name]

    def dram(self, name, shape, dt):
        h = self.nc.dram_tensor(name, list(shape), dt)
        return h.ap()

    def dout(self, name, shape, dt=F32):
        return self.nc.dram_tensor(name, list(shape), dt, kind="ExternalOutput").ap()

    def finish(self):
        self.stack.close()


RG4 = [[0, 1, 2, 3], [4, 5, 6, 7]]
RG8 = [[0, 1, 2, 3, 4, 5, 6, 7]]


def emit_exchange(p, colls):
    cc = p.sem("cc")
    @p.block.gpsimd
    def _(g):
        for kind, i_ap, o_ap, rg in colls:
            if kind == "AllToAll":
                with allow_ici_all_to_all():
                    ins_ = g.collective_compute(
                        kind, ALU.bypass, replica_groups=rg,
                        ins=[i_ap.opt()], outs=[o_ap.opt()])
            else:
                ins_ = g.collective_compute(
                    kind, ALU.bypass, replica_groups=rg,
                    ins=[i_ap.opt()], outs=[o_ap.opt()])
            ins_.then_inc(cc.h, 16)
            cc.inc(16)
            g.wait_ge(cc.h, cc.n)


def rep_row(eng, dst, src_row, n_rep=128):
    rep_ap = AP(src_row.tensor, src_row.offset,
                [[src_row.ap[0][0], 1], [0, n_rep], list(src_row.ap[-1])])
    return eng.dma_start(out=dst, in_=rep_ap)


def tchunks_of(T, step=512):
    return [(s, min(s + step, T)) for s in range(0, T, step)]



def emit_ln_stats(p, x_tiles, x2_pp, psA, psB, ones_col, ready, mu, var, r1s, init_tok=None):
    block = p.block
    pe, act, dve = p.sem("pe"), p.sem("act"), p.sem("dve")
    n_k = len(x_tiles)
    act_base, pe_base = act.n, pe.n

    @block.scalar
    def _(s):
        s.wait_ge(ready[0].h, ready[1])
        for i in range(n_k):
            if i >= 2:
                s.wait_ge(pe.h, pe_base + i - 1)
            s.activation(x2_pp[i % 2], x_tiles[i], AF.Square).then_inc(act.h, 1)
    act.n += n_k

    T = x_tiles[0].shape[-1]
    tch = tchunks_of(T)

    @block.tensor
    def _(t):
        for i in range(n_k):
            if i == 0:
                t.wait_ge(ready[0].h, ready[1])
                if init_tok is not None:
                    t.wait_ge(init_tok[0].h, init_tok[1])
            for (t0, t1) in tch:
                t.matmul(psA[:, t0:t1], ones_col, x_tiles[i][:, t0:t1],
                         start=(i == 0), stop=(i == n_k - 1))
            t.wait_ge(act.h, act_base + i + 1)
            last = None
            for (t0, t1) in tch:
                last = t.matmul(psB[:, t0:t1], ones_col, x2_pp[i % 2][:, t0:t1],
                                start=(i == 0), stop=(i == n_k - 1))
            last.then_inc(pe.h, 1)
    pe.n += n_k
    pe_done = pe.n
    inv_n = 1.0 / (128 * n_k)

    @block.vector
    def _(v):
        v.wait_ge(pe.h, pe_done)
        v.tensor_scalar(mu, psA[0:1, :], inv_n, None, ALU.mult)
        v.tensor_scalar(var, psB[0:1, :], inv_n, None, ALU.mult)
        v.tensor_mul(r1s, mu, mu)
        v.tensor_sub(var, var, r1s).then_inc(dve.h, 1)
    dve.inc(1)
    return dve.tok


def emit_rsqrt_row(p, out_row, in_row, eps_ap, after):
    block = p.block
    act = p.sem("act")

    @block.scalar
    def _(s):
        s.wait_ge(after[0].h, after[1])
        s.activation(out_row, in_row, AF.Ln, bias=eps_ap)
        s.activation(out_row, out_row, AF.Exp, scale=-0.5).then_inc(act.h, 1)
    act.inc(1)
    return act.tok


def emit_proj(p, w_dram, in_tiles, n_do, wbufs, psums, epilogue, in_ready,
              w_rows0=0, extra_in=None, prev_tail=None):
    block = p.block
    pe, dC = p.sem("pe"), p.sem("dC")
    n_k = len(in_tiles)
    T = in_tiles[0].shape[-1]
    tch = tchunks_of(T)

    def w_ap(do):
        return w_dram[:, (w_rows0 + do) * 128:(w_rows0 + do + 1) * 128] \
            .rearrange("(ki p) c -> p ki c", p=128)

    consumed = {}
    load_tok = {}
    wdve = os.environ.get("WDVE", "0") == "1"
    wstage = getattr(p, "_w_f32_stage", None)
    dve_s = p.sem("dve")

    gp_scr = getattr(p, "_gp_scr", None)
    pool_s = p.sem("pool")
    gptouch = os.environ.get("GPTOUCH", "1") == "1"

    def emit_load(do, buf_idx, gate_tok=None):
        if not wdve:
            @block.gpsimd
            def _(g, do=do, buf_idx=buf_idx):
                if gate_tok is not None:
                    g.wait_ge(gate_tok[0].h, gate_tok[1])
                if do < 2 and prev_tail:
                    for tok in prev_tail:
                        g.wait_ge(tok[0].h, tok[1])
                g.dma_start(out=wbufs[buf_idx][:].rearrange("p (ki c) -> p ki c", c=128),
                            in_=w_ap(do)).then_inc(dC.h, 16)
            dC.inc(16)
            if gptouch:
                _ld = dC.n
                @block.gpsimd
                def _(g, buf_idx=buf_idx, _ld=_ld):
                    g.wait_ge(dC.h, _ld)
                    g.tensor_copy(gp_scr, wbufs[buf_idx][:, 1020:1024]).then_inc(pool_s.h, 1)
                pool_s.inc(1)
                return ("pool", pool_s.n)
            return ("dC", dC.n)
        else:
            @block.gpsimd
            def _(g, do=do, buf_idx=buf_idx):
                if gate_tok is not None:
                    g.wait_ge(gate_tok[0].h, gate_tok[1])
                g.dma_start(out=wstage[:].rearrange("p (ki c) -> p ki c", c=128),
                            in_=w_ap(do).bitcast(F32) if False else w_ap(do)).then_inc(dC.h, 16)
            dC.inc(16)
            _ld = dC.n
            @block.vector
            def _(v, buf_idx=buf_idx, _ld=_ld):
                v.wait_ge(dC.h, _ld)
                v.tensor_copy(wbufs[buf_idx][:], wstage[:]).then_inc(dve_s.h, 1)
            dve_s.inc(1)
            return ("dve", dve_s.n)

    for do in range(min(2, n_do)):
        gate = None
        if wdve and do > 0:
            gate = (dve_s, load_tok[do - 1][1])
        load_tok[do] = emit_load(do, do, gate_tok=gate)

    barrier = getattr(p, "_dbg_barrier", None)
    for do in range(n_do):
        ps = psums[do % 2]
        @block.tensor
        def _(t, do=do, ps=ps):
            lt = load_tok[do]
            t.wait_ge({"dC": dC, "dve": dve_s, "pool": p.sem("pool")}[lt[0]].h, lt[1])
            if do == 0:
                if "init" in p.sems:
                    t.wait_ge(p.sems["init"].h, p._z1)
                t.wait_ge(in_ready[0].h, in_ready[1])
                if extra_in is not None:
                    t.wait_ge(extra_in[0].h, extra_in[1])
                if prev_tail:
                    for tok in prev_tail:
                        t.wait_ge(tok[0].h, tok[1])
                if barrier:
                    for (sh, sv) in barrier:
                        t.wait_ge(sh, sv)
            if os.environ.get("SERIAL", "0") == "1" and do >= 1:
                tok = consumed[do - 1]
                t.wait_ge(tok[0].h, tok[1])
            if do >= 2:
                tok = consumed[do - 2]
                t.wait_ge(tok[0].h, tok[1])
            last = None
            for (t0, t1) in (tch[::-1] if getattr(p, "_dbg_revchunk", False) else tch):
                for ki in range(n_k):
                    last = t.matmul(ps[:, t0:t1],
                                    wbufs[do % 2][:, ki * 128:(ki + 1) * 128],
                                    in_tiles[ki][:, t0:t1],
                                    start=(ki == 0), stop=(ki == n_k - 1))
            last.then_inc(pe.h, 1)
        pe.inc(1)
        if do == 0 and getattr(p, "_dbg_proj", None) is not None:
            o_ps, o_wb = p._dbg_proj
            tok = pe.tok
            stg = p._dbg_stage
            p.sem("act").inc(1)
            _t = p.sem("act").n
            @block.scalar
            def _(s, ps=ps, tok=tok, stg=stg, _t=_t):
                s.wait_ge(tok[0].h, tok[1])
                s.copy(stg, ps[:]).then_inc(p.sem("act").h, 1)
                s.wait_ge(p.sem("act").h, _t)
                s.dma_start(out=o_ps[:], in_=stg).then_inc(p.sem("dB").h, 16)
                s.dma_start(out=o_wb[:], in_=wbufs[0][:]).then_inc(p.sem("dB").h, 16)
            p.sem("dB").inc(32)
        consumed[do] = epilogue(do, ps, pe.tok)
        if do + 2 < n_do:
            gate = consumed[do]
            if wdve:
                gate2 = (dve_s, load_tok[do + 1][1])
                @block.gpsimd
                def _(g, gate2=gate2):
                    g.wait_ge(gate2[0].h, gate2[1])
            load_tok[do + 2] = emit_load(do + 2, do % 2, gate_tok=gate)
    return consumed


def emit_softplus_from_psum(p, out_bf, ps, bias_col, s1, s2, mm_tok):
    block = p.block
    act, dve = p.sem("act"), p.sem("dve")

    @block.scalar
    def _(s):
        s.wait_ge(mm_tok[0].h, mm_tok[1])
        s.activation(s1[:], ps, AF.Relu, bias=bias_col)
        s.activation(s2[:], ps, AF.Abs, bias=bias_col).then_inc(act.h, 1)
        s.activation(s2[:], s2[:], AF.Exp, scale=-1.0)
        s.activation(s2[:], s2[:], AF.Ln, bias=1.0).then_inc(act.h, 1)
    act.inc(2)
    ps_free = (act, act.n - 1)
    ln_done = act.n

    @block.vector
    def _(v):
        v.wait_ge(act.h, ln_done)
        v.tensor_add(out_bf[:], s1[:], s2[:]).then_inc(dve.h, 1)
    dve.inc(1)
    return ps_free, dve.tok



def phase_tokA(p, io, inner_ln=True, with_gb=True, dbg=False):
    xT = io["xT"]
    ln_gb = io["ln_gb"]
    W_in = io["W_in"]
    o_xbr = io["o_xbr"]
    o_sz = io["o_sz"]
    o_edge = io["o_edge"]
    if dbg:
        o_xn = p.dout("o_xn", (D, TC), BF16)
        o_rows = p.dout("o_rows", (1, 6 * TC), F32)
        p._dbg_proj = (p.dout("o_ps", (128, TC), F32), p.dout("o_wb", (128, 1024), BF16))

    n_kD = D // 128
    fa = Arena(p.sbuf("fa", [128, 37 * 1024], F32))
    ba = Arena(p.sbuf("ba", [128, 25 * 1024], BF16))
    psA = p.psum("psA", [128, TC], F32)
    psB = p.psum("psB", [128, TC], F32)

    x_t = [fa.alloc(TC) for _ in range(n_kD)]
    x2_pp = [fa.alloc(TC) for _ in range(2)]
    reps = fa.alloc(2 * TC)
    rows = fa.alloc(6 * TC, rows=1)
    gbt = p.sbuf("gbt", [128, 2 * n_kD], F32)
    xn_t = [ba.alloc(TC) for _ in range(n_kD)]
    wbufs = [p.sbuf(f"wb{i}", [128, 128 * n_kD], BF16) for i in range(2)]
    ones128 = p.sbuf("ones128", [128, 128], F32)
    wstage_t = p.sbuf("wstage", [128, 1024], F32)
    gp_scr_t = p.sbuf("gp_scr", [128, 4], BF16)
    stage = [ba.alloc(TC) for _ in range(4)]

    p.begin(consts=(1.0, EPS), zero_psums=(psA, psB))
    ones_c = ones128[:]
    block = p.block
    dA, dB, dC = p.sem("dA"), p.sem("dB"), p.sem("dC")
    dve, act = p.sem("dve"), p.sem("act")

    @block.sync
    def _(s):
        for i in range(n_kD):
            s.dma_start(out=x_t[i], in_=xT[i * 128:(i + 1) * 128, :]).then_inc(dA.h, 16)
        s.dma_start(out=gbt[:], in_=ln_gb[:]).then_inc(dA.h, 16)
    dA.n += 16 * (n_kD + 1)
    x_loaded = (dA, dA.n)

    @block.vector
    def _(v):
        v.wait_ge(p.sem("init").h, p._z1)
        v.memset(ones128[:], 1.0).then_inc(p.sem("init").h, 1)
    p.sem("init").inc(1)

    R = lambda i: rows[0:1, i * TC:(i + 1) * TC]
    mu, var, r0, r1, rr, mr = R(0), R(1), R(2), R(3), R(4), R(5)
    scr = R(3)
    init_tok = (p.sem("init"), p._z1 + 1)
    import os as _os
    if _os.environ.get("STATS", "1") == "0":
        @block.vector
        def _(v):
            v.wait_ge(x_loaded[0].h, x_loaded[1])
            v.wait_ge(init_tok[0].h, init_tok[1])
            v.memset(mu, 0.0)
            v.memset(var, 1.0).then_inc(p.sem("dve").h, 1)
        p.sem("dve").inc(1)
    elif _os.environ.get("VIRGIN", "0") == "1":
        emit_ln_stats(p, x_t, x2_pp, psB[:, :], psB[:, :], ones_c,
                      x_loaded, mu, var, scr, init_tok)
    else:
        emit_ln_stats(p, x_t, x2_pp, psA[:, :], psB[:, :], ones_c,
                      x_loaded, mu, var, scr, init_tok)
    t0 = emit_rsqrt_row(p, r0, var, p.constf(EPS, 0, 1), p.sem("dve").tok)
    if inner_ln:
        @block.vector
        def _(v):
            v.wait_ge(t0[0].h, t0[1])
            v.tensor_mul(r1, r0, r0)
            v.tensor_mul(r1, r1, var).then_inc(dve.h, 1)
        dve.inc(1)
        t1 = emit_rsqrt_row(p, r1, r1, p.constf(EPS, 0, 1), dve.tok)
        @block.vector
        def _(v):
            v.wait_ge(t1[0].h, t1[1])
            v.tensor_mul(rr, r0, r1)
            v.tensor_mul(mr, mu, rr).then_inc(dve.h, 1)
        dve.inc(1)
    else:
        @block.vector
        def _(v):
            v.wait_ge(t0[0].h, t0[1])
            v.tensor_copy(rr, r0)
            v.tensor_mul(mr, mu, r0).then_inc(dve.h, 1)
        dve.inc(1)
    rows_tok = dve.tok

    @block.gpsimd
    def _(g):
        g.wait_ge(rows_tok[0].h, rows_tok[1])
        rep_row(g, reps[:, 0:TC], rr).then_inc(dC.h, 16)
        rep_row(g, reps[:, TC:2 * TC], mr).then_inc(dC.h, 16)
    dC.inc(32)
    reps_tok = (dC, dC.n)

    tmp = x2_pp[0]
    @block.vector
    def _(v):
        v.wait_ge(reps_tok[0].h, reps_tok[1])
        last = None
        for i in range(n_kD):
            v.tensor_mul(tmp, x_t[i], reps[:, 0:TC])
            v.tensor_sub(tmp, tmp, reps[:, TC:2 * TC])
            if with_gb:
                last = v.tensor_scalar(xn_t[i], tmp, gbt[:, i:i + 1],
                                       gbt[:, n_kD + i:n_kD + i + 1], ALU.mult, ALU.add)
            else:
                last = v.tensor_copy(xn_t[i], tmp)
        last.then_inc(dve.h, 1)
    dve.inc(1)
    if os.environ.get("DELAY", "0") == "1":
        @block.vector
        def _(v):
            last2 = None
            for _i in range(10):
                last2 = v.tensor_copy(tmp[:, 0:512], x_t[0][:, 0:512])
            last2.then_inc(dve.h, 1)
        dve.inc(1)
    xn_tok = dve.tok

    p._dbg_stage = x2_pp[1]
    p._w_f32_stage = wstage_t[:]
    p._gp_scr = gp_scr_t[:]
    if dbg:
        o_pre = p.dout("o_pre", (128, TC), F32)
        _do_pre = os.environ.get("PRE", "1") == "1"
        if _do_pre:
            act.inc(1)
            _pt = act.n
            @block.scalar
            def _(s, _pt=_pt):
                s.wait_ge(p.sem("dve").h, p.sem("dve").n)
                s.copy(x2_pp[1], psA[:]).then_inc(act.h, 1)
                s.wait_ge(act.h, _pt)
                s.dma_start(out=o_pre[:], in_=x2_pp[1]).then_inc(dB.h, 16)
            dB.inc(16)
        else:
            @block.gpsimd
            def _(g):
                g.dma_start(out=o_pre[:], in_=x2_pp[1]).then_inc(dB.h, 16)
            dB.inc(16)
    if dbg:
        p._dbg_barrier = [(p.sem(n).h, p.sem(n).n) for n in ("dA", "dB", "dC", "dve", "act")]
        import os as _os
        p._dbg_revchunk = _os.environ.get("REV", "0") == "1"
    store_tok = {}

    def epi(do, ps, mm_tok):
        st = stage[do % 4]
        act.inc(1)
        ps_free = (act, act.n)
        @block.scalar
        def _(s, do=do, ps=ps, mm_tok=mm_tok, st=st, psf=ps_free[1]):
            s.wait_ge(mm_tok[0].h, mm_tok[1])
            if do >= 4:
                s.wait_ge(dB.h, 16 * store_tok[do - 4])
            if do < 16:
                if os.environ.get("VEVAC", "0") == "1":
                    pass
                s.copy(st, ps[:]).then_inc(act.h, 1)
            else:
                s.activation(st, ps[:], AF.Silu).then_inc(act.h, 1)
            s.wait_ge(act.h, psf)
            dst = o_xbr if do < 16 else o_sz
            r0_ = (do % 16) * 128
            s.dma_start(out=dst[r0_:r0_ + 128, :], in_=st).then_inc(dB.h, 16)
            if do < 16:
                s.dma_start(out=o_edge[r0_:r0_ + 128, 0:3], in_=st[:, 0:3]) \
                    .then_inc(dB.h, 16)
                s.dma_start(out=o_edge[r0_:r0_ + 128, 3:6], in_=st[:, TC - 3:TC]) \
                    .then_inc(dB.h, 16)
        dB.inc(48 if do < 16 else 16)
        store_tok[do] = dB.n // 16
        return ps_free

    if dbg:
        @block.gpsimd
        def _(g):
            g.wait_ge(xn_tok[0].h, xn_tok[1])
            for i in range(n_kD):
                g.dma_start(out=o_xn[i * 128:(i + 1) * 128, :], in_=xn_t[i]).then_inc(dC.h, 16)
            g.dma_start(out=o_rows[:], in_=rows).then_inc(dC.h, 16)
        dC.inc(16 * (n_kD + 1))

    emit_proj(p, W_in, xn_t, 2 * DI // 128, wbufs, [psA, psB], epi, xn_tok)

    @block.gpsimd
    def _(g):
        g.wait_ge(dB.h, dB.n)
    assert fa.hwm <= fa.cols and ba.hwm <= ba.cols




def emit_conv_silu(p, xbr_tiles, xb_tiles, convw, convb, cacc, xbr_ready, o_xb=None,
                   halo=3):
    block = p.block
    dve, act, dB = p.sem("dve"), p.sem("act"), p.sem("dB")
    toks = []
    n_dt = len(xbr_tiles)
    T = xbr_tiles[0].shape[-1] - halo
    prev_silu = None
    for dt in range(n_dt):
        rd = xbr_ready[dt]
        @block.vector
        def _(v, dt=dt, rd=rd, prev_silu=prev_silu):
            v.wait_ge(rd[0].h, rd[1])
            if prev_silu is not None:
                v.wait_ge(prev_silu[0].h, prev_silu[1])
            sh = halo - 3
            v.tensor_scalar(cacc[:, 0:T], xbr_tiles[dt][:, sh:sh + T],
                            convw[:, dt * 4:dt * 4 + 1], None, ALU.mult)
            last = None
            for j in range(1, 4):
                last = v.scalar_tensor_tensor(cacc[:, 0:T],
                                              xbr_tiles[dt][:, sh + j:sh + j + T],
                                              convw[:, dt * 4 + j:dt * 4 + j + 1],
                                              cacc[:, 0:T], ALU.mult, ALU.add)
            last.then_inc(dve.h, 1)
        dve.inc(1)
        cv = dve.n
        act.inc(1)
        silu_tok = (act, act.n)
        if o_xb is not None:
            dB.inc(16)
            st_tok = dB.n
        @block.scalar
        def _(s, dt=dt, cv=cv, silu_tok=silu_tok):
            s.wait_ge(dve.h, cv)
            s.activation(xb_tiles[dt], cacc[:, 0:T], AF.Silu,
                         bias=convb[:, dt:dt + 1]).then_inc(act.h, 1)
            if o_xb is not None:
                s.wait_ge(act.h, silu_tok[1])
                s.dma_start(out=o_xb[dt * 128:(dt + 1) * 128, :], in_=xb_tiles[dt]) \
                    .then_inc(dB.h, 16)
        prev_silu = silu_tok
        toks.append(silu_tok)
    return toks


def phase_tokB_a(p, io):
    n_dt = DI // 128
    i_body = io["i_xbr_body"]
    edgeAG = io["edgeAG"]
    hmask_in = io["hmask"]
    convw_in = io["convw"]
    convb_in = io["convb_bdel"]
    W_delta = io["W_delta"]
    W_bc = io["W_bc"]
    o_xb = io["o_xb"]
    o_delta = io["o_delta"]
    o_bc = io["o_bc"]

    fa = Arena(p.sbuf("fa", [128, 8 * 1024], F32))
    ba = Arena(p.sbuf("ba", [128, 71 * 1024], BF16))
    psA = p.psum("psA", [128, TC], F32)
    psB = p.psum("psB", [128, TC], F32)

    xbr_t = [ba.alloc(3 + TC) for _ in range(n_dt)]
    xb_t = [ba.alloc(TC) for _ in range(n_dt)]
    stage = [ba.alloc(TC) for _ in range(2)]
    cacc = fa.alloc(TC)
    sp1, sp2 = fa.alloc(TC), fa.alloc(TC)
    bc_stage = ba.alloc(TC, rows=32)
    cw = p.sbuf("cw", [128, n_dt * 4], F32)
    cb = p.sbuf("cb", [128, 2 * n_dt], F32)
    wbufs = [p.sbuf(f"wb{i}", [128, 128 * n_dt], BF16) for i in range(2)]
    gp_scr_t = p.sbuf("gp_scr", [128, 4], BF16)
    ehall = p.sbuf("ehall", [128, n_dt * 12], BF16)
    hm = p.sbuf("hm", [128, 4], F32)

    p.begin(consts=(1.0,), zero_psums=(psA, psB))
    block = p.block
    dA, dB, dC = p.sem("dA"), p.sem("dB"), p.sem("dC")
    dve, act = p.sem("dve"), p.sem("act")
    p._gp_scr = gp_scr_t[:]

    @block.sync
    def _(s):
        for i in range(n_dt):
            s.dma_start(out=xbr_t[i][:, 3:3 + TC],
                        in_=i_body[i * 128:(i + 1) * 128, :]).then_inc(dA.h, 16)
            for j in range(4):
                s.dma_start(out=ehall[:, i * 12 + j * 3:i * 12 + (j + 1) * 3],
                            in_=edgeAG[j * DI + i * 128:j * DI + (i + 1) * 128, 3:6]) \
                    .then_inc(dA.h, 16)
        s.dma_start(out=cw[:], in_=convw_in[:]).then_inc(dA.h, 16)
        s.dma_start(out=cb[:], in_=convb_in[:]).then_inc(dA.h, 16)
        s.dma_start(out=hm[:], in_=hmask_in[:]).then_inc(dA.h, 16)
    dA.n += 16 * (5 * n_dt + 3)
    all_ld = dA.n
    @block.vector
    def _(v):
        v.wait_ge(dA.h, all_ld)
        last = None
        for i in range(n_dt):
            v.tensor_scalar(cacc[:, 0:3], ehall[:, i * 12:i * 12 + 3],
                            hm[:, 0:1], None, ALU.mult)
            for j in range(1, 4):
                v.scalar_tensor_tensor(cacc[:, 0:3],
                                       ehall[:, i * 12 + j * 3:i * 12 + (j + 1) * 3],
                                       hm[:, j:j + 1], cacc[:, 0:3],
                                       ALU.mult, ALU.add)
            last = v.tensor_copy(xbr_t[i][:, 0:3], cacc[:, 0:3])
        last.then_inc(dve.h, 1)
    dve.inc(1)
    xbr_ready = [(dve, dve.n)] * n_dt

    xb_toks = emit_conv_silu(p, xbr_t, xb_t, cw[:], cb[:, 0:n_dt], cacc,
                             xbr_ready, o_xb=o_xb)
    xb_all = xb_toks[-1]

    store_tok = {}
    sp_guard = {}

    def epi_delta(do, ps, mm_tok):
        st = stage[do % 2]
        act_l, dve_l, dB_l = p.sem("act"), p.sem("dve"), p.sem("dB")
        prev_sum = sp_guard.get(do - 1)
        @block.scalar
        def _(s, do=do, ps=ps, mm_tok=mm_tok, st=st, prev_sum=prev_sum):
            s.wait_ge(mm_tok[0].h, mm_tok[1])
            if prev_sum is not None:
                s.wait_ge(dve_l.h, prev_sum)
            bcol = cb[:, n_dt + do:n_dt + do + 1]
            s.activation(sp1, ps[:], AF.Relu, bias=bcol)
            s.activation(sp2, ps[:], AF.Abs, bias=bcol).then_inc(act_l.h, 1)
            s.activation(sp2, sp2, AF.Exp, scale=-1.0)
            s.activation(sp2, sp2, AF.Ln, bias=1.0).then_inc(act_l.h, 1)
        act_l.inc(2)
        ps_free = (act_l, act_l.n - 1)
        ln_done = act_l.n
        @block.vector
        def _(v, do=do, ln_done=ln_done, st=st):
            v.wait_ge(act_l.h, ln_done)
            if do >= 2:
                v.wait_ge(dB_l.h, 16 * store_tok[do - 2])
            v.tensor_add(st, sp1, sp2).then_inc(dve_l.h, 1)
        dve_l.inc(1)
        sp_guard[do] = dve_l.n
        sum_done = dve_l.n
        @block.scalar
        def _(s, do=do, st=st, sum_done=sum_done):
            s.wait_ge(dve_l.h, sum_done)
            s.dma_start(out=o_delta[do * 128:(do + 1) * 128, :], in_=st).then_inc(dB_l.h, 16)
        dB_l.inc(16)
        store_tok[do] = dB_l.n // 16
        return ps_free

    emit_proj(p, W_delta, [xb_t[i] for i in range(n_dt)], n_dt, wbufs,
              [psA, psB], epi_delta, xb_all)
    wdel_done = p.sem("act").tok
    pe = p.sem("pe")
    wdel_mm_done = pe.tok

    last_ps_free = None
    dCl = p.sem("dC")
    @block.gpsimd
    def _(g):
        g.wait_ge(pe.h, wdel_mm_done[1])
        g.dma_start(out=wbufs[0][:, 0:2 * NS].rearrange("p (ki c) -> p ki c", c=2 * NS)
                    if False else wbufs[0][:].rearrange("p (ki c) -> p ki c", c=128)[:, 0:n_dt, 0:2 * NS],
                    in_=W_bc.rearrange("(ki p) c -> p ki c", p=128)).then_inc(dCl.h, 16)
        g.wait_ge(dCl.h, dCl.n + 16)
        g.tensor_copy(gp_scr_t[:, 0:4],
                      wbufs[0][:, 15 * 128 + 2 * NS - 4:15 * 128 + 2 * NS]) \
            .then_inc(p.sem("pool").h, 1)
    dCl.inc(16)
    p.sem("pool").inc(1)
    bc_w_tok = (p.sem("pool"), p.sem("pool").n)

    act_all = p.sem("act").n
    @block.tensor
    def _(t):
        t.wait_ge(bc_w_tok[0].h, bc_w_tok[1])
        t.wait_ge(p.sem("act").h, act_all)
        last = None
        for (t0, t1) in tchunks_of(TC):
            for ki in range(n_dt):
                last = t.matmul(psA[0:32, t0:t1],
                                wbufs[0][:].rearrange("p (ki c) -> p ki c", c=128)[:, ki, 0:2 * NS],
                                xb_t[ki][:, t0:t1],
                                start=(ki == 0), stop=(ki == n_dt - 1))
        last.then_inc(p.sem("pe").h, 1)
    p.sem("pe").inc(1)
    bc_done = p.sem("pe").tok

    @block.vector
    def _(v):
        v.wait_ge(bc_done[0].h, bc_done[1])
        v.tensor_copy(bc_stage, psA[0:32, :]).then_inc(dve.h, 1)
    dve.inc(1)
    bcc = dve.n
    @block.scalar
    def _(s):
        s.wait_ge(dve.h, bcc)
        s.dma_start(out=o_bc[:], in_=bc_stage).then_inc(dB.h, 16)
    dB.inc(16)

    @block.gpsimd
    def _(g):
        g.wait_ge(dB.h, p.sem("dB").n)




def phase_scan(p, io, reverse=False):
    n_dt = 2
    NQ = 8
    TQ = 2048
    i_delta = io["dOut"]
    i_xb = io["xbOut"]
    i_bc = io["bcAG"]
    i_logA = io["i_logA"]
    ident = io["ident"]
    o_y = io["yIn"]

    ba = Arena(p.sbuf("ba", [128, 45 * 2048], BF16))
    psA = p.psum("psA", [128, TQ], F32)
    psB = p.psum("psB", [128, TQ], F32)
    psums = [psA, psB]

    del_t = [ba.alloc(TQ) for _ in range(n_dt)]
    u_t = [ba.alloc(TQ) for _ in range(n_dt)]
    brep = [ba.alloc(TQ) for _ in range(NS)]
    crep = [ba.alloc(TQ) for _ in range(NS)]
    faa = Arena(p.sbuf("faa", [128, 4 * 1024], F32))
    a_pp = [faa.alloc(TQ) for _ in range(2)]
    b_pp = [ba.alloc(TQ) for _ in range(2)]
    h_pp = [ba.alloc(TQ) for _ in range(2)]
    hc_pp = b_pp
    ystage = [ba.alloc(TQ)]
    idt = p.sbuf("idt", [128, 128], BF16)
    At = p.sbuf("At", [128, 32], F32)
    carr = p.sbuf("carr", [128, 64], F32)
    gp_scr_t = p.sbuf("gp_scr", [128, 4], BF16)
    gp_scr = gp_scr_t[:]
    scan_done_list = []
    prev_evac = {}
    ystore = {}

    p.begin(consts=(1.0,), zero_psums=(psA, psB))
    block = p.block
    dA, dB, dC = p.sem("dA"), p.sem("dB"), p.sem("dC")
    dve, act, pe, pool = p.sem("dve"), p.sem("act"), p.sem("pe"), p.sem("pool")

    @block.sync
    def _(s):
        s.dma_start(out=At[:], in_=i_logA[:]).then_inc(dA.h, 16)
        s.dma_start(out=idt[:], in_=ident[:]).then_inc(dA.h, 16)
    dA.inc(32)
    consts_loaded = dA.n

    @block.scalar
    def _(s):
        s.wait_ge(dA.h, consts_loaded)
        s.activation(At[:], At[:], AF.Exp).then_inc(act.h, 1)
    act.inc(1)
    @block.vector
    def _(v):
        v.wait_ge(act.h, act.n)
        v.tensor_scalar(At[:], At[:], -1.0, None, ALU.mult).then_inc(dve.h, 1)
    dve.inc(1)
    A_ready = dve.n

    Hs = list(range(NQ))
    if reverse:
        Hs = Hs[::-1]
    units = []
    for hi, H in enumerate(Hs):
        for dt in range(n_dt):
            for n in range(NS):
                units.append((hi, H, dt, n))

    scan_done = {}
    yma = {}
    aexp = {}
    evac_tok = {}
    h_load = {}
    rep_tok = {}

    for hi, H in enumerate(Hs):
        @block.sync
        def _(s, H=H, hi=hi):
            if hi >= 1:
                kprev = hi * 32 - 1
                s.wait_ge(dve.h, scan_done_list[kprev])
            for dt in range(n_dt):
                r0 = H * 256 + dt * 128
                s.dma_start(out=del_t[dt], in_=i_delta[r0:r0 + 128, 0:TQ]) \
                    .then_inc(dA.h, 16)
                s.dma_start(out=u_t[dt], in_=i_xb[r0:r0 + 128, 0:TQ]) \
                    .then_inc(dA.h, 16)
        dA.inc(16 * 2 * n_dt)
        q_loaded = dA.n
        @block.vector
        def _(v, q_loaded=q_loaded):
            v.wait_ge(dA.h, q_loaded)
            last = None
            for dt in range(n_dt):
                last = v.tensor_mul(u_t[dt], u_t[dt], del_t[dt])
            last.then_inc(dve.h, 1)
        dve.inc(1)
        h_load[hi] = (q_loaded, dve.n)

        @block.gpsimd
        def _(g, H=H, hi=hi):
            g.wait_ge(dA.h, consts_loaded)
            if hi >= 1:
                kprev = hi * 32 - 1
                g.wait_ge(dve.h, scan_done_list[kprev])
            for n in range(NS):
                rb = H * 32 + n
                rep_row(g, brep[n], i_bc[rb:rb + 1, 0:TQ]).then_inc(dC.h, 16)
                rep_row(g, crep[n], i_bc[rb + NS:rb + NS + 1, 0:TQ]) \
                    .then_inc(dC.h, 16)
            g.wait_ge(dC.h, dC.n + 32 * NS)
            last = None
            for n in range(NS):
                last = g.tensor_copy(gp_scr[:, 0:2], brep[n][:, TQ - 2:TQ])
                last = g.tensor_copy(gp_scr[:, 2:4], crep[n][:, TQ - 2:TQ])
            last.then_inc(pool.h, 1)
        dC.inc(32 * NS)
        pool.inc(1)
        reps_ready = pool.n

        for dt in range(n_dt):
            for n in range(NS):
                k = hi * 32 + dt * 16 + n
                u = dt * 16 + n
                psY = psums[dt % 2]
                @block.scalar
                def _(s, k=k, dt=dt, u=u, hi=hi):
                    if k == 0:
                        s.wait_ge(dve.h, A_ready)
                    s.wait_ge(dA.h, h_load[hi][0])
                    if k >= 2:
                        s.wait_ge(dve.h, scan_done.get(k - 2, 0))
                    s.activation(a_pp[k % 2], del_t[dt], AF.Exp,
                                 scale=At[:, u:u + 1]).then_inc(act.h, 1)
                act.inc(1)
                aexp[k] = act.n
                @block.vector
                def _(v, k=k, dt=dt, n=n, u=u, hi=hi, psY=psY):
                    v.wait_ge(dve.h, h_load[hi][1])
                    v.wait_ge(pool.h, reps_ready)
                    if k >= 2:
                        v.wait_ge(pe.h, yma.get(k - 2, 0))
                    v.tensor_mul(b_pp[k % 2], u_t[dt], brep[n])
                    v.wait_ge(act.h, aexp[k])
                    init = 0.0 if hi % 4 == 0 else carr[:, u:u + 1]
                    if reverse:
                        v.tensor_tensor_scan(h_pp[k % 2][:, ::-1], a_pp[k % 2][:, ::-1],
                                             b_pp[k % 2][:, ::-1], init, ALU.mult,
                                             ALU.add).then_inc(dve.h, 1)
                        v.wait_ge(dve.h, dve.n + 1)
                        v.tensor_copy(carr[:, u:u + 1], h_pp[k % 2][:, 0:1])
                    else:
                        v.tensor_tensor_scan(h_pp[k % 2], a_pp[k % 2], b_pp[k % 2],
                                             init, ALU.mult, ALU.add).then_inc(dve.h, 1)
                        v.wait_ge(dve.h, dve.n + 1)
                        v.tensor_copy(carr[:, u:u + 1], h_pp[k % 2][:, TQ - 1:TQ])
                    v.tensor_mul(hc_pp[k % 2], h_pp[k % 2], crep[n]).then_inc(dve.h, 1)
                dve.inc(2)
                scan_done[k] = dve.n
                @block.tensor
                def _(t, k=k, dt=dt, n=n, hi=hi, psY=psY):
                    t.wait_ge(dve.h, scan_done[k])
                    if n == 0:
                        pv = prev_evac.get(dt % 2)
                        if pv is not None:
                            t.wait_ge(act.h, pv)
                        if k == 0:
                            t.wait_ge(p.sem("init").h, p._z1)
                            t.wait_ge(dA.h, consts_loaded)
                    last = None
                    for tc in range(TQ // 512):
                        last = t.matmul(psY[:, tc * 512:(tc + 1) * 512], idt[:],
                                        hc_pp[k % 2][:, tc * 512:(tc + 1) * 512],
                                        start=(n == 0), stop=(n == NS - 1))
                    last.then_inc(pe.h, 1)
                pe.inc(1)
                yma[k] = pe.n

            klast = hi * 32 + dt * 16 + NS - 1
            act.inc(1)
            ev = act.n
            dB.inc(16)
            stq = dB.n
            @block.scalar
            def _(s, hi=hi, H=H, dt=dt, klast=klast, ev=ev, psY=psums[dt % 2]):
                s.wait_ge(pe.h, yma[klast])
                if (hi * 2 + dt) >= 1:
                    s.wait_ge(dB.h, 16 * ystore[(hi * 2 + dt) - 1])
                s.copy(ystage[0], psY[:]).then_inc(act.h, 1)
                s.wait_ge(act.h, ev)
                s.dma_start(out=o_y[H * 256 + dt * 128:H * 256 + (dt + 1) * 128, 0:TQ],
                            in_=ystage[0]).then_inc(dB.h, 16)
            prev_evac[dt % 2] = ev
            evac_tok[(hi, dt)] = ev
            ystore[hi * 2 + dt] = dB.n // 16
        scan_done_list.extend([scan_done[hi * 32 + i] for i in range(32)])

    @block.gpsimd
    def _(g):
        g.wait_ge(dB.h, dB.n)




def phase_l0a_out(p, io):
    n_dt = DI // 128
    n_do = D // 128
    xT = io["xT"]
    i_y = io["i_y"]
    i_xb = io["i_xb"]
    i_sz = io["i_sz"]
    dp128 = io["dp128"]
    W_out = io["W_out"]
    o_x = io["o_x"]

    fa = Arena(p.sbuf("fa", [128, 30 * 1024], F32))
    ba = Arena(p.sbuf("ba", [128, 39 * 1024], BF16))
    psA = p.psum("psA", [128, TC], F32)
    psB = p.psum("psB", [128, TC], F32)

    x_t = [fa.alloc(TC) for _ in range(n_do)]
    x2_pp = [fa.alloc(TC) for _ in range(2)]
    reps = fa.alloc(2 * TC)
    rows = fa.alloc(3 * TC, rows=1)
    ostage = x2_pp[1]
    g_t = [ba.alloc(TC) for _ in range(n_dt)]
    in_pp = [ba.alloc(TC) for _ in range(3)]
    wbufs = [p.sbuf(f"wb{i}", [128, 128 * n_dt], BF16) for i in range(2)]
    dpt = p.sbuf("dpt", [128, n_dt], F32)
    ones128 = p.sbuf("ones128", [128, 128], F32)
    gp_scr_t = p.sbuf("gp_scr", [128, 4], BF16)

    p.begin(consts=(1.0, EPS), zero_psums=(psA, psB))
    block = p.block
    dA, dB, dC = p.sem("dA"), p.sem("dB"), p.sem("dC")
    dve, act, pe = p.sem("dve"), p.sem("act"), p.sem("pe")
    p._gp_scr = gp_scr_t[:]

    @block.sync
    def _(s):
        for i in range(n_do):
            s.dma_start(out=x_t[i], in_=xT[i * 128:(i + 1) * 128, :]).then_inc(dA.h, 16)
        s.dma_start(out=dpt[:], in_=dp128[:]).then_inc(dA.h, 16)
    dA.n += 16 * (n_do + 1)
    x_loaded = (dA, dA.n)

    @block.vector
    def _(v):
        v.wait_ge(p.sem("init").h, p._z1)
        v.memset(ones128[:], 1.0).then_inc(p.sem("init").h, 1)
    p.sem("init").inc(1)
    init_tok = (p.sem("init"), p._z1 + 1)

    R = lambda i: rows[0:1, i * TC:(i + 1) * TC]
    mu, var, r0 = R(0), R(1), R(2)
    m0 = R(1)
    emit_ln_stats(p, x_t, x2_pp, psA[:, :], psB[:, :], ones128[:],
                  x_loaded, mu, var, R(2), init_tok)
    t0 = emit_rsqrt_row(p, r0, var, p.constf(EPS, 0, 1), p.sem("dve").tok)
    @block.vector
    def _(v):
        v.wait_ge(t0[0].h, t0[1])
        v.tensor_mul(m0, mu, r0).then_inc(dve.h, 1)
    dve.inc(1)
    rows_tok = dve.tok

    @block.gpsimd
    def _(g):
        g.wait_ge(rows_tok[0].h, rows_tok[1])
        rep_row(g, reps[:, 0:TC], r0).then_inc(dC.h, 16)
        rep_row(g, reps[:, TC:2 * TC], m0).then_inc(dC.h, 16)
    dC.inc(32)
    reps_tok = (dC, dC.n)

    @block.vector
    def _(v):
        v.wait_ge(reps_tok[0].h, reps_tok[1])
        last = None
        for i in range(n_do):
            v.tensor_mul(x2_pp[0], x_t[i], reps[:, 0:TC])
            v.tensor_add(x_t[i], x_t[i], x2_pp[0])
            last = v.tensor_sub(x_t[i], x_t[i], reps[:, TC:2 * TC])
        last.then_inc(dve.h, 1)
    dve.inc(1)
    xp_tok = dve.tok

    gtoks = {}
    ld_tok = {}
    for dt in range(n_dt):
        buf = [in_pp[j] for j in range(3)]
        @block.sync
        def _(s, dt=dt, buf=buf):
            if dt >= 1:
                s.wait_ge(dve.h, gtoks[dt - 1])
            s.dma_start(out=buf[0], in_=i_y[dt * 128:(dt + 1) * 128, :]).then_inc(dA.h, 16)
            s.dma_start(out=buf[1], in_=i_xb[dt * 128:(dt + 1) * 128, :]).then_inc(dA.h, 16)
            s.dma_start(out=buf[2], in_=i_sz[dt * 128:(dt + 1) * 128, :]).then_inc(dA.h, 16)
        dA.inc(48)
        ld_tok[dt] = dA.n
        @block.vector
        def _(v, dt=dt, buf=buf):
            v.wait_ge(dA.h, ld_tok[dt])
            v.scalar_tensor_tensor(x2_pp[1], buf[1], dpt[:, dt:dt + 1], buf[0],
                                   ALU.mult, ALU.add)
            v.tensor_mul(g_t[dt], x2_pp[1], buf[2]).then_inc(dve.h, 1)
        dve.inc(1)
        gtoks[dt] = dve.n
    g_all = (dve, dve.n)

    store_tok = {}

    def epi_out(do, ps, mm_tok):
        dve_l, dB_l = p.sem("dve"), p.sem("dB")
        @block.vector
        def _(v, do=do, ps=ps, mm_tok=mm_tok):
            v.wait_ge(mm_tok[0].h, mm_tok[1])
            if do >= 1:
                v.wait_ge(dB_l.h, 16 * store_tok[do - 1])
            v.tensor_add(ostage, ps[:], x_t[do]).then_inc(dve_l.h, 1)
        dve_l.inc(1)
        ps_free = (dve_l, dve_l.n)
        @block.scalar
        def _(s, do=do, psf=ps_free[1]):
            s.wait_ge(dve_l.h, psf)
            s.dma_start(out=o_x[do * 128:(do + 1) * 128, :], in_=ostage).then_inc(dB_l.h, 16)
        dB_l.inc(16)
        store_tok[do] = dB_l.n // 16
        return ps_free

    emit_proj(p, W_out, [g_t[i] for i in range(n_dt)], n_do, wbufs,
              [psA, psB], epi_out, g_all, extra_in=xp_tok)

    @block.gpsimd
    def _(g):
        g.wait_ge(dB.h, p.sem("dB").n)




def emit_ln_streamed(p, xT, n_k, T, xs_pp, sq_pp, psA_l, psB_l, ones_col, init_tok,
                     xn_t, reps, rows_mu, rows_var, rows_r0, rows_m0, eps_ap,
                     gb_cols=None, extra_scale_row=None):
    block = p.block
    dA, dC = p.sem("dA"), p.sem("dC")
    dve, act, pe = p.sem("dve"), p.sem("act"), p.sem("pe")
    tch = tchunks_of(T)
    sq_done = {}
    mm_done = {}
    ld = {}
    for i in range(n_k):
        @block.sync
        def _(s, i=i):
            if i >= 2:
                s.wait_ge(pe.h, mm_done[i - 2])
            s.dma_start(out=xs_pp[i % 2], in_=xT[i * 128:(i + 1) * 128, :]).then_inc(dA.h, 16)
        dA.inc(16)
        ld[i] = dA.n
        @block.scalar
        def _(s, i=i):
            s.wait_ge(dA.h, ld[i])
            if i >= 2:
                s.wait_ge(pe.h, mm_done[i - 2])
            s.activation(sq_pp[i % 2], xs_pp[i % 2], AF.Square).then_inc(act.h, 1)
        act.inc(1)
        sq_done[i] = act.n
        @block.tensor
        def _(t, i=i):
            if i == 0:
                t.wait_ge(init_tok[0].h, init_tok[1])
            t.wait_ge(act.h, sq_done[i])
            last = None
            for ci, (t0, t1) in enumerate(tch):
                t.matmul(psA_l[ci][:, 0:t1 - t0], ones_col, xs_pp[i % 2][:, t0:t1],
                         start=(i == 0), stop=(i == n_k - 1))
                last = t.matmul(psB_l[ci][:, 0:t1 - t0], ones_col, sq_pp[i % 2][:, t0:t1],
                                start=(i == 0), stop=(i == n_k - 1))
            last.then_inc(pe.h, 1)
        pe.inc(1)
        mm_done[i] = pe.n
    stats_done = pe.n
    inv_n = 1.0 / (128 * n_k)
    @block.vector
    def _(v):
        v.wait_ge(pe.h, stats_done)
        for ci, (t0, t1) in enumerate(tch):
            v.tensor_scalar(rows_mu[:, t0:t1], psA_l[ci][0:1, 0:t1 - t0], inv_n, None, ALU.mult)
            v.tensor_scalar(rows_var[:, t0:t1], psB_l[ci][0:1, 0:t1 - t0], inv_n, None, ALU.mult)
        v.tensor_mul(rows_r0, rows_mu, rows_mu)
        v.tensor_sub(rows_var, rows_var, rows_r0).then_inc(dve.h, 1)
    dve.inc(1)
    t0 = emit_rsqrt_row(p, rows_r0, rows_var, eps_ap, p.sem("dve").tok)
    @block.vector
    def _(v):
        v.wait_ge(t0[0].h, t0[1])
        v.tensor_mul(rows_m0, rows_mu, rows_r0).then_inc(dve.h, 1)
    dve.inc(1)
    rows_tok = dve.tok
    @block.gpsimd
    def _(g):
        g.wait_ge(rows_tok[0].h, rows_tok[1])
        rep_row(g, reps[:, 0:T], rows_r0).then_inc(dC.h, 16)
        rep_row(g, reps[:, T:2 * T], rows_m0).then_inc(dC.h, 16)
    dC.inc(32)
    reps_tok = dC.n
    ap_done = {}
    for i in range(n_k):
        @block.sync
        def _(s, i=i):
            if i == 0:
                s.wait_ge(pe.h, stats_done)
            if i >= 2:
                s.wait_ge(dve.h, ap_done[i - 2])
            s.dma_start(out=xs_pp[i % 2], in_=xT[i * 128:(i + 1) * 128, :]).then_inc(dA.h, 16)
        dA.inc(16)
        l2 = dA.n
        @block.vector
        def _(v, i=i, l2=l2):
            v.wait_ge(dA.h, l2)
            v.wait_ge(dC.h, reps_tok)
            v.tensor_mul(sq_pp[i % 2], xs_pp[i % 2], reps[:, 0:T])
            last = v.tensor_sub(xn_t[i], sq_pp[i % 2], reps[:, T:2 * T])
            last.then_inc(dve.h, 1)
        dve.inc(1)
        ap_done[i] = dve.n
    return (p.sem("dve"), p.sem("dve").n)


def phase_ffn(p, io):
    DH = 4096
    n_kD, n_dh = D // 128, DH // 128
    TH = 512
    NTH = TC // TH
    xT = io["xT"]
    w12 = io["w12"]
    s12 = io["s12"]
    h1T = io["h1T"]
    h1s = io["h1s"]
    h2T = io["h2T"]
    h2s = io["h2s"]
    w3T = io["w3T"]
    w3s = io["w3s"]
    o_x = io["o_x"]

    fa = Arena(p.sbuf("fa", [128, 21 * 1024], F32))
    ba = Arena(p.sbuf("ba", [128, 49 * 1024], BF16))
    psall = [p.psum(f"ps{i}", [128, 512], F32) for i in range(8)]

    xs_pp = [fa.alloc(TC) for _ in range(2)]
    sq_pp = [fa.alloc(TC) for _ in range(2)]
    reps = fa.alloc(2 * TC)
    rows = fa.alloc(3 * TC, rows=1)
    silu_scr = fa.alloc(TH)
    ostage = fa.alloc(TH)
    xn_t = [ba.alloc(TC) for _ in range(n_kD)]
    h_t = [ba.alloc(TH) for _ in range(n_dh)]
    g_t = [ba.alloc(TH) for _ in range(n_dh)]
    wb12 = [p.sbuf(f"wb12_{i}", [128, 2 * 128 * n_kD], BF16) for i in range(2)]
    wbh = [p.sbuf(f"wbh{i}", [128, 128 * n_dh], BF16) for i in range(2)]
    sc = p.sbuf("sc", [128, 4 * n_dh + n_kD], F32)
    ones128 = p.sbuf("ones128", [128, 128], F32)
    gp_scr_t = p.sbuf("gp_scr", [128, 4], BF16)

    p.begin(consts=(1.0, EPS), zero_psums=psall)
    block = p.block
    dA, dB, dC = p.sem("dA"), p.sem("dB"), p.sem("dC")
    dve, act, pe = p.sem("dve"), p.sem("act"), p.sem("pe")
    p._gp_scr = gp_scr_t[:]

    @block.sync
    def _(s):
        s.dma_start(out=sc[:, 0:2 * n_dh], in_=s12[:]).then_inc(dA.h, 16)
        s.dma_start(out=sc[:, 2 * n_dh:3 * n_dh], in_=h1s[:]).then_inc(dA.h, 16)
        s.dma_start(out=sc[:, 3 * n_dh:4 * n_dh], in_=h2s[:]).then_inc(dA.h, 16)
        s.dma_start(out=sc[:, 4 * n_dh:4 * n_dh + n_kD], in_=w3s[:]).then_inc(dA.h, 16)
    dA.inc(64)
    sc_loaded = (dA, dA.n)

    @block.vector
    def _(v):
        v.wait_ge(p.sem("init").h, p._z1)
        v.memset(ones128[:], 1.0).then_inc(p.sem("init").h, 1)
    p.sem("init").inc(1)
    init_tok = (p.sem("init"), p._z1 + 1)

    R = lambda i: rows[0:1, i * TC:(i + 1) * TC]
    xn_tok = emit_ln_streamed(p, xT, n_kD, TC, xs_pp, sq_pp, psall[0:4], psall[4:8],
                              ones128[:], init_tok, xn_t, reps,
                              R(0), R(1), R(2), R(1), p.constf(EPS, 0, 1))

    prev_tail = [xn_tok, sc_loaded]
    add_done = {}
    store_tok = {}
    for th in range(NTH):
        tsl = slice(th * TH, (th + 1) * TH)
        xn_h = [t[:, tsl] for t in xn_t]
        h_toks = {}

        def epi12(do, psa, psb, mm_tok, th=th):
            act_l, dve_l = p.sem("act"), p.sem("dve")
            @block.scalar
            def _(s, do=do, psa=psa, mm_tok=mm_tok):
                s.wait_ge(mm_tok[0].h, mm_tok[1])
                if do >= 2:
                    s.wait_ge(dve_l.h, h_toks[do - 2])
                s.activation(silu_scr, psa[:], AF.Silu,
                             scale=sc[:, do:do + 1]).then_inc(act_l.h, 1)
            act_l.inc(1)
            silu_done = act_l.n
            @block.vector
            def _(v, do=do, psb=psb, silu_done=silu_done):
                v.wait_ge(act_l.h, silu_done)
                v.scalar_tensor_tensor(h_t[do][:], psb[:], sc[:, n_dh + do:n_dh + do + 1],
                                       silu_scr, ALU.mult, ALU.mult).then_inc(dve_l.h, 1)
            dve_l.inc(1)
            h_toks[do] = dve_l.n
            return (dve_l, dve_l.n)

        cons = emit_proj12(p, w12, xn_h, n_dh, wb12, [psall[0], psall[2]],
                           [psall[1], psall[3]], epi12, prev_tail[0],
                           prev_tail=prev_tail)
        h_all = (p.sem("dve"), p.sem("dve").n)
        p12_tail = [cons[n_dh - 2], cons[n_dh - 1]]

        def mk_epi_gelu(dst_tiles, scol0, toks):
            def epi(do, ps, mm_tok):
                act_l = p.sem("act")
                @block.scalar
                def _(s, do=do, ps=ps, mm_tok=mm_tok):
                    s.wait_ge(mm_tok[0].h, mm_tok[1])
                    s.activation(dst_tiles[do][:], ps[:], AF.Gelu,
                                 scale=sc[:, scol0 + do:scol0 + do + 1]).then_inc(act_l.h, 1)
                act_l.inc(1)
                toks[do] = act_l.n
                return (act_l, act_l.n)
            return epi

        g1_toks = {}
        cons = emit_proj(p, h1T, [t[:] for t in h_t], n_dh, wbh, [psall[4], psall[5]],
                         mk_epi_gelu(g_t, 2 * n_dh, g1_toks), h_all,
                         prev_tail=p12_tail)
        g1_all = (p.sem("act"), p.sem("act").n)
        g1_tail = [cons[n_dh - 2], cons[n_dh - 1]]
        g2_toks = {}
        cons = emit_proj(p, h2T, [t[:] for t in g_t], n_dh, wbh, [psall[6], psall[7]],
                         mk_epi_gelu(h_t, 3 * n_dh, g2_toks), g1_all,
                         prev_tail=g1_tail)
        g2_all = (p.sem("act"), p.sem("act").n)
        g2_tail = [cons[n_dh - 2], cons[n_dh - 1]]

        def epi_out(do, ps, mm_tok, th=th):
            dve_l, dB_l, dA_l = p.sem("dve"), p.sem("dB"), p.sem("dA")
            xr = xs_pp[do % 2][:, 0:TH]
            gi = th * n_kD + do
            @block.sync
            def _(s, do=do, th=th, xr=xr, gi=gi):
                if gi >= 2:
                    s.wait_ge(dve_l.h, add_done[gi - 2])
                else:
                    s.wait_ge(xn_tok[0].h, xn_tok[1])
                s.dma_start(out=xr, in_=xT[do * 128:(do + 1) * 128, th * TH:(th + 1) * TH]) \
                    .then_inc(dA_l.h, 16)
            dA_l.inc(16)
            xl = dA_l.n
            @block.vector
            def _(v, do=do, ps=ps, mm_tok=mm_tok, xl=xl, xr=xr, gi=gi):
                v.wait_ge(mm_tok[0].h, mm_tok[1])
                v.wait_ge(dA_l.h, xl)
                if gi >= 1:
                    v.wait_ge(dB_l.h, 16 * store_tok[gi - 1])
                v.scalar_tensor_tensor(ostage, ps[:], sc[:, 4 * n_dh + do:4 * n_dh + do + 1],
                                       xr, ALU.mult, ALU.add).then_inc(dve_l.h, 1)
            dve_l.inc(1)
            add_done[gi] = dve_l.n
            ps_free = (dve_l, dve_l.n)
            @block.scalar
            def _(s, do=do, th=th, psf=ps_free[1], gi=gi):
                s.wait_ge(dve_l.h, psf)
                s.dma_start(out=o_x[do * 128:(do + 1) * 128, th * TH:(th + 1) * TH],
                            in_=ostage).then_inc(dB_l.h, 16)
            dB_l.inc(16)
            store_tok[gi] = dB_l.n // 16
            return ps_free

        cons = emit_proj(p, w3T, [t[:] for t in h_t], n_kD, wbh, [psall[4], psall[5]],
                         epi_out, g2_all, prev_tail=g2_tail)
        prev_tail = [cons[n_kD - 2], cons[n_kD - 1]]

    @block.gpsimd
    def _(g):
        g.wait_ge(p.sem("dB").h, p.sem("dB").n)


def emit_proj12(p, w_dram, in_tiles, n_do, wbufs, psumsA, psumsB, epilogue, in_ready,
                 prev_tail=None):
    block = p.block
    pe, dC, pool = p.sem("pe"), p.sem("dC"), p.sem("pool")
    n_k = len(in_tiles)
    T = in_tiles[0].shape[-1]
    tch = tchunks_of(T)
    DHcols = w_dram.shape[-1] // 2
    gp_scr = p._gp_scr

    def w_ap(do, half):
        c0 = half * DHcols + do * 128
        return w_dram[:, c0:c0 + 128].rearrange("(ki p) c -> p ki c", p=128)

    consumed = {}
    load_tok = {}

    def emit_load(do, buf_idx):
        @block.gpsimd
        def _(g, do=do, buf_idx=buf_idx):
            if do >= 2:
                tok = consumed[do - 2]
                g.wait_ge(tok[0].h, tok[1])
            elif prev_tail:
                for tok in prev_tail:
                    g.wait_ge(tok[0].h, tok[1])
            g.dma_start(out=wbufs[buf_idx][:, 0:128 * n_k]
                        .rearrange("p (ki c) -> p ki c", c=128),
                        in_=w_ap(do, 0)).then_inc(dC.h, 16)
            g.dma_start(out=wbufs[buf_idx][:, 128 * n_k:256 * n_k]
                        .rearrange("p (ki c) -> p ki c", c=128),
                        in_=w_ap(do, 1)).then_inc(dC.h, 16)
            g.wait_ge(dC.h, dC.n + 32)
            g.tensor_copy(gp_scr[:, 0:4],
                          wbufs[buf_idx][:, 256 * n_k - 4:256 * n_k]).then_inc(pool.h, 1)
        dC.inc(32)
        pool.inc(1)
        load_tok[do] = pool.n

    for do in range(min(2, n_do)):
        emit_load(do, do)
    for do in range(n_do):
        psa, psb = psumsA[do % 2], psumsB[do % 2]
        @block.tensor
        def _(t, do=do, psa=psa, psb=psb):
            t.wait_ge(pool.h, load_tok[do])
            if do == 0:
                if "init" in p.sems:
                    t.wait_ge(p.sems["init"].h, p._z1)
                t.wait_ge(in_ready[0].h, in_ready[1])
                if prev_tail:
                    for tok in prev_tail:
                        t.wait_ge(tok[0].h, tok[1])
            if do >= 2:
                tok = consumed[do - 2]
                t.wait_ge(tok[0].h, tok[1])
            last = None
            for (t0, t1) in tch:
                for ki in range(n_k):
                    t.matmul(psa[:, t0:t1], wbufs[do % 2][:, ki * 128:(ki + 1) * 128],
                             in_tiles[ki][:, t0:t1], start=(ki == 0), stop=(ki == n_k - 1))
                for ki in range(n_k):
                    last = t.matmul(psb[:, t0:t1],
                                    wbufs[do % 2][:, 128 * n_k + ki * 128:128 * n_k + (ki + 1) * 128],
                                    in_tiles[ki][:, t0:t1], start=(ki == 0), stop=(ki == n_k - 1))
            last.then_inc(pe.h, 1)
        pe.inc(1)
        consumed[do] = epilogue(do, psa, psb, pe.tok)
        if do + 2 < n_do:
            emit_load(do + 2, do % 2)
    return consumed




def phase_tokB_b(p, io):
    n_dt = DI // 128
    i_body = io["i_xbr_body"]
    edgeAG = io["edgeAG"]
    hmaskL_in = io["hmaskL"]
    hmaskR_in = io["hmaskR"]
    convw_f = io["convw_f"]
    convw_r = io["convw_r"]
    cb4 = io["cb4"]
    xprojT_f = io["xprojT_f"]
    xprojT_r = io["xprojT_r"]
    dtwT_f = io["dtwT_f"]
    dtwT_r = io["dtwT_r"]
    outs = {}
    for br in ("f", "r"):
        outs[f"xb_{br}"] = io[f"o_xb_{br}"]
        outs[f"delta_{br}"] = io[f"o_delta_{br}"]
        outs[f"bc_{br}"] = io[f"o_bc_{br}"]

    fa = Arena(p.sbuf("fa", [128, 12 * 1024], F32))
    ba = Arena(p.sbuf("ba", [128, 65 * 1024], BF16))
    psA = p.psum("psA", [128, TC], F32)
    psB = p.psum("psB", [128, TC], F32)

    xbr_t = [ba.alloc(6 + TC) for _ in range(n_dt)]
    bmark = ba.mark()
    cacc = fa.alloc(TC)
    sp1, sp2 = fa.alloc(TC), fa.alloc(TC)
    xdbl = p.sbuf("xdbl", [96, TC], BF16)[:, :]
    cwf = p.sbuf("cwf", [128, n_dt * 4], F32)
    cwr = p.sbuf("cwr", [128, n_dt * 4], F32)
    cbt = p.sbuf("cbt", [128, 4 * n_dt], F32)
    ehL = p.sbuf("ehL", [128, n_dt * 12], BF16)
    ehR = p.sbuf("ehR", [128, n_dt * 12], BF16)
    hmL = p.sbuf("hmL", [128, 4], F32)
    hmR = p.sbuf("hmR", [128, 4], F32)
    wpj = p.sbuf("wpj", [128, 16 * 96], BF16)
    wdt = p.sbuf("wdt", [64, DI], BF16)
    gp_scr_t = p.sbuf("gp_scr", [128, 4], BF16)

    p.begin(consts=(1.0,), zero_psums=(psA, psB))
    block = p.block
    dA, dB, dC = p.sem("dA"), p.sem("dB"), p.sem("dC")
    dve, act, pe, pool = p.sem("dve"), p.sem("act"), p.sem("pe"), p.sem("pool")
    p._gp_scr = gp_scr_t[:]

    @block.sync
    def _(s):
        for i in range(n_dt):
            s.dma_start(out=xbr_t[i][:, 3:3 + TC],
                        in_=i_body[i * 128:(i + 1) * 128, :]).then_inc(dA.h, 16)
            for j in range(4):
                s.dma_start(out=ehL[:, i * 12 + j * 3:i * 12 + (j + 1) * 3],
                            in_=edgeAG[j * DI + i * 128:j * DI + (i + 1) * 128, 3:6]) \
                    .then_inc(dA.h, 16)
                s.dma_start(out=ehR[:, i * 12 + j * 3:i * 12 + (j + 1) * 3],
                            in_=edgeAG[j * DI + i * 128:j * DI + (i + 1) * 128, 0:3]) \
                    .then_inc(dA.h, 16)
        s.dma_start(out=cwf[:], in_=convw_f[:]).then_inc(dA.h, 16)
        s.dma_start(out=cwr[:], in_=convw_r[:]).then_inc(dA.h, 16)
        s.dma_start(out=cbt[:], in_=cb4[:]).then_inc(dA.h, 16)
        s.dma_start(out=hmL[:], in_=hmaskL_in[:]).then_inc(dA.h, 16)
        s.dma_start(out=hmR[:], in_=hmaskR_in[:]).then_inc(dA.h, 16)
    dA.n += 16 * (9 * n_dt + 5)
    all_ld = dA.n
    @block.vector
    def _(v):
        v.wait_ge(dA.h, all_ld)
        last = None
        for i in range(n_dt):
            for (eh, hm_, c0) in ((ehL, hmL, 0), (ehR, hmR, 3 + TC)):
                v.tensor_scalar(cacc[:, 0:3], eh[:, i * 12:i * 12 + 3],
                                hm_[:, 0:1], None, ALU.mult)
                for j in range(1, 4):
                    v.scalar_tensor_tensor(cacc[:, 0:3],
                                           eh[:, i * 12 + j * 3:i * 12 + (j + 1) * 3],
                                           hm_[:, j:j + 1], cacc[:, 0:3],
                                           ALU.mult, ALU.add)
                last = v.tensor_copy(xbr_t[i][:, c0:c0 + 3], cacc[:, 0:3])
        last.then_inc(dve.h, 1)
    dve.inc(1)
    loaded = (dve, dve.n)

    for bi, br in enumerate(("f", "r")):
        mark = ba.mark()
        xb_t = [ba.alloc(TC) for _ in range(n_dt)]
        cw = cwf if br == "f" else cwr
        halo_off = 0 if br == "f" else 3
        xb_toks = emit_conv_silu(p, [t[:, halo_off:halo_off + 3 + TC] for t in xbr_t],
                                 xb_t, cw[:], cbt[:, bi * n_dt:(bi + 1) * n_dt], cacc,
                                 [loaded] * n_dt, o_xb=outs[f"xb_{br}"])
        xb_all = xb_toks[-1]

        @block.gpsimd
        def _(g, br=br):
            if br == "r":
                g.wait_ge(pe.h, pe.n)
            g.dma_start(out=wpj[:].rearrange("p (ki c) -> p ki c", c=96),
                        in_=(xprojT_f if br == "f" else xprojT_r)
                        .rearrange("(ki p) c -> p ki c", p=128)).then_inc(dC.h, 16)
            g.dma_start(out=wdt[:], in_=(dtwT_f if br == "f" else dtwT_r)) \
                .then_inc(dC.h, 16)
            g.wait_ge(dC.h, dC.n + 32)
            g.tensor_copy(gp_scr_t[:, 0:4], wpj[:, 15 * 96 + 92:15 * 96 + 96]) \
                .then_inc(pool.h, 1)
            g.tensor_copy(gp_scr_t[0:64, 0:4], wdt[:, DI - 4:DI]).then_inc(pool.h, 1)
        dC.inc(32)
        pool.inc(2)
        w_tok = pool.n

        @block.tensor
        def _(t, br=br, w_tok=w_tok, xb_all=xb_all, xb_t=xb_t):
            t.wait_ge(pool.h, w_tok)
            t.wait_ge(xb_all[0].h, xb_all[1])
            if br == "f":
                t.wait_ge(p.sem("init").h, p._z1)
            last = None
            for (t0, t1) in tchunks_of(TC):
                for ki in range(n_dt):
                    last = t.matmul(psA[0:96, t0:t1], wpj[:, ki * 96:(ki + 1) * 96],
                                    xb_t[ki][:, t0:t1], start=(ki == 0), stop=(ki == n_dt - 1))
            last.then_inc(pe.h, 1)
        pe.inc(1)
        xp_done = pe.n

        @block.vector
        def _(v, xp_done=xp_done):
            v.wait_ge(pe.h, xp_done)
            v.tensor_copy(xdbl, psA[0:96, :]).then_inc(dve.h, 1)
        dve.inc(1)
        xd_tok = dve.n
        @block.scalar
        def _(s, br=br, xd_tok=xd_tok):
            s.wait_ge(dve.h, xd_tok)
            s.dma_start(out=outs[f"bc_{br}"][:], in_=xdbl[64:96, :]).then_inc(dB.h, 16)
        dB.inc(16)

        store_tok = {}
        sp_guard = {}
        cons_d = {}
        for do in range(n_dt):
            @block.tensor
            def _(t, do=do, xd_tok=xd_tok):
                t.wait_ge(dve.h, xd_tok)
                if do >= 2:
                    tok = cons_d[do - 2]
                    t.wait_ge(tok[0].h, tok[1])
                last = None
                for (t0, t1) in tchunks_of(TC):
                    last = t.matmul((psB if do % 2 == 0 else psA)[:, t0:t1],
                                    wdt[:, do * 128:(do + 1) * 128],
                                    xdbl[0:64, t0:t1], start=True, stop=True)
                last.then_inc(pe.h, 1)
            pe.inc(1)
            mm_tok = pe.tok
            ps = psB if do % 2 == 0 else psA
            st = None
            act_l, dve_l, dB_l = act, dve, dB
            prev_sum = sp_guard.get(do - 1)
            @block.scalar
            def _(s, do=do, ps=ps, mm_tok=mm_tok, prev_sum=prev_sum, bi=bi):
                s.wait_ge(mm_tok[0].h, mm_tok[1])
                if prev_sum is not None:
                    s.wait_ge(dve_l.h, prev_sum)
                bcol = cbt[:, (2 + bi) * n_dt + do:(2 + bi) * n_dt + do + 1]
                s.activation(sp1, ps[:], AF.Relu, bias=bcol)
                s.activation(sp2, ps[:], AF.Abs, bias=bcol).then_inc(act_l.h, 1)
                s.activation(sp2, sp2, AF.Exp, scale=-1.0)
                s.activation(sp2, sp2, AF.Ln, bias=1.0).then_inc(act_l.h, 1)
            act_l.inc(2)
            ps_free = (act_l, act_l.n - 1)
            ln_done = act_l.n
            dstage = xb_t[do]
            @block.vector
            def _(v, do=do, ln_done=ln_done, dstage=dstage):
                v.wait_ge(act_l.h, ln_done)
                v.tensor_add(dstage[:], sp1, sp2).then_inc(dve_l.h, 1)
            dve_l.inc(1)
            sp_guard[do] = dve_l.n
            sum_done = dve_l.n
            @block.scalar
            def _(s, do=do, br=br, sum_done=sum_done, dstage=dstage):
                s.wait_ge(dve_l.h, sum_done)
                s.dma_start(out=outs[f"delta_{br}"][do * 128:(do + 1) * 128, :],
                            in_=dstage[:]).then_inc(dB_l.h, 16)
            dB_l.inc(16)
            store_tok[do] = dB_l.n
            cons_d[do] = ps_free
        ba.release(mark)

    @block.gpsimd
    def _(g):
        g.wait_ge(dB.h, dB.n)




def phase_l0b_out(p, io):
    n_dt = DI // 128
    n_do = D // 128
    xT = io["xT"]
    i_y_f = io["i_y_f"]
    i_y_r = io["i_y_r"]
    i_xb_f = io["i_xb_f"]
    i_xb_r = io["i_xb_r"]
    i_sz = io["i_sz"]
    d2 = io["d2"]
    W_outT = io["W_outT"]
    o_x = io["o_x"]

    fa = Arena(p.sbuf("fa", [128, 22 * 1024], F32))
    ba = Arena(p.sbuf("ba", [128, 43 * 1024], BF16))
    psA = p.psum("psA", [128, TC], F32)
    psB = p.psum("psB", [128, TC], F32)

    xs_pp = [fa.alloc(TC) for _ in range(2)]
    sq_pp = [fa.alloc(TC) for _ in range(2)]
    reps = fa.alloc(2 * TC)
    rows = fa.alloc(3 * TC, rows=1)
    vscr = fa.alloc(TC)
    ostage = fa.alloc(TC)
    g_t = [ba.alloc(TC) for _ in range(n_dt)]
    in_pp = [ba.alloc(TC) for _ in range(5)]
    wbufs = [p.sbuf(f"wb{i}", [128, 128 * n_dt], BF16) for i in range(2)]
    dpt = p.sbuf("dpt", [128, 2 * n_dt], F32)
    ones128 = p.sbuf("ones128", [128, 128], F32)
    gp_scr_t = p.sbuf("gp_scr", [128, 4], BF16)

    p.begin(consts=(1.0, EPS), zero_psums=(psA, psB))
    block = p.block
    dA, dB, dC = p.sem("dA"), p.sem("dB"), p.sem("dC")
    dve, act, pe = p.sem("dve"), p.sem("act"), p.sem("pe")
    p._gp_scr = gp_scr_t[:]

    @block.sync
    def _(s):
        s.dma_start(out=dpt[:], in_=d2[:]).then_inc(dA.h, 16)
    dA.inc(16)

    @block.vector
    def _(v):
        v.wait_ge(p.sem("init").h, p._z1)
        v.memset(ones128[:], 1.0).then_inc(p.sem("init").h, 1)
    p.sem("init").inc(1)
    init_tok = (p.sem("init"), p._z1 + 1)

    R = lambda i: rows[0:1, i * TC:(i + 1) * TC]
    tch = tchunks_of(TC)
    sq_done, mm_done, ld = {}, {}, {}
    for i in range(n_do):
        @block.sync
        def _(s, i=i):
            if i >= 2:
                s.wait_ge(pe.h, mm_done[i - 2])
            s.dma_start(out=xs_pp[i % 2], in_=xT[i * 128:(i + 1) * 128, :]).then_inc(dA.h, 16)
        dA.inc(16)
        ld[i] = dA.n
        @block.scalar
        def _(s, i=i):
            s.wait_ge(dA.h, ld[i])
            if i >= 2:
                s.wait_ge(pe.h, mm_done[i - 2])
            s.activation(sq_pp[i % 2], xs_pp[i % 2], AF.Square).then_inc(act.h, 1)
        act.inc(1)
        sq_done[i] = act.n
        @block.tensor
        def _(t, i=i):
            if i == 0:
                t.wait_ge(init_tok[0].h, init_tok[1])
            t.wait_ge(act.h, sq_done[i])
            last = None
            for ci, (t0, t1) in enumerate(tch):
                t.matmul(psA[:, t0:t1], ones128[:], xs_pp[i % 2][:, t0:t1],
                         start=(i == 0), stop=(i == n_do - 1))
                last = t.matmul(psB[:, t0:t1], ones128[:], sq_pp[i % 2][:, t0:t1],
                                start=(i == 0), stop=(i == n_do - 1))
            last.then_inc(pe.h, 1)
        pe.inc(1)
        mm_done[i] = pe.n
    stats_done = pe.n
    mu, var, r0 = R(0), R(1), R(2)
    m0 = R(1)
    @block.vector
    def _(v):
        v.wait_ge(pe.h, stats_done)
        v.tensor_scalar(mu, psA[0:1, :], 1.0 / D, None, ALU.mult)
        v.tensor_scalar(var, psB[0:1, :], 1.0 / D, None, ALU.mult)
        v.tensor_mul(r0, mu, mu)
        v.tensor_sub(var, var, r0).then_inc(dve.h, 1)
    dve.inc(1)
    t0tok = emit_rsqrt_row(p, r0, var, p.constf(EPS, 0, 1), p.sem("dve").tok)
    @block.vector
    def _(v):
        v.wait_ge(t0tok[0].h, t0tok[1])
        v.tensor_mul(m0, mu, r0)
        v.tensor_scalar(r0, r0, 1.0, None, ALU.add).then_inc(dve.h, 1)
    dve.inc(1)
    rows_tok = dve.tok
    @block.gpsimd
    def _(g):
        g.wait_ge(rows_tok[0].h, rows_tok[1])
        rep_row(g, reps[:, 0:TC], r0).then_inc(dC.h, 16)
        rep_row(g, reps[:, TC:2 * TC], m0).then_inc(dC.h, 16)
    dC.inc(32)
    reps_tok = (dC, dC.n)

    gtoks, gld = {}, {}
    for dt in range(n_dt):
        buf = in_pp
        @block.sync
        def _(s, dt=dt, buf=buf):
            if dt >= 1:
                s.wait_ge(dve.h, gtoks[dt - 1])
            s.dma_start(out=buf[0], in_=i_y_f[dt * 128:(dt + 1) * 128, :]).then_inc(dA.h, 16)
            s.dma_start(out=buf[1], in_=i_xb_f[dt * 128:(dt + 1) * 128, :]).then_inc(dA.h, 16)
            s.dma_start(out=buf[2], in_=i_y_r[dt * 128:(dt + 1) * 128, :]).then_inc(dA.h, 16)
            s.dma_start(out=buf[3], in_=i_xb_r[dt * 128:(dt + 1) * 128, :]).then_inc(dA.h, 16)
            s.dma_start(out=buf[4], in_=i_sz[dt * 128:(dt + 1) * 128, :]).then_inc(dA.h, 16)
        dA.inc(80)
        gld[dt] = dA.n
        @block.vector
        def _(v, dt=dt, buf=buf):
            v.wait_ge(dA.h, gld[dt])
            v.scalar_tensor_tensor(vscr, buf[1], dpt[:, dt:dt + 1], buf[0],
                                   ALU.mult, ALU.add)
            v.scalar_tensor_tensor(sq_pp[0], buf[3], dpt[:, n_dt + dt:n_dt + dt + 1],
                                   buf[2], ALU.mult, ALU.add)
            v.tensor_add(vscr, vscr, sq_pp[0])
            v.tensor_mul(g_t[dt], vscr, buf[4]).then_inc(dve.h, 1)
        dve.inc(1)
        gtoks[dt] = dve.n
    g_all = (dve, dve.n)

    store_tok, add_done = {}, {}

    def epi_out(do, ps, mm_tok):
        dve_l, dB_l, dA_l = p.sem("dve"), p.sem("dB"), p.sem("dA")
        xr = xs_pp[do % 2]
        @block.sync
        def _(s, do=do, xr=xr):
            if do >= 2:
                s.wait_ge(dve_l.h, add_done[do - 2])
            s.dma_start(out=xr, in_=xT[do * 128:(do + 1) * 128, :]).then_inc(dA_l.h, 16)
        dA_l.inc(16)
        xl = dA_l.n
        @block.vector
        def _(v, do=do, ps=ps, mm_tok=mm_tok, xl=xl, xr=xr):
            v.wait_ge(mm_tok[0].h, mm_tok[1])
            v.wait_ge(dA_l.h, xl)
            v.wait_ge(dC.h, reps_tok[1])
            if do >= 1:
                v.wait_ge(dB_l.h, 16 * store_tok[do - 1])
            v.tensor_mul(ostage, xr, reps[:, 0:TC])
            v.tensor_sub(ostage, ostage, reps[:, TC:2 * TC])
            v.tensor_add(ostage, ostage, ps[:]).then_inc(dve_l.h, 1)
        dve_l.inc(1)
        add_done[do] = dve_l.n
        ps_free = (dve_l, dve_l.n)
        @block.scalar
        def _(s, do=do, psf=ps_free[1]):
            s.wait_ge(dve_l.h, psf)
            s.dma_start(out=o_x[do * 128:(do + 1) * 128, :], in_=ostage).then_inc(dB_l.h, 16)
        dB_l.inc(16)
        store_tok[do] = dB_l.n // 16
        return ps_free

    emit_proj(p, W_outT, [g_t[i] for i in range(n_dt)], n_do, wbufs,
              [psA, psB], epi_out, g_all)

    @block.gpsimd
    def _(g):
        g.wait_ge(dB.h, p.sem("dB").n)



def phase_ln_dump(p, io):
    n_kD = D // 128
    xT = io["xT"]
    o_xn = io["o_xn"]
    o_edge = io["o_edge"]
    fa = Arena(p.sbuf("fa", [128, 19 * 1024], F32))
    ba = Arena(p.sbuf("ba", [128, 17 * 1024], BF16))
    psA = p.psum("psA", [128, TC], F32)
    psB = p.psum("psB", [128, TC], F32)
    xs_pp = [fa.alloc(TC) for _ in range(2)]
    sq_pp = [fa.alloc(TC) for _ in range(2)]
    reps = fa.alloc(2 * TC)
    rows = fa.alloc(3 * TC, rows=1)
    xn_t = [ba.alloc(TC) for _ in range(n_kD)]
    ones128 = p.sbuf("ones128", [128, 128], F32)
    p.begin(consts=(1.0, EPS), zero_psums=(psA, psB))
    block = p.block
    dve, dB = p.sem("dve"), p.sem("dB")
    @block.vector
    def _(v):
        v.wait_ge(p.sem("init").h, p._z1)
        v.memset(ones128[:], 1.0).then_inc(p.sem("init").h, 1)
    p.sem("init").inc(1)
    init_tok = (p.sem("init"), p._z1 + 1)
    R = lambda i: rows[0:1, i * TC:(i + 1) * TC]
    psA_l = [psA[:, i * 512:(i + 1) * 512] for i in range(4)]
    psB_l = [psB[:, i * 512:(i + 1) * 512] for i in range(4)]
    xn_tok = emit_ln_streamed(p, xT, n_kD, TC, xs_pp, sq_pp, psA_l, psB_l,
                              ones128[:], init_tok, xn_t, reps,
                              R(0), R(1), R(2), R(1), p.constf(EPS, 0, 1))
    st = {}
    @block.scalar
    def _(s):
        s.wait_ge(xn_tok[0].h, xn_tok[1])
        for i in range(n_kD):
            s.dma_start(out=o_xn[i * 128:(i + 1) * 128, :], in_=xn_t[i]).then_inc(dB.h, 16)
            s.dma_start(out=o_edge[i * 128:(i + 1) * 128, 0:1],
                        in_=xn_t[i][:, TC - 1:TC]).then_inc(dB.h, 16)
    dB.inc(32 * n_kD)
    @block.gpsimd
    def _(g):
        g.wait_ge(dB.h, dB.n)


def phase_l1f(p, io):
    n_kD = D // 128
    xT = io["xT"]
    i_xnb = io["i_xn_body"]
    edgeAG = io["edgeAG"]
    hmask_in = io["hmask"]
    W_rT = io["W_rT"]
    W_kT = io["W_kT"]
    W_vT = io["W_vT"]
    mu2 = io["mu2"]
    o_x = io["o_x"]

    fa = Arena(p.sbuf("fa", [128, 10 * 1024], F32))
    ba = Arena(p.sbuf("ba", [128, 64 * 1024], BF16))
    psA = p.psum("psA", [128, TC], F32)
    psB = p.psum("psB", [128, TC], F32)

    xs_pp = [fa.alloc(TC) for _ in range(2)]
    ostage = fa.alloc(TC)
    vscr = fa.alloc(TC)
    xr_t = [ba.alloc(TC) for _ in range(n_kD)]
    xk_t = [ba.alloc(TC) for _ in range(n_kD)]
    _xn_mark = ba.mark()
    xn_t = [ba.alloc(1 + TC) for _ in range(n_kD)]
    ba.release(_xn_mark)
    r_t = [ba.alloc(TC) for _ in range(n_kD)]
    k2_t = [ba.alloc(TC) for _ in range(n_kD)]
    assert r_t[0].offset // 1 >= 0
    wbufs = [p.sbuf(f"wb{i}", [128, 128 * n_kD], BF16) for i in range(2)]
    mut = p.sbuf("mut", [128, 4 * n_kD], F32)
    gp_scr_t = p.sbuf("gp_scr", [128, 4], BF16)
    eh1 = p.sbuf("eh1", [128, 4 * n_kD], BF16)
    hm1 = p.sbuf("hm1", [128, 4], F32)

    p.begin(consts=(1.0,), zero_psums=(psA, psB))
    block = p.block
    dA, dB, dC = p.sem("dA"), p.sem("dB"), p.sem("dC")
    dve, act, pe = p.sem("dve"), p.sem("act"), p.sem("pe")
    p._gp_scr = gp_scr_t[:]

    @block.sync
    def _(s):
        for i in range(n_kD):
            s.dma_start(out=xn_t[i][:, 1:1 + TC],
                        in_=i_xnb[i * 128:(i + 1) * 128, :]).then_inc(dA.h, 16)
            for j in range(4):
                s.dma_start(out=eh1[:, i * 4 + j:i * 4 + j + 1],
                            in_=edgeAG[j * D + i * 128:j * D + (i + 1) * 128, 0:1]) \
                    .then_inc(dA.h, 16)
        s.dma_start(out=mut[:, 0:2 * n_kD], in_=mu2[:]).then_inc(dA.h, 16)
        s.dma_start(out=hm1[:], in_=hmask_in[:]).then_inc(dA.h, 16)
    dA.n += 16 * (5 * n_kD + 2)
    loaded = (dA, dA.n)

    @block.vector
    def _(v):
        v.wait_ge(loaded[0].h, loaded[1])
        v.wait_ge(p.sem("init").h, p._z1)
        for i in range(n_kD):
            v.tensor_scalar(vscr[:, 0:1], eh1[:, i * 4:i * 4 + 1],
                            hm1[:, 0:1], None, ALU.mult)
            for j in range(1, 4):
                v.scalar_tensor_tensor(vscr[:, 0:1], eh1[:, i * 4 + j:i * 4 + j + 1],
                                       hm1[:, j:j + 1], vscr[:, 0:1],
                                       ALU.mult, ALU.add)
            v.tensor_copy(xn_t[i][:, 0:1], vscr[:, 0:1])
        v.tensor_scalar(mut[:, 2 * n_kD:4 * n_kD], mut[:, 0:2 * n_kD], -1.0, 1.0,
                        ALU.mult, ALU.add)
        last = None
        for i in range(n_kD):
            cur = xn_t[i][:, 1:1 + TC]
            prv = xn_t[i][:, 0:TC]
            v.tensor_scalar(vscr, cur, mut[:, i:i + 1], None, ALU.mult)
            v.scalar_tensor_tensor(xr_t[i], prv, mut[:, 2 * n_kD + i:2 * n_kD + i + 1],
                                   vscr, ALU.mult, ALU.add)
            v.tensor_scalar(vscr, cur, mut[:, n_kD + i:n_kD + i + 1], None, ALU.mult)
            last = v.scalar_tensor_tensor(xk_t[i], prv,
                                          mut[:, 3 * n_kD + i:3 * n_kD + i + 1],
                                          vscr, ALU.mult, ALU.add)
        last.then_inc(dve.h, 1)
    dve.inc(1)
    mix_tok = (dve, dve.n)

    def epi_r(do, ps, mm_tok):
        act_l = p.sem("act")
        @block.scalar
        def _(s, do=do, ps=ps, mm_tok=mm_tok):
            s.wait_ge(mm_tok[0].h, mm_tok[1])
            s.activation(r_t[do][:], ps[:], AF.Sigmoid).then_inc(act_l.h, 1)
        act_l.inc(1)
        return (act_l, act_l.n)

    cons = emit_proj(p, W_rT, [t[:] for t in xr_t], n_kD, wbufs, [psA, psB],
                     epi_r, mix_tok)
    r_tail = [cons[n_kD - 2], cons[n_kD - 1]]

    def epi_k(do, ps, mm_tok):
        act_l = p.sem("act")
        @block.scalar
        def _(s, do=do, ps=ps, mm_tok=mm_tok):
            s.wait_ge(mm_tok[0].h, mm_tok[1])
            s.activation(vscr, ps[:], AF.Relu).then_inc(act_l.h, 1)
            s.wait_ge(act_l.h, act_l.n + 1)
            s.activation(k2_t[do][:], vscr, AF.Square).then_inc(act_l.h, 1)
        act_l.inc(2)
        return (act_l, act_l.n)

    cons = emit_proj(p, W_kT, [t[:] for t in xk_t], n_kD, wbufs, [psA, psB],
                     epi_k, mix_tok, prev_tail=r_tail)
    k_tail = [cons[n_kD - 2], cons[n_kD - 1]]
    k_all = (p.sem("act"), p.sem("act").n)

    store_tok, add_done = {}, {}

    def epi_v(do, ps, mm_tok):
        dve_l, dB_l, dA_l = p.sem("dve"), p.sem("dB"), p.sem("dA")
        xr = xs_pp[do % 2]
        @block.sync
        def _(s, do=do, xr=xr):
            if do >= 2:
                s.wait_ge(dve_l.h, add_done[do - 2])
            s.dma_start(out=xr, in_=xT[do * 128:(do + 1) * 128, :]).then_inc(dA_l.h, 16)
        dA_l.inc(16)
        xl = dA_l.n
        @block.vector
        def _(v, do=do, ps=ps, mm_tok=mm_tok, xl=xl, xr=xr):
            v.wait_ge(mm_tok[0].h, mm_tok[1])
            v.wait_ge(dA_l.h, xl)
            if do >= 1:
                v.wait_ge(dB_l.h, 16 * store_tok[do - 1])
            v.tensor_mul(ostage, ps[:], r_t[do][:])
            v.tensor_add(ostage, ostage, xr).then_inc(dve_l.h, 1)
        dve_l.inc(1)
        add_done[do] = dve_l.n
        ps_free = (dve_l, dve_l.n)
        @block.scalar
        def _(s, do=do, psf=ps_free[1]):
            s.wait_ge(dve_l.h, psf)
            s.dma_start(out=o_x[do * 128:(do + 1) * 128, :], in_=ostage).then_inc(dB_l.h, 16)
        dB_l.inc(16)
        store_tok[do] = dB_l.n // 16
        return ps_free

    emit_proj(p, W_vT, [t[:] for t in k2_t], n_kD, wbufs, [psA, psB],
              epi_v, k_all, prev_tail=k_tail)

    @block.gpsimd
    def _(g):
        g.wait_ge(dB.h, p.sem("dB").n)




def phase_l1a_xd(p, io):
    n_kD = D // 128
    xT = io["xT"]
    W_down = io["W_down"]
    o_xd = io["o_xd"]
    fa = Arena(p.sbuf("fa", [128, 19 * 1024], F32))
    ba = Arena(p.sbuf("ba", [128, 21 * 1024], BF16))
    psA = p.psum("psA", [128, TC], F32)
    psB = p.psum("psB", [128, TC], F32)
    xs_pp = [fa.alloc(TC) for _ in range(2)]
    sq_pp = [fa.alloc(TC) for _ in range(2)]
    reps = fa.alloc(2 * TC)
    rows = fa.alloc(3 * TC, rows=1)
    xn_t = [ba.alloc(TC) for _ in range(n_kD)]
    stage = [ba.alloc(TC) for _ in range(2)]
    wbufs = [p.sbuf(f"wb{i}", [128, 128 * n_kD], BF16) for i in range(2)]
    ones128 = p.sbuf("ones128", [128, 128], F32)
    gp_scr_t = p.sbuf("gp_scr", [128, 4], BF16)
    p.begin(consts=(1.0, EPS), zero_psums=(psA, psB))
    block = p.block
    dB = p.sem("dB")
    p._gp_scr = gp_scr_t[:]
    @block.vector
    def _(v):
        v.wait_ge(p.sem("init").h, p._z1)
        v.memset(ones128[:], 1.0).then_inc(p.sem("init").h, 1)
    p.sem("init").inc(1)
    init_tok = (p.sem("init"), p._z1 + 1)
    R = lambda i: rows[0:1, i * TC:(i + 1) * TC]
    psA_l = [psA[:, i * 512:(i + 1) * 512] for i in range(4)]
    psB_l = [psB[:, i * 512:(i + 1) * 512] for i in range(4)]
    xn_tok = emit_ln_streamed(p, xT, n_kD, TC, xs_pp, sq_pp, psA_l, psB_l,
                              ones128[:], init_tok, xn_t, reps,
                              R(0), R(1), R(2), R(1), p.constf(EPS, 0, 1))
    store_tok = {}

    def epi(do, ps, mm_tok):
        act_l, dB_l = p.sem("act"), p.sem("dB")
        st = stage[do % 2]
        @block.scalar
        def _(s, do=do, ps=ps, mm_tok=mm_tok, st=st):
            s.wait_ge(mm_tok[0].h, mm_tok[1])
            if do >= 2:
                s.wait_ge(dB_l.h, 16 * store_tok[do - 2])
            s.copy(st, ps[:]).then_inc(act_l.h, 1)
            s.wait_ge(act_l.h, act_l.n + 1)
            s.dma_start(out=o_xd[do * 128:(do + 1) * 128, :], in_=st).then_inc(dB_l.h, 16)
        act_l.inc(1)
        dB_l.inc(16)
        store_tok[do] = dB_l.n // 16
        return (act_l, act_l.n)

    emit_proj(p, W_down, [t[:] for t in xn_t], n_kD, wbufs, [psA, psB], epi, xn_tok)
    @block.gpsimd
    def _(g):
        g.wait_ge(dB.h, p.sem("dB").n)




def phase_l1a_att(p, io):
    HEADS, S = 16, 64
    KC = 2048
    NKC = L // KC
    QG = 1024
    NQG = TC // QG
    SSC = float(S) ** (-0.25)
    SS2 = float(S) ** (-0.5)
    i_q = io["i_q"]
    kvAG = io["kvAG"]
    tkT = io["tkT"]
    tqT = io["tqT"]
    tvT = io["tvT"]
    W_unify = io["W_unify"]
    bu = io["bu"]
    xT = io["xT"]
    o_x = io["o_x"]
    import os as _os
    _dbg = _os.environ.get("ATTDBG", "0") == "1"
    if _dbg:
        o_hs = p.dout("o_hs", (1, 68), F32)
        o_q0 = p.dout("o_q0", (65, 1024), BF16)
        o_k0 = p.dout("o_k0", (65, 2048), BF16)
        o_acc0 = p.dout("o_acc0", (65, 1024), F32)

    n_kD = D // 128
    fa = Arena(p.sbuf("fa", [128, 7 * 1024], F32))
    ba = Arena(p.sbuf("ba", [128, 22 * 1024], BF16))
    psA = p.psum("psA", [128, TC], F32)
    psB = p.psum("psB", [128, TC], F32)

    Kh = [p.sbuf(f"Kh{i}", [65, KC], BF16) for i in range(2)]
    Vt = [p.sbuf(f"Vt{i}", [128, (KC // 128) * 65], BF16) for i in range(2)]
    Qh = [p.sbuf(f"Qh{i}", [65, QG], BF16) for i in range(HEADS)]
    acch = [p.sbuf(f"acc{i}", [65, QG], F32) for i in range(HEADS)]
    P_t = ba.alloc(2048)
    xdc_pp = [ba.alloc(KC) for _ in range(2)]
    qsrc = [ba.alloc(QG) for _ in range(n_kD)]
    ocat = [ba.alloc(QG) for _ in range(n_kD)]
    ostage = fa.alloc(QG)
    orep = fa.alloc(QG)
    xs_pp = [fa.alloc(QG) for _ in range(2)]
    recr = fa.alloc(1024, rows=1)
    tkbuf = p.sbuf("tkbuf", [128, 3 * HEADS * S], BF16)
    hs = p.sbuf("hs", [1, 4 * HEADS + 4], F32)
    wbufs = [p.sbuf(f"wb{i}", [128, 128 * n_kD], BF16) for i in range(2)]
    but = p.sbuf("but", [128, n_kD], F32)
    ones128 = p.sbuf("ones128", [128, 128], F32)
    onesb = p.sbuf("onesb", [128, 128], BF16)
    qsq = p.sbuf("qsq", [64, 1024], BF16)
    sqscr = p.sbuf("sqscr", [64, HEADS * S], F32)
    gp_scr_t = p.sbuf("gp_scr", [128, 4], BF16)

    p.begin(consts=(1.0, EPS, SSC), zero_psums=(psA, psB))
    block = p.block
    dA, dB, dC = p.sem("dA"), p.sem("dB"), p.sem("dC")
    dve, act, pe, pool = p.sem("dve"), p.sem("act"), p.sem("pe"), p.sem("pool")
    p._gp_scr = gp_scr_t[:]

    @block.vector
    def _(v):
        v.wait_ge(p.sem("init").h, p._z1)
        v.memset(ones128[:], 1.0)
        v.memset(onesb[:], 1.0)
        for i in range(2):
            v.memset(Kh[i][64:65, :], -1.0)
            v.memset(Vt[i][:], 0.0)
        last = None
        for i in range(2):
            last = v.memset(
                Vt[i][:].rearrange("p (kt c) -> p kt c", c=65)[:, :, 64:65], 1.0)
        last.then_inc(p.sem("init").h, 1)
    p.sem("init").inc(1)
    init_tok = (p.sem("init"), p._z1 + 1)

    @block.gpsimd
    def _(g):
        for j, w in enumerate((tkT, tqT, tvT)):
            g.dma_start(out=tkbuf[0:64, j * 1024:(j + 1) * 1024], in_=w[:]).then_inc(dA.h, 16)
            g.dma_start(out=tkbuf[64:128, j * 1024:(j + 1) * 1024], in_=w[:]).then_inc(dA.h, 16)
        g.wait_ge(dA.h, dA.n + 96)
        g.tensor_copy(gp_scr_t[:, 0:4], tkbuf[:, 3 * 1024 - 4:3 * 1024]).then_inc(pool.h, 1)
    dA.inc(96)
    pool.inc(1)
    tk_pool_tok = pool.n
    @block.sync
    def _(s):
        s.dma_start(out=but[:], in_=bu[:]).then_inc(dA.h, 16)
    dA.inc(16)
    tk_tok = dA.n

    mm_tok0 = {}
    cp_tok0 = {}
    for q4 in range(4):
        ps = psA if q4 % 2 == 0 else psB
        for i in range(n_kD):
            part = q4 * n_kD + i
            @block.sync
            def _(s, q4=q4, i=i, part=part):
                if part >= 2:
                    s.wait_ge(pe.h, mm_tok0[part - 2])
                s.dma_start(out=xdc_pp[part % 2][:, 0:KC],
                            in_=kvAG[q4 * D + i * 128:q4 * D + (i + 1) * 128,
                                     0:KC]).then_inc(dA.h, 16)
            dA.inc(16)
            ld1 = dA.n
            @block.scalar
            def _(s, part=part, ld1=ld1):
                s.wait_ge(dA.h, ld1)
                if part >= 2:
                    s.wait_ge(pe.h, mm_tok0[part - 2])
                s.activation(xdc_pp[part % 2][:, 0:KC], xdc_pp[part % 2][:, 0:KC],
                             AF.Square).then_inc(act.h, 1)
            act.inc(1)
            sq1 = act.n
            @block.tensor
            def _(t, q4=q4, i=i, part=part, sq1=sq1, ps=ps):
                if part == 0:
                    t.wait_ge(init_tok[0].h, init_tok[1])
                if i == 0 and q4 >= 2:
                    t.wait_ge(dve.h, cp_tok0[q4 - 2])
                t.wait_ge(act.h, sq1)
                last = None
                for tc in range(4):
                    last = t.matmul(ps[:, tc * 512:(tc + 1) * 512],
                                    onesb[:],
                                    xdc_pp[part % 2][:, tc * 512:(tc + 1) * 512],
                                    start=(i == 0), stop=(i == n_kD - 1))
                last.then_inc(pe.h, 1)
            pe.inc(1)
            mm_tok0[part] = pe.n
        done = pe.n
        @block.vector
        def _(v, q4=q4, done=done, ps=ps):
            v.wait_ge(pe.h, done)
            v.tensor_reduce(hs[0:1, 2 * HEADS + q4:2 * HEADS + q4 + 1], ps[0:1, :],
                            mybir.AxisListType.X, ALU.max).then_inc(dve.h, 1)
        dve.inc(1)
        cp_tok0[q4] = dve.n

    @block.vector
    def _(v):
        v.wait_ge(dve.h, cp_tok0[3])
        v.tensor_reduce(hs[0:1, 4 * HEADS:4 * HEADS + 1], hs[0:1, 2 * HEADS:2 * HEADS + 4],
                        mybir.AxisListType.X, ALU.max).then_inc(dve.h, 1)
    dve.inc(1)
    xdmax_tok = dve.n

    @block.scalar
    def _(s):
        s.wait_ge(pool.h, tk_pool_tok)
        s.wait_ge(dA.h, tk_tok)
        s.activation(sqscr[:], tkbuf[0:64, 0:1024], AF.Square).then_inc(act.h, 1)
    act.inc(1)
    sqk = act.n
    @block.tensor
    def _(t):
        t.wait_ge(act.h, sqk)
        t.wait_ge(dve.h, xdmax_tok)
        last = None
        for tc in range(2):
            last = t.matmul(psB[:, 1024 + tc * 512:1024 + (tc + 1) * 512],
                            ones128[0:64, :], sqscr[:, tc * 512:(tc + 1) * 512],
                            start=True, stop=True)
        last.then_inc(pe.h, 1)
    pe.inc(1)
    frok = pe.n
    @block.vector
    def _(v):
        v.wait_ge(pe.h, frok)
        for h in range(HEADS):
            v.tensor_reduce(hs[0:1, h:h + 1], psB[0:1, 1024 + h * 64:1024 + (h + 1) * 64],
                            mybir.AxisListType.X, ALU.add)
        v.tensor_scalar(hs[0:1, HEADS:2 * HEADS], hs[0:1, 0:HEADS],
                        hs[0:1, 4 * HEADS:4 * HEADS + 1], SS2, ALU.mult, ALU.mult) \
            .then_inc(dve.h, 1)
    dve.inc(1)
    mf_tok = dve.n

    ld_gate = {(0, 0, 0): frok}
    psA_half_gate = {}
    P_half_gate = {}
    accslot_gate = {}
    last_exp_d = {"v": None}
    unify_done = None
    unify_tail = None
    for qg in range(NQG):
        q0 = qg * QG
        @block.sync
        def _(s, qg=qg):
            if qg == 1:
                s.wait_ge(unify_done[0].h, unify_done[1])
            for j in range(n_kD):
                s.dma_start(out=qsrc[j], in_=i_q[j * 128:(j + 1) * 128, q0:q0 + QG]) \
                    .then_inc(dA.h, 16)
        dA.inc(16 * n_kD)
        q_ld = dA.n

        qb = {}
        for h in range(HEADS):
            e0 = 64 * (h % 2)
            @block.tensor
            def _(t, h=h, qg=qg, q_ld=q_ld, e0=e0):
                t.wait_ge(dA.h, q_ld)
                t.wait_ge(dve.h, mf_tok)
                if qg == 1:
                    t.wait_ge(unify_done[0].h, unify_done[1])
                if h >= 1:
                    t.wait_ge(act.h, qb[h - 1])
                last = None
                for tc in range(2):
                    last = t.matmul(psB[0:64, 1024 + tc * 512:1024 + tc * 512 + 512],
                                    tkbuf[e0:e0 + 64, 1024 + h * 64:1024 + (h + 1) * 64],
                                    qsrc[h // 2][e0:e0 + 64, tc * 512:(tc + 1) * 512],
                                    start=True, stop=True)
                last.then_inc(pe.h, 1)
            pe.inc(1)
            qmm = pe.n
            @block.scalar
            def _(s, h=h, qmm=qmm):
                s.wait_ge(pe.h, qmm)
                s.activation(Qh[h][0:64, :], psB[0:64, 1024:2048], AF.Copy,
                             scale=p.constf(SSC, 0, 64)).then_inc(act.h, 1)
                s.wait_ge(act.h, act.n + 1)
                s.activation(qsq[0:64, :], Qh[h][0:64, :], AF.Square).then_inc(act.h, 1)
            act.inc(2)
            qevac = act.n
            @block.tensor
            def _(t, h=h, qevac=qevac):
                t.wait_ge(act.h, qevac)
                last = None
                for tc in range(2):
                    last = t.matmul(psB[0:64, 1024 + tc * 512:1024 + tc * 512 + 512],
                                    onesb[0:64, 0:64], qsq[0:64, tc * 512:(tc + 1) * 512],
                                    start=True, stop=True)
                last.then_inc(pe.h, 1)
            pe.inc(1)
            qn2m = pe.n
            @block.scalar
            def _(s, h=h, qn2m=qn2m):
                s.wait_ge(pe.h, qn2m)
                s.activation(Qh[h][64:65, :], psB[0:1, 1024:2048], AF.Sqrt,
                             scale=hs[0:1, HEADS + h:HEADS + h + 1]).then_inc(act.h, 1)
            act.inc(1)
            qb[h] = act.n
        q_all = (act, act.n)

        @block.vector
        def _(v, qg=qg):
            v.wait_ge(q_all[0].h, q_all[1])
            last = None
            for h in range(HEADS):
                last = v.memset(acch[h][:], 0.0)
            last.then_inc(dve.h, 1)
        dve.inc(1)
        acc0 = dve.n

        for kc in range(NKC):
            k0 = kc * KC
            for hp in range(HEADS // 2):
                @block.sync
                def _(s, kc=kc, hp=hp, qg=qg):
                    gate = ld_gate.get((qg, kc, hp))
                    if gate is not None:
                        s.wait_ge(pe.h, gate)
                    s.dma_start(out=xdc_pp[hp % 2][:, 0:KC],
                                in_=kvAG[kc * D + hp * 128:kc * D + (hp + 1) * 128,
                                         0:KC]).then_inc(dA.h, 16)
                dA.inc(16)
                xld = dA.n
                for i01 in range(2):
                    h = 2 * hp + i01
                    e0 = 64 * i01
                    le = last_exp_d["v"]
                    @block.tensor
                    def _(t, h=h, i01=i01, xld=xld, hp=hp, e0=e0, le=le):
                        t.wait_ge(dA.h, xld)
                        t.wait_ge(dve.h, acc0)
                        if le is not None:
                            t.wait_ge(act.h, le)
                        last = None
                        for tc in range(4):
                            last = t.matmul(psA[0:64, tc * 512:(tc + 1) * 512],
                                            tkbuf[e0:e0 + 64, h * 64:(h + 1) * 64],
                                            xdc_pp[hp % 2][e0:e0 + 64,
                                                           tc * 512:(tc + 1) * 512],
                                            start=True, stop=True)
                        last.then_inc(pe.h, 1)
                    pe.inc(1)
                    kmm = pe.n
                    @block.scalar
                    def _(s, i01=i01, kmm=kmm):
                        s.wait_ge(pe.h, kmm)
                        s.activation(Kh[i01][0:64, :], psA[0:64, :], AF.Copy,
                                     scale=p.constf(SSC, 0, 64)).then_inc(act.h, 1)
                    act.inc(1)
                    kevac = act.n
                    @block.tensor
                    def _(t, h=h, i01=i01, hp=hp, kevac=kevac, e0=e0):
                        t.wait_ge(act.h, kevac)
                        last = None
                        for kt in range(16):
                            last = t.matmul(psA[:, kt * 64:(kt + 1) * 64],
                                            xdc_pp[hp % 2][e0:e0 + 64,
                                                           kt * 128:(kt + 1) * 128],
                                            tkbuf[e0:e0 + 64,
                                                  2048 + h * 64:2048 + (h + 1) * 64],
                                            start=True, stop=True)
                        last.then_inc(pe.h, 1)
                    pe.inc(1)
                    vmm = pe.n
                    @block.scalar
                    def _(s, i01=i01, vmm=vmm):
                        s.wait_ge(pe.h, vmm)
                        s.activation(
                            Vt[i01][:].rearrange("p (kt c) -> p kt c", c=65)[:, :, 0:64],
                            psA[:, 0:1024].rearrange("p (kt c) -> p kt c", c=64),
                            AF.Copy).then_inc(act.h, 1)
                    act.inc(1)
                    kv_ready = act.n

                    for qc in range(QG // 512):
                        slot = qc % 2
                        for ktg in range(KC // 256):
                            halfsel = ktg % 2
                            ghalf = psA_half_gate.get(halfsel)
                            @block.tensor
                            def _(t, h=h, i01=i01, qc=qc, ktg=ktg, halfsel=halfsel,
                                  kv_ready=kv_ready, slot=slot, ghalf=ghalf):
                                t.wait_ge(act.h, kv_ready)
                                if ghalf is not None:
                                    t.wait_ge(act.h, ghalf)
                                if ktg == 0:
                                    ga = accslot_gate.get(slot)
                                    if ga is not None:
                                        t.wait_ge(dve.h, ga)
                                last = None
                                for j in range(2):
                                    kt = ktg * 2 + j
                                    last = t.matmul(
                                        psA[:, halfsel * 1024 + j * 512:
                                            halfsel * 1024 + (j + 1) * 512],
                                        Kh[i01][:, kt * 128:(kt + 1) * 128],
                                        Qh[h][:, qc * 512:(qc + 1) * 512],
                                        start=True, stop=True)
                                last.then_inc(pe.h, 1)
                            pe.inc(1)
                            smm = pe.n
                            gp_ = P_half_gate.get(halfsel)
                            @block.scalar
                            def _(s, halfsel=halfsel, smm=smm, gp_=gp_):
                                s.wait_ge(pe.h, smm)
                                if gp_ is not None:
                                    s.wait_ge(pe.h, gp_)
                                s.activation(P_t[:, halfsel * 1024:(halfsel + 1) * 1024],
                                             psA[:, halfsel * 1024:(halfsel + 1) * 1024],
                                             AF.Exp).then_inc(act.h, 1)
                            act.inc(1)
                            last_exp_d["v"] = act.n
                            psA_half_gate[halfsel] = act.n
                            @block.tensor
                            def _(t, h=h, i01=i01, qc=qc, ktg=ktg, halfsel=halfsel,
                                  slot=slot, myexp=act.n):
                                t.wait_ge(act.h, myexp)
                                last = None
                                for j in range(2):
                                    kt = ktg * 2 + j
                                    last = t.matmul(
                                        psB[0:65, slot * 512:slot * 512 + 512],
                                        Vt[i01][:, kt * 65:(kt + 1) * 65],
                                        P_t[:, halfsel * 1024 + j * 512:
                                            halfsel * 1024 + (j + 1) * 512],
                                        start=(ktg == 0 and j == 0),
                                        stop=(ktg == KC // 256 - 1 and j == 1))
                                last.then_inc(pe.h, 1)
                            pe.inc(1)
                            P_half_gate[halfsel] = pe.n
                        avl = pe.n
                        @block.vector
                        def _(v, h=h, qc=qc, slot=slot, avl=avl):
                            v.wait_ge(pe.h, avl)
                            v.tensor_add(acch[h][:, qc * 512:(qc + 1) * 512],
                                         acch[h][:, qc * 512:(qc + 1) * 512],
                                         psB[0:65, slot * 512:slot * 512 + 512]) \
                                .then_inc(dve.h, 1)
                        dve.inc(1)
                        accslot_gate[slot] = dve.n
                ld_gate[(qg, kc, hp + 1)] = pe.n
                if hp + 1 == HEADS // 2:
                    if kc + 1 < NKC:
                        ld_gate[(qg, kc + 1, 0)] = pe.n
                    elif qg + 1 < NQG:
                        ld_gate[(qg + 1, 0, 0)] = pe.n

        if _dbg and qg == 0:
            accslot_snapshot = accslot_gate.get(1, 0)
            @block.scalar
            def _(s, accslot_snapshot=accslot_snapshot):
                s.wait_ge(dve.h, p.sems["dve"].n)
                s.dma_start(out=o_hs[:], in_=hs[:]).then_inc(dB.h, 16)
                s.dma_start(out=o_q0[:], in_=Qh[0][:]).then_inc(dB.h, 16)
                s.dma_start(out=o_k0[:], in_=Kh[0][:]).then_inc(dB.h, 16)
                s.dma_start(out=o_acc0[:], in_=acch[0][:]).then_inc(dB.h, 16)
            dB.inc(64)

        fin = {}
        for h in range(HEADS):
            @block.vector
            def _(v, h=h):
                v.reciprocal(recr, acch[h][64:65, :]).then_inc(dve.h, 1)
            dve.inc(1)
            rc = dve.n
            @block.gpsimd
            def _(g, h=h, rc=rc):
                g.wait_ge(dve.h, rc)
                if h >= 1:
                    g.wait_ge(dve.h, fin[h - 1])
                rep_row(g, orep[0:64, :], recr, n_rep=64).then_inc(dC.h, 16)
            dC.inc(16)
            rr = dC.n
            @block.vector
            def _(v, h=h, rr=rr):
                v.wait_ge(dC.h, rr)
                v.tensor_mul(ocat[h // 2][64 * (h % 2):64 * (h % 2) + 64, :],
                             acch[h][0:64, :], orep[0:64, :]).then_inc(dve.h, 1)
            dve.inc(1)
            fin[h] = dve.n
        ocat_tok = (p.sem("dve"), p.sem("dve").n)

        store_tok = {}
        add_done = {}

        def epi_u(do, ps, mm_tok, qg=qg, store_tok=store_tok, add_done=add_done):
            dve_l, dB_l, dA_l = p.sem("dve"), p.sem("dB"), p.sem("dA")
            xr = xs_pp[do % 2]
            @block.sync
            def _(s, do=do, xr=xr, qg=qg):
                if do >= 2:
                    s.wait_ge(dve_l.h, add_done[do - 2])
                s.dma_start(out=xr, in_=xT[do * 128:(do + 1) * 128,
                                           qg * QG:(qg + 1) * QG]).then_inc(dA_l.h, 16)
            dA_l.inc(16)
            xl = dA_l.n
            @block.vector
            def _(v, do=do, ps=ps, mm_tok=mm_tok, xl=xl, xr=xr):
                v.wait_ge(mm_tok[0].h, mm_tok[1])
                v.wait_ge(dA_l.h, xl)
                if do >= 1:
                    v.wait_ge(dB_l.h, 16 * store_tok[do - 1])
                v.scalar_tensor_tensor(ostage, ps[:, 0:QG], but[:, do:do + 1], xr,
                                       ALU.add, ALU.add).then_inc(dve_l.h, 1)
            dve_l.inc(1)
            add_done[do] = dve_l.n
            ps_free = (dve_l, dve_l.n)
            @block.scalar
            def _(s, do=do, qg=qg, psf=ps_free[1]):
                s.wait_ge(dve_l.h, psf)
                s.dma_start(out=o_x[do * 128:(do + 1) * 128, qg * QG:(qg + 1) * QG],
                            in_=ostage).then_inc(dB_l.h, 16)
            dB_l.inc(16)
            store_tok[do] = dB_l.n // 16
            return ps_free

        ptail = [ocat_tok] + (unify_tail if unify_tail else [])
        cons = emit_proj(p, W_unify, [t2[:] for t2 in ocat], n_kD, wbufs,
                         [psA, psB], epi_u, ocat_tok, prev_tail=ptail)
        unify_tail = [cons[n_kD - 2], cons[n_kD - 1]]
        unify_done = (p.sem("dve"), p.sem("dve").n)

    @block.gpsimd
    def _(g):
        g.wait_ge(dB.h, p.sem("dB").n)





def phase_l1b_tok(p, io):
    n_kD = D // 128
    S = 128
    xT = io["xT"]
    Wqkvg = io["Wqkvg"]
    bqkvg = io["bqkvg"]
    outs = [io[f"o_{n}"] for n in ("q", "k", "v", "g")]
    fa = Arena(p.sbuf("fa", [128, 41 * 1024], F32))
    ba = Arena(p.sbuf("ba", [128, 2 * 1024], BF16))
    psA = p.psum("psA", [128, TC], F32)
    psB = p.psum("psB", [128, TC], F32)
    xs_pp = [fa.alloc(TC) for _ in range(2)]
    sq_pp = [fa.alloc(TC) for _ in range(2)]
    reps = fa.alloc(2 * TC)
    rows = fa.alloc(3 * TC, rows=1)
    xn_t = [fa.alloc(TC) for _ in range(n_kD)]
    stage = [fa.alloc(TC) for _ in range(2)]
    gstage = sq_pp[1]
    wq = p.sbuf("wq", [128, 4 * n_kD * S], F32)
    bt = p.sbuf("bt", [128, 4], F32)
    ones128 = p.sbuf("ones128", [128, 128], F32)
    gp_scr_t = p.sbuf("gp_scr", [128, 4], BF16)
    p.begin(consts=(1.0, EPS), zero_psums=(psA, psB))
    block = p.block
    dA, dB, dC = p.sem("dA"), p.sem("dB"), p.sem("dC")
    dve, act, pe, pool = p.sem("dve"), p.sem("act"), p.sem("pe"), p.sem("pool")
    p._gp_scr = gp_scr_t[:]
    @block.vector
    def _(v):
        v.wait_ge(p.sem("init").h, p._z1)
        v.memset(ones128[:], 1.0).then_inc(p.sem("init").h, 1)
    p.sem("init").inc(1)
    init_tok = (p.sem("init"), p._z1 + 1)
    @block.sync
    def _(s):
        s.dma_start(out=bt[:], in_=bqkvg[:]).then_inc(dA.h, 16)
    dA.inc(16)
    @block.sync
    def _(s):
        s.dma_start(out=wq[:].rearrange("p (ki c) -> p ki c", c=4 * S),
                    in_=Wqkvg.rearrange("(ki p) c -> p ki c", p=128)).then_inc(dC.h, 16)
    dC.inc(16)
    w_tok_dma = dC.n
    @block.gpsimd
    def _(g):
        g.wait_ge(dC.h, w_tok_dma)
        g.tensor_copy(gp_scr_t[:, 0:2], wq[:, 4 * n_kD * S - 1:4 * n_kD * S].bitcast(BF16)) \
            .then_inc(pool.h, 1)
    pool.inc(1)
    w_tok = pool.n
    R = lambda i: rows[0:1, i * TC:(i + 1) * TC]
    psA_l = [psA[:, i * 512:(i + 1) * 512] for i in range(4)]
    psB_l = [psB[:, i * 512:(i + 1) * 512] for i in range(4)]
    xn_tok = emit_ln_streamed(p, xT, n_kD, TC, xs_pp, sq_pp, psA_l, psB_l,
                              ones128[:], init_tok, xn_t, reps,
                              R(0), R(1), R(2), R(1), p.constf(EPS, 0, 1))
    prev = {}
    for j in range(4):
        ps = psA if j % 2 == 0 else psB
        @block.tensor
        def _(t, j=j, ps=ps):
            t.wait_ge(pool.h, w_tok)
            if j == 0:
                t.wait_ge(xn_tok[0].h, xn_tok[1])
            if j >= 2:
                t.wait_ge(p.sem("dB").h, 16 * prev[j - 2])
            last = None
            for (t0, t1) in tchunks_of(TC):
                for ki in range(n_kD):
                    last = t.matmul(ps[:, t0:t1],
                                    wq[:].rearrange("p (ki c) -> p ki c", c=4 * S)
                                    [:, ki, j * S:(j + 1) * S],
                                    xn_t[ki][:, t0:t1],
                                    start=(ki == 0), stop=(ki == n_kD - 1))
            last.then_inc(pe.h, 1)
        pe.inc(1)
        mm = pe.n
        if j == 3:
            @block.scalar
            def _(s, j=j, ps=ps, mm=mm):
                s.wait_ge(pe.h, mm)
                s.wait_ge(p.sem("dB").h, 16 * prev[j - 2])
                s.activation(gstage, ps[:], AF.Sigmoid, bias=bt[:, j:j + 1]) \
                    .then_inc(act.h, 1)
                s.wait_ge(act.h, act.n + 1)
                s.dma_start(out=outs[j][:], in_=gstage).then_inc(p.sem("dB").h, 16)
            act.inc(1)
            p.sem("dB").inc(16)
            prev[j] = p.sem("dB").n // 16
        else:
            @block.vector
            def _(v, j=j, ps=ps, mm=mm):
                v.wait_ge(pe.h, mm)
                if j >= 2:
                    v.wait_ge(p.sem("dB").h, 16 * prev[j - 2])
                v.tensor_scalar(stage[j % 2], ps[:], bt[:, j:j + 1], None, ALU.add) \
                    .then_inc(dve.h, 1)
            dve.inc(1)
            sd = dve.n
            @block.scalar
            def _(s, j=j, sd=sd):
                s.wait_ge(dve.h, sd)
                s.dma_start(out=outs[j][:], in_=stage[j % 2]).then_inc(p.sem("dB").h, 16)
            p.sem("dB").inc(16)
            prev[j] = p.sem("dB").n // 16
    @block.gpsimd
    def _(g):
        g.wait_ge(p.sem("dB").h, p.sem("dB").n)


def phase_l1b_scan(p, io):
    NV = 16
    TW = 2048
    NTW = 8
    qAG = io["qAG"]
    kAG = io["kAG"]
    gAG = io["gAG"]
    vOut = io["vOut"]
    o_y = io["yIn2"]
    _dbg = os.environ.get("GLADBG", "0") == "1"
    if _dbg:
        o_carr = p.dout("o_carr", (128, NV * NTW), F32)
    fga = Arena(p.sbuf("fga", [128, 32 * 1024], F32))
    fga2 = Arena(p.sbuf("fga2", [128, 14 * 1024], F32))
    psA = p.psum("psA", [128, TW], F32)
    psB = p.psum("psB", [128, TW], F32)
    qc = [fga.alloc(TW) for _ in range(2)]
    kc = [fga.alloc(TW) for _ in range(2)]
    gc = [fga.alloc(TW) for _ in range(2)]
    vc = [fga.alloc(TW, rows=NV) for _ in range(2)]
    vrep = [fga2.alloc(TW) for _ in range(2)]
    b_pp = [fga2.alloc(TW) for _ in range(2)]
    h_pp = [fga2.alloc(TW) for _ in range(2)]
    ystage = fga2.alloc(TW, rows=NV)
    sel = p.sbuf("sel", [128, NV * 128], F32)
    carr = p.sbuf("carr", [128, NV], F32)
    gp_scr_t = p.sbuf("gp_scr", [128, 4], BF16)
    p.begin(consts=(1.0,), zero_psums=(psA, psB))
    block = p.block
    dA, dB, dC = p.sem("dA"), p.sem("dB"), p.sem("dC")
    dve, act, pe, pool = p.sem("dve"), p.sem("act"), p.sem("pe"), p.sem("pool")
    p._gp_scr = gp_scr_t[:]
    @block.vector
    def _(v):
        v.wait_ge(p.sem("init").h, p._z1)
        v.memset(sel[:], 0.0)
        last = None
        for j in range(NV):
            last = v.memset(sel[:, j * 128 + j:j * 128 + j + 1], 1.0)
        last.then_inc(p.sem("init").h, 1)
    p.sem("init").inc(1)
    init_tok = (p.sem("init"), p._z1 + 1)
    rep_tok = {}
    ystore = {}
    prev_evac = {}
    yma = {}
    scan_done = {}
    ld = {}
    for tw in range(NTW):
        psY = psA if tw % 2 == 0 else psB
        @block.sync
        def _(s, tw=tw):
            if tw >= 2:
                s.wait_ge(dve.h, scan_done[(tw - 1) * NV - 1])
                s.wait_ge(pool.h, rep_tok[(tw - 1) * NV - 1])
            s.dma_start(out=qc[tw % 2], in_=qAG[tw * 128:(tw + 1) * 128, :]) \
                .then_inc(dA.h, 16)
            s.dma_start(out=kc[tw % 2], in_=kAG[tw * 128:(tw + 1) * 128, :]) \
                .then_inc(dA.h, 16)
            s.dma_start(out=gc[tw % 2], in_=gAG[tw * 128:(tw + 1) * 128, :]) \
                .then_inc(dA.h, 16)
            s.dma_start(out=vc[tw % 2], in_=vOut[tw * NV:(tw + 1) * NV, :]) \
                .then_inc(dA.h, 16)
        dA.inc(64)
        ld[tw] = dA.n
        for vv in range(NV):
            u = tw * NV + vv
            @block.gpsimd
            def _(g, tw=tw, vv=vv, u=u):
                g.wait_ge(dA.h, ld[tw])
                if u >= 2:
                    g.wait_ge(dve.h, scan_done[u - 2])
                rep_row(g, vrep[u % 2], vc[tw % 2][vv:vv + 1, :]) \
                    .then_inc(dC.h, 16)
                g.wait_ge(dC.h, dC.n + 16)
                g.tensor_copy(gp_scr_t[:, 0:2], vrep[u % 2][:, TW - 2:TW]) \
                    .then_inc(pool.h, 1)
            dC.inc(16)
            pool.inc(1)
            rep_tok[u] = pool.n
            @block.vector
            def _(v, tw=tw, vv=vv, u=u):
                v.wait_ge(pool.h, rep_tok[u])
                v.wait_ge(dA.h, ld[tw])
                if u >= 2:
                    v.wait_ge(pe.h, yma.get(u - 2, 0))
                v.tensor_mul(b_pp[u % 2], kc[tw % 2][:, :], vrep[u % 2])
                init = 0.0 if tw % 4 == 0 else carr[:, vv:vv + 1]
                v.tensor_tensor_scan(h_pp[u % 2], gc[tw % 2][:, :],
                                     b_pp[u % 2], init, ALU.mult, ALU.add) \
                    .then_inc(dve.h, 1)
                v.wait_ge(dve.h, dve.n + 1)
                v.tensor_copy(carr[:, vv:vv + 1], h_pp[u % 2][:, TW - 1:TW])
                v.tensor_mul(b_pp[u % 2], h_pp[u % 2], qc[tw % 2][:, :]) \
                    .then_inc(dve.h, 1)
            dve.inc(2)
            scan_done[u] = dve.n
            @block.tensor
            def _(t, tw=tw, vv=vv, u=u, psY=psY):
                t.wait_ge(dve.h, scan_done[u])
                if vv == 0:
                    pv = prev_evac.get(tw % 2)
                    if pv is not None:
                        t.wait_ge(act.h, pv)
                    t.wait_ge(init_tok[0].h, init_tok[1])
                last = None
                for tc in range(TW // 512):
                    last = t.matmul(psY[:, tc * 512:(tc + 1) * 512],
                                    sel[:, vv * 128:(vv + 1) * 128],
                                    b_pp[u % 2][:, tc * 512:(tc + 1) * 512],
                                    start=(vv == 0), stop=(vv == NV - 1))
                last.then_inc(pe.h, 1)
            pe.inc(1)
            yma[u] = pe.n
        if _dbg:
            lastu = tw * NV + NV - 1
            @block.scalar
            def _(s, tw=tw, lastu=lastu):
                s.wait_ge(dve.h, scan_done[lastu])
                s.dma_start(out=o_carr[:, tw * NV:(tw + 1) * NV], in_=carr[:]) \
                    .then_inc(dB.h, 16)
            dB.inc(16)
        klast = tw * NV + NV - 1
        act.inc(1)
        ev = act.n
        dB.inc(16)
        @block.scalar
        def _(s, tw=tw, klast=klast, ev=ev, psY=psY):
            s.wait_ge(pe.h, yma[klast])
            if tw >= 1:
                s.wait_ge(dB.h, 16 * ystore[tw - 1])
            s.copy(ystage[0:NV, :], psY[0:NV, :]).then_inc(act.h, 1)
            s.wait_ge(act.h, ev)
            s.dma_start(out=o_y[tw * NV:(tw + 1) * NV, 0:TW], in_=ystage[0:NV, :]) \
                .then_inc(dB.h, 16)
        ystore[tw] = dB.n // 16
        prev_evac[tw % 2] = ev
    @block.gpsimd
    def _(g):
        g.wait_ge(dB.h, dB.n)




def phase_l1b_out(p, io):
    n_kD = D // 128
    xT = io["xT"]
    i_y = io["i_y"]
    Wo = io["Wo"]
    bgb = io["bgb"]
    o_x = io["o_x"]
    fa = Arena(p.sbuf("fa", [128, 39 * 1024], F32))
    ba = Arena(p.sbuf("ba", [128, 1 * 1024], BF16))
    psA = p.psum("psA", [128, TC], F32)
    psB = p.psum("psB", [128, TC], F32)
    xs_pp = [fa.alloc(TC) for _ in range(2)]
    sq_pp = [fa.alloc(TC) for _ in range(2)]
    reps = fa.alloc(2 * TC)
    rows = fa.alloc(4 * TC, rows=1)
    y2_t = [fa.alloc(TC) for _ in range(n_kD)]
    ostage = sq_pp[1]
    y_sb = fa.alloc(TC)
    wo_sb = p.sbuf("wo_sb", [128, D], F32)
    bg = p.sbuf("bg", [128, 3 * n_kD], F32)
    ones128 = p.sbuf("ones128", [128, 128], F32)
    gp_scr_t = p.sbuf("gp_scr", [128, 4], BF16)
    p.begin(consts=(1.0, EPS), zero_psums=(psA, psB))
    block = p.block
    dA, dB, dC = p.sem("dA"), p.sem("dB"), p.sem("dC")
    dve, act, pe, pool = p.sem("dve"), p.sem("act"), p.sem("pe"), p.sem("pool")
    p._gp_scr = gp_scr_t[:]
    @block.vector
    def _(v):
        v.wait_ge(p.sem("init").h, p._z1)
        v.memset(ones128[:], 1.0).then_inc(p.sem("init").h, 1)
    p.sem("init").inc(1)
    init_tok = (p.sem("init"), p._z1 + 1)
    @block.sync
    def _(s):
        s.dma_start(out=bg[:], in_=bgb[:]).then_inc(dA.h, 16)
        s.dma_start(out=y_sb, in_=i_y[:]).then_inc(dA.h, 16)
    dA.inc(32)
    y_tok = dA.n
    @block.gpsimd
    def _(g):
        g.dma_start(out=wo_sb[:], in_=Wo[:]).then_inc(dC.h, 16)
        g.wait_ge(dC.h, dC.n + 16)
        g.tensor_copy(gp_scr_t[0:128, 0:1].bitcast(F32) if False else gp_scr_t[:, 0:2],
                      wo_sb[:, D - 1:D].bitcast(BF16) if False else wo_sb[:, D - 1:D]
                      .bitcast(BF16)).then_inc(pool.h, 1)
    dC.inc(16)
    pool.inc(1)
    wo_tok = pool.n

    R = lambda i: rows[0:1, i * TC:(i + 1) * TC]
    mu, var, r0 = R(0), R(1), R(2)
    m0 = R(1)
    psA_l = [psA[:, i * 512:(i + 1) * 512] for i in range(4)]
    psB_l = [psB[:, i * 512:(i + 1) * 512] for i in range(4)]
    tch = tchunks_of(TC)
    mm_done, ld, sqd = {}, {}, {}
    for i in range(n_kD):
        @block.sync
        def _(s, i=i):
            if i >= 2:
                s.wait_ge(pe.h, mm_done[i - 2])
            s.dma_start(out=xs_pp[i % 2], in_=xT[i * 128:(i + 1) * 128, :]).then_inc(dA.h, 16)
        dA.inc(16)
        ld[i] = dA.n
        @block.scalar
        def _(s, i=i):
            s.wait_ge(dA.h, ld[i])
            if i >= 2:
                s.wait_ge(pe.h, mm_done[i - 2])
            s.activation(sq_pp[i % 2], xs_pp[i % 2], AF.Square).then_inc(act.h, 1)
        act.inc(1)
        sqd[i] = act.n
        @block.tensor
        def _(t, i=i):
            if i == 0:
                t.wait_ge(init_tok[0].h, init_tok[1])
            t.wait_ge(act.h, sqd[i])
            last = None
            for ci, (t0, t1) in enumerate(tch):
                t.matmul(psA_l[ci], ones128[:], xs_pp[i % 2][:, t0:t1],
                         start=(i == 0), stop=(i == n_kD - 1))
                last = t.matmul(psB_l[ci], ones128[:], sq_pp[i % 2][:, t0:t1],
                                start=(i == 0), stop=(i == n_kD - 1))
            last.then_inc(pe.h, 1)
        pe.inc(1)
        mm_done[i] = pe.n
    stats_done = pe.n
    @block.vector
    def _(v):
        v.wait_ge(pe.h, stats_done)
        v.tensor_scalar(mu, psA[0:1, :], 1.0 / D, None, ALU.mult)
        v.tensor_scalar(var, psB[0:1, :], 1.0 / D, None, ALU.mult)
        v.tensor_mul(r0, mu, mu)
        v.tensor_sub(var, var, r0).then_inc(dve.h, 1)
    dve.inc(1)
    t0tok = emit_rsqrt_row(p, r0, var, p.constf(EPS, 0, 1), p.sem("dve").tok)
    @block.vector
    def _(v):
        v.wait_ge(t0tok[0].h, t0tok[1])
        v.tensor_mul(m0, mu, r0).then_inc(dve.h, 1)
    dve.inc(1)
    rows_tok = dve.tok
    @block.gpsimd
    def _(g):
        g.wait_ge(rows_tok[0].h, rows_tok[1])
        rep_row(g, reps[:, 0:TC], r0).then_inc(dC.h, 16)
        rep_row(g, reps[:, TC:2 * TC], m0).then_inc(dC.h, 16)
    dC.inc(32)
    reps_tok = dC.n

    y2d = {}
    for do in range(n_kD):
        ps = psA if do % 2 == 0 else psB
        @block.tensor
        def _(t, do=do, ps=ps):
            t.wait_ge(pool.h, wo_tok)
            t.wait_ge(dA.h, y_tok)
            if do < 2:
                t.wait_ge(pe.h, stats_done)
            if do < 2:
                t.wait_ge(dve.h, rows_tok[1])
            if do >= 2:
                t.wait_ge(dve.h, y2d[do - 2])
            last = None
            for (t0, t1) in tchunks_of(TC):
                last = t.matmul(ps[:, t0:t1], wo_sb[:, do * 128:(do + 1) * 128],
                                y_sb[:, t0:t1], start=True, stop=True)
            last.then_inc(pe.h, 1)
        pe.inc(1)
        mm = pe.n
        @block.sync
        def _(s, do=do, mm=mm):
            if do >= 2:
                s.wait_ge(dve.h, y2d[do - 2])
            else:
                s.wait_ge(pe.h, stats_done)
            s.dma_start(out=xs_pp[do % 2], in_=xT[do * 128:(do + 1) * 128, :]) \
                .then_inc(dA.h, 16)
        dA.inc(16)
        xl = dA.n
        @block.vector
        def _(v, do=do, ps=ps, mm=mm, xl=xl):
            v.wait_ge(pe.h, mm)
            v.wait_ge(dA.h, xl)
            v.wait_ge(dC.h, reps_tok)
            v.tensor_mul(y2_t[do], xs_pp[do % 2], reps[:, 0:TC])
            v.tensor_sub(y2_t[do], y2_t[do], reps[:, TC:2 * TC])
            v.scalar_tensor_tensor(y2_t[do], ps[:], bg[:, do:do + 1], y2_t[do],
                                   ALU.add, ALU.add).then_inc(dve.h, 1)
        dve.inc(1)
        y2d[do] = dve.n
    y2_all = dve.n

    mu2, var2, r2 = R(3), R(0), R(3)
    mu2, var2 = R(0), R(2)
    r2, m2 = R(3), R(2)
    sq2 = {}
    mmd2 = {}
    for i in range(n_kD):
        @block.scalar
        def _(s, i=i):
            s.wait_ge(dve.h, y2_all)
            if i >= 2:
                s.wait_ge(pe.h, mmd2.get(i - 2, 0))
            s.activation(sq_pp[i % 2], y2_t[i], AF.Square).then_inc(act.h, 1)
        act.inc(1)
        sq2[i] = act.n
        @block.tensor
        def _(t, i=i):
            t.wait_ge(act.h, sq2[i])
            if i == 0:
                t.wait_ge(dve.h, y2_all)
            last = None
            for ci, (t0, t1) in enumerate(tch):
                t.matmul(psA_l[ci], ones128[:], y2_t[i][:, t0:t1],
                         start=(i == 0), stop=(i == n_kD - 1))
                last = t.matmul(psB_l[ci], ones128[:], sq_pp[i % 2][:, t0:t1],
                                start=(i == 0), stop=(i == n_kD - 1))
            last.then_inc(pe.h, 1)
        pe.inc(1)
        mmd2[i] = pe.n
    st2 = pe.n
    @block.vector
    def _(v):
        v.wait_ge(pe.h, st2)
        v.tensor_scalar(mu2, psA[0:1, :], 1.0 / D, None, ALU.mult)
        v.tensor_scalar(var2, psB[0:1, :], 1.0 / D, None, ALU.mult)
        v.tensor_mul(r2, mu2, mu2)
        v.tensor_sub(var2, var2, r2).then_inc(dve.h, 1)
    dve.inc(1)
    t2tok = emit_rsqrt_row(p, r2, var2, p.constf(EPS, 0, 1), p.sem("dve").tok)
    @block.vector
    def _(v):
        v.wait_ge(t2tok[0].h, t2tok[1])
        v.tensor_mul(m2, mu2, r2).then_inc(dve.h, 1)
    dve.inc(1)
    rt2 = dve.tok
    @block.gpsimd
    def _(g):
        g.wait_ge(rt2[0].h, rt2[1])
        rep_row(g, reps[:, 0:TC], r2).then_inc(dC.h, 16)
        rep_row(g, reps[:, TC:2 * TC], m2).then_inc(dC.h, 16)
    dC.inc(32)
    reps2_tok = dC.n

    store_tok = {}
    app = {}
    for do in range(n_kD):
        @block.sync
        def _(s, do=do):
            if do >= 2:
                s.wait_ge(dve.h, app.get(do - 2, 0))
            else:
                s.wait_ge(dve.h, y2_all)
            s.dma_start(out=xs_pp[do % 2], in_=xT[do * 128:(do + 1) * 128, :]) \
                .then_inc(dA.h, 16)
        dA.inc(16)
        xl = dA.n
        @block.vector
        def _(v, do=do, xl=xl):
            v.wait_ge(dA.h, xl)
            v.wait_ge(dC.h, reps2_tok)
            if do >= 1:
                v.wait_ge(dB.h, 16 * store_tok[do - 1])
            v.tensor_mul(ostage, y2_t[do], reps[:, 0:TC])
            v.tensor_sub(ostage, ostage, reps[:, TC:2 * TC])
            v.tensor_scalar(ostage, ostage, bg[:, n_kD + do:n_kD + do + 1],
                            bg[:, 2 * n_kD + do:2 * n_kD + do + 1], ALU.mult, ALU.add)
            v.tensor_add(ostage, ostage, xs_pp[do % 2]).then_inc(dve.h, 1)
        dve.inc(1)
        app[do] = dve.n
        @block.scalar
        def _(s, do=do, ad=app[do]):
            s.wait_ge(dve.h, ad)
            s.dma_start(out=o_x[do * 128:(do + 1) * 128, :], in_=ostage).then_inc(dB.h, 16)
        dB.inc(16)
        store_tok[do] = dB.n // 16
    @block.gpsimd
    def _(g):
        g.wait_ge(dB.h, dB.n)



def col128(*vecs):
    cols = [np.ascontiguousarray(v.reshape(-1, 128).T) for v in vecs]
    return np.ascontiguousarray(np.concatenate(cols, axis=1))


def shard_tokens_T(x):
    out = []
    for c in range(NCORE):
        b = c // GROUP
        t0 = (c % GROUP) * TC
        out.append(np.ascontiguousarray(x[b, t0:t0 + TC, :].T))
    return out


def _conv128(w):
    w = np.ascontiguousarray(w.reshape(DI, 4))
    return np.ascontiguousarray(w.reshape(16, 128, 4).transpose(1, 0, 2).reshape(128, 64))


def _tt_heads(w):
    return np.ascontiguousarray(w.transpose(2, 0, 1).reshape(64, 1024))


def _cat(*a, axis=1):
    return np.ascontiguousarray(np.concatenate(a, axis=axis))


def mk_mask(shift):
    out = []
    for c in range(NCORE):
        j = c % GROUP + shift
        m = np.zeros((128, 4), np.float32)
        if 0 <= j < GROUP:
            m[:, j] = 1.0
        out.append(m)
    return out


def lA_prep(key):
    def fn(ins):
        logA = ins[key]
        out = []
        for c in range(NCORE):
            ch0 = c * 256
            out.append(np.ascontiguousarray(
                logA[ch0:ch0 + 256].reshape(2, 128, NS).transpose(1, 0, 2)
                .reshape(128, 32)))
        return out
    return fn


PCORES = list(range(NCORE))
_SSC = float(64) ** (-0.25)
_CONSTS = (1.0, EPS, _SSC)


def percore(key):
    return lambda env, _k=key: env[_k]


def ag4(arrs):
    return [np.ascontiguousarray(
        np.concatenate([arrs[(c // GROUP) * GROUP + j] for j in range(GROUP)],
                       axis=0)) for c in range(NCORE)]


def ag8(arrs):
    cat = np.ascontiguousarray(np.concatenate(arrs, axis=0))
    return [cat] * NCORE


def a2a8(arrs, rows):
    return [np.ascontiguousarray(
        np.concatenate([a[k * rows:(k + 1) * rows] for a in arrs], axis=0))
        for k in range(NCORE)]


def _new_prog(name):
    p = Prog(name)
    p.begin_program(consts=_CONSTS)
    return p


def _fin(p):
    p.barrier()
    p.finish()
    return p


def _ph(p, tag, fn, io, **kw):
    p.scope_open(tag)
    fn(p, io, **kw)
    p.scope_close()


def build_LA1():
    p = _new_prog("LA1")
    E = p.ext
    _ph(p, "a", phase_tokA, {
        "xT": E("xT", (D, TC), F32,
                lambda env: shard_tokens_T(env["L0_attn_a_x"])),
        "ln_gb": E("ln_gb", (128, 16), F32,
                   lambda env: [col128(env["L0_attn_a_ln_g"],
                                       env["L0_attn_a_ln_b"])] * NCORE),
        "W_in": E("W_in", (D, 2 * DI), BF16,
                  lambda env: [env["L0_attn_a_W_in"].astype(bf)] * NCORE),
        "o_xbr": p.dout("o_xbr", (DI, TC), BF16),
        "o_sz": p.dout("o_sz", (DI, TC), BF16),
        "o_edge": p.dout("o_edge", (DI, 6), BF16)})
    return _fin(p)


def _tokB_a_io(p):
    E = p.ext
    return {
        "i_xbr_body": E("i_body", (DI, TC), BF16, percore("xbrA")),
        "edgeAG": E("edgeAG", (4 * DI, 6), BF16, percore("edgeA_AG")),
        "hmask": E("hmask", (128, 4), F32, lambda env: mk_mask(-1)),
        "convw": E("convw", (128, 64), F32,
                   lambda env: [_conv128(env["L0_attn_a_conv_w"])] * NCORE),
        "convb_bdel": E("cbb", (128, 32), F32,
                        lambda env: [col128(env["L0_attn_a_conv_b"],
                                            env["L0_attn_a_b_delta"])] * NCORE),
        "W_delta": E("W_delta", (DI, DI), BF16,
                     lambda env: [env["L0_attn_a_W_delta"].astype(bf)] * NCORE),
        "W_bc": E("W_bc", (DI, 2 * NS), BF16,
                  lambda env: [_cat(env["L0_attn_a_W_b"],
                                    env["L0_attn_a_W_c"]).astype(bf)] * NCORE),
        "o_xb": p.dout("o_xb", (DI, TC), BF16),
        "o_delta": p.dout("o_delta", (DI, TC), BF16),
        "o_bc": p.dout("o_bc", (2 * NS, TC), BF16)}


def build_LA2():
    p = _new_prog("LA2")
    _ph(p, "a", phase_tokB_a, _tokB_a_io(p))
    return _fin(p)


def _scan_io(p, sfx, dkey, xkey, bckey, lakey):
    E = p.ext
    return {
        "dOut": E("dOut" + sfx, (DI, TC), BF16, percore(dkey)),
        "xbOut": E("xbOut" + sfx, (DI, TC), BF16, percore(xkey)),
        "bcAG": E("bcAG" + sfx, (16 * NS, TC), BF16, percore(bckey)),
        "i_logA": E("logA" + sfx, (128, 32), F32, lA_prep(lakey)),
        "ident": E("ident", (128, 128), BF16,
                   lambda env: [np.eye(128, dtype=bf)] * NCORE),
        "yIn": p.dout("o_y" + sfx, (DI, TC), BF16)}


def build_LA3():
    p = _new_prog("LA3")
    _ph(p, "s", phase_scan,
        _scan_io(p, "", "delA_o", "xbA_o", "bcA_AG", "L0_attn_a_log_A"),
        reverse=False)
    return _fin(p)


def build_LA4():
    p = _new_prog("LA4")
    E = p.ext
    x1 = p.dram("x1", (D, TC), F32)
    x2 = p.dout("o_x2", (D, TC), F32)
    _ph(p, "o", phase_l0a_out, {
        "xT": E("xT", (D, TC), F32,
                lambda env: shard_tokens_T(env["L0_attn_a_x"])),
        "i_y": E("i_y", (DI, TC), BF16, percore("yA_out")),
        "i_xb": E("i_xb", (DI, TC), BF16, percore("xbA")),
        "i_sz": E("i_sz", (DI, TC), BF16, percore("szA")),
        "dp128": E("dp128", (128, 16), F32,
                   lambda env: [col128(env["L0_attn_a_Dp"])] * NCORE),
        "W_out": E("W_out", (DI, D), BF16,
                   lambda env: [env["L0_attn_a_W_out"].astype(bf)] * NCORE),
        "o_x": x1})
    _ph(p, "f", phase_ffn, _ffn_io(p, x1, x2))
    _ph(p, "t", phase_tokA, {
        "xT": x2,
        "ln_gb": E("ln_gb", (128, 16), F32,
                   lambda env: [col128(env["L0_attn_b_norm_w"],
                                       env["L0_attn_b_norm_b"])] * NCORE),
        "W_in": E("W_in", (D, 2 * DI), BF16,
                  lambda env: [np.ascontiguousarray(
                      env["L0_attn_b_in_proj_w"].T).astype(bf)] * NCORE),
        "o_xbr": p.dout("o_xbr", (DI, TC), BF16),
        "o_sz": p.dout("o_sz", (DI, TC), BF16),
        "o_edge": p.dout("o_edge", (DI, 6), BF16)})
    return _fin(p)


def _ffn_io(p, xin, xout):
    E = p.ext
    return {
        "xT": xin,
        "w12": E("ffn_w12", (D, 8192), BF16,
                 lambda env: [_cat(env["L0_ffn_w1_w"].T,
                                   env["L0_ffn_w2_w"].T).astype(bf)] * NCORE),
        "s12": E("ffn_s12", (128, 64), F32,
                 lambda env: [col128(env["L0_ffn_w1_s"],
                                     env["L0_ffn_w2_s"])] * NCORE),
        "h1T": E("ffn_h1T", (4096, 4096), BF16,
                 lambda env: [np.ascontiguousarray(
                     env["L0_ffn_h1_w"].T).astype(bf)] * NCORE),
        "h1s": E("ffn_h1s", (128, 32), F32,
                 lambda env: [col128(env["L0_ffn_h1_s"])] * NCORE),
        "h2T": E("ffn_h2T", (4096, 4096), BF16,
                 lambda env: [np.ascontiguousarray(
                     env["L0_ffn_h2_w"].T).astype(bf)] * NCORE),
        "h2s": E("ffn_h2s", (128, 32), F32,
                 lambda env: [col128(env["L0_ffn_h2_s"])] * NCORE),
        "w3T": E("ffn_w3T", (4096, D), BF16,
                 lambda env: [np.ascontiguousarray(
                     env["L0_ffn_w3_w"].T).astype(bf)] * NCORE),
        "w3s": E("ffn_w3s", (128, 8), F32,
                 lambda env: [col128(env["L0_ffn_w3_s"])] * NCORE),
        "o_x": xout}


def build_LA5():
    p = _new_prog("LA5")
    E = p.ext
    io = {
        "i_xbr_body": E("i_body", (DI, TC), BF16, percore("xbrB")),
        "edgeAG": E("edgeAG", (4 * DI, 6), BF16, percore("edgeB_AG")),
        "hmaskL": E("hmaskL", (128, 4), F32, lambda env: mk_mask(-1)),
        "hmaskR": E("hmaskR", (128, 4), F32, lambda env: mk_mask(+1)),
        "convw_f": E("cwf", (128, 64), F32,
                     lambda env: [_conv128(env["L0_attn_b_conv_w_f"])] * NCORE),
        "convw_r": E("cwr", (128, 64), F32,
                     lambda env: [_conv128(
                         env["L0_attn_b_conv_w_r"][:, :, ::-1])] * NCORE),
        "cb4": E("cb4", (128, 64), F32,
                 lambda env: [col128(env["L0_attn_b_conv_b_f"],
                                     env["L0_attn_b_conv_b_r"],
                                     env["L0_attn_b_dt_proj_b_f"],
                                     env["L0_attn_b_dt_proj_b_r"])] * NCORE),
        "xprojT_f": E("xpf", (DI, 96), BF16,
                      lambda env: [np.ascontiguousarray(
                          env["L0_attn_b_x_proj_w_f"].T).astype(bf)] * NCORE),
        "xprojT_r": E("xpr", (DI, 96), BF16,
                      lambda env: [np.ascontiguousarray(
                          env["L0_attn_b_x_proj_w_r"].T).astype(bf)] * NCORE),
        "dtwT_f": E("dtf", (64, DI), BF16,
                    lambda env: [np.ascontiguousarray(
                        env["L0_attn_b_dt_proj_w_f"].T).astype(bf)] * NCORE),
        "dtwT_r": E("dtr", (64, DI), BF16,
                    lambda env: [np.ascontiguousarray(
                        env["L0_attn_b_dt_proj_w_r"].T).astype(bf)] * NCORE)}
    for br in ("f", "r"):
        io[f"o_xb_{br}"] = p.dout(f"o_xb_{br}", (DI, TC), BF16)
        io[f"o_delta_{br}"] = p.dout(f"o_delta_{br}", (DI, TC), BF16)
        io[f"o_bc_{br}"] = p.dout(f"o_bc_{br}", (2 * NS, TC), BF16)
    _ph(p, "b", phase_tokB_b, io)
    return _fin(p)


def build_LA6():
    p = _new_prog("LA6")
    _ph(p, "f", phase_scan,
        _scan_io(p, "_f", "delF_o", "xbF_o", "bcF_AG", "L0_attn_b_A_log_f"),
        reverse=False)
    _ph(p, "r", phase_scan,
        _scan_io(p, "_r", "delR_o", "xbR_o", "bcR_AG", "L0_attn_b_A_log_r"),
        reverse=True)
    return _fin(p)


def build_LA7():
    p = _new_prog("LA7")
    E = p.ext
    x3 = p.dram("x3", (D, TC), F32)
    x4 = p.dout("o_x4", (D, TC), F32)
    _ph(p, "o", phase_l0b_out, {
        "xT": E("xT", (D, TC), F32, percore("x2")),
        "i_y_f": E("i_y_f", (DI, TC), BF16, percore("yF_out")),
        "i_y_r": E("i_y_r", (DI, TC), BF16, percore("yR_out")),
        "i_xb_f": E("i_xb_f", (DI, TC), BF16, percore("xbF")),
        "i_xb_r": E("i_xb_r", (DI, TC), BF16, percore("xbR")),
        "i_sz": E("i_sz", (DI, TC), BF16, percore("szB")),
        "d2": E("d2", (128, 32), F32,
                lambda env: [col128(env["L0_attn_b_D_f"],
                                    env["L0_attn_b_D_r"])] * NCORE),
        "W_outT": E("W_outT", (DI, D), BF16,
                    lambda env: [np.ascontiguousarray(
                        env["L0_attn_b_out_proj_w"].T).astype(bf)] * NCORE),
        "o_x": x3})
    _ph(p, "f", phase_ffn, _ffn_io(p, x3, x4))
    _ph(p, "d", phase_l1a_xd, {
        "xT": x4,
        "W_down": E("W_down", (D, D), BF16,
                    lambda env: [env["L1_attn_a_W_down"].astype(bf)] * NCORE),
        "o_xd": p.dout("o_xd", (D, TC), BF16)})
    return _fin(p)


def build_LA8():
    p = _new_prog("LA8")
    E = p.ext
    x4 = E("x4", (D, TC), F32, percore("x4"))
    x5 = p.dout("o_x5", (D, TC), F32)
    _ph(p, "a", phase_l1a_att, {
        "i_q": E("i_q", (D, TC), BF16, percore("xd")),
        "kvAG": E("kvAG", (4 * D, TC), BF16, percore("kvAG")),
        "tkT": E("tkT", (64, 1024), BF16,
                 lambda env: [_tt_heads(env["L1_attn_a_tokeys"]).astype(bf)] * NCORE),
        "tqT": E("tqT", (64, 1024), BF16,
                 lambda env: [_tt_heads(env["L1_attn_a_toqueries"]).astype(bf)] * NCORE),
        "tvT": E("tvT", (64, 1024), BF16,
                 lambda env: [_tt_heads(env["L1_attn_a_tovalues"]).astype(bf)] * NCORE),
        "W_unify": E("W_unify", (D, D), BF16,
                     lambda env: [env["L1_attn_a_W_unify"].astype(bf)] * NCORE),
        "bu": E("bu", (128, 8), F32,
                lambda env: [col128(env["L1_attn_a_b_unify"])] * NCORE),
        "xT": x4, "o_x": x5})
    _ph(p, "n", phase_ln_dump, {
        "xT": x5,
        "o_xn": p.dout("o_xn", (D, TC), BF16),
        "o_edge": p.dout("o_edge", (D, 1), BF16)})
    return _fin(p)


def _l1f_io(p, xin, xnkey, edgekey, xout):
    E = p.ext
    return {
        "xT": xin,
        "i_xn_body": E("i_xnb", (D, TC), BF16, percore(xnkey)),
        "edgeAG": E("edgeAG1", (4 * D, 1), BF16, percore(edgekey)),
        "hmask": E("hmask", (128, 4), F32, lambda env: mk_mask(-1)),
        "W_rT": E("l1f_WrT", (D, D), BF16,
                  lambda env: [np.ascontiguousarray(
                      env["L1_ffn_W_r"].T).astype(bf)] * NCORE),
        "W_kT": E("l1f_WkT", (D, D), BF16,
                  lambda env: [np.ascontiguousarray(
                      env["L1_ffn_W_k"].T).astype(bf)] * NCORE),
        "W_vT": E("l1f_WvT", (D, D), BF16,
                  lambda env: [np.ascontiguousarray(
                      env["L1_ffn_W_v"].T).astype(bf)] * NCORE),
        "mu2": E("l1f_mu2", (128, 16), F32,
                 lambda env: [col128(env["L1_ffn_mu_r"],
                                     env["L1_ffn_mu_k"])] * NCORE),
        "o_x": xout}


def build_LA9():
    p = _new_prog("LA9")
    E = p.ext
    x5 = E("x5", (D, TC), F32, percore("x5"))
    x6 = p.dout("o_x6", (D, TC), F32)
    _ph(p, "f", phase_l1f, _l1f_io(p, x5, "xn1", "edge1_AG", x6))
    io = {
        "xT": x6,
        "Wqkvg": E("W4", (D, 512), F32,
                   lambda env: [_cat(env["L1_attn_b_Wq"], env["L1_attn_b_Wk"],
                                     env["L1_attn_b_Wv"],
                                     env["L1_attn_b_Wg"])] * NCORE),
        "bqkvg": E("b4", (128, 4), F32,
                   lambda env: [col128(env["L1_attn_b_bq"], env["L1_attn_b_bk"],
                                       env["L1_attn_b_bv"],
                                       env["L1_attn_b_bg"])] * NCORE)}
    for n in ("q", "k", "v", "g"):
        io[f"o_{n}"] = p.dout(f"o_{n}", (128, TC), F32)
    _ph(p, "t", phase_l1b_tok, io)
    return _fin(p)


def build_LA10():
    p = _new_prog("LA10")
    E = p.ext
    _ph(p, "s", phase_l1b_scan, {
        "qAG": E("qAG", (1024, TC), F32, percore("qAG")),
        "kAG": E("kAG", (1024, TC), F32, percore("kAG")),
        "gAG": E("gAG", (1024, TC), F32, percore("gAG")),
        "vOut": E("vOut", (128, TC), F32, percore("vOut")),
        "yIn2": p.dout("o_y2", (128, TC), F32)})
    return _fin(p)


def build_LA11():
    p = _new_prog("LA11")
    E = p.ext
    x6 = E("x6", (D, TC), F32, percore("x6"))
    x7 = p.dout("o_x7", (D, TC), F32)
    _ph(p, "o", phase_l1b_out, {
        "xT": x6,
        "i_y": E("i_y", (128, TC), F32, percore("yOut2")),
        "Wo": E("Wo", (128, D), F32,
                lambda env: [env["L1_attn_b_Wo"]] * NCORE),
        "bgb": E("bgb", (128, 24), F32,
                 lambda env: [col128(env["L1_attn_b_bo"],
                                     env["L1_attn_b_gamma"],
                                     env["L1_attn_b_beta"])] * NCORE),
        "o_x": x7})
    _ph(p, "n", phase_ln_dump, {
        "xT": x7,
        "o_xn": p.dout("o_xn", (D, TC), BF16),
        "o_edge": p.dout("o_edge", (D, 1), BF16)})
    return _fin(p)


def build_LA12():
    p = _new_prog("LA12")
    E = p.ext
    x7 = E("x7", (D, TC), F32, percore("x7"))
    _ph(p, "f", phase_l1f,
        _l1f_io(p, x7, "xn2", "edge2_AG", p.dout("o_x", (D, TC), F32)))
    return _fin(p)


_LP = {}


def _get(name, builder):
    if name not in _LP:
        _LP[name] = builder()
    return _LP[name]


def _launch(p, env):
    maps = [dict() for _ in range(NCORE)]
    for name, fn in p.prep.items():
        arrs = fn(env)
        for c in range(NCORE):
            maps[c][name] = arrs[c]
    res = run_bass_kernel_spmd(p.nc, maps, core_ids=PCORES)
    return res.results


def kernel(**ins):
    env = dict(ins)
    g = lambda r, k: [r[c][k] for c in range(NCORE)]

    r = _launch(_get("LA1", build_LA1), env)
    env["xbrA"], env["szA"] = g(r, "o_xbr"), g(r, "o_sz")
    env["edgeA_AG"] = ag4(g(r, "o_edge"))

    r = _launch(_get("LA2", build_LA2), env)
    env["xbA"] = g(r, "o_xb")
    env["delA_o"] = a2a8(g(r, "o_delta"), 256)
    env["xbA_o"] = a2a8(g(r, "o_xb"), 256)
    env["bcA_AG"] = ag8(g(r, "o_bc"))

    r = _launch(_get("LA3", build_LA3), env)
    env["yA_out"] = a2a8(g(r, "o_y"), 256)

    r = _launch(_get("LA4", build_LA4), env)
    env["x2"] = g(r, "o_x2")
    env["xbrB"], env["szB"] = g(r, "o_xbr"), g(r, "o_sz")
    env["edgeB_AG"] = ag4(g(r, "o_edge"))

    r = _launch(_get("LA5", build_LA5), env)
    env["xbF"], env["xbR"] = g(r, "o_xb_f"), g(r, "o_xb_r")
    env["delF_o"] = a2a8(g(r, "o_delta_f"), 256)
    env["xbF_o"] = a2a8(g(r, "o_xb_f"), 256)
    env["bcF_AG"] = ag8(g(r, "o_bc_f"))
    env["delR_o"] = a2a8(g(r, "o_delta_r"), 256)
    env["xbR_o"] = a2a8(g(r, "o_xb_r"), 256)
    env["bcR_AG"] = ag8(g(r, "o_bc_r"))

    r = _launch(_get("LA6", build_LA6), env)
    env["yF_out"] = a2a8(g(r, "o_y_f"), 256)
    env["yR_out"] = a2a8(g(r, "o_y_r"), 256)

    r = _launch(_get("LA7", build_LA7), env)
    env["x4"], env["xd"] = g(r, "o_x4"), g(r, "o_xd")
    env["kvAG"] = ag4(env["xd"])

    r = _launch(_get("LA8", build_LA8), env)
    env["x5"], env["xn1"] = g(r, "o_x5"), g(r, "o_xn")
    env["edge1_AG"] = ag4(g(r, "o_edge"))

    r = _launch(_get("LA9", build_LA9), env)
    env["x6"] = g(r, "o_x6")
    env["qAG"] = ag8(g(r, "o_q"))
    env["kAG"] = ag8(g(r, "o_k"))
    env["gAG"] = ag8(g(r, "o_g"))
    env["vOut"] = a2a8(g(r, "o_v"), 16)

    r = _launch(_get("LA10", build_LA10), env)
    env["yOut2"] = a2a8(g(r, "o_y2"), 16)

    r = _launch(_get("LA11", build_LA11), env)
    env["x7"], env["xn2"] = g(r, "o_x7"), g(r, "o_xn")
    env["edge2_AG"] = ag4(g(r, "o_edge"))

    r = _launch(_get("LA12", build_LA12), env)
    out = np.empty((B, L, D), np.float32)
    for c in range(NCORE):
        b_, t0 = c // GROUP, (c % GROUP) * TC
        out[b_, t0:t0 + TC, :] = r[c]["o_x"].T
    return out


# version 06 (final)
# speedup vs baseline: 1.2138x; 1.1252x over previous
import math
from contextlib import ExitStack

import os
import numpy as np
import ml_dtypes

import concourse.bass as bass
import concourse.mybir as mybir
from concourse.bass_types import AP
from concourse.bass_utils import run_bass_kernel_spmd
from concourse.aot_env import allow_ici_all_to_all

F32 = mybir.dt.float32
BF16 = mybir.dt.bfloat16
AF = mybir.ActivationFunctionType
ALU = mybir.AluOpType

NCORE = 8
B = 2
L = 8192
D = 1024
TC = 2048
DI = 2048
NS = 16
GROUP = 4
EPS = 1e-5
bf = ml_dtypes.bfloat16



class Sem:
    def __init__(self, stack, nc, name):
        self.h = stack.enter_context(nc.semaphore(name))
        self.n = 0

    def inc(self, k=1):
        self.n += k
        return self.n

    @property
    def tok(self):
        return (self, self.n)


class Arena:
    def __init__(self, t):
        self.t = t
        self.cols = t.shape[1]
        self.off = 0
        self.hwm = 0

    def alloc(self, w, rows=128):
        assert self.off + w <= self.cols, f"arena overflow {self.off}+{w}>{self.cols}"
        ap = self.t[0:rows, self.off:self.off + w]
        self.off += w
        self.hwm = max(self.hwm, self.off)
        return ap

    def mark(self):
        return self.off

    def release(self, m):
        self.off = m


class Prog:

    def __init__(self, name="prog"):
        self.nc = bass.Bass("TRN2", target_bir_lowering=False)
        self.stack = ExitStack()
        self.scopes = []
        self.prefix = ""
        self.block = None
        self.sems = {}
        self.name = name
        self.ext_cache = {}
        self.prep = {}
        self._z1 = 0

    def sem(self, name):
        if name not in self.sems:
            self.sems[name] = Sem(self.stack, self.nc, name)
        return self.sems[name]

    def _st(self):
        return self.scopes[-1] if self.scopes else self.stack

    def sbuf(self, name, shape, dt):
        return self._st().enter_context(
            self.nc.sbuf_tensor(self.prefix + name, shape, dt))

    def psum(self, name, shape, dt=F32):
        return self._st().enter_context(
            self.nc.psum_tensor(self.prefix + name, shape, dt))

    def begin_program(self, consts):
        self.block = self.stack.enter_context(self.nc.Block())
        self.cmap = {}
        self.cpool = self.stack.enter_context(
            self.nc.sbuf_tensor("cpool", [128, max(1, len(consts))], F32))
        init = self.sem("init")
        @self.block.vector
        def _(v):
            last = None
            for i, val in enumerate(consts):
                self.cmap[val] = i
                last = v.memset(self.cpool[:, i:i + 1], val)
            last.then_inc(init.h, 1)
        init.inc(1)

    def begin(self, consts=(), zero_psums=()):
        init = self.sem("init")
        @self.block.vector
        def _(v):
            last = None
            for ps in zero_psums:
                last = v.memset(ps[:], 0.0)
            if last is None:
                last = v.memset(self.cpool[:, 0:1], 1.0)
            last.then_inc(init.h, 1)
        init.inc(1)
        self._z1 = init.n

    def barrier(self):
        snap = [(s.h, s.n) for s in self.sems.values() if s.n > 0]
        blk = self.block
        for engf in (blk.sync, blk.scalar, blk.vector, blk.tensor, blk.gpsimd):
            @engf
            def _(e, snap=snap):
                for h, v in snap:
                    e.wait_ge(h, v)

    def scope_open(self, tag):
        self.barrier()
        self.scopes.append(ExitStack())
        self.prefix = tag + "_"
        self._dbg_barrier = None
        self._dbg_proj = None
        self._dbg_revchunk = False

    def scope_close(self):
        self.scopes.pop().close()
        self.prefix = ""

    def constf(self, val, p0=0, nrow=128):
        c = self.cmap[val]
        return self.cpool[p0:p0 + nrow, c:c + 1]

    def ext(self, name, shape, dt, fn):
        if name not in self.ext_cache:
            self.ext_cache[name] = self.nc.dram_tensor(
                name, list(shape), dt, kind="ExternalInput").ap()
            self.prep[name] = fn
        return self.ext_cache[name]

    def dram(self, name, shape, dt):
        h = self.nc.dram_tensor(name, list(shape), dt)
        return h.ap()

    def dout(self, name, shape, dt=F32):
        return self.nc.dram_tensor(name, list(shape), dt, kind="ExternalOutput").ap()

    def finish(self):
        self.stack.close()


RG4 = [[0, 1, 2, 3], [4, 5, 6, 7]]
RG8 = [[0, 1, 2, 3, 4, 5, 6, 7]]


def emit_exchange(p, colls):
    cc = p.sem("cc")
    @p.block.gpsimd
    def _(g):
        for kind, i_ap, o_ap, rg in colls:
            if kind == "AllToAll":
                with allow_ici_all_to_all():
                    ins_ = g.collective_compute(
                        kind, ALU.bypass, replica_groups=rg,
                        ins=[i_ap.opt()], outs=[o_ap.opt()])
            else:
                ins_ = g.collective_compute(
                    kind, ALU.bypass, replica_groups=rg,
                    ins=[i_ap.opt()], outs=[o_ap.opt()])
            ins_.then_inc(cc.h, 16)
            cc.inc(16)
            g.wait_ge(cc.h, cc.n)


def rep_row(eng, dst, src_row, n_rep=128):
    rep_ap = AP(src_row.tensor, src_row.offset,
                [[src_row.ap[0][0], 1], [0, n_rep], list(src_row.ap[-1])])
    return eng.dma_start(out=dst, in_=rep_ap)


def tchunks_of(T, step=512):
    return [(s, min(s + step, T)) for s in range(0, T, step)]



def emit_ln_stats(p, x_tiles, x2_pp, psA, psB, ones_col, ready, mu, var, r1s, init_tok=None):
    block = p.block
    pe, act, dve = p.sem("pe"), p.sem("act"), p.sem("dve")
    n_k = len(x_tiles)
    act_base, pe_base = act.n, pe.n

    @block.scalar
    def _(s):
        s.wait_ge(ready[0].h, ready[1])
        for i in range(n_k):
            if i >= 2:
                s.wait_ge(pe.h, pe_base + i - 1)
            s.activation(x2_pp[i % 2], x_tiles[i], AF.Square).then_inc(act.h, 1)
    act.n += n_k

    T = x_tiles[0].shape[-1]
    tch = tchunks_of(T)

    @block.tensor
    def _(t):
        for i in range(n_k):
            if i == 0:
                t.wait_ge(ready[0].h, ready[1])
                if init_tok is not None:
                    t.wait_ge(init_tok[0].h, init_tok[1])
            for (t0, t1) in tch:
                t.matmul(psA[:, t0:t1], ones_col, x_tiles[i][:, t0:t1],
                         start=(i == 0), stop=(i == n_k - 1))
            t.wait_ge(act.h, act_base + i + 1)
            last = None
            for (t0, t1) in tch:
                last = t.matmul(psB[:, t0:t1], ones_col, x2_pp[i % 2][:, t0:t1],
                                start=(i == 0), stop=(i == n_k - 1))
            last.then_inc(pe.h, 1)
    pe.n += n_k
    pe_done = pe.n
    inv_n = 1.0 / (128 * n_k)

    @block.vector
    def _(v):
        v.wait_ge(pe.h, pe_done)
        v.tensor_scalar(mu, psA[0:1, :], inv_n, None, ALU.mult)
        v.tensor_scalar(var, psB[0:1, :], inv_n, None, ALU.mult)
        v.tensor_mul(r1s, mu, mu)
        v.tensor_sub(var, var, r1s).then_inc(dve.h, 1)
    dve.inc(1)
    return dve.tok


def emit_rsqrt_row(p, out_row, in_row, eps_ap, after):
    block = p.block
    act = p.sem("act")

    @block.scalar
    def _(s):
        s.wait_ge(after[0].h, after[1])
        s.activation(out_row, in_row, AF.Ln, bias=eps_ap)
        s.activation(out_row, out_row, AF.Exp, scale=-0.5).then_inc(act.h, 1)
    act.inc(1)
    return act.tok


def emit_proj(p, w_dram, in_tiles, n_do, wbufs, psums, epilogue, in_ready,
              w_rows0=0, extra_in=None, prev_tail=None):
    block = p.block
    pe, dC = p.sem("pe"), p.sem("dC")
    n_k = len(in_tiles)
    T = in_tiles[0].shape[-1]
    tch = tchunks_of(T)

    def w_ap(do):
        return w_dram[:, (w_rows0 + do) * 128:(w_rows0 + do + 1) * 128] \
            .rearrange("(ki p) c -> p ki c", p=128)

    consumed = {}
    load_tok = {}
    wdve = os.environ.get("WDVE", "0") == "1"
    wstage = getattr(p, "_w_f32_stage", None)
    dve_s = p.sem("dve")

    gp_scr = getattr(p, "_gp_scr", None)
    pool_s = p.sem("pool")
    gptouch = os.environ.get("GPTOUCH", "1") == "1"

    def emit_load(do, buf_idx, gate_tok=None):
        if not wdve:
            @block.gpsimd
            def _(g, do=do, buf_idx=buf_idx):
                if gate_tok is not None:
                    g.wait_ge(gate_tok[0].h, gate_tok[1])
                if do < 2 and prev_tail:
                    for tok in prev_tail:
                        g.wait_ge(tok[0].h, tok[1])
                g.dma_start(out=wbufs[buf_idx][:].rearrange("p (ki c) -> p ki c", c=128),
                            in_=w_ap(do)).then_inc(dC.h, 16)
            dC.inc(16)
            if gptouch:
                _ld = dC.n
                @block.gpsimd
                def _(g, buf_idx=buf_idx, _ld=_ld):
                    g.wait_ge(dC.h, _ld)
                    g.tensor_copy(gp_scr, wbufs[buf_idx][:, 1020:1024]).then_inc(pool_s.h, 1)
                pool_s.inc(1)
                return ("pool", pool_s.n)
            return ("dC", dC.n)
        else:
            @block.gpsimd
            def _(g, do=do, buf_idx=buf_idx):
                if gate_tok is not None:
                    g.wait_ge(gate_tok[0].h, gate_tok[1])
                g.dma_start(out=wstage[:].rearrange("p (ki c) -> p ki c", c=128),
                            in_=w_ap(do).bitcast(F32) if False else w_ap(do)).then_inc(dC.h, 16)
            dC.inc(16)
            _ld = dC.n
            @block.vector
            def _(v, buf_idx=buf_idx, _ld=_ld):
                v.wait_ge(dC.h, _ld)
                v.tensor_copy(wbufs[buf_idx][:], wstage[:]).then_inc(dve_s.h, 1)
            dve_s.inc(1)
            return ("dve", dve_s.n)

    for do in range(min(2, n_do)):
        gate = None
        if wdve and do > 0:
            gate = (dve_s, load_tok[do - 1][1])
        load_tok[do] = emit_load(do, do, gate_tok=gate)

    barrier = getattr(p, "_dbg_barrier", None)
    for do in range(n_do):
        ps = psums[do % 2]
        @block.tensor
        def _(t, do=do, ps=ps):
            lt = load_tok[do]
            t.wait_ge({"dC": dC, "dve": dve_s, "pool": p.sem("pool")}[lt[0]].h, lt[1])
            if do == 0:
                if "init" in p.sems:
                    t.wait_ge(p.sems["init"].h, p._z1)
                t.wait_ge(in_ready[0].h, in_ready[1])
                if extra_in is not None:
                    t.wait_ge(extra_in[0].h, extra_in[1])
                if prev_tail:
                    for tok in prev_tail:
                        t.wait_ge(tok[0].h, tok[1])
                if barrier:
                    for (sh, sv) in barrier:
                        t.wait_ge(sh, sv)
            if os.environ.get("SERIAL", "0") == "1" and do >= 1:
                tok = consumed[do - 1]
                t.wait_ge(tok[0].h, tok[1])
            if do >= 2:
                tok = consumed[do - 2]
                t.wait_ge(tok[0].h, tok[1])
            last = None
            for (t0, t1) in (tch[::-1] if getattr(p, "_dbg_revchunk", False) else tch):
                for ki in range(n_k):
                    last = t.matmul(ps[:, t0:t1],
                                    wbufs[do % 2][:, ki * 128:(ki + 1) * 128],
                                    in_tiles[ki][:, t0:t1],
                                    start=(ki == 0), stop=(ki == n_k - 1))
            last.then_inc(pe.h, 1)
        pe.inc(1)
        if do == 0 and getattr(p, "_dbg_proj", None) is not None:
            o_ps, o_wb = p._dbg_proj
            tok = pe.tok
            stg = p._dbg_stage
            p.sem("act").inc(1)
            _t = p.sem("act").n
            @block.scalar
            def _(s, ps=ps, tok=tok, stg=stg, _t=_t):
                s.wait_ge(tok[0].h, tok[1])
                s.copy(stg, ps[:]).then_inc(p.sem("act").h, 1)
                s.wait_ge(p.sem("act").h, _t)
                s.dma_start(out=o_ps[:], in_=stg).then_inc(p.sem("dB").h, 16)
                s.dma_start(out=o_wb[:], in_=wbufs[0][:]).then_inc(p.sem("dB").h, 16)
            p.sem("dB").inc(32)
        consumed[do] = epilogue(do, ps, pe.tok)
        if do + 2 < n_do:
            gate = consumed[do]
            if wdve:
                gate2 = (dve_s, load_tok[do + 1][1])
                @block.gpsimd
                def _(g, gate2=gate2):
                    g.wait_ge(gate2[0].h, gate2[1])
            load_tok[do + 2] = emit_load(do + 2, do % 2, gate_tok=gate)
    return consumed


def emit_softplus_from_psum(p, out_bf, ps, bias_col, s1, s2, mm_tok):
    block = p.block
    act, dve = p.sem("act"), p.sem("dve")

    @block.scalar
    def _(s):
        s.wait_ge(mm_tok[0].h, mm_tok[1])
        s.activation(s1[:], ps, AF.Relu, bias=bias_col)
        s.activation(s2[:], ps, AF.Abs, bias=bias_col).then_inc(act.h, 1)
        s.activation(s2[:], s2[:], AF.Exp, scale=-1.0)
        s.activation(s2[:], s2[:], AF.Ln, bias=1.0).then_inc(act.h, 1)
    act.inc(2)
    ps_free = (act, act.n - 1)
    ln_done = act.n

    @block.vector
    def _(v):
        v.wait_ge(act.h, ln_done)
        v.tensor_add(out_bf[:], s1[:], s2[:]).then_inc(dve.h, 1)
    dve.inc(1)
    return ps_free, dve.tok



def phase_tokA(p, io, inner_ln=True, with_gb=True, dbg=False):
    xT = io["xT"]
    ln_gb = io["ln_gb"]
    W_in = io["W_in"]
    o_xbr = io["o_xbr"]
    o_sz = io["o_sz"]
    o_edge = io["o_edge"]
    if dbg:
        o_xn = p.dout("o_xn", (D, TC), BF16)
        o_rows = p.dout("o_rows", (1, 6 * TC), F32)
        p._dbg_proj = (p.dout("o_ps", (128, TC), F32), p.dout("o_wb", (128, 1024), BF16))

    n_kD = D // 128
    fa = Arena(p.sbuf("fa", [128, 37 * 1024], F32))
    ba = Arena(p.sbuf("ba", [128, 25 * 1024], BF16))
    psA = p.psum("psA", [128, TC], F32)
    psB = p.psum("psB", [128, TC], F32)

    x_t = [fa.alloc(TC) for _ in range(n_kD)]
    x2_pp = [fa.alloc(TC) for _ in range(2)]
    reps = fa.alloc(2 * TC)
    rows = fa.alloc(6 * TC, rows=1)
    gbt = p.sbuf("gbt", [128, 2 * n_kD], F32)
    xn_t = [ba.alloc(TC) for _ in range(n_kD)]
    wbufs = [p.sbuf(f"wb{i}", [128, 128 * n_kD], BF16) for i in range(2)]
    ones128 = p.sbuf("ones128", [128, 128], F32)
    wstage_t = p.sbuf("wstage", [128, 1024], F32)
    gp_scr_t = p.sbuf("gp_scr", [128, 4], BF16)
    stage = [ba.alloc(TC) for _ in range(4)]

    p.begin(consts=(1.0, EPS), zero_psums=(psA, psB))
    ones_c = ones128[:]
    block = p.block
    dA, dB, dC = p.sem("dA"), p.sem("dB"), p.sem("dC")
    dve, act = p.sem("dve"), p.sem("act")

    @block.sync
    def _(s):
        for i in range(n_kD):
            s.dma_start(out=x_t[i], in_=xT[i * 128:(i + 1) * 128, :]).then_inc(dA.h, 16)
        s.dma_start(out=gbt[:], in_=ln_gb[:]).then_inc(dA.h, 16)
    dA.n += 16 * (n_kD + 1)
    x_loaded = (dA, dA.n)

    @block.vector
    def _(v):
        v.wait_ge(p.sem("init").h, p._z1)
        v.memset(ones128[:], 1.0).then_inc(p.sem("init").h, 1)
    p.sem("init").inc(1)

    R = lambda i: rows[0:1, i * TC:(i + 1) * TC]
    mu, var, r0, r1, rr, mr = R(0), R(1), R(2), R(3), R(4), R(5)
    scr = R(3)
    init_tok = (p.sem("init"), p._z1 + 1)
    import os as _os
    if _os.environ.get("STATS", "1") == "0":
        @block.vector
        def _(v):
            v.wait_ge(x_loaded[0].h, x_loaded[1])
            v.wait_ge(init_tok[0].h, init_tok[1])
            v.memset(mu, 0.0)
            v.memset(var, 1.0).then_inc(p.sem("dve").h, 1)
        p.sem("dve").inc(1)
    elif _os.environ.get("VIRGIN", "0") == "1":
        emit_ln_stats(p, x_t, x2_pp, psB[:, :], psB[:, :], ones_c,
                      x_loaded, mu, var, scr, init_tok)
    else:
        emit_ln_stats(p, x_t, x2_pp, psA[:, :], psB[:, :], ones_c,
                      x_loaded, mu, var, scr, init_tok)
    t0 = emit_rsqrt_row(p, r0, var, p.constf(EPS, 0, 1), p.sem("dve").tok)
    if inner_ln:
        @block.vector
        def _(v):
            v.wait_ge(t0[0].h, t0[1])
            v.tensor_mul(r1, r0, r0)
            v.tensor_mul(r1, r1, var).then_inc(dve.h, 1)
        dve.inc(1)
        t1 = emit_rsqrt_row(p, r1, r1, p.constf(EPS, 0, 1), dve.tok)
        @block.vector
        def _(v):
            v.wait_ge(t1[0].h, t1[1])
            v.tensor_mul(rr, r0, r1)
            v.tensor_mul(mr, mu, rr).then_inc(dve.h, 1)
        dve.inc(1)
    else:
        @block.vector
        def _(v):
            v.wait_ge(t0[0].h, t0[1])
            v.tensor_copy(rr, r0)
            v.tensor_mul(mr, mu, r0).then_inc(dve.h, 1)
        dve.inc(1)
    rows_tok = dve.tok

    @block.gpsimd
    def _(g):
        g.wait_ge(rows_tok[0].h, rows_tok[1])
        rep_row(g, reps[:, 0:TC], rr).then_inc(dC.h, 16)
        rep_row(g, reps[:, TC:2 * TC], mr).then_inc(dC.h, 16)
    dC.inc(32)
    reps_tok = (dC, dC.n)

    tmp = x2_pp[0]
    @block.vector
    def _(v):
        v.wait_ge(reps_tok[0].h, reps_tok[1])
        last = None
        for i in range(n_kD):
            v.tensor_mul(tmp, x_t[i], reps[:, 0:TC])
            v.tensor_sub(tmp, tmp, reps[:, TC:2 * TC])
            if with_gb:
                last = v.tensor_scalar(xn_t[i], tmp, gbt[:, i:i + 1],
                                       gbt[:, n_kD + i:n_kD + i + 1], ALU.mult, ALU.add)
            else:
                last = v.tensor_copy(xn_t[i], tmp)
        last.then_inc(dve.h, 1)
    dve.inc(1)
    if os.environ.get("DELAY", "0") == "1":
        @block.vector
        def _(v):
            last2 = None
            for _i in range(10):
                last2 = v.tensor_copy(tmp[:, 0:512], x_t[0][:, 0:512])
            last2.then_inc(dve.h, 1)
        dve.inc(1)
    xn_tok = dve.tok

    p._dbg_stage = x2_pp[1]
    p._w_f32_stage = wstage_t[:]
    p._gp_scr = gp_scr_t[:]
    if dbg:
        o_pre = p.dout("o_pre", (128, TC), F32)
        _do_pre = os.environ.get("PRE", "1") == "1"
        if _do_pre:
            act.inc(1)
            _pt = act.n
            @block.scalar
            def _(s, _pt=_pt):
                s.wait_ge(p.sem("dve").h, p.sem("dve").n)
                s.copy(x2_pp[1], psA[:]).then_inc(act.h, 1)
                s.wait_ge(act.h, _pt)
                s.dma_start(out=o_pre[:], in_=x2_pp[1]).then_inc(dB.h, 16)
            dB.inc(16)
        else:
            @block.gpsimd
            def _(g):
                g.dma_start(out=o_pre[:], in_=x2_pp[1]).then_inc(dB.h, 16)
            dB.inc(16)
    if dbg:
        p._dbg_barrier = [(p.sem(n).h, p.sem(n).n) for n in ("dA", "dB", "dC", "dve", "act")]
        import os as _os
        p._dbg_revchunk = _os.environ.get("REV", "0") == "1"
    store_tok = {}

    def epi(do, ps, mm_tok):
        st = stage[do % 4]
        act.inc(1)
        ps_free = (act, act.n)
        @block.scalar
        def _(s, do=do, ps=ps, mm_tok=mm_tok, st=st, psf=ps_free[1]):
            s.wait_ge(mm_tok[0].h, mm_tok[1])
            if do >= 4:
                s.wait_ge(dB.h, 16 * store_tok[do - 4])
            if do < 16:
                if os.environ.get("VEVAC", "0") == "1":
                    pass
                s.copy(st, ps[:]).then_inc(act.h, 1)
            else:
                s.activation(st, ps[:], AF.Silu).then_inc(act.h, 1)
            s.wait_ge(act.h, psf)
            dst = o_xbr if do < 16 else o_sz
            r0_ = (do % 16) * 128
            s.dma_start(out=dst[r0_:r0_ + 128, :], in_=st).then_inc(dB.h, 16)
            if do < 16:
                s.dma_start(out=o_edge[r0_:r0_ + 128, 0:3], in_=st[:, 0:3]) \
                    .then_inc(dB.h, 16)
                s.dma_start(out=o_edge[r0_:r0_ + 128, 3:6], in_=st[:, TC - 3:TC]) \
                    .then_inc(dB.h, 16)
        dB.inc(48 if do < 16 else 16)
        store_tok[do] = dB.n // 16
        return ps_free

    if dbg:
        @block.gpsimd
        def _(g):
            g.wait_ge(xn_tok[0].h, xn_tok[1])
            for i in range(n_kD):
                g.dma_start(out=o_xn[i * 128:(i + 1) * 128, :], in_=xn_t[i]).then_inc(dC.h, 16)
            g.dma_start(out=o_rows[:], in_=rows).then_inc(dC.h, 16)
        dC.inc(16 * (n_kD + 1))

    emit_proj(p, W_in, xn_t, 2 * DI // 128, wbufs, [psA, psB], epi, xn_tok)

    @block.gpsimd
    def _(g):
        g.wait_ge(dB.h, dB.n)
    assert fa.hwm <= fa.cols and ba.hwm <= ba.cols




def emit_conv_silu(p, xbr_tiles, xb_tiles, convw, convb, cacc, xbr_ready, o_xb=None,
                   halo=3):
    block = p.block
    dve, act, dB = p.sem("dve"), p.sem("act"), p.sem("dB")
    toks = []
    n_dt = len(xbr_tiles)
    T = xbr_tiles[0].shape[-1] - halo
    prev_silu = None
    for dt in range(n_dt):
        rd = xbr_ready[dt]
        @block.vector
        def _(v, dt=dt, rd=rd, prev_silu=prev_silu):
            v.wait_ge(rd[0].h, rd[1])
            if prev_silu is not None:
                v.wait_ge(prev_silu[0].h, prev_silu[1])
            sh = halo - 3
            v.tensor_scalar(cacc[:, 0:T], xbr_tiles[dt][:, sh:sh + T],
                            convw[:, dt * 4:dt * 4 + 1], None, ALU.mult)
            last = None
            for j in range(1, 4):
                last = v.scalar_tensor_tensor(cacc[:, 0:T],
                                              xbr_tiles[dt][:, sh + j:sh + j + T],
                                              convw[:, dt * 4 + j:dt * 4 + j + 1],
                                              cacc[:, 0:T], ALU.mult, ALU.add)
            last.then_inc(dve.h, 1)
        dve.inc(1)
        cv = dve.n
        act.inc(1)
        silu_tok = (act, act.n)
        if o_xb is not None:
            dB.inc(16)
            st_tok = dB.n
        @block.scalar
        def _(s, dt=dt, cv=cv, silu_tok=silu_tok):
            s.wait_ge(dve.h, cv)
            s.activation(xb_tiles[dt], cacc[:, 0:T], AF.Silu,
                         bias=convb[:, dt:dt + 1]).then_inc(act.h, 1)
            if o_xb is not None:
                s.wait_ge(act.h, silu_tok[1])
                s.dma_start(out=o_xb[dt * 128:(dt + 1) * 128, :], in_=xb_tiles[dt]) \
                    .then_inc(dB.h, 16)
        prev_silu = silu_tok
        toks.append(silu_tok)
    return toks


def phase_tokB_a(p, io):
    n_dt = DI // 128
    i_body = io["i_xbr_body"]
    edgeAG = io["edgeAG"]
    hmask_in = io["hmask"]
    convw_in = io["convw"]
    convb_in = io["convb_bdel"]
    W_delta = io["W_delta"]
    W_bc = io["W_bc"]
    o_xb = io["o_xb"]
    o_delta = io["o_delta"]
    o_bc = io["o_bc"]

    fa = Arena(p.sbuf("fa", [128, 8 * 1024], F32))
    ba = Arena(p.sbuf("ba", [128, 71 * 1024], BF16))
    psA = p.psum("psA", [128, TC], F32)
    psB = p.psum("psB", [128, TC], F32)

    xbr_t = [ba.alloc(3 + TC) for _ in range(n_dt)]
    xb_t = [ba.alloc(TC) for _ in range(n_dt)]
    stage = [ba.alloc(TC) for _ in range(2)]
    cacc = fa.alloc(TC)
    sp1, sp2 = fa.alloc(TC), fa.alloc(TC)
    bc_stage = ba.alloc(TC, rows=32)
    cw = p.sbuf("cw", [128, n_dt * 4], F32)
    cb = p.sbuf("cb", [128, 2 * n_dt], F32)
    wbufs = [p.sbuf(f"wb{i}", [128, 128 * n_dt], BF16) for i in range(2)]
    gp_scr_t = p.sbuf("gp_scr", [128, 4], BF16)
    ehall = p.sbuf("ehall", [128, n_dt * 12], BF16)
    hm = p.sbuf("hm", [128, 4], F32)

    p.begin(consts=(1.0,), zero_psums=(psA, psB))
    block = p.block
    dA, dB, dC = p.sem("dA"), p.sem("dB"), p.sem("dC")
    dve, act = p.sem("dve"), p.sem("act")
    p._gp_scr = gp_scr_t[:]

    @block.sync
    def _(s):
        for i in range(n_dt):
            s.dma_start(out=xbr_t[i][:, 3:3 + TC],
                        in_=i_body[i * 128:(i + 1) * 128, :]).then_inc(dA.h, 16)
            for j in range(4):
                s.dma_start(out=ehall[:, i * 12 + j * 3:i * 12 + (j + 1) * 3],
                            in_=edgeAG[j * DI + i * 128:j * DI + (i + 1) * 128, 3:6]) \
                    .then_inc(dA.h, 16)
        s.dma_start(out=cw[:], in_=convw_in[:]).then_inc(dA.h, 16)
        s.dma_start(out=cb[:], in_=convb_in[:]).then_inc(dA.h, 16)
        s.dma_start(out=hm[:], in_=hmask_in[:]).then_inc(dA.h, 16)
    dA.n += 16 * (5 * n_dt + 3)
    all_ld = dA.n
    @block.vector
    def _(v):
        v.wait_ge(dA.h, all_ld)
        last = None
        for i in range(n_dt):
            v.tensor_scalar(cacc[:, 0:3], ehall[:, i * 12:i * 12 + 3],
                            hm[:, 0:1], None, ALU.mult)
            for j in range(1, 4):
                v.scalar_tensor_tensor(cacc[:, 0:3],
                                       ehall[:, i * 12 + j * 3:i * 12 + (j + 1) * 3],
                                       hm[:, j:j + 1], cacc[:, 0:3],
                                       ALU.mult, ALU.add)
            last = v.tensor_copy(xbr_t[i][:, 0:3], cacc[:, 0:3])
        last.then_inc(dve.h, 1)
    dve.inc(1)
    xbr_ready = [(dve, dve.n)] * n_dt

    xb_toks = emit_conv_silu(p, xbr_t, xb_t, cw[:], cb[:, 0:n_dt], cacc,
                             xbr_ready, o_xb=o_xb)
    xb_all = xb_toks[-1]

    store_tok = {}
    sp_guard = {}

    def epi_delta(do, ps, mm_tok):
        st = stage[do % 2]
        act_l, dve_l, dB_l = p.sem("act"), p.sem("dve"), p.sem("dB")
        prev_sum = sp_guard.get(do - 1)
        @block.scalar
        def _(s, do=do, ps=ps, mm_tok=mm_tok, st=st, prev_sum=prev_sum):
            s.wait_ge(mm_tok[0].h, mm_tok[1])
            if prev_sum is not None:
                s.wait_ge(dve_l.h, prev_sum)
            bcol = cb[:, n_dt + do:n_dt + do + 1]
            s.activation(sp1, ps[:], AF.Relu, bias=bcol)
            s.activation(sp2, ps[:], AF.Abs, bias=bcol).then_inc(act_l.h, 1)
            s.activation(sp2, sp2, AF.Exp, scale=-1.0)
            s.activation(sp2, sp2, AF.Ln, bias=1.0).then_inc(act_l.h, 1)
        act_l.inc(2)
        ps_free = (act_l, act_l.n - 1)
        ln_done = act_l.n
        @block.vector
        def _(v, do=do, ln_done=ln_done, st=st):
            v.wait_ge(act_l.h, ln_done)
            if do >= 2:
                v.wait_ge(dB_l.h, 16 * store_tok[do - 2])
            v.tensor_add(st, sp1, sp2).then_inc(dve_l.h, 1)
        dve_l.inc(1)
        sp_guard[do] = dve_l.n
        sum_done = dve_l.n
        @block.scalar
        def _(s, do=do, st=st, sum_done=sum_done):
            s.wait_ge(dve_l.h, sum_done)
            s.dma_start(out=o_delta[do * 128:(do + 1) * 128, :], in_=st).then_inc(dB_l.h, 16)
        dB_l.inc(16)
        store_tok[do] = dB_l.n // 16
        return ps_free

    emit_proj(p, W_delta, [xb_t[i] for i in range(n_dt)], n_dt, wbufs,
              [psA, psB], epi_delta, xb_all)
    wdel_done = p.sem("act").tok
    pe = p.sem("pe")
    wdel_mm_done = pe.tok

    last_ps_free = None
    dCl = p.sem("dC")
    @block.gpsimd
    def _(g):
        g.wait_ge(pe.h, wdel_mm_done[1])
        g.dma_start(out=wbufs[0][:, 0:2 * NS].rearrange("p (ki c) -> p ki c", c=2 * NS)
                    if False else wbufs[0][:].rearrange("p (ki c) -> p ki c", c=128)[:, 0:n_dt, 0:2 * NS],
                    in_=W_bc.rearrange("(ki p) c -> p ki c", p=128)).then_inc(dCl.h, 16)
        g.wait_ge(dCl.h, dCl.n + 16)
        g.tensor_copy(gp_scr_t[:, 0:4],
                      wbufs[0][:, 15 * 128 + 2 * NS - 4:15 * 128 + 2 * NS]) \
            .then_inc(p.sem("pool").h, 1)
    dCl.inc(16)
    p.sem("pool").inc(1)
    bc_w_tok = (p.sem("pool"), p.sem("pool").n)

    act_all = p.sem("act").n
    @block.tensor
    def _(t):
        t.wait_ge(bc_w_tok[0].h, bc_w_tok[1])
        t.wait_ge(p.sem("act").h, act_all)
        last = None
        for (t0, t1) in tchunks_of(TC):
            for ki in range(n_dt):
                last = t.matmul(psA[0:32, t0:t1],
                                wbufs[0][:].rearrange("p (ki c) -> p ki c", c=128)[:, ki, 0:2 * NS],
                                xb_t[ki][:, t0:t1],
                                start=(ki == 0), stop=(ki == n_dt - 1))
        last.then_inc(p.sem("pe").h, 1)
    p.sem("pe").inc(1)
    bc_done = p.sem("pe").tok

    @block.vector
    def _(v):
        v.wait_ge(bc_done[0].h, bc_done[1])
        v.tensor_copy(bc_stage, psA[0:32, :]).then_inc(dve.h, 1)
    dve.inc(1)
    bcc = dve.n
    @block.scalar
    def _(s):
        s.wait_ge(dve.h, bcc)
        s.dma_start(out=o_bc[:], in_=bc_stage).then_inc(dB.h, 16)
    dB.inc(16)

    @block.gpsimd
    def _(g):
        g.wait_ge(dB.h, p.sem("dB").n)




def phase_scan(p, io, reverse=False):
    n_dt = 2
    NQ = 8
    TQ = 2048
    i_delta = io["dOut"]
    i_xb = io["xbOut"]
    i_bc = io["bcAG"]
    i_logA = io["i_logA"]
    ident = io["ident"]
    o_y = io["yIn"]

    ba = Arena(p.sbuf("ba", [128, 45 * 2048], BF16))
    psA = p.psum("psA", [128, TQ], F32)
    psB = p.psum("psB", [128, TQ], F32)
    psums = [psA, psB]

    del_t = [ba.alloc(TQ) for _ in range(n_dt)]
    u_t = [ba.alloc(TQ) for _ in range(n_dt)]
    brep = [ba.alloc(TQ) for _ in range(NS)]
    crep = [ba.alloc(TQ) for _ in range(NS)]
    faa = Arena(p.sbuf("faa", [128, 4 * 1024], F32))
    a_pp = [faa.alloc(TQ) for _ in range(2)]
    b_pp = [ba.alloc(TQ) for _ in range(2)]
    h_pp = [ba.alloc(TQ) for _ in range(2)]
    hc_pp = b_pp
    ystage = [ba.alloc(TQ)]
    idt = p.sbuf("idt", [128, 128], BF16)
    At = p.sbuf("At", [128, 32], F32)
    carr = p.sbuf("carr", [128, 64], F32)
    gp_scr_t = p.sbuf("gp_scr", [128, 4], BF16)
    gp_scr = gp_scr_t[:]
    scan_done_list = []
    prev_evac = {}
    ystore = {}

    p.begin(consts=(1.0,), zero_psums=(psA, psB))
    block = p.block
    dA, dB, dC = p.sem("dA"), p.sem("dB"), p.sem("dC")
    dve, act, pe, pool = p.sem("dve"), p.sem("act"), p.sem("pe"), p.sem("pool")

    @block.sync
    def _(s):
        s.dma_start(out=At[:], in_=i_logA[:]).then_inc(dA.h, 16)
        s.dma_start(out=idt[:], in_=ident[:]).then_inc(dA.h, 16)
    dA.inc(32)
    consts_loaded = dA.n

    @block.scalar
    def _(s):
        s.wait_ge(dA.h, consts_loaded)
        s.activation(At[:], At[:], AF.Exp).then_inc(act.h, 1)
    act.inc(1)
    @block.vector
    def _(v):
        v.wait_ge(act.h, act.n)
        v.tensor_scalar(At[:], At[:], -1.0, None, ALU.mult).then_inc(dve.h, 1)
    dve.inc(1)
    A_ready = dve.n

    Hs = list(range(NQ))
    if reverse:
        Hs = Hs[::-1]
    units = []
    for hi, H in enumerate(Hs):
        for dt in range(n_dt):
            for n in range(NS):
                units.append((hi, H, dt, n))

    scan_done = {}
    yma = {}
    aexp = {}
    evac_tok = {}
    h_load = {}
    rep_tok = {}

    for hi, H in enumerate(Hs):
        @block.sync
        def _(s, H=H, hi=hi):
            if hi >= 1:
                kprev = hi * 32 - 1
                s.wait_ge(dve.h, scan_done_list[kprev])
            for dt in range(n_dt):
                r0 = H * 256 + dt * 128
                s.dma_start(out=del_t[dt], in_=i_delta[r0:r0 + 128, 0:TQ]) \
                    .then_inc(dA.h, 16)
                s.dma_start(out=u_t[dt], in_=i_xb[r0:r0 + 128, 0:TQ]) \
                    .then_inc(dA.h, 16)
        dA.inc(16 * 2 * n_dt)
        q_loaded = dA.n
        @block.vector
        def _(v, q_loaded=q_loaded):
            v.wait_ge(dA.h, q_loaded)
            last = None
            for dt in range(n_dt):
                last = v.tensor_mul(u_t[dt], u_t[dt], del_t[dt])
            last.then_inc(dve.h, 1)
        dve.inc(1)
        h_load[hi] = (q_loaded, dve.n)

        @block.gpsimd
        def _(g, H=H, hi=hi):
            g.wait_ge(dA.h, consts_loaded)
            if hi >= 1:
                kprev = hi * 32 - 1
                g.wait_ge(dve.h, scan_done_list[kprev])
            for n in range(NS):
                rb = H * 32 + n
                rep_row(g, brep[n], i_bc[rb:rb + 1, 0:TQ]).then_inc(dC.h, 16)
                rep_row(g, crep[n], i_bc[rb + NS:rb + NS + 1, 0:TQ]) \
                    .then_inc(dC.h, 16)
            g.wait_ge(dC.h, dC.n + 32 * NS)
            last = None
            for n in range(NS):
                last = g.tensor_copy(gp_scr[:, 0:2], brep[n][:, TQ - 2:TQ])
                last = g.tensor_copy(gp_scr[:, 2:4], crep[n][:, TQ - 2:TQ])
            last.then_inc(pool.h, 1)
        dC.inc(32 * NS)
        pool.inc(1)
        reps_ready = pool.n

        for dt in range(n_dt):
            for n in range(NS):
                k = hi * 32 + dt * 16 + n
                u = dt * 16 + n
                psY = psums[dt % 2]
                @block.scalar
                def _(s, k=k, dt=dt, u=u, hi=hi):
                    if k == 0:
                        s.wait_ge(dve.h, A_ready)
                    s.wait_ge(dA.h, h_load[hi][0])
                    if k >= 2:
                        s.wait_ge(dve.h, scan_done.get(k - 2, 0))
                    s.activation(a_pp[k % 2], del_t[dt], AF.Exp,
                                 scale=At[:, u:u + 1]).then_inc(act.h, 1)
                act.inc(1)
                aexp[k] = act.n
                @block.vector
                def _(v, k=k, dt=dt, n=n, u=u, hi=hi, psY=psY):
                    v.wait_ge(dve.h, h_load[hi][1])
                    v.wait_ge(pool.h, reps_ready)
                    if k >= 2:
                        v.wait_ge(pe.h, yma.get(k - 2, 0))
                    v.tensor_mul(b_pp[k % 2], u_t[dt], brep[n])
                    v.wait_ge(act.h, aexp[k])
                    init = 0.0 if hi % 4 == 0 else carr[:, u:u + 1]
                    if reverse:
                        v.tensor_tensor_scan(h_pp[k % 2][:, ::-1], a_pp[k % 2][:, ::-1],
                                             b_pp[k % 2][:, ::-1], init, ALU.mult,
                                             ALU.add).then_inc(dve.h, 1)
                        v.wait_ge(dve.h, dve.n + 1)
                        v.tensor_copy(carr[:, u:u + 1], h_pp[k % 2][:, 0:1])
                    else:
                        v.tensor_tensor_scan(h_pp[k % 2], a_pp[k % 2], b_pp[k % 2],
                                             init, ALU.mult, ALU.add).then_inc(dve.h, 1)
                        v.wait_ge(dve.h, dve.n + 1)
                        v.tensor_copy(carr[:, u:u + 1], h_pp[k % 2][:, TQ - 1:TQ])
                    v.tensor_mul(hc_pp[k % 2], h_pp[k % 2], crep[n]).then_inc(dve.h, 1)
                dve.inc(2)
                scan_done[k] = dve.n
                @block.tensor
                def _(t, k=k, dt=dt, n=n, hi=hi, psY=psY):
                    t.wait_ge(dve.h, scan_done[k])
                    if n == 0:
                        pv = prev_evac.get(dt % 2)
                        if pv is not None:
                            t.wait_ge(act.h, pv)
                        if k == 0:
                            t.wait_ge(p.sem("init").h, p._z1)
                            t.wait_ge(dA.h, consts_loaded)
                    last = None
                    for tc in range(TQ // 512):
                        last = t.matmul(psY[:, tc * 512:(tc + 1) * 512], idt[:],
                                        hc_pp[k % 2][:, tc * 512:(tc + 1) * 512],
                                        start=(n == 0), stop=(n == NS - 1))
                    last.then_inc(pe.h, 1)
                pe.inc(1)
                yma[k] = pe.n

            klast = hi * 32 + dt * 16 + NS - 1
            act.inc(1)
            ev = act.n
            dB.inc(16)
            stq = dB.n
            @block.scalar
            def _(s, hi=hi, H=H, dt=dt, klast=klast, ev=ev, psY=psums[dt % 2]):
                s.wait_ge(pe.h, yma[klast])
                if (hi * 2 + dt) >= 1:
                    s.wait_ge(dB.h, 16 * ystore[(hi * 2 + dt) - 1])
                s.copy(ystage[0], psY[:]).then_inc(act.h, 1)
                s.wait_ge(act.h, ev)
                s.dma_start(out=o_y[H * 256 + dt * 128:H * 256 + (dt + 1) * 128, 0:TQ],
                            in_=ystage[0]).then_inc(dB.h, 16)
            prev_evac[dt % 2] = ev
            evac_tok[(hi, dt)] = ev
            ystore[hi * 2 + dt] = dB.n // 16
        scan_done_list.extend([scan_done[hi * 32 + i] for i in range(32)])

    @block.gpsimd
    def _(g):
        g.wait_ge(dB.h, dB.n)




def phase_l0a_out(p, io):
    n_dt = DI // 128
    n_do = D // 128
    xT = io["xT"]
    i_y = io["i_y"]
    i_xb = io["i_xb"]
    i_sz = io["i_sz"]
    dp128 = io["dp128"]
    W_out = io["W_out"]
    o_x = io["o_x"]

    fa = Arena(p.sbuf("fa", [128, 30 * 1024], F32))
    ba = Arena(p.sbuf("ba", [128, 39 * 1024], BF16))
    psA = p.psum("psA", [128, TC], F32)
    psB = p.psum("psB", [128, TC], F32)

    x_t = [fa.alloc(TC) for _ in range(n_do)]
    x2_pp = [fa.alloc(TC) for _ in range(2)]
    reps = fa.alloc(2 * TC)
    rows = fa.alloc(3 * TC, rows=1)
    ostage = x2_pp[1]
    g_t = [ba.alloc(TC) for _ in range(n_dt)]
    in_pp = [ba.alloc(TC) for _ in range(3)]
    wbufs = [p.sbuf(f"wb{i}", [128, 128 * n_dt], BF16) for i in range(2)]
    dpt = p.sbuf("dpt", [128, n_dt], F32)
    ones128 = p.sbuf("ones128", [128, 128], F32)
    gp_scr_t = p.sbuf("gp_scr", [128, 4], BF16)

    p.begin(consts=(1.0, EPS), zero_psums=(psA, psB))
    block = p.block
    dA, dB, dC = p.sem("dA"), p.sem("dB"), p.sem("dC")
    dve, act, pe = p.sem("dve"), p.sem("act"), p.sem("pe")
    p._gp_scr = gp_scr_t[:]

    @block.sync
    def _(s):
        for i in range(n_do):
            s.dma_start(out=x_t[i], in_=xT[i * 128:(i + 1) * 128, :]).then_inc(dA.h, 16)
        s.dma_start(out=dpt[:], in_=dp128[:]).then_inc(dA.h, 16)
    dA.n += 16 * (n_do + 1)
    x_loaded = (dA, dA.n)

    @block.vector
    def _(v):
        v.wait_ge(p.sem("init").h, p._z1)
        v.memset(ones128[:], 1.0).then_inc(p.sem("init").h, 1)
    p.sem("init").inc(1)
    init_tok = (p.sem("init"), p._z1 + 1)

    R = lambda i: rows[0:1, i * TC:(i + 1) * TC]
    mu, var, r0 = R(0), R(1), R(2)
    m0 = R(1)
    emit_ln_stats(p, x_t, x2_pp, psA[:, :], psB[:, :], ones128[:],
                  x_loaded, mu, var, R(2), init_tok)
    t0 = emit_rsqrt_row(p, r0, var, p.constf(EPS, 0, 1), p.sem("dve").tok)
    @block.vector
    def _(v):
        v.wait_ge(t0[0].h, t0[1])
        v.tensor_mul(m0, mu, r0).then_inc(dve.h, 1)
    dve.inc(1)
    rows_tok = dve.tok

    @block.gpsimd
    def _(g):
        g.wait_ge(rows_tok[0].h, rows_tok[1])
        rep_row(g, reps[:, 0:TC], r0).then_inc(dC.h, 16)
        rep_row(g, reps[:, TC:2 * TC], m0).then_inc(dC.h, 16)
    dC.inc(32)
    reps_tok = (dC, dC.n)

    @block.vector
    def _(v):
        v.wait_ge(reps_tok[0].h, reps_tok[1])
        last = None
        for i in range(n_do):
            v.tensor_mul(x2_pp[0], x_t[i], reps[:, 0:TC])
            v.tensor_add(x_t[i], x_t[i], x2_pp[0])
            last = v.tensor_sub(x_t[i], x_t[i], reps[:, TC:2 * TC])
        last.then_inc(dve.h, 1)
    dve.inc(1)
    xp_tok = dve.tok

    gtoks = {}
    ld_tok = {}
    for dt in range(n_dt):
        buf = [in_pp[j] for j in range(3)]
        @block.sync
        def _(s, dt=dt, buf=buf):
            if dt >= 1:
                s.wait_ge(dve.h, gtoks[dt - 1])
            s.dma_start(out=buf[0], in_=i_y[dt * 128:(dt + 1) * 128, :]).then_inc(dA.h, 16)
            s.dma_start(out=buf[1], in_=i_xb[dt * 128:(dt + 1) * 128, :]).then_inc(dA.h, 16)
            s.dma_start(out=buf[2], in_=i_sz[dt * 128:(dt + 1) * 128, :]).then_inc(dA.h, 16)
        dA.inc(48)
        ld_tok[dt] = dA.n
        @block.vector
        def _(v, dt=dt, buf=buf):
            v.wait_ge(dA.h, ld_tok[dt])
            v.scalar_tensor_tensor(x2_pp[1], buf[1], dpt[:, dt:dt + 1], buf[0],
                                   ALU.mult, ALU.add)
            v.tensor_mul(g_t[dt], x2_pp[1], buf[2]).then_inc(dve.h, 1)
        dve.inc(1)
        gtoks[dt] = dve.n
    g_all = (dve, dve.n)

    store_tok = {}

    def epi_out(do, ps, mm_tok):
        dve_l, dB_l = p.sem("dve"), p.sem("dB")
        @block.vector
        def _(v, do=do, ps=ps, mm_tok=mm_tok):
            v.wait_ge(mm_tok[0].h, mm_tok[1])
            if do >= 1:
                v.wait_ge(dB_l.h, 16 * store_tok[do - 1])
            v.tensor_add(ostage, ps[:], x_t[do]).then_inc(dve_l.h, 1)
        dve_l.inc(1)
        ps_free = (dve_l, dve_l.n)
        @block.scalar
        def _(s, do=do, psf=ps_free[1]):
            s.wait_ge(dve_l.h, psf)
            s.dma_start(out=o_x[do * 128:(do + 1) * 128, :], in_=ostage).then_inc(dB_l.h, 16)
        dB_l.inc(16)
        store_tok[do] = dB_l.n // 16
        return ps_free

    emit_proj(p, W_out, [g_t[i] for i in range(n_dt)], n_do, wbufs,
              [psA, psB], epi_out, g_all, extra_in=xp_tok)

    @block.gpsimd
    def _(g):
        g.wait_ge(dB.h, p.sem("dB").n)




def emit_ln_streamed(p, xT, n_k, T, xs_pp, sq_pp, psA_l, psB_l, ones_col, init_tok,
                     xn_t, reps, rows_mu, rows_var, rows_r0, rows_m0, eps_ap,
                     gb_cols=None, extra_scale_row=None):
    block = p.block
    dA, dC = p.sem("dA"), p.sem("dC")
    dve, act, pe = p.sem("dve"), p.sem("act"), p.sem("pe")
    tch = tchunks_of(T)
    sq_done = {}
    mm_done = {}
    ld = {}
    for i in range(n_k):
        @block.sync
        def _(s, i=i):
            if i >= 2:
                s.wait_ge(pe.h, mm_done[i - 2])
            s.dma_start(out=xs_pp[i % 2], in_=xT[i * 128:(i + 1) * 128, :]).then_inc(dA.h, 16)
        dA.inc(16)
        ld[i] = dA.n
        @block.scalar
        def _(s, i=i):
            s.wait_ge(dA.h, ld[i])
            if i >= 2:
                s.wait_ge(pe.h, mm_done[i - 2])
            s.activation(sq_pp[i % 2], xs_pp[i % 2], AF.Square).then_inc(act.h, 1)
        act.inc(1)
        sq_done[i] = act.n
        @block.tensor
        def _(t, i=i):
            if i == 0:
                t.wait_ge(init_tok[0].h, init_tok[1])
            t.wait_ge(act.h, sq_done[i])
            last = None
            for ci, (t0, t1) in enumerate(tch):
                t.matmul(psA_l[ci][:, 0:t1 - t0], ones_col, xs_pp[i % 2][:, t0:t1],
                         start=(i == 0), stop=(i == n_k - 1))
                last = t.matmul(psB_l[ci][:, 0:t1 - t0], ones_col, sq_pp[i % 2][:, t0:t1],
                                start=(i == 0), stop=(i == n_k - 1))
            last.then_inc(pe.h, 1)
        pe.inc(1)
        mm_done[i] = pe.n
    stats_done = pe.n
    inv_n = 1.0 / (128 * n_k)
    @block.vector
    def _(v):
        v.wait_ge(pe.h, stats_done)
        for ci, (t0, t1) in enumerate(tch):
            v.tensor_scalar(rows_mu[:, t0:t1], psA_l[ci][0:1, 0:t1 - t0], inv_n, None, ALU.mult)
            v.tensor_scalar(rows_var[:, t0:t1], psB_l[ci][0:1, 0:t1 - t0], inv_n, None, ALU.mult)
        v.tensor_mul(rows_r0, rows_mu, rows_mu)
        v.tensor_sub(rows_var, rows_var, rows_r0).then_inc(dve.h, 1)
    dve.inc(1)
    t0 = emit_rsqrt_row(p, rows_r0, rows_var, eps_ap, p.sem("dve").tok)
    @block.vector
    def _(v):
        v.wait_ge(t0[0].h, t0[1])
        v.tensor_mul(rows_m0, rows_mu, rows_r0).then_inc(dve.h, 1)
    dve.inc(1)
    rows_tok = dve.tok
    @block.gpsimd
    def _(g):
        g.wait_ge(rows_tok[0].h, rows_tok[1])
        rep_row(g, reps[:, 0:T], rows_r0).then_inc(dC.h, 16)
        rep_row(g, reps[:, T:2 * T], rows_m0).then_inc(dC.h, 16)
    dC.inc(32)
    reps_tok = dC.n
    ap_done = {}
    for i in range(n_k):
        @block.sync
        def _(s, i=i):
            if i == 0:
                s.wait_ge(pe.h, stats_done)
            if i >= 2:
                s.wait_ge(dve.h, ap_done[i - 2])
            s.dma_start(out=xs_pp[i % 2], in_=xT[i * 128:(i + 1) * 128, :]).then_inc(dA.h, 16)
        dA.inc(16)
        l2 = dA.n
        @block.vector
        def _(v, i=i, l2=l2):
            v.wait_ge(dA.h, l2)
            v.wait_ge(dC.h, reps_tok)
            v.tensor_mul(sq_pp[i % 2], xs_pp[i % 2], reps[:, 0:T])
            last = v.tensor_sub(xn_t[i], sq_pp[i % 2], reps[:, T:2 * T])
            last.then_inc(dve.h, 1)
        dve.inc(1)
        ap_done[i] = dve.n
    return (p.sem("dve"), p.sem("dve").n)


def phase_ffn(p, io):
    DH = 4096
    n_kD, n_dh = D // 128, DH // 128
    TH = 512
    NTH = TC // TH
    xT = io["xT"]
    w12 = io["w12"]
    s12 = io["s12"]
    h1T = io["h1T"]
    h1s = io["h1s"]
    h2T = io["h2T"]
    h2s = io["h2s"]
    w3T = io["w3T"]
    w3s = io["w3s"]
    o_x = io["o_x"]

    fa = Arena(p.sbuf("fa", [128, 21 * 1024], F32))
    ba = Arena(p.sbuf("ba", [128, 49 * 1024], BF16))
    psall = [p.psum(f"ps{i}", [128, 512], F32) for i in range(8)]

    xs_pp = [fa.alloc(TC) for _ in range(2)]
    sq_pp = [fa.alloc(TC) for _ in range(2)]
    reps = fa.alloc(2 * TC)
    rows = fa.alloc(3 * TC, rows=1)
    silu_scr = fa.alloc(TH)
    ostage = fa.alloc(TH)
    xn_t = [ba.alloc(TC) for _ in range(n_kD)]
    h_t = [ba.alloc(TH) for _ in range(n_dh)]
    g_t = [ba.alloc(TH) for _ in range(n_dh)]
    wb12 = [p.sbuf(f"wb12_{i}", [128, 2 * 128 * n_kD], BF16) for i in range(2)]
    wbh = [p.sbuf(f"wbh{i}", [128, 128 * n_dh], BF16) for i in range(2)]
    sc = p.sbuf("sc", [128, 4 * n_dh + n_kD], F32)
    ones128 = p.sbuf("ones128", [128, 128], F32)
    gp_scr_t = p.sbuf("gp_scr", [128, 4], BF16)

    p.begin(consts=(1.0, EPS), zero_psums=psall)
    block = p.block
    dA, dB, dC = p.sem("dA"), p.sem("dB"), p.sem("dC")
    dve, act, pe = p.sem("dve"), p.sem("act"), p.sem("pe")
    p._gp_scr = gp_scr_t[:]

    @block.sync
    def _(s):
        s.dma_start(out=sc[:, 0:2 * n_dh], in_=s12[:]).then_inc(dA.h, 16)
        s.dma_start(out=sc[:, 2 * n_dh:3 * n_dh], in_=h1s[:]).then_inc(dA.h, 16)
        s.dma_start(out=sc[:, 3 * n_dh:4 * n_dh], in_=h2s[:]).then_inc(dA.h, 16)
        s.dma_start(out=sc[:, 4 * n_dh:4 * n_dh + n_kD], in_=w3s[:]).then_inc(dA.h, 16)
    dA.inc(64)
    sc_loaded = (dA, dA.n)

    @block.vector
    def _(v):
        v.wait_ge(p.sem("init").h, p._z1)
        v.memset(ones128[:], 1.0).then_inc(p.sem("init").h, 1)
    p.sem("init").inc(1)
    init_tok = (p.sem("init"), p._z1 + 1)

    R = lambda i: rows[0:1, i * TC:(i + 1) * TC]
    xn_tok = emit_ln_streamed(p, xT, n_kD, TC, xs_pp, sq_pp, psall[0:4], psall[4:8],
                              ones128[:], init_tok, xn_t, reps,
                              R(0), R(1), R(2), R(1), p.constf(EPS, 0, 1))

    prev_tail = [xn_tok, sc_loaded]
    add_done = {}
    store_tok = {}
    for th in range(NTH):
        tsl = slice(th * TH, (th + 1) * TH)
        xn_h = [t[:, tsl] for t in xn_t]
        h_toks = {}

        def epi12(do, psa, psb, mm_tok, th=th):
            act_l, dve_l = p.sem("act"), p.sem("dve")
            @block.scalar
            def _(s, do=do, psa=psa, mm_tok=mm_tok):
                s.wait_ge(mm_tok[0].h, mm_tok[1])
                if do >= 2:
                    s.wait_ge(dve_l.h, h_toks[do - 2])
                s.activation(silu_scr, psa[:], AF.Silu,
                             scale=sc[:, do:do + 1]).then_inc(act_l.h, 1)
            act_l.inc(1)
            silu_done = act_l.n
            @block.vector
            def _(v, do=do, psb=psb, silu_done=silu_done):
                v.wait_ge(act_l.h, silu_done)
                v.scalar_tensor_tensor(h_t[do][:], psb[:], sc[:, n_dh + do:n_dh + do + 1],
                                       silu_scr, ALU.mult, ALU.mult).then_inc(dve_l.h, 1)
            dve_l.inc(1)
            h_toks[do] = dve_l.n
            return (dve_l, dve_l.n)

        cons = emit_proj12(p, w12, xn_h, n_dh, wb12, [psall[0], psall[2]],
                           [psall[1], psall[3]], epi12, prev_tail[0],
                           prev_tail=prev_tail)
        h_all = (p.sem("dve"), p.sem("dve").n)
        p12_tail = [cons[n_dh - 2], cons[n_dh - 1]]

        def mk_epi_gelu(dst_tiles, scol0, toks):
            def epi(do, ps, mm_tok):
                act_l = p.sem("act")
                @block.scalar
                def _(s, do=do, ps=ps, mm_tok=mm_tok):
                    s.wait_ge(mm_tok[0].h, mm_tok[1])
                    s.activation(dst_tiles[do][:], ps[:], AF.Gelu,
                                 scale=sc[:, scol0 + do:scol0 + do + 1]).then_inc(act_l.h, 1)
                act_l.inc(1)
                toks[do] = act_l.n
                return (act_l, act_l.n)
            return epi

        g1_toks = {}
        cons = emit_proj(p, h1T, [t[:] for t in h_t], n_dh, wbh, [psall[4], psall[5]],
                         mk_epi_gelu(g_t, 2 * n_dh, g1_toks), h_all,
                         prev_tail=p12_tail)
        g1_all = (p.sem("act"), p.sem("act").n)
        g1_tail = [cons[n_dh - 2], cons[n_dh - 1]]
        g2_toks = {}
        cons = emit_proj(p, h2T, [t[:] for t in g_t], n_dh, wbh, [psall[6], psall[7]],
                         mk_epi_gelu(h_t, 3 * n_dh, g2_toks), g1_all,
                         prev_tail=g1_tail)
        g2_all = (p.sem("act"), p.sem("act").n)
        g2_tail = [cons[n_dh - 2], cons[n_dh - 1]]

        def epi_out(do, ps, mm_tok, th=th):
            dve_l, dB_l, dA_l = p.sem("dve"), p.sem("dB"), p.sem("dA")
            xr = xs_pp[do % 2][:, 0:TH]
            gi = th * n_kD + do
            @block.sync
            def _(s, do=do, th=th, xr=xr, gi=gi):
                if gi >= 2:
                    s.wait_ge(dve_l.h, add_done[gi - 2])
                else:
                    s.wait_ge(xn_tok[0].h, xn_tok[1])
                s.dma_start(out=xr, in_=xT[do * 128:(do + 1) * 128, th * TH:(th + 1) * TH]) \
                    .then_inc(dA_l.h, 16)
            dA_l.inc(16)
            xl = dA_l.n
            @block.vector
            def _(v, do=do, ps=ps, mm_tok=mm_tok, xl=xl, xr=xr, gi=gi):
                v.wait_ge(mm_tok[0].h, mm_tok[1])
                v.wait_ge(dA_l.h, xl)
                if gi >= 1:
                    v.wait_ge(dB_l.h, 16 * store_tok[gi - 1])
                v.scalar_tensor_tensor(ostage, ps[:], sc[:, 4 * n_dh + do:4 * n_dh + do + 1],
                                       xr, ALU.mult, ALU.add).then_inc(dve_l.h, 1)
            dve_l.inc(1)
            add_done[gi] = dve_l.n
            ps_free = (dve_l, dve_l.n)
            @block.scalar
            def _(s, do=do, th=th, psf=ps_free[1], gi=gi):
                s.wait_ge(dve_l.h, psf)
                s.dma_start(out=o_x[do * 128:(do + 1) * 128, th * TH:(th + 1) * TH],
                            in_=ostage).then_inc(dB_l.h, 16)
            dB_l.inc(16)
            store_tok[gi] = dB_l.n // 16
            return ps_free

        cons = emit_proj(p, w3T, [t[:] for t in h_t], n_kD, wbh, [psall[4], psall[5]],
                         epi_out, g2_all, prev_tail=g2_tail)
        prev_tail = [cons[n_kD - 2], cons[n_kD - 1]]

    @block.gpsimd
    def _(g):
        g.wait_ge(p.sem("dB").h, p.sem("dB").n)


def emit_proj12(p, w_dram, in_tiles, n_do, wbufs, psumsA, psumsB, epilogue, in_ready,
                 prev_tail=None):
    block = p.block
    pe, dC, pool = p.sem("pe"), p.sem("dC"), p.sem("pool")
    n_k = len(in_tiles)
    T = in_tiles[0].shape[-1]
    tch = tchunks_of(T)
    DHcols = w_dram.shape[-1] // 2
    gp_scr = p._gp_scr

    def w_ap(do, half):
        c0 = half * DHcols + do * 128
        return w_dram[:, c0:c0 + 128].rearrange("(ki p) c -> p ki c", p=128)

    consumed = {}
    load_tok = {}

    def emit_load(do, buf_idx):
        @block.gpsimd
        def _(g, do=do, buf_idx=buf_idx):
            if do >= 2:
                tok = consumed[do - 2]
                g.wait_ge(tok[0].h, tok[1])
            elif prev_tail:
                for tok in prev_tail:
                    g.wait_ge(tok[0].h, tok[1])
            g.dma_start(out=wbufs[buf_idx][:, 0:128 * n_k]
                        .rearrange("p (ki c) -> p ki c", c=128),
                        in_=w_ap(do, 0)).then_inc(dC.h, 16)
            g.dma_start(out=wbufs[buf_idx][:, 128 * n_k:256 * n_k]
                        .rearrange("p (ki c) -> p ki c", c=128),
                        in_=w_ap(do, 1)).then_inc(dC.h, 16)
            g.wait_ge(dC.h, dC.n + 32)
            g.tensor_copy(gp_scr[:, 0:4],
                          wbufs[buf_idx][:, 256 * n_k - 4:256 * n_k]).then_inc(pool.h, 1)
        dC.inc(32)
        pool.inc(1)
        load_tok[do] = pool.n

    for do in range(min(2, n_do)):
        emit_load(do, do)
    for do in range(n_do):
        psa, psb = psumsA[do % 2], psumsB[do % 2]
        @block.tensor
        def _(t, do=do, psa=psa, psb=psb):
            t.wait_ge(pool.h, load_tok[do])
            if do == 0:
                if "init" in p.sems:
                    t.wait_ge(p.sems["init"].h, p._z1)
                t.wait_ge(in_ready[0].h, in_ready[1])
                if prev_tail:
                    for tok in prev_tail:
                        t.wait_ge(tok[0].h, tok[1])
            if do >= 2:
                tok = consumed[do - 2]
                t.wait_ge(tok[0].h, tok[1])
            last = None
            for (t0, t1) in tch:
                for ki in range(n_k):
                    t.matmul(psa[:, t0:t1], wbufs[do % 2][:, ki * 128:(ki + 1) * 128],
                             in_tiles[ki][:, t0:t1], start=(ki == 0), stop=(ki == n_k - 1))
                for ki in range(n_k):
                    last = t.matmul(psb[:, t0:t1],
                                    wbufs[do % 2][:, 128 * n_k + ki * 128:128 * n_k + (ki + 1) * 128],
                                    in_tiles[ki][:, t0:t1], start=(ki == 0), stop=(ki == n_k - 1))
            last.then_inc(pe.h, 1)
        pe.inc(1)
        consumed[do] = epilogue(do, psa, psb, pe.tok)
        if do + 2 < n_do:
            emit_load(do + 2, do % 2)
    return consumed




def phase_tokB_b(p, io):
    n_dt = DI // 128
    i_body = io["i_xbr_body"]
    edgeAG = io["edgeAG"]
    hmaskL_in = io["hmaskL"]
    hmaskR_in = io["hmaskR"]
    convw_f = io["convw_f"]
    convw_r = io["convw_r"]
    cb4 = io["cb4"]
    xprojT_f = io["xprojT_f"]
    xprojT_r = io["xprojT_r"]
    dtwT_f = io["dtwT_f"]
    dtwT_r = io["dtwT_r"]
    outs = {}
    for br in ("f", "r"):
        outs[f"xb_{br}"] = io[f"o_xb_{br}"]
        outs[f"delta_{br}"] = io[f"o_delta_{br}"]
        outs[f"bc_{br}"] = io[f"o_bc_{br}"]

    fa = Arena(p.sbuf("fa", [128, 12 * 1024], F32))
    ba = Arena(p.sbuf("ba", [128, 65 * 1024], BF16))
    psA = p.psum("psA", [128, TC], F32)
    psB = p.psum("psB", [128, TC], F32)

    xbr_t = [ba.alloc(6 + TC) for _ in range(n_dt)]
    bmark = ba.mark()
    cacc = fa.alloc(TC)
    sp1, sp2 = fa.alloc(TC), fa.alloc(TC)
    xdbl = p.sbuf("xdbl", [96, TC], BF16)[:, :]
    cwf = p.sbuf("cwf", [128, n_dt * 4], F32)
    cwr = p.sbuf("cwr", [128, n_dt * 4], F32)
    cbt = p.sbuf("cbt", [128, 4 * n_dt], F32)
    ehL = p.sbuf("ehL", [128, n_dt * 12], BF16)
    ehR = p.sbuf("ehR", [128, n_dt * 12], BF16)
    hmL = p.sbuf("hmL", [128, 4], F32)
    hmR = p.sbuf("hmR", [128, 4], F32)
    wpj = p.sbuf("wpj", [128, 16 * 96], BF16)
    wdt = p.sbuf("wdt", [64, DI], BF16)
    gp_scr_t = p.sbuf("gp_scr", [128, 4], BF16)

    p.begin(consts=(1.0,), zero_psums=(psA, psB))
    block = p.block
    dA, dB, dC = p.sem("dA"), p.sem("dB"), p.sem("dC")
    dve, act, pe, pool = p.sem("dve"), p.sem("act"), p.sem("pe"), p.sem("pool")
    p._gp_scr = gp_scr_t[:]

    @block.sync
    def _(s):
        for i in range(n_dt):
            s.dma_start(out=xbr_t[i][:, 3:3 + TC],
                        in_=i_body[i * 128:(i + 1) * 128, :]).then_inc(dA.h, 16)
            for j in range(4):
                s.dma_start(out=ehL[:, i * 12 + j * 3:i * 12 + (j + 1) * 3],
                            in_=edgeAG[j * DI + i * 128:j * DI + (i + 1) * 128, 3:6]) \
                    .then_inc(dA.h, 16)
                s.dma_start(out=ehR[:, i * 12 + j * 3:i * 12 + (j + 1) * 3],
                            in_=edgeAG[j * DI + i * 128:j * DI + (i + 1) * 128, 0:3]) \
                    .then_inc(dA.h, 16)
        s.dma_start(out=cwf[:], in_=convw_f[:]).then_inc(dA.h, 16)
        s.dma_start(out=cwr[:], in_=convw_r[:]).then_inc(dA.h, 16)
        s.dma_start(out=cbt[:], in_=cb4[:]).then_inc(dA.h, 16)
        s.dma_start(out=hmL[:], in_=hmaskL_in[:]).then_inc(dA.h, 16)
        s.dma_start(out=hmR[:], in_=hmaskR_in[:]).then_inc(dA.h, 16)
    dA.n += 16 * (9 * n_dt + 5)
    all_ld = dA.n
    @block.vector
    def _(v):
        v.wait_ge(dA.h, all_ld)
        last = None
        for i in range(n_dt):
            for (eh, hm_, c0) in ((ehL, hmL, 0), (ehR, hmR, 3 + TC)):
                v.tensor_scalar(cacc[:, 0:3], eh[:, i * 12:i * 12 + 3],
                                hm_[:, 0:1], None, ALU.mult)
                for j in range(1, 4):
                    v.scalar_tensor_tensor(cacc[:, 0:3],
                                           eh[:, i * 12 + j * 3:i * 12 + (j + 1) * 3],
                                           hm_[:, j:j + 1], cacc[:, 0:3],
                                           ALU.mult, ALU.add)
                last = v.tensor_copy(xbr_t[i][:, c0:c0 + 3], cacc[:, 0:3])
        last.then_inc(dve.h, 1)
    dve.inc(1)
    loaded = (dve, dve.n)

    for bi, br in enumerate(("f", "r")):
        mark = ba.mark()
        xb_t = [ba.alloc(TC) for _ in range(n_dt)]
        cw = cwf if br == "f" else cwr
        halo_off = 0 if br == "f" else 3
        xb_toks = emit_conv_silu(p, [t[:, halo_off:halo_off + 3 + TC] for t in xbr_t],
                                 xb_t, cw[:], cbt[:, bi * n_dt:(bi + 1) * n_dt], cacc,
                                 [loaded] * n_dt, o_xb=outs[f"xb_{br}"])
        xb_all = xb_toks[-1]

        @block.gpsimd
        def _(g, br=br):
            if br == "r":
                g.wait_ge(pe.h, pe.n)
            g.dma_start(out=wpj[:].rearrange("p (ki c) -> p ki c", c=96),
                        in_=(xprojT_f if br == "f" else xprojT_r)
                        .rearrange("(ki p) c -> p ki c", p=128)).then_inc(dC.h, 16)
            g.dma_start(out=wdt[:], in_=(dtwT_f if br == "f" else dtwT_r)) \
                .then_inc(dC.h, 16)
            g.wait_ge(dC.h, dC.n + 32)
            g.tensor_copy(gp_scr_t[:, 0:4], wpj[:, 15 * 96 + 92:15 * 96 + 96]) \
                .then_inc(pool.h, 1)
            g.tensor_copy(gp_scr_t[0:64, 0:4], wdt[:, DI - 4:DI]).then_inc(pool.h, 1)
        dC.inc(32)
        pool.inc(2)
        w_tok = pool.n

        @block.tensor
        def _(t, br=br, w_tok=w_tok, xb_all=xb_all, xb_t=xb_t):
            t.wait_ge(pool.h, w_tok)
            t.wait_ge(xb_all[0].h, xb_all[1])
            if br == "f":
                t.wait_ge(p.sem("init").h, p._z1)
            last = None
            for (t0, t1) in tchunks_of(TC):
                for ki in range(n_dt):
                    last = t.matmul(psA[0:96, t0:t1], wpj[:, ki * 96:(ki + 1) * 96],
                                    xb_t[ki][:, t0:t1], start=(ki == 0), stop=(ki == n_dt - 1))
            last.then_inc(pe.h, 1)
        pe.inc(1)
        xp_done = pe.n

        @block.vector
        def _(v, xp_done=xp_done):
            v.wait_ge(pe.h, xp_done)
            v.tensor_copy(xdbl, psA[0:96, :]).then_inc(dve.h, 1)
        dve.inc(1)
        xd_tok = dve.n
        @block.scalar
        def _(s, br=br, xd_tok=xd_tok):
            s.wait_ge(dve.h, xd_tok)
            s.dma_start(out=outs[f"bc_{br}"][:], in_=xdbl[64:96, :]).then_inc(dB.h, 16)
        dB.inc(16)

        store_tok = {}
        sp_guard = {}
        cons_d = {}
        for do in range(n_dt):
            @block.tensor
            def _(t, do=do, xd_tok=xd_tok):
                t.wait_ge(dve.h, xd_tok)
                if do >= 2:
                    tok = cons_d[do - 2]
                    t.wait_ge(tok[0].h, tok[1])
                last = None
                for (t0, t1) in tchunks_of(TC):
                    last = t.matmul((psB if do % 2 == 0 else psA)[:, t0:t1],
                                    wdt[:, do * 128:(do + 1) * 128],
                                    xdbl[0:64, t0:t1], start=True, stop=True)
                last.then_inc(pe.h, 1)
            pe.inc(1)
            mm_tok = pe.tok
            ps = psB if do % 2 == 0 else psA
            st = None
            act_l, dve_l, dB_l = act, dve, dB
            prev_sum = sp_guard.get(do - 1)
            @block.scalar
            def _(s, do=do, ps=ps, mm_tok=mm_tok, prev_sum=prev_sum, bi=bi):
                s.wait_ge(mm_tok[0].h, mm_tok[1])
                if prev_sum is not None:
                    s.wait_ge(dve_l.h, prev_sum)
                bcol = cbt[:, (2 + bi) * n_dt + do:(2 + bi) * n_dt + do + 1]
                s.activation(sp1, ps[:], AF.Relu, bias=bcol)
                s.activation(sp2, ps[:], AF.Abs, bias=bcol).then_inc(act_l.h, 1)
                s.activation(sp2, sp2, AF.Exp, scale=-1.0)
                s.activation(sp2, sp2, AF.Ln, bias=1.0).then_inc(act_l.h, 1)
            act_l.inc(2)
            ps_free = (act_l, act_l.n - 1)
            ln_done = act_l.n
            dstage = xb_t[do]
            @block.vector
            def _(v, do=do, ln_done=ln_done, dstage=dstage):
                v.wait_ge(act_l.h, ln_done)
                v.tensor_add(dstage[:], sp1, sp2).then_inc(dve_l.h, 1)
            dve_l.inc(1)
            sp_guard[do] = dve_l.n
            sum_done = dve_l.n
            @block.scalar
            def _(s, do=do, br=br, sum_done=sum_done, dstage=dstage):
                s.wait_ge(dve_l.h, sum_done)
                s.dma_start(out=outs[f"delta_{br}"][do * 128:(do + 1) * 128, :],
                            in_=dstage[:]).then_inc(dB_l.h, 16)
            dB_l.inc(16)
            store_tok[do] = dB_l.n
            cons_d[do] = ps_free
        ba.release(mark)

    @block.gpsimd
    def _(g):
        g.wait_ge(dB.h, dB.n)




def phase_l0b_out(p, io):
    n_dt = DI // 128
    n_do = D // 128
    xT = io["xT"]
    i_y_f = io["i_y_f"]
    i_y_r = io["i_y_r"]
    i_xb_f = io["i_xb_f"]
    i_xb_r = io["i_xb_r"]
    i_sz = io["i_sz"]
    d2 = io["d2"]
    W_outT = io["W_outT"]
    o_x = io["o_x"]

    fa = Arena(p.sbuf("fa", [128, 22 * 1024], F32))
    ba = Arena(p.sbuf("ba", [128, 43 * 1024], BF16))
    psA = p.psum("psA", [128, TC], F32)
    psB = p.psum("psB", [128, TC], F32)

    xs_pp = [fa.alloc(TC) for _ in range(2)]
    sq_pp = [fa.alloc(TC) for _ in range(2)]
    reps = fa.alloc(2 * TC)
    rows = fa.alloc(3 * TC, rows=1)
    vscr = fa.alloc(TC)
    ostage = fa.alloc(TC)
    g_t = [ba.alloc(TC) for _ in range(n_dt)]
    in_pp = [ba.alloc(TC) for _ in range(5)]
    wbufs = [p.sbuf(f"wb{i}", [128, 128 * n_dt], BF16) for i in range(2)]
    dpt = p.sbuf("dpt", [128, 2 * n_dt], F32)
    ones128 = p.sbuf("ones128", [128, 128], F32)
    gp_scr_t = p.sbuf("gp_scr", [128, 4], BF16)

    p.begin(consts=(1.0, EPS), zero_psums=(psA, psB))
    block = p.block
    dA, dB, dC = p.sem("dA"), p.sem("dB"), p.sem("dC")
    dve, act, pe = p.sem("dve"), p.sem("act"), p.sem("pe")
    p._gp_scr = gp_scr_t[:]

    @block.sync
    def _(s):
        s.dma_start(out=dpt[:], in_=d2[:]).then_inc(dA.h, 16)
    dA.inc(16)

    @block.vector
    def _(v):
        v.wait_ge(p.sem("init").h, p._z1)
        v.memset(ones128[:], 1.0).then_inc(p.sem("init").h, 1)
    p.sem("init").inc(1)
    init_tok = (p.sem("init"), p._z1 + 1)

    R = lambda i: rows[0:1, i * TC:(i + 1) * TC]
    tch = tchunks_of(TC)
    sq_done, mm_done, ld = {}, {}, {}
    for i in range(n_do):
        @block.sync
        def _(s, i=i):
            if i >= 2:
                s.wait_ge(pe.h, mm_done[i - 2])
            s.dma_start(out=xs_pp[i % 2], in_=xT[i * 128:(i + 1) * 128, :]).then_inc(dA.h, 16)
        dA.inc(16)
        ld[i] = dA.n
        @block.scalar
        def _(s, i=i):
            s.wait_ge(dA.h, ld[i])
            if i >= 2:
                s.wait_ge(pe.h, mm_done[i - 2])
            s.activation(sq_pp[i % 2], xs_pp[i % 2], AF.Square).then_inc(act.h, 1)
        act.inc(1)
        sq_done[i] = act.n
        @block.tensor
        def _(t, i=i):
            if i == 0:
                t.wait_ge(init_tok[0].h, init_tok[1])
            t.wait_ge(act.h, sq_done[i])
            last = None
            for ci, (t0, t1) in enumerate(tch):
                t.matmul(psA[:, t0:t1], ones128[:], xs_pp[i % 2][:, t0:t1],
                         start=(i == 0), stop=(i == n_do - 1))
                last = t.matmul(psB[:, t0:t1], ones128[:], sq_pp[i % 2][:, t0:t1],
                                start=(i == 0), stop=(i == n_do - 1))
            last.then_inc(pe.h, 1)
        pe.inc(1)
        mm_done[i] = pe.n
    stats_done = pe.n
    mu, var, r0 = R(0), R(1), R(2)
    m0 = R(1)
    @block.vector
    def _(v):
        v.wait_ge(pe.h, stats_done)
        v.tensor_scalar(mu, psA[0:1, :], 1.0 / D, None, ALU.mult)
        v.tensor_scalar(var, psB[0:1, :], 1.0 / D, None, ALU.mult)
        v.tensor_mul(r0, mu, mu)
        v.tensor_sub(var, var, r0).then_inc(dve.h, 1)
    dve.inc(1)
    t0tok = emit_rsqrt_row(p, r0, var, p.constf(EPS, 0, 1), p.sem("dve").tok)
    @block.vector
    def _(v):
        v.wait_ge(t0tok[0].h, t0tok[1])
        v.tensor_mul(m0, mu, r0)
        v.tensor_scalar(r0, r0, 1.0, None, ALU.add).then_inc(dve.h, 1)
    dve.inc(1)
    rows_tok = dve.tok
    @block.gpsimd
    def _(g):
        g.wait_ge(rows_tok[0].h, rows_tok[1])
        rep_row(g, reps[:, 0:TC], r0).then_inc(dC.h, 16)
        rep_row(g, reps[:, TC:2 * TC], m0).then_inc(dC.h, 16)
    dC.inc(32)
    reps_tok = (dC, dC.n)

    gtoks, gld = {}, {}
    for dt in range(n_dt):
        buf = in_pp
        @block.sync
        def _(s, dt=dt, buf=buf):
            if dt >= 1:
                s.wait_ge(dve.h, gtoks[dt - 1])
            s.dma_start(out=buf[0], in_=i_y_f[dt * 128:(dt + 1) * 128, :]).then_inc(dA.h, 16)
            s.dma_start(out=buf[1], in_=i_xb_f[dt * 128:(dt + 1) * 128, :]).then_inc(dA.h, 16)
            s.dma_start(out=buf[2], in_=i_y_r[dt * 128:(dt + 1) * 128, :]).then_inc(dA.h, 16)
            s.dma_start(out=buf[3], in_=i_xb_r[dt * 128:(dt + 1) * 128, :]).then_inc(dA.h, 16)
            s.dma_start(out=buf[4], in_=i_sz[dt * 128:(dt + 1) * 128, :]).then_inc(dA.h, 16)
        dA.inc(80)
        gld[dt] = dA.n
        @block.vector
        def _(v, dt=dt, buf=buf):
            v.wait_ge(dA.h, gld[dt])
            v.scalar_tensor_tensor(vscr, buf[1], dpt[:, dt:dt + 1], buf[0],
                                   ALU.mult, ALU.add)
            v.scalar_tensor_tensor(sq_pp[0], buf[3], dpt[:, n_dt + dt:n_dt + dt + 1],
                                   buf[2], ALU.mult, ALU.add)
            v.tensor_add(vscr, vscr, sq_pp[0])
            v.tensor_mul(g_t[dt], vscr, buf[4]).then_inc(dve.h, 1)
        dve.inc(1)
        gtoks[dt] = dve.n
    g_all = (dve, dve.n)

    store_tok, add_done = {}, {}

    def epi_out(do, ps, mm_tok):
        dve_l, dB_l, dA_l = p.sem("dve"), p.sem("dB"), p.sem("dA")
        xr = xs_pp[do % 2]
        @block.sync
        def _(s, do=do, xr=xr):
            if do >= 2:
                s.wait_ge(dve_l.h, add_done[do - 2])
            s.dma_start(out=xr, in_=xT[do * 128:(do + 1) * 128, :]).then_inc(dA_l.h, 16)
        dA_l.inc(16)
        xl = dA_l.n
        @block.vector
        def _(v, do=do, ps=ps, mm_tok=mm_tok, xl=xl, xr=xr):
            v.wait_ge(mm_tok[0].h, mm_tok[1])
            v.wait_ge(dA_l.h, xl)
            v.wait_ge(dC.h, reps_tok[1])
            if do >= 1:
                v.wait_ge(dB_l.h, 16 * store_tok[do - 1])
            v.tensor_mul(ostage, xr, reps[:, 0:TC])
            v.tensor_sub(ostage, ostage, reps[:, TC:2 * TC])
            v.tensor_add(ostage, ostage, ps[:]).then_inc(dve_l.h, 1)
        dve_l.inc(1)
        add_done[do] = dve_l.n
        ps_free = (dve_l, dve_l.n)
        @block.scalar
        def _(s, do=do, psf=ps_free[1]):
            s.wait_ge(dve_l.h, psf)
            s.dma_start(out=o_x[do * 128:(do + 1) * 128, :], in_=ostage).then_inc(dB_l.h, 16)
        dB_l.inc(16)
        store_tok[do] = dB_l.n // 16
        return ps_free

    emit_proj(p, W_outT, [g_t[i] for i in range(n_dt)], n_do, wbufs,
              [psA, psB], epi_out, g_all)

    @block.gpsimd
    def _(g):
        g.wait_ge(dB.h, p.sem("dB").n)



def phase_ln_dump(p, io):
    n_kD = D // 128
    xT = io["xT"]
    o_xn = io["o_xn"]
    o_edge = io["o_edge"]
    fa = Arena(p.sbuf("fa", [128, 19 * 1024], F32))
    ba = Arena(p.sbuf("ba", [128, 17 * 1024], BF16))
    psA = p.psum("psA", [128, TC], F32)
    psB = p.psum("psB", [128, TC], F32)
    xs_pp = [fa.alloc(TC) for _ in range(2)]
    sq_pp = [fa.alloc(TC) for _ in range(2)]
    reps = fa.alloc(2 * TC)
    rows = fa.alloc(3 * TC, rows=1)
    xn_t = [ba.alloc(TC) for _ in range(n_kD)]
    ones128 = p.sbuf("ones128", [128, 128], F32)
    p.begin(consts=(1.0, EPS), zero_psums=(psA, psB))
    block = p.block
    dve, dB = p.sem("dve"), p.sem("dB")
    @block.vector
    def _(v):
        v.wait_ge(p.sem("init").h, p._z1)
        v.memset(ones128[:], 1.0).then_inc(p.sem("init").h, 1)
    p.sem("init").inc(1)
    init_tok = (p.sem("init"), p._z1 + 1)
    R = lambda i: rows[0:1, i * TC:(i + 1) * TC]
    psA_l = [psA[:, i * 512:(i + 1) * 512] for i in range(4)]
    psB_l = [psB[:, i * 512:(i + 1) * 512] for i in range(4)]
    xn_tok = emit_ln_streamed(p, xT, n_kD, TC, xs_pp, sq_pp, psA_l, psB_l,
                              ones128[:], init_tok, xn_t, reps,
                              R(0), R(1), R(2), R(1), p.constf(EPS, 0, 1))
    st = {}
    @block.scalar
    def _(s):
        s.wait_ge(xn_tok[0].h, xn_tok[1])
        for i in range(n_kD):
            s.dma_start(out=o_xn[i * 128:(i + 1) * 128, :], in_=xn_t[i]).then_inc(dB.h, 16)
            s.dma_start(out=o_edge[i * 128:(i + 1) * 128, 0:1],
                        in_=xn_t[i][:, TC - 1:TC]).then_inc(dB.h, 16)
    dB.inc(32 * n_kD)
    @block.gpsimd
    def _(g):
        g.wait_ge(dB.h, dB.n)


def phase_l1f(p, io):
    n_kD = D // 128
    xT = io["xT"]
    i_xnb = io["i_xn_body"]
    edgeAG = io["edgeAG"]
    hmask_in = io["hmask"]
    W_rT = io["W_rT"]
    W_kT = io["W_kT"]
    W_vT = io["W_vT"]
    mu2 = io["mu2"]
    o_x = io["o_x"]

    fa = Arena(p.sbuf("fa", [128, 10 * 1024], F32))
    ba = Arena(p.sbuf("ba", [128, 64 * 1024], BF16))
    psA = p.psum("psA", [128, TC], F32)
    psB = p.psum("psB", [128, TC], F32)

    xs_pp = [fa.alloc(TC) for _ in range(2)]
    ostage = fa.alloc(TC)
    vscr = fa.alloc(TC)
    xr_t = [ba.alloc(TC) for _ in range(n_kD)]
    xk_t = [ba.alloc(TC) for _ in range(n_kD)]
    _xn_mark = ba.mark()
    xn_t = [ba.alloc(1 + TC) for _ in range(n_kD)]
    ba.release(_xn_mark)
    r_t = [ba.alloc(TC) for _ in range(n_kD)]
    k2_t = [ba.alloc(TC) for _ in range(n_kD)]
    assert r_t[0].offset // 1 >= 0
    wbufs = [p.sbuf(f"wb{i}", [128, 128 * n_kD], BF16) for i in range(2)]
    mut = p.sbuf("mut", [128, 4 * n_kD], F32)
    gp_scr_t = p.sbuf("gp_scr", [128, 4], BF16)
    eh1 = p.sbuf("eh1", [128, 4 * n_kD], BF16)
    hm1 = p.sbuf("hm1", [128, 4], F32)

    p.begin(consts=(1.0,), zero_psums=(psA, psB))
    block = p.block
    dA, dB, dC = p.sem("dA"), p.sem("dB"), p.sem("dC")
    dve, act, pe = p.sem("dve"), p.sem("act"), p.sem("pe")
    p._gp_scr = gp_scr_t[:]

    @block.sync
    def _(s):
        for i in range(n_kD):
            s.dma_start(out=xn_t[i][:, 1:1 + TC],
                        in_=i_xnb[i * 128:(i + 1) * 128, :]).then_inc(dA.h, 16)
            for j in range(4):
                s.dma_start(out=eh1[:, i * 4 + j:i * 4 + j + 1],
                            in_=edgeAG[j * D + i * 128:j * D + (i + 1) * 128, 0:1]) \
                    .then_inc(dA.h, 16)
        s.dma_start(out=mut[:, 0:2 * n_kD], in_=mu2[:]).then_inc(dA.h, 16)
        s.dma_start(out=hm1[:], in_=hmask_in[:]).then_inc(dA.h, 16)
    dA.n += 16 * (5 * n_kD + 2)
    loaded = (dA, dA.n)

    @block.vector
    def _(v):
        v.wait_ge(loaded[0].h, loaded[1])
        v.wait_ge(p.sem("init").h, p._z1)
        for i in range(n_kD):
            v.tensor_scalar(vscr[:, 0:1], eh1[:, i * 4:i * 4 + 1],
                            hm1[:, 0:1], None, ALU.mult)
            for j in range(1, 4):
                v.scalar_tensor_tensor(vscr[:, 0:1], eh1[:, i * 4 + j:i * 4 + j + 1],
                                       hm1[:, j:j + 1], vscr[:, 0:1],
                                       ALU.mult, ALU.add)
            v.tensor_copy(xn_t[i][:, 0:1], vscr[:, 0:1])
        v.tensor_scalar(mut[:, 2 * n_kD:4 * n_kD], mut[:, 0:2 * n_kD], -1.0, 1.0,
                        ALU.mult, ALU.add)
        last = None
        for i in range(n_kD):
            cur = xn_t[i][:, 1:1 + TC]
            prv = xn_t[i][:, 0:TC]
            v.tensor_scalar(vscr, cur, mut[:, i:i + 1], None, ALU.mult)
            v.scalar_tensor_tensor(xr_t[i], prv, mut[:, 2 * n_kD + i:2 * n_kD + i + 1],
                                   vscr, ALU.mult, ALU.add)
            v.tensor_scalar(vscr, cur, mut[:, n_kD + i:n_kD + i + 1], None, ALU.mult)
            last = v.scalar_tensor_tensor(xk_t[i], prv,
                                          mut[:, 3 * n_kD + i:3 * n_kD + i + 1],
                                          vscr, ALU.mult, ALU.add)
        last.then_inc(dve.h, 1)
    dve.inc(1)
    mix_tok = (dve, dve.n)

    def epi_r(do, ps, mm_tok):
        act_l = p.sem("act")
        @block.scalar
        def _(s, do=do, ps=ps, mm_tok=mm_tok):
            s.wait_ge(mm_tok[0].h, mm_tok[1])
            s.activation(r_t[do][:], ps[:], AF.Sigmoid).then_inc(act_l.h, 1)
        act_l.inc(1)
        return (act_l, act_l.n)

    cons = emit_proj(p, W_rT, [t[:] for t in xr_t], n_kD, wbufs, [psA, psB],
                     epi_r, mix_tok)
    r_tail = [cons[n_kD - 2], cons[n_kD - 1]]

    def epi_k(do, ps, mm_tok):
        act_l = p.sem("act")
        @block.scalar
        def _(s, do=do, ps=ps, mm_tok=mm_tok):
            s.wait_ge(mm_tok[0].h, mm_tok[1])
            s.activation(vscr, ps[:], AF.Relu).then_inc(act_l.h, 1)
            s.wait_ge(act_l.h, act_l.n + 1)
            s.activation(k2_t[do][:], vscr, AF.Square).then_inc(act_l.h, 1)
        act_l.inc(2)
        return (act_l, act_l.n)

    cons = emit_proj(p, W_kT, [t[:] for t in xk_t], n_kD, wbufs, [psA, psB],
                     epi_k, mix_tok, prev_tail=r_tail)
    k_tail = [cons[n_kD - 2], cons[n_kD - 1]]
    k_all = (p.sem("act"), p.sem("act").n)

    store_tok, add_done = {}, {}

    def epi_v(do, ps, mm_tok):
        dve_l, dB_l, dA_l = p.sem("dve"), p.sem("dB"), p.sem("dA")
        xr = xs_pp[do % 2]
        @block.sync
        def _(s, do=do, xr=xr):
            if do >= 2:
                s.wait_ge(dve_l.h, add_done[do - 2])
            s.dma_start(out=xr, in_=xT[do * 128:(do + 1) * 128, :]).then_inc(dA_l.h, 16)
        dA_l.inc(16)
        xl = dA_l.n
        @block.vector
        def _(v, do=do, ps=ps, mm_tok=mm_tok, xl=xl, xr=xr):
            v.wait_ge(mm_tok[0].h, mm_tok[1])
            v.wait_ge(dA_l.h, xl)
            if do >= 1:
                v.wait_ge(dB_l.h, 16 * store_tok[do - 1])
            v.tensor_mul(ostage, ps[:], r_t[do][:])
            v.tensor_add(ostage, ostage, xr).then_inc(dve_l.h, 1)
        dve_l.inc(1)
        add_done[do] = dve_l.n
        ps_free = (dve_l, dve_l.n)
        @block.scalar
        def _(s, do=do, psf=ps_free[1]):
            s.wait_ge(dve_l.h, psf)
            s.dma_start(out=o_x[do * 128:(do + 1) * 128, :], in_=ostage).then_inc(dB_l.h, 16)
        dB_l.inc(16)
        store_tok[do] = dB_l.n // 16
        return ps_free

    emit_proj(p, W_vT, [t[:] for t in k2_t], n_kD, wbufs, [psA, psB],
              epi_v, k_all, prev_tail=k_tail)

    @block.gpsimd
    def _(g):
        g.wait_ge(dB.h, p.sem("dB").n)




def phase_l1a_xd(p, io):
    n_kD = D // 128
    xT = io["xT"]
    W_down = io["W_down"]
    o_xd = io["o_xd"]
    fa = Arena(p.sbuf("fa", [128, 19 * 1024], F32))
    ba = Arena(p.sbuf("ba", [128, 21 * 1024], BF16))
    psA = p.psum("psA", [128, TC], F32)
    psB = p.psum("psB", [128, TC], F32)
    xs_pp = [fa.alloc(TC) for _ in range(2)]
    sq_pp = [fa.alloc(TC) for _ in range(2)]
    reps = fa.alloc(2 * TC)
    rows = fa.alloc(3 * TC, rows=1)
    xn_t = [ba.alloc(TC) for _ in range(n_kD)]
    stage = [ba.alloc(TC) for _ in range(2)]
    wbufs = [p.sbuf(f"wb{i}", [128, 128 * n_kD], BF16) for i in range(2)]
    ones128 = p.sbuf("ones128", [128, 128], F32)
    gp_scr_t = p.sbuf("gp_scr", [128, 4], BF16)
    p.begin(consts=(1.0, EPS), zero_psums=(psA, psB))
    block = p.block
    dB = p.sem("dB")
    p._gp_scr = gp_scr_t[:]
    @block.vector
    def _(v):
        v.wait_ge(p.sem("init").h, p._z1)
        v.memset(ones128[:], 1.0).then_inc(p.sem("init").h, 1)
    p.sem("init").inc(1)
    init_tok = (p.sem("init"), p._z1 + 1)
    R = lambda i: rows[0:1, i * TC:(i + 1) * TC]
    psA_l = [psA[:, i * 512:(i + 1) * 512] for i in range(4)]
    psB_l = [psB[:, i * 512:(i + 1) * 512] for i in range(4)]
    xn_tok = emit_ln_streamed(p, xT, n_kD, TC, xs_pp, sq_pp, psA_l, psB_l,
                              ones128[:], init_tok, xn_t, reps,
                              R(0), R(1), R(2), R(1), p.constf(EPS, 0, 1))
    store_tok = {}

    def epi(do, ps, mm_tok):
        act_l, dB_l = p.sem("act"), p.sem("dB")
        st = stage[do % 2]
        @block.scalar
        def _(s, do=do, ps=ps, mm_tok=mm_tok, st=st):
            s.wait_ge(mm_tok[0].h, mm_tok[1])
            if do >= 2:
                s.wait_ge(dB_l.h, 16 * store_tok[do - 2])
            s.copy(st, ps[:]).then_inc(act_l.h, 1)
            s.wait_ge(act_l.h, act_l.n + 1)
            s.dma_start(out=o_xd[do * 128:(do + 1) * 128, :], in_=st).then_inc(dB_l.h, 16)
        act_l.inc(1)
        dB_l.inc(16)
        store_tok[do] = dB_l.n // 16
        return (act_l, act_l.n)

    emit_proj(p, W_down, [t[:] for t in xn_t], n_kD, wbufs, [psA, psB], epi, xn_tok)
    @block.gpsimd
    def _(g):
        g.wait_ge(dB.h, p.sem("dB").n)




def phase_l1a_att(p, io):
    HEADS, S = 16, 64
    KC = 2048
    NKC = L // KC
    QG = 1024
    NQG = TC // QG
    SSC = float(S) ** (-0.25)
    SS2 = float(S) ** (-0.5)
    i_q = io["i_q"]
    kvAG = io["kvAG"]
    tkT = io["tkT"]
    tqT = io["tqT"]
    tvT = io["tvT"]
    W_unify = io["W_unify"]
    bu = io["bu"]
    xT = io["xT"]
    o_x = io["o_x"]
    import os as _os
    _dbg = _os.environ.get("ATTDBG", "0") == "1"
    if _dbg:
        o_hs = p.dout("o_hs", (1, 68), F32)
        o_q0 = p.dout("o_q0", (65, 1024), BF16)
        o_k0 = p.dout("o_k0", (65, 2048), BF16)
        o_acc0 = p.dout("o_acc0", (65, 1024), F32)

    n_kD = D // 128
    fa = Arena(p.sbuf("fa", [128, 7 * 1024], F32))
    ba = Arena(p.sbuf("ba", [128, 22 * 1024], BF16))
    psA = p.psum("psA", [128, TC], F32)
    psB = p.psum("psB", [128, TC], F32)

    Kh = [p.sbuf(f"Kh{i}", [65, KC], BF16) for i in range(2)]
    Vt = [p.sbuf(f"Vt{i}", [128, (KC // 128) * 65], BF16) for i in range(2)]
    Qh = [p.sbuf(f"Qh{i}", [65, QG], BF16) for i in range(HEADS)]
    acch = [p.sbuf(f"acc{i}", [65, QG], F32) for i in range(HEADS)]
    P_t = ba.alloc(2048)
    xdc_pp = [ba.alloc(KC) for _ in range(2)]
    qsrc = [ba.alloc(QG) for _ in range(n_kD)]
    ocat = [ba.alloc(QG) for _ in range(n_kD)]
    ostage = fa.alloc(QG)
    orep = fa.alloc(QG)
    xs_pp = [fa.alloc(QG) for _ in range(2)]
    recr = fa.alloc(1024, rows=1)
    tkbuf = p.sbuf("tkbuf", [128, 3 * HEADS * S], BF16)
    hs = p.sbuf("hs", [1, 4 * HEADS + 4], F32)
    wbufs = [p.sbuf(f"wb{i}", [128, 128 * n_kD], BF16) for i in range(2)]
    but = p.sbuf("but", [128, n_kD], F32)
    ones128 = p.sbuf("ones128", [128, 128], F32)
    onesb = p.sbuf("onesb", [128, 128], BF16)
    qsq = p.sbuf("qsq", [64, 1024], BF16)
    sqscr = p.sbuf("sqscr", [64, HEADS * S], F32)
    gp_scr_t = p.sbuf("gp_scr", [128, 4], BF16)

    p.begin(consts=(1.0, EPS, SSC), zero_psums=(psA, psB))
    block = p.block
    dA, dB, dC = p.sem("dA"), p.sem("dB"), p.sem("dC")
    dve, act, pe, pool = p.sem("dve"), p.sem("act"), p.sem("pe"), p.sem("pool")
    p._gp_scr = gp_scr_t[:]

    @block.vector
    def _(v):
        v.wait_ge(p.sem("init").h, p._z1)
        v.memset(ones128[:], 1.0)
        v.memset(onesb[:], 1.0)
        for i in range(2):
            v.memset(Kh[i][64:65, :], -1.0)
            v.memset(Vt[i][:], 0.0)
        last = None
        for i in range(2):
            last = v.memset(
                Vt[i][:].rearrange("p (kt c) -> p kt c", c=65)[:, :, 64:65], 1.0)
        last.then_inc(p.sem("init").h, 1)
    p.sem("init").inc(1)
    init_tok = (p.sem("init"), p._z1 + 1)

    @block.gpsimd
    def _(g):
        for j, w in enumerate((tkT, tqT, tvT)):
            g.dma_start(out=tkbuf[0:64, j * 1024:(j + 1) * 1024], in_=w[:]).then_inc(dA.h, 16)
            g.dma_start(out=tkbuf[64:128, j * 1024:(j + 1) * 1024], in_=w[:]).then_inc(dA.h, 16)
        g.wait_ge(dA.h, dA.n + 96)
        g.tensor_copy(gp_scr_t[:, 0:4], tkbuf[:, 3 * 1024 - 4:3 * 1024]).then_inc(pool.h, 1)
    dA.inc(96)
    pool.inc(1)
    tk_pool_tok = pool.n
    @block.sync
    def _(s):
        s.dma_start(out=but[:], in_=bu[:]).then_inc(dA.h, 16)
    dA.inc(16)
    tk_tok = dA.n

    mm_tok0 = {}
    cp_tok0 = {}
    for q4 in range(4):
        ps = psA if q4 % 2 == 0 else psB
        for i in range(n_kD):
            part = q4 * n_kD + i
            @block.sync
            def _(s, q4=q4, i=i, part=part):
                if part >= 2:
                    s.wait_ge(pe.h, mm_tok0[part - 2])
                s.dma_start(out=xdc_pp[part % 2][:, 0:KC],
                            in_=kvAG[q4 * D + i * 128:q4 * D + (i + 1) * 128,
                                     0:KC]).then_inc(dA.h, 16)
            dA.inc(16)
            ld1 = dA.n
            @block.scalar
            def _(s, part=part, ld1=ld1):
                s.wait_ge(dA.h, ld1)
                if part >= 2:
                    s.wait_ge(pe.h, mm_tok0[part - 2])
                s.activation(xdc_pp[part % 2][:, 0:KC], xdc_pp[part % 2][:, 0:KC],
                             AF.Square).then_inc(act.h, 1)
            act.inc(1)
            sq1 = act.n
            @block.tensor
            def _(t, q4=q4, i=i, part=part, sq1=sq1, ps=ps):
                if part == 0:
                    t.wait_ge(init_tok[0].h, init_tok[1])
                if i == 0 and q4 >= 2:
                    t.wait_ge(dve.h, cp_tok0[q4 - 2])
                t.wait_ge(act.h, sq1)
                last = None
                for tc in range(4):
                    last = t.matmul(ps[:, tc * 512:(tc + 1) * 512],
                                    onesb[:],
                                    xdc_pp[part % 2][:, tc * 512:(tc + 1) * 512],
                                    start=(i == 0), stop=(i == n_kD - 1))
                last.then_inc(pe.h, 1)
            pe.inc(1)
            mm_tok0[part] = pe.n
        done = pe.n
        @block.vector
        def _(v, q4=q4, done=done, ps=ps):
            v.wait_ge(pe.h, done)
            v.tensor_reduce(hs[0:1, 2 * HEADS + q4:2 * HEADS + q4 + 1], ps[0:1, :],
                            mybir.AxisListType.X, ALU.max).then_inc(dve.h, 1)
        dve.inc(1)
        cp_tok0[q4] = dve.n

    @block.vector
    def _(v):
        v.wait_ge(dve.h, cp_tok0[3])
        v.tensor_reduce(hs[0:1, 4 * HEADS:4 * HEADS + 1], hs[0:1, 2 * HEADS:2 * HEADS + 4],
                        mybir.AxisListType.X, ALU.max).then_inc(dve.h, 1)
    dve.inc(1)
    xdmax_tok = dve.n

    @block.scalar
    def _(s):
        s.wait_ge(pool.h, tk_pool_tok)
        s.wait_ge(dA.h, tk_tok)
        s.activation(sqscr[:], tkbuf[0:64, 0:1024], AF.Square).then_inc(act.h, 1)
    act.inc(1)
    sqk = act.n
    @block.tensor
    def _(t):
        t.wait_ge(act.h, sqk)
        t.wait_ge(dve.h, xdmax_tok)
        last = None
        for tc in range(2):
            last = t.matmul(psB[:, 1024 + tc * 512:1024 + (tc + 1) * 512],
                            ones128[0:64, :], sqscr[:, tc * 512:(tc + 1) * 512],
                            start=True, stop=True)
        last.then_inc(pe.h, 1)
    pe.inc(1)
    frok = pe.n
    @block.vector
    def _(v):
        v.wait_ge(pe.h, frok)
        for h in range(HEADS):
            v.tensor_reduce(hs[0:1, h:h + 1], psB[0:1, 1024 + h * 64:1024 + (h + 1) * 64],
                            mybir.AxisListType.X, ALU.add)
        v.tensor_scalar(hs[0:1, HEADS:2 * HEADS], hs[0:1, 0:HEADS],
                        hs[0:1, 4 * HEADS:4 * HEADS + 1], SS2, ALU.mult, ALU.mult) \
            .then_inc(dve.h, 1)
    dve.inc(1)
    mf_tok = dve.n

    ld_gate = {(0, 0, 0): frok}
    psA_half_gate = {}
    P_half_gate = {}
    accslot_gate = {}
    last_exp_d = {"v": None}
    unify_done = None
    unify_tail = None
    for qg in range(NQG):
        q0 = qg * QG
        @block.sync
        def _(s, qg=qg):
            if qg == 1:
                s.wait_ge(unify_done[0].h, unify_done[1])
            for j in range(n_kD):
                s.dma_start(out=qsrc[j], in_=i_q[j * 128:(j + 1) * 128, q0:q0 + QG]) \
                    .then_inc(dA.h, 16)
        dA.inc(16 * n_kD)
        q_ld = dA.n

        qb = {}
        for h in range(HEADS):
            e0 = 64 * (h % 2)
            @block.tensor
            def _(t, h=h, qg=qg, q_ld=q_ld, e0=e0):
                t.wait_ge(dA.h, q_ld)
                t.wait_ge(dve.h, mf_tok)
                if qg == 1:
                    t.wait_ge(unify_done[0].h, unify_done[1])
                if h >= 1:
                    t.wait_ge(act.h, qb[h - 1])
                last = None
                for tc in range(2):
                    last = t.matmul(psB[0:64, 1024 + tc * 512:1024 + tc * 512 + 512],
                                    tkbuf[e0:e0 + 64, 1024 + h * 64:1024 + (h + 1) * 64],
                                    qsrc[h // 2][e0:e0 + 64, tc * 512:(tc + 1) * 512],
                                    start=True, stop=True)
                last.then_inc(pe.h, 1)
            pe.inc(1)
            qmm = pe.n
            @block.scalar
            def _(s, h=h, qmm=qmm):
                s.wait_ge(pe.h, qmm)
                s.activation(Qh[h][0:64, :], psB[0:64, 1024:2048], AF.Copy,
                             scale=p.constf(SSC, 0, 64)).then_inc(act.h, 1)
                s.wait_ge(act.h, act.n + 1)
                s.activation(qsq[0:64, :], Qh[h][0:64, :], AF.Square).then_inc(act.h, 1)
            act.inc(2)
            qevac = act.n
            @block.tensor
            def _(t, h=h, qevac=qevac):
                t.wait_ge(act.h, qevac)
                last = None
                for tc in range(2):
                    last = t.matmul(psB[0:64, 1024 + tc * 512:1024 + tc * 512 + 512],
                                    onesb[0:64, 0:64], qsq[0:64, tc * 512:(tc + 1) * 512],
                                    start=True, stop=True)
                last.then_inc(pe.h, 1)
            pe.inc(1)
            qn2m = pe.n
            @block.scalar
            def _(s, h=h, qn2m=qn2m):
                s.wait_ge(pe.h, qn2m)
                s.activation(Qh[h][64:65, :], psB[0:1, 1024:2048], AF.Sqrt,
                             scale=hs[0:1, HEADS + h:HEADS + h + 1]).then_inc(act.h, 1)
            act.inc(1)
            qb[h] = act.n
        q_all = (act, act.n)

        @block.vector
        def _(v, qg=qg):
            v.wait_ge(q_all[0].h, q_all[1])
            last = None
            for h in range(HEADS):
                last = v.memset(acch[h][:], 0.0)
            last.then_inc(dve.h, 1)
        dve.inc(1)
        acc0 = dve.n

        for kc in range(NKC):
            k0 = kc * KC
            for hp in range(HEADS // 2):
                @block.sync
                def _(s, kc=kc, hp=hp, qg=qg):
                    gate = ld_gate.get((qg, kc, hp))
                    if gate is not None:
                        s.wait_ge(pe.h, gate)
                    s.dma_start(out=xdc_pp[hp % 2][:, 0:KC],
                                in_=kvAG[kc * D + hp * 128:kc * D + (hp + 1) * 128,
                                         0:KC]).then_inc(dA.h, 16)
                dA.inc(16)
                xld = dA.n
                for i01 in range(2):
                    h = 2 * hp + i01
                    e0 = 64 * i01
                    le = last_exp_d["v"]
                    @block.tensor
                    def _(t, h=h, i01=i01, xld=xld, hp=hp, e0=e0, le=le):
                        t.wait_ge(dA.h, xld)
                        t.wait_ge(dve.h, acc0)
                        if le is not None:
                            t.wait_ge(act.h, le)
                        last = None
                        for tc in range(4):
                            last = t.matmul(psA[0:64, tc * 512:(tc + 1) * 512],
                                            tkbuf[e0:e0 + 64, h * 64:(h + 1) * 64],
                                            xdc_pp[hp % 2][e0:e0 + 64,
                                                           tc * 512:(tc + 1) * 512],
                                            start=True, stop=True)
                        last.then_inc(pe.h, 1)
                    pe.inc(1)
                    kmm = pe.n
                    @block.vector
                    def _(v, i01=i01, kmm=kmm):
                        v.wait_ge(pe.h, kmm)
                        v.tensor_scalar(Kh[i01][0:64, :], psA[0:64, :], SSC,
                                        None, ALU.mult).then_inc(dve.h, 1)
                    dve.inc(1)
                    kevac = dve.n
                    @block.tensor
                    def _(t, h=h, i01=i01, hp=hp, kevac=kevac, e0=e0):
                        t.wait_ge(dve.h, kevac)
                        last = None
                        for kt in range(16):
                            last = t.matmul(psA[:, kt * 64:(kt + 1) * 64],
                                            xdc_pp[hp % 2][e0:e0 + 64,
                                                           kt * 128:(kt + 1) * 128],
                                            tkbuf[e0:e0 + 64,
                                                  2048 + h * 64:2048 + (h + 1) * 64],
                                            start=True, stop=True)
                        last.then_inc(pe.h, 1)
                    pe.inc(1)
                    vmm = pe.n
                    @block.vector
                    def _(v, i01=i01, vmm=vmm):
                        v.wait_ge(pe.h, vmm)
                        v.tensor_copy(
                            Vt[i01][:].rearrange("p (kt c) -> p kt c", c=65)[:, :, 0:64],
                            psA[:, 0:1024].rearrange("p (kt c) -> p kt c", c=64)) \
                            .then_inc(dve.h, 1)
                    dve.inc(1)
                    kv_ready = dve.n

                    for qc in range(QG // 512):
                        slot = qc % 2
                        for ktg in range(KC // 256):
                            halfsel = ktg % 2
                            ghalf = psA_half_gate.get(halfsel)
                            @block.tensor
                            def _(t, h=h, i01=i01, qc=qc, ktg=ktg, halfsel=halfsel,
                                  kv_ready=kv_ready, slot=slot, ghalf=ghalf):
                                t.wait_ge(dve.h, kv_ready)
                                if ghalf is not None:
                                    t.wait_ge(act.h, ghalf)
                                if ktg == 0:
                                    ga = accslot_gate.get(slot)
                                    if ga is not None:
                                        t.wait_ge(dve.h, ga)
                                last = None
                                for j in range(2):
                                    kt = ktg * 2 + j
                                    last = t.matmul(
                                        psA[:, halfsel * 1024 + j * 512:
                                            halfsel * 1024 + (j + 1) * 512],
                                        Kh[i01][:, kt * 128:(kt + 1) * 128],
                                        Qh[h][:, qc * 512:(qc + 1) * 512],
                                        start=True, stop=True)
                                last.then_inc(pe.h, 1)
                            pe.inc(1)
                            smm = pe.n
                            gp_ = P_half_gate.get(halfsel)
                            @block.scalar
                            def _(s, halfsel=halfsel, smm=smm, gp_=gp_):
                                s.wait_ge(pe.h, smm)
                                if gp_ is not None:
                                    s.wait_ge(pe.h, gp_)
                                s.activation(P_t[:, halfsel * 1024:(halfsel + 1) * 1024],
                                             psA[:, halfsel * 1024:(halfsel + 1) * 1024],
                                             AF.Exp).then_inc(act.h, 1)
                            act.inc(1)
                            last_exp_d["v"] = act.n
                            psA_half_gate[halfsel] = act.n
                            @block.tensor
                            def _(t, h=h, i01=i01, qc=qc, ktg=ktg, halfsel=halfsel,
                                  slot=slot, myexp=act.n):
                                t.wait_ge(act.h, myexp)
                                last = None
                                for j in range(2):
                                    kt = ktg * 2 + j
                                    last = t.matmul(
                                        psB[0:65, slot * 512:slot * 512 + 512],
                                        Vt[i01][:, kt * 65:(kt + 1) * 65],
                                        P_t[:, halfsel * 1024 + j * 512:
                                            halfsel * 1024 + (j + 1) * 512],
                                        start=(ktg == 0 and j == 0),
                                        stop=(ktg == KC // 256 - 1 and j == 1))
                                last.then_inc(pe.h, 1)
                            pe.inc(1)
                            P_half_gate[halfsel] = pe.n
                        avl = pe.n
                        @block.vector
                        def _(v, h=h, qc=qc, slot=slot, avl=avl):
                            v.wait_ge(pe.h, avl)
                            v.tensor_add(acch[h][:, qc * 512:(qc + 1) * 512],
                                         acch[h][:, qc * 512:(qc + 1) * 512],
                                         psB[0:65, slot * 512:slot * 512 + 512]) \
                                .then_inc(dve.h, 1)
                        dve.inc(1)
                        accslot_gate[slot] = dve.n
                ld_gate[(qg, kc, hp + 1)] = pe.n
                if hp + 1 == HEADS // 2:
                    if kc + 1 < NKC:
                        ld_gate[(qg, kc + 1, 0)] = pe.n
                    elif qg + 1 < NQG:
                        ld_gate[(qg + 1, 0, 0)] = pe.n

        if _dbg and qg == 0:
            accslot_snapshot = accslot_gate.get(1, 0)
            @block.scalar
            def _(s, accslot_snapshot=accslot_snapshot):
                s.wait_ge(dve.h, p.sems["dve"].n)
                s.dma_start(out=o_hs[:], in_=hs[:]).then_inc(dB.h, 16)
                s.dma_start(out=o_q0[:], in_=Qh[0][:]).then_inc(dB.h, 16)
                s.dma_start(out=o_k0[:], in_=Kh[0][:]).then_inc(dB.h, 16)
                s.dma_start(out=o_acc0[:], in_=acch[0][:]).then_inc(dB.h, 16)
            dB.inc(64)

        fin = {}
        for h in range(HEADS):
            @block.vector
            def _(v, h=h):
                v.reciprocal(recr, acch[h][64:65, :]).then_inc(dve.h, 1)
            dve.inc(1)
            rc = dve.n
            @block.gpsimd
            def _(g, h=h, rc=rc):
                g.wait_ge(dve.h, rc)
                if h >= 1:
                    g.wait_ge(dve.h, fin[h - 1])
                rep_row(g, orep[0:64, :], recr, n_rep=64).then_inc(dC.h, 16)
            dC.inc(16)
            rr = dC.n
            @block.vector
            def _(v, h=h, rr=rr):
                v.wait_ge(dC.h, rr)
                v.tensor_mul(ocat[h // 2][64 * (h % 2):64 * (h % 2) + 64, :],
                             acch[h][0:64, :], orep[0:64, :]).then_inc(dve.h, 1)
            dve.inc(1)
            fin[h] = dve.n
        ocat_tok = (p.sem("dve"), p.sem("dve").n)

        store_tok = {}
        add_done = {}

        def epi_u(do, ps, mm_tok, qg=qg, store_tok=store_tok, add_done=add_done):
            dve_l, dB_l, dA_l = p.sem("dve"), p.sem("dB"), p.sem("dA")
            xr = xs_pp[do % 2]
            @block.sync
            def _(s, do=do, xr=xr, qg=qg):
                if do >= 2:
                    s.wait_ge(dve_l.h, add_done[do - 2])
                s.dma_start(out=xr, in_=xT[do * 128:(do + 1) * 128,
                                           qg * QG:(qg + 1) * QG]).then_inc(dA_l.h, 16)
            dA_l.inc(16)
            xl = dA_l.n
            @block.vector
            def _(v, do=do, ps=ps, mm_tok=mm_tok, xl=xl, xr=xr):
                v.wait_ge(mm_tok[0].h, mm_tok[1])
                v.wait_ge(dA_l.h, xl)
                if do >= 1:
                    v.wait_ge(dB_l.h, 16 * store_tok[do - 1])
                v.scalar_tensor_tensor(ostage, ps[:, 0:QG], but[:, do:do + 1], xr,
                                       ALU.add, ALU.add).then_inc(dve_l.h, 1)
            dve_l.inc(1)
            add_done[do] = dve_l.n
            ps_free = (dve_l, dve_l.n)
            @block.scalar
            def _(s, do=do, qg=qg, psf=ps_free[1]):
                s.wait_ge(dve_l.h, psf)
                s.dma_start(out=o_x[do * 128:(do + 1) * 128, qg * QG:(qg + 1) * QG],
                            in_=ostage).then_inc(dB_l.h, 16)
            dB_l.inc(16)
            store_tok[do] = dB_l.n // 16
            return ps_free

        ptail = [ocat_tok] + (unify_tail if unify_tail else [])
        cons = emit_proj(p, W_unify, [t2[:] for t2 in ocat], n_kD, wbufs,
                         [psA, psB], epi_u, ocat_tok, prev_tail=ptail)
        unify_tail = [cons[n_kD - 2], cons[n_kD - 1]]
        unify_done = (p.sem("dve"), p.sem("dve").n)

    @block.gpsimd
    def _(g):
        g.wait_ge(dB.h, p.sem("dB").n)





def phase_l1b_tok(p, io):
    n_kD = D // 128
    S = 128
    xT = io["xT"]
    Wqkvg = io["Wqkvg"]
    bqkvg = io["bqkvg"]
    outs = [io[f"o_{n}"] for n in ("q", "k", "v", "g")]
    fa = Arena(p.sbuf("fa", [128, 41 * 1024], F32))
    ba = Arena(p.sbuf("ba", [128, 2 * 1024], BF16))
    psA = p.psum("psA", [128, TC], F32)
    psB = p.psum("psB", [128, TC], F32)
    xs_pp = [fa.alloc(TC) for _ in range(2)]
    sq_pp = [fa.alloc(TC) for _ in range(2)]
    reps = fa.alloc(2 * TC)
    rows = fa.alloc(3 * TC, rows=1)
    xn_t = [fa.alloc(TC) for _ in range(n_kD)]
    stage = [fa.alloc(TC) for _ in range(2)]
    gstage = sq_pp[1]
    wq = p.sbuf("wq", [128, 4 * n_kD * S], F32)
    bt = p.sbuf("bt", [128, 4], F32)
    ones128 = p.sbuf("ones128", [128, 128], F32)
    gp_scr_t = p.sbuf("gp_scr", [128, 4], BF16)
    p.begin(consts=(1.0, EPS), zero_psums=(psA, psB))
    block = p.block
    dA, dB, dC = p.sem("dA"), p.sem("dB"), p.sem("dC")
    dve, act, pe, pool = p.sem("dve"), p.sem("act"), p.sem("pe"), p.sem("pool")
    p._gp_scr = gp_scr_t[:]
    @block.vector
    def _(v):
        v.wait_ge(p.sem("init").h, p._z1)
        v.memset(ones128[:], 1.0).then_inc(p.sem("init").h, 1)
    p.sem("init").inc(1)
    init_tok = (p.sem("init"), p._z1 + 1)
    @block.sync
    def _(s):
        s.dma_start(out=bt[:], in_=bqkvg[:]).then_inc(dA.h, 16)
    dA.inc(16)
    @block.sync
    def _(s):
        s.dma_start(out=wq[:].rearrange("p (ki c) -> p ki c", c=4 * S),
                    in_=Wqkvg.rearrange("(ki p) c -> p ki c", p=128)).then_inc(dC.h, 16)
    dC.inc(16)
    w_tok_dma = dC.n
    @block.gpsimd
    def _(g):
        g.wait_ge(dC.h, w_tok_dma)
        g.tensor_copy(gp_scr_t[:, 0:2], wq[:, 4 * n_kD * S - 1:4 * n_kD * S].bitcast(BF16)) \
            .then_inc(pool.h, 1)
    pool.inc(1)
    w_tok = pool.n
    R = lambda i: rows[0:1, i * TC:(i + 1) * TC]
    psA_l = [psA[:, i * 512:(i + 1) * 512] for i in range(4)]
    psB_l = [psB[:, i * 512:(i + 1) * 512] for i in range(4)]
    xn_tok = emit_ln_streamed(p, xT, n_kD, TC, xs_pp, sq_pp, psA_l, psB_l,
                              ones128[:], init_tok, xn_t, reps,
                              R(0), R(1), R(2), R(1), p.constf(EPS, 0, 1))
    prev = {}
    for j in range(4):
        ps = psA if j % 2 == 0 else psB
        @block.tensor
        def _(t, j=j, ps=ps):
            t.wait_ge(pool.h, w_tok)
            if j == 0:
                t.wait_ge(xn_tok[0].h, xn_tok[1])
            if j >= 2:
                t.wait_ge(p.sem("dB").h, 16 * prev[j - 2])
            last = None
            for (t0, t1) in tchunks_of(TC):
                for ki in range(n_kD):
                    last = t.matmul(ps[:, t0:t1],
                                    wq[:].rearrange("p (ki c) -> p ki c", c=4 * S)
                                    [:, ki, j * S:(j + 1) * S],
                                    xn_t[ki][:, t0:t1],
                                    start=(ki == 0), stop=(ki == n_kD - 1))
            last.then_inc(pe.h, 1)
        pe.inc(1)
        mm = pe.n
        if j == 3:
            @block.scalar
            def _(s, j=j, ps=ps, mm=mm):
                s.wait_ge(pe.h, mm)
                s.wait_ge(p.sem("dB").h, 16 * prev[j - 2])
                s.activation(gstage, ps[:], AF.Sigmoid, bias=bt[:, j:j + 1]) \
                    .then_inc(act.h, 1)
                s.wait_ge(act.h, act.n + 1)
                s.dma_start(out=outs[j][:], in_=gstage).then_inc(p.sem("dB").h, 16)
            act.inc(1)
            p.sem("dB").inc(16)
            prev[j] = p.sem("dB").n // 16
        else:
            @block.vector
            def _(v, j=j, ps=ps, mm=mm):
                v.wait_ge(pe.h, mm)
                if j >= 2:
                    v.wait_ge(p.sem("dB").h, 16 * prev[j - 2])
                v.tensor_scalar(stage[j % 2], ps[:], bt[:, j:j + 1], None, ALU.add) \
                    .then_inc(dve.h, 1)
            dve.inc(1)
            sd = dve.n
            @block.scalar
            def _(s, j=j, sd=sd):
                s.wait_ge(dve.h, sd)
                s.dma_start(out=outs[j][:], in_=stage[j % 2]).then_inc(p.sem("dB").h, 16)
            p.sem("dB").inc(16)
            prev[j] = p.sem("dB").n // 16
    @block.gpsimd
    def _(g):
        g.wait_ge(p.sem("dB").h, p.sem("dB").n)


def phase_l1b_scan(p, io):
    NV = 16
    TW = 2048
    NTW = 8
    qAG = io["qAG"]
    kAG = io["kAG"]
    gAG = io["gAG"]
    vBC = io["vBC"]
    o_y = io["yIn2"]
    _dbg = os.environ.get("GLADBG", "0") == "1"
    if _dbg:
        o_carr = p.dout("o_carr", (128, NV * NTW), F32)
    fga = Arena(p.sbuf("fga", [128, 32 * 1024], F32))
    fga2 = Arena(p.sbuf("fga2", [128, 14 * 1024], F32))
    psA = p.psum("psA", [128, TW], F32)
    psB = p.psum("psB", [128, TW], F32)
    qc = [fga.alloc(TW) for _ in range(2)]
    kc = [fga.alloc(TW) for _ in range(2)]
    gc = [fga.alloc(TW) for _ in range(2)]
    vrep = [fga2.alloc(TW) for _ in range(2)]
    b_pp = [fga2.alloc(TW) for _ in range(2)]
    h_pp = [fga2.alloc(TW) for _ in range(2)]
    ystage = fga2.alloc(TW, rows=NV)
    sel = p.sbuf("sel", [128, NV * 128], F32)
    carr = p.sbuf("carr", [128, NV], F32)
    gp_scr_t = p.sbuf("gp_scr", [128, 4], BF16)
    p.begin(consts=(1.0,), zero_psums=(psA, psB))
    block = p.block
    dA, dB, dC = p.sem("dA"), p.sem("dB"), p.sem("dC")
    dve, act, pe, pool = p.sem("dve"), p.sem("act"), p.sem("pe"), p.sem("pool")
    p._gp_scr = gp_scr_t[:]
    @block.vector
    def _(v):
        v.wait_ge(p.sem("init").h, p._z1)
        v.memset(sel[:], 0.0)
        last = None
        for j in range(NV):
            last = v.memset(sel[:, j * 128 + j:j * 128 + j + 1], 1.0)
        last.then_inc(p.sem("init").h, 1)
    p.sem("init").inc(1)
    init_tok = (p.sem("init"), p._z1 + 1)
    rep_tok = {}
    ystore = {}
    prev_evac = {}
    yma = {}
    scan_done = {}
    ld = {}
    for tw in range(NTW):
        psY = psA if tw % 2 == 0 else psB
        @block.sync
        def _(s, tw=tw):
            if tw >= 2:
                s.wait_ge(dve.h, scan_done[(tw - 1) * NV - 1])
                s.wait_ge(pool.h, rep_tok[(tw - 1) * NV - 1])
            s.dma_start(out=qc[tw % 2], in_=qAG[tw * 128:(tw + 1) * 128, :]) \
                .then_inc(dA.h, 16)
            s.dma_start(out=kc[tw % 2], in_=kAG[tw * 128:(tw + 1) * 128, :]) \
                .then_inc(dA.h, 16)
            s.dma_start(out=gc[tw % 2], in_=gAG[tw * 128:(tw + 1) * 128, :]) \
                .then_inc(dA.h, 16)
        dA.inc(48)
        ld[tw] = dA.n
        for vv in range(NV):
            u = tw * NV + vv
            @block.gpsimd
            def _(g, tw=tw, vv=vv, u=u):
                if u >= 2:
                    g.wait_ge(dve.h, scan_done[u - 2])
                r0 = (tw * NV + vv) * 128
                g.dma_start(out=vrep[u % 2], in_=vBC[r0:r0 + 128, :]) \
                    .then_inc(dC.h, 16)
                g.wait_ge(dC.h, dC.n + 16)
                g.tensor_copy(gp_scr_t[:, 0:2], vrep[u % 2][:, TW - 2:TW]) \
                    .then_inc(pool.h, 1)
            dC.inc(16)
            pool.inc(1)
            rep_tok[u] = pool.n
            @block.vector
            def _(v, tw=tw, vv=vv, u=u):
                v.wait_ge(pool.h, rep_tok[u])
                v.wait_ge(dA.h, ld[tw])
                if u >= 2:
                    v.wait_ge(pe.h, yma.get(u - 2, 0))
                v.tensor_mul(b_pp[u % 2], kc[tw % 2][:, :], vrep[u % 2])
                init = 0.0 if tw % 4 == 0 else carr[:, vv:vv + 1]
                v.tensor_tensor_scan(h_pp[u % 2], gc[tw % 2][:, :],
                                     b_pp[u % 2], init, ALU.mult, ALU.add) \
                    .then_inc(dve.h, 1)
                v.wait_ge(dve.h, dve.n + 1)
                v.tensor_copy(carr[:, vv:vv + 1], h_pp[u % 2][:, TW - 1:TW])
                v.tensor_mul(b_pp[u % 2], h_pp[u % 2], qc[tw % 2][:, :]) \
                    .then_inc(dve.h, 1)
            dve.inc(2)
            scan_done[u] = dve.n
            @block.tensor
            def _(t, tw=tw, vv=vv, u=u, psY=psY):
                t.wait_ge(dve.h, scan_done[u])
                if vv == 0:
                    pv = prev_evac.get(tw % 2)
                    if pv is not None:
                        t.wait_ge(act.h, pv)
                    t.wait_ge(init_tok[0].h, init_tok[1])
                last = None
                for tc in range(TW // 512):
                    last = t.matmul(psY[:, tc * 512:(tc + 1) * 512],
                                    sel[:, vv * 128:(vv + 1) * 128],
                                    b_pp[u % 2][:, tc * 512:(tc + 1) * 512],
                                    start=(vv == 0), stop=(vv == NV - 1))
                last.then_inc(pe.h, 1)
            pe.inc(1)
            yma[u] = pe.n
        if _dbg:
            lastu = tw * NV + NV - 1
            @block.scalar
            def _(s, tw=tw, lastu=lastu):
                s.wait_ge(dve.h, scan_done[lastu])
                s.dma_start(out=o_carr[:, tw * NV:(tw + 1) * NV], in_=carr[:]) \
                    .then_inc(dB.h, 16)
            dB.inc(16)
        klast = tw * NV + NV - 1
        act.inc(1)
        ev = act.n
        dB.inc(16)
        @block.scalar
        def _(s, tw=tw, klast=klast, ev=ev, psY=psY):
            s.wait_ge(pe.h, yma[klast])
            if tw >= 1:
                s.wait_ge(dB.h, 16 * ystore[tw - 1])
            s.copy(ystage[0:NV, :], psY[0:NV, :]).then_inc(act.h, 1)
            s.wait_ge(act.h, ev)
            s.dma_start(out=o_y[tw * NV:(tw + 1) * NV, 0:TW], in_=ystage[0:NV, :]) \
                .then_inc(dB.h, 16)
        ystore[tw] = dB.n // 16
        prev_evac[tw % 2] = ev
    @block.gpsimd
    def _(g):
        g.wait_ge(dB.h, dB.n)




def phase_l1b_out(p, io):
    n_kD = D // 128
    xT = io["xT"]
    i_y = io["i_y"]
    Wo = io["Wo"]
    bgb = io["bgb"]
    o_x = io["o_x"]
    fa = Arena(p.sbuf("fa", [128, 39 * 1024], F32))
    ba = Arena(p.sbuf("ba", [128, 1 * 1024], BF16))
    psA = p.psum("psA", [128, TC], F32)
    psB = p.psum("psB", [128, TC], F32)
    xs_pp = [fa.alloc(TC) for _ in range(2)]
    sq_pp = [fa.alloc(TC) for _ in range(2)]
    reps = fa.alloc(2 * TC)
    rows = fa.alloc(4 * TC, rows=1)
    y2_t = [fa.alloc(TC) for _ in range(n_kD)]
    ostage = sq_pp[1]
    y_sb = fa.alloc(TC)
    wo_sb = p.sbuf("wo_sb", [128, D], F32)
    bg = p.sbuf("bg", [128, 3 * n_kD], F32)
    ones128 = p.sbuf("ones128", [128, 128], F32)
    gp_scr_t = p.sbuf("gp_scr", [128, 4], BF16)
    p.begin(consts=(1.0, EPS), zero_psums=(psA, psB))
    block = p.block
    dA, dB, dC = p.sem("dA"), p.sem("dB"), p.sem("dC")
    dve, act, pe, pool = p.sem("dve"), p.sem("act"), p.sem("pe"), p.sem("pool")
    p._gp_scr = gp_scr_t[:]
    @block.vector
    def _(v):
        v.wait_ge(p.sem("init").h, p._z1)
        v.memset(ones128[:], 1.0).then_inc(p.sem("init").h, 1)
    p.sem("init").inc(1)
    init_tok = (p.sem("init"), p._z1 + 1)
    @block.sync
    def _(s):
        s.dma_start(out=bg[:], in_=bgb[:]).then_inc(dA.h, 16)
        s.dma_start(out=y_sb, in_=i_y[:]).then_inc(dA.h, 16)
    dA.inc(32)
    y_tok = dA.n
    @block.gpsimd
    def _(g):
        g.dma_start(out=wo_sb[:], in_=Wo[:]).then_inc(dC.h, 16)
        g.wait_ge(dC.h, dC.n + 16)
        g.tensor_copy(gp_scr_t[0:128, 0:1].bitcast(F32) if False else gp_scr_t[:, 0:2],
                      wo_sb[:, D - 1:D].bitcast(BF16) if False else wo_sb[:, D - 1:D]
                      .bitcast(BF16)).then_inc(pool.h, 1)
    dC.inc(16)
    pool.inc(1)
    wo_tok = pool.n

    R = lambda i: rows[0:1, i * TC:(i + 1) * TC]
    mu, var, r0 = R(0), R(1), R(2)
    m0 = R(1)
    psA_l = [psA[:, i * 512:(i + 1) * 512] for i in range(4)]
    psB_l = [psB[:, i * 512:(i + 1) * 512] for i in range(4)]
    tch = tchunks_of(TC)
    mm_done, ld, sqd = {}, {}, {}
    for i in range(n_kD):
        @block.sync
        def _(s, i=i):
            if i >= 2:
                s.wait_ge(pe.h, mm_done[i - 2])
            s.dma_start(out=xs_pp[i % 2], in_=xT[i * 128:(i + 1) * 128, :]).then_inc(dA.h, 16)
        dA.inc(16)
        ld[i] = dA.n
        @block.scalar
        def _(s, i=i):
            s.wait_ge(dA.h, ld[i])
            if i >= 2:
                s.wait_ge(pe.h, mm_done[i - 2])
            s.activation(sq_pp[i % 2], xs_pp[i % 2], AF.Square).then_inc(act.h, 1)
        act.inc(1)
        sqd[i] = act.n
        @block.tensor
        def _(t, i=i):
            if i == 0:
                t.wait_ge(init_tok[0].h, init_tok[1])
            t.wait_ge(act.h, sqd[i])
            last = None
            for ci, (t0, t1) in enumerate(tch):
                t.matmul(psA_l[ci], ones128[:], xs_pp[i % 2][:, t0:t1],
                         start=(i == 0), stop=(i == n_kD - 1))
                last = t.matmul(psB_l[ci], ones128[:], sq_pp[i % 2][:, t0:t1],
                                start=(i == 0), stop=(i == n_kD - 1))
            last.then_inc(pe.h, 1)
        pe.inc(1)
        mm_done[i] = pe.n
    stats_done = pe.n
    @block.vector
    def _(v):
        v.wait_ge(pe.h, stats_done)
        v.tensor_scalar(mu, psA[0:1, :], 1.0 / D, None, ALU.mult)
        v.tensor_scalar(var, psB[0:1, :], 1.0 / D, None, ALU.mult)
        v.tensor_mul(r0, mu, mu)
        v.tensor_sub(var, var, r0).then_inc(dve.h, 1)
    dve.inc(1)
    t0tok = emit_rsqrt_row(p, r0, var, p.constf(EPS, 0, 1), p.sem("dve").tok)
    @block.vector
    def _(v):
        v.wait_ge(t0tok[0].h, t0tok[1])
        v.tensor_mul(m0, mu, r0).then_inc(dve.h, 1)
    dve.inc(1)
    rows_tok = dve.tok
    @block.gpsimd
    def _(g):
        g.wait_ge(rows_tok[0].h, rows_tok[1])
        rep_row(g, reps[:, 0:TC], r0).then_inc(dC.h, 16)
        rep_row(g, reps[:, TC:2 * TC], m0).then_inc(dC.h, 16)
    dC.inc(32)
    reps_tok = dC.n

    y2d = {}
    for do in range(n_kD):
        ps = psA if do % 2 == 0 else psB
        @block.tensor
        def _(t, do=do, ps=ps):
            t.wait_ge(pool.h, wo_tok)
            t.wait_ge(dA.h, y_tok)
            if do < 2:
                t.wait_ge(pe.h, stats_done)
            if do < 2:
                t.wait_ge(dve.h, rows_tok[1])
            if do >= 2:
                t.wait_ge(dve.h, y2d[do - 2])
            last = None
            for (t0, t1) in tchunks_of(TC):
                last = t.matmul(ps[:, t0:t1], wo_sb[:, do * 128:(do + 1) * 128],
                                y_sb[:, t0:t1], start=True, stop=True)
            last.then_inc(pe.h, 1)
        pe.inc(1)
        mm = pe.n
        @block.sync
        def _(s, do=do, mm=mm):
            if do >= 2:
                s.wait_ge(dve.h, y2d[do - 2])
            else:
                s.wait_ge(pe.h, stats_done)
            s.dma_start(out=xs_pp[do % 2], in_=xT[do * 128:(do + 1) * 128, :]) \
                .then_inc(dA.h, 16)
        dA.inc(16)
        xl = dA.n
        @block.vector
        def _(v, do=do, ps=ps, mm=mm, xl=xl):
            v.wait_ge(pe.h, mm)
            v.wait_ge(dA.h, xl)
            v.wait_ge(dC.h, reps_tok)
            v.tensor_mul(y2_t[do], xs_pp[do % 2], reps[:, 0:TC])
            v.tensor_sub(y2_t[do], y2_t[do], reps[:, TC:2 * TC])
            v.scalar_tensor_tensor(y2_t[do], ps[:], bg[:, do:do + 1], y2_t[do],
                                   ALU.add, ALU.add).then_inc(dve.h, 1)
        dve.inc(1)
        y2d[do] = dve.n
    y2_all = dve.n

    mu2, var2, r2 = R(3), R(0), R(3)
    mu2, var2 = R(0), R(2)
    r2, m2 = R(3), R(2)
    sq2 = {}
    mmd2 = {}
    for i in range(n_kD):
        @block.scalar
        def _(s, i=i):
            s.wait_ge(dve.h, y2_all)
            if i >= 2:
                s.wait_ge(pe.h, mmd2.get(i - 2, 0))
            s.activation(sq_pp[i % 2], y2_t[i], AF.Square).then_inc(act.h, 1)
        act.inc(1)
        sq2[i] = act.n
        @block.tensor
        def _(t, i=i):
            t.wait_ge(act.h, sq2[i])
            if i == 0:
                t.wait_ge(dve.h, y2_all)
            last = None
            for ci, (t0, t1) in enumerate(tch):
                t.matmul(psA_l[ci], ones128[:], y2_t[i][:, t0:t1],
                         start=(i == 0), stop=(i == n_kD - 1))
                last = t.matmul(psB_l[ci], ones128[:], sq_pp[i % 2][:, t0:t1],
                                start=(i == 0), stop=(i == n_kD - 1))
            last.then_inc(pe.h, 1)
        pe.inc(1)
        mmd2[i] = pe.n
    st2 = pe.n
    @block.vector
    def _(v):
        v.wait_ge(pe.h, st2)
        v.tensor_scalar(mu2, psA[0:1, :], 1.0 / D, None, ALU.mult)
        v.tensor_scalar(var2, psB[0:1, :], 1.0 / D, None, ALU.mult)
        v.tensor_mul(r2, mu2, mu2)
        v.tensor_sub(var2, var2, r2).then_inc(dve.h, 1)
    dve.inc(1)
    t2tok = emit_rsqrt_row(p, r2, var2, p.constf(EPS, 0, 1), p.sem("dve").tok)
    @block.vector
    def _(v):
        v.wait_ge(t2tok[0].h, t2tok[1])
        v.tensor_mul(m2, mu2, r2).then_inc(dve.h, 1)
    dve.inc(1)
    rt2 = dve.tok
    @block.gpsimd
    def _(g):
        g.wait_ge(rt2[0].h, rt2[1])
        rep_row(g, reps[:, 0:TC], r2).then_inc(dC.h, 16)
        rep_row(g, reps[:, TC:2 * TC], m2).then_inc(dC.h, 16)
    dC.inc(32)
    reps2_tok = dC.n

    store_tok = {}
    app = {}
    for do in range(n_kD):
        @block.sync
        def _(s, do=do):
            if do >= 2:
                s.wait_ge(dve.h, app.get(do - 2, 0))
            else:
                s.wait_ge(dve.h, y2_all)
            s.dma_start(out=xs_pp[do % 2], in_=xT[do * 128:(do + 1) * 128, :]) \
                .then_inc(dA.h, 16)
        dA.inc(16)
        xl = dA.n
        @block.vector
        def _(v, do=do, xl=xl):
            v.wait_ge(dA.h, xl)
            v.wait_ge(dC.h, reps2_tok)
            if do >= 1:
                v.wait_ge(dB.h, 16 * store_tok[do - 1])
            v.tensor_mul(ostage, y2_t[do], reps[:, 0:TC])
            v.tensor_sub(ostage, ostage, reps[:, TC:2 * TC])
            v.tensor_scalar(ostage, ostage, bg[:, n_kD + do:n_kD + do + 1],
                            bg[:, 2 * n_kD + do:2 * n_kD + do + 1], ALU.mult, ALU.add)
            v.tensor_add(ostage, ostage, xs_pp[do % 2]).then_inc(dve.h, 1)
        dve.inc(1)
        app[do] = dve.n
        @block.scalar
        def _(s, do=do, ad=app[do]):
            s.wait_ge(dve.h, ad)
            s.dma_start(out=o_x[do * 128:(do + 1) * 128, :], in_=ostage).then_inc(dB.h, 16)
        dB.inc(16)
        store_tok[do] = dB.n // 16
    @block.gpsimd
    def _(g):
        g.wait_ge(dB.h, dB.n)



def col128(*vecs):
    cols = [np.ascontiguousarray(v.reshape(-1, 128).T) for v in vecs]
    return np.ascontiguousarray(np.concatenate(cols, axis=1))


def shard_tokens_T(x):
    out = []
    for c in range(NCORE):
        b = c // GROUP
        t0 = (c % GROUP) * TC
        out.append(np.ascontiguousarray(x[b, t0:t0 + TC, :].T))
    return out


def _conv128(w):
    w = np.ascontiguousarray(w.reshape(DI, 4))
    return np.ascontiguousarray(w.reshape(16, 128, 4).transpose(1, 0, 2).reshape(128, 64))


def _tt_heads(w):
    return np.ascontiguousarray(w.transpose(2, 0, 1).reshape(64, 1024))


def _cat(*a, axis=1):
    return np.ascontiguousarray(np.concatenate(a, axis=axis))


def mk_mask(shift):
    out = []
    for c in range(NCORE):
        j = c % GROUP + shift
        m = np.zeros((128, 4), np.float32)
        if 0 <= j < GROUP:
            m[:, j] = 1.0
        out.append(m)
    return out


def lA_prep(key):
    def fn(ins):
        logA = ins[key]
        out = []
        for c in range(NCORE):
            ch0 = c * 256
            out.append(np.ascontiguousarray(
                logA[ch0:ch0 + 256].reshape(2, 128, NS).transpose(1, 0, 2)
                .reshape(128, 32)))
        return out
    return fn


PCORES = list(range(NCORE))
_SSC = float(64) ** (-0.25)
_CONSTS = (1.0, EPS, _SSC)


def percore(key):
    return lambda env, _k=key: env[_k]


def ag4(arrs):
    return [np.ascontiguousarray(
        np.concatenate([arrs[(c // GROUP) * GROUP + j] for j in range(GROUP)],
                       axis=0)) for c in range(NCORE)]


def ag8(arrs):
    cat = np.ascontiguousarray(np.concatenate(arrs, axis=0))
    return [cat] * NCORE


def a2a8(arrs, rows):
    return [np.ascontiguousarray(
        np.concatenate([a[k * rows:(k + 1) * rows] for a in arrs], axis=0))
        for k in range(NCORE)]


def _new_prog(name):
    p = Prog(name)
    p.begin_program(consts=_CONSTS)
    return p


def _fin(p):
    p.barrier()
    p.finish()
    return p


def _ph(p, tag, fn, io, **kw):
    p.scope_open(tag)
    fn(p, io, **kw)
    p.scope_close()


def build_LA1():
    p = _new_prog("LA1")
    E = p.ext
    _ph(p, "a", phase_tokA, {
        "xT": E("xT", (D, TC), F32,
                lambda env: shard_tokens_T(env["L0_attn_a_x"])),
        "ln_gb": E("ln_gb", (128, 16), F32,
                   lambda env: [col128(env["L0_attn_a_ln_g"],
                                       env["L0_attn_a_ln_b"])] * NCORE),
        "W_in": E("W_in", (D, 2 * DI), BF16,
                  lambda env: [env["L0_attn_a_W_in"].astype(bf)] * NCORE),
        "o_xbr": p.dout("o_xbr", (DI, TC), BF16),
        "o_sz": p.dout("o_sz", (DI, TC), BF16),
        "o_edge": p.dout("o_edge", (DI, 6), BF16)})
    return _fin(p)


def _tokB_a_io(p):
    E = p.ext
    return {
        "i_xbr_body": E("i_body", (DI, TC), BF16, percore("xbrA")),
        "edgeAG": E("edgeAG", (4 * DI, 6), BF16, percore("edgeA_AG")),
        "hmask": E("hmask", (128, 4), F32, lambda env: mk_mask(-1)),
        "convw": E("convw", (128, 64), F32,
                   lambda env: [_conv128(env["L0_attn_a_conv_w"])] * NCORE),
        "convb_bdel": E("cbb", (128, 32), F32,
                        lambda env: [col128(env["L0_attn_a_conv_b"],
                                            env["L0_attn_a_b_delta"])] * NCORE),
        "W_delta": E("W_delta", (DI, DI), BF16,
                     lambda env: [env["L0_attn_a_W_delta"].astype(bf)] * NCORE),
        "W_bc": E("W_bc", (DI, 2 * NS), BF16,
                  lambda env: [_cat(env["L0_attn_a_W_b"],
                                    env["L0_attn_a_W_c"]).astype(bf)] * NCORE),
        "o_xb": p.dout("o_xb", (DI, TC), BF16),
        "o_delta": p.dout("o_delta", (DI, TC), BF16),
        "o_bc": p.dout("o_bc", (2 * NS, TC), BF16)}


def build_LA2():
    p = _new_prog("LA2")
    _ph(p, "a", phase_tokB_a, _tokB_a_io(p))
    return _fin(p)


def _scan_io(p, sfx, dkey, xkey, bckey, lakey):
    E = p.ext
    return {
        "dOut": E("dOut" + sfx, (DI, TC), BF16, percore(dkey)),
        "xbOut": E("xbOut" + sfx, (DI, TC), BF16, percore(xkey)),
        "bcAG": E("bcAG" + sfx, (16 * NS, TC), BF16, percore(bckey)),
        "i_logA": E("logA" + sfx, (128, 32), F32, lA_prep(lakey)),
        "ident": E("ident", (128, 128), BF16,
                   lambda env: [np.eye(128, dtype=bf)] * NCORE),
        "yIn": p.dout("o_y" + sfx, (DI, TC), BF16)}


def build_LA3():
    p = _new_prog("LA3")
    _ph(p, "s", phase_scan,
        _scan_io(p, "", "delA_o", "xbA_o", "bcA_AG", "L0_attn_a_log_A"),
        reverse=False)
    return _fin(p)


def build_LA4():
    p = _new_prog("LA4")
    E = p.ext
    x1 = p.dram("x1", (D, TC), F32)
    x2 = p.dout("o_x2", (D, TC), F32)
    _ph(p, "o", phase_l0a_out, {
        "xT": E("xT", (D, TC), F32,
                lambda env: shard_tokens_T(env["L0_attn_a_x"])),
        "i_y": E("i_y", (DI, TC), BF16, percore("yA_out")),
        "i_xb": E("i_xb", (DI, TC), BF16, percore("xbA")),
        "i_sz": E("i_sz", (DI, TC), BF16, percore("szA")),
        "dp128": E("dp128", (128, 16), F32,
                   lambda env: [col128(env["L0_attn_a_Dp"])] * NCORE),
        "W_out": E("W_out", (DI, D), BF16,
                   lambda env: [env["L0_attn_a_W_out"].astype(bf)] * NCORE),
        "o_x": x1})
    _ph(p, "f", phase_ffn, _ffn_io(p, x1, x2))
    _ph(p, "t", phase_tokA, {
        "xT": x2,
        "ln_gb": E("ln_gb", (128, 16), F32,
                   lambda env: [col128(env["L0_attn_b_norm_w"],
                                       env["L0_attn_b_norm_b"])] * NCORE),
        "W_in": E("W_in", (D, 2 * DI), BF16,
                  lambda env: [np.ascontiguousarray(
                      env["L0_attn_b_in_proj_w"].T).astype(bf)] * NCORE),
        "o_xbr": p.dout("o_xbr", (DI, TC), BF16),
        "o_sz": p.dout("o_sz", (DI, TC), BF16),
        "o_edge": p.dout("o_edge", (DI, 6), BF16)})
    return _fin(p)


def _ffn_io(p, xin, xout):
    E = p.ext
    return {
        "xT": xin,
        "w12": E("ffn_w12", (D, 8192), BF16,
                 lambda env: [_cat(env["L0_ffn_w1_w"].T,
                                   env["L0_ffn_w2_w"].T).astype(bf)] * NCORE),
        "s12": E("ffn_s12", (128, 64), F32,
                 lambda env: [col128(env["L0_ffn_w1_s"],
                                     env["L0_ffn_w2_s"])] * NCORE),
        "h1T": E("ffn_h1T", (4096, 4096), BF16,
                 lambda env: [np.ascontiguousarray(
                     env["L0_ffn_h1_w"].T).astype(bf)] * NCORE),
        "h1s": E("ffn_h1s", (128, 32), F32,
                 lambda env: [col128(env["L0_ffn_h1_s"])] * NCORE),
        "h2T": E("ffn_h2T", (4096, 4096), BF16,
                 lambda env: [np.ascontiguousarray(
                     env["L0_ffn_h2_w"].T).astype(bf)] * NCORE),
        "h2s": E("ffn_h2s", (128, 32), F32,
                 lambda env: [col128(env["L0_ffn_h2_s"])] * NCORE),
        "w3T": E("ffn_w3T", (4096, D), BF16,
                 lambda env: [np.ascontiguousarray(
                     env["L0_ffn_w3_w"].T).astype(bf)] * NCORE),
        "w3s": E("ffn_w3s", (128, 8), F32,
                 lambda env: [col128(env["L0_ffn_w3_s"])] * NCORE),
        "o_x": xout}


def build_LA5():
    p = _new_prog("LA5")
    E = p.ext
    io = {
        "i_xbr_body": E("i_body", (DI, TC), BF16, percore("xbrB")),
        "edgeAG": E("edgeAG", (4 * DI, 6), BF16, percore("edgeB_AG")),
        "hmaskL": E("hmaskL", (128, 4), F32, lambda env: mk_mask(-1)),
        "hmaskR": E("hmaskR", (128, 4), F32, lambda env: mk_mask(+1)),
        "convw_f": E("cwf", (128, 64), F32,
                     lambda env: [_conv128(env["L0_attn_b_conv_w_f"])] * NCORE),
        "convw_r": E("cwr", (128, 64), F32,
                     lambda env: [_conv128(
                         env["L0_attn_b_conv_w_r"][:, :, ::-1])] * NCORE),
        "cb4": E("cb4", (128, 64), F32,
                 lambda env: [col128(env["L0_attn_b_conv_b_f"],
                                     env["L0_attn_b_conv_b_r"],
                                     env["L0_attn_b_dt_proj_b_f"],
                                     env["L0_attn_b_dt_proj_b_r"])] * NCORE),
        "xprojT_f": E("xpf", (DI, 96), BF16,
                      lambda env: [np.ascontiguousarray(
                          env["L0_attn_b_x_proj_w_f"].T).astype(bf)] * NCORE),
        "xprojT_r": E("xpr", (DI, 96), BF16,
                      lambda env: [np.ascontiguousarray(
                          env["L0_attn_b_x_proj_w_r"].T).astype(bf)] * NCORE),
        "dtwT_f": E("dtf", (64, DI), BF16,
                    lambda env: [np.ascontiguousarray(
                        env["L0_attn_b_dt_proj_w_f"].T).astype(bf)] * NCORE),
        "dtwT_r": E("dtr", (64, DI), BF16,
                    lambda env: [np.ascontiguousarray(
                        env["L0_attn_b_dt_proj_w_r"].T).astype(bf)] * NCORE)}
    for br in ("f", "r"):
        io[f"o_xb_{br}"] = p.dout(f"o_xb_{br}", (DI, TC), BF16)
        io[f"o_delta_{br}"] = p.dout(f"o_delta_{br}", (DI, TC), BF16)
        io[f"o_bc_{br}"] = p.dout(f"o_bc_{br}", (2 * NS, TC), BF16)
    _ph(p, "b", phase_tokB_b, io)
    return _fin(p)


def build_LA6():
    p = _new_prog("LA6")
    _ph(p, "f", phase_scan,
        _scan_io(p, "_f", "delF_o", "xbF_o", "bcF_AG", "L0_attn_b_A_log_f"),
        reverse=False)
    _ph(p, "r", phase_scan,
        _scan_io(p, "_r", "delR_o", "xbR_o", "bcR_AG", "L0_attn_b_A_log_r"),
        reverse=True)
    return _fin(p)


def build_LA7():
    p = _new_prog("LA7")
    E = p.ext
    x3 = p.dram("x3", (D, TC), F32)
    x4 = p.dout("o_x4", (D, TC), F32)
    _ph(p, "o", phase_l0b_out, {
        "xT": E("xT", (D, TC), F32, percore("x2")),
        "i_y_f": E("i_y_f", (DI, TC), BF16, percore("yF_out")),
        "i_y_r": E("i_y_r", (DI, TC), BF16, percore("yR_out")),
        "i_xb_f": E("i_xb_f", (DI, TC), BF16, percore("xbF")),
        "i_xb_r": E("i_xb_r", (DI, TC), BF16, percore("xbR")),
        "i_sz": E("i_sz", (DI, TC), BF16, percore("szB")),
        "d2": E("d2", (128, 32), F32,
                lambda env: [col128(env["L0_attn_b_D_f"],
                                    env["L0_attn_b_D_r"])] * NCORE),
        "W_outT": E("W_outT", (DI, D), BF16,
                    lambda env: [np.ascontiguousarray(
                        env["L0_attn_b_out_proj_w"].T).astype(bf)] * NCORE),
        "o_x": x3})
    _ph(p, "f", phase_ffn, _ffn_io(p, x3, x4))
    _ph(p, "d", phase_l1a_xd, {
        "xT": x4,
        "W_down": E("W_down", (D, D), BF16,
                    lambda env: [env["L1_attn_a_W_down"].astype(bf)] * NCORE),
        "o_xd": p.dout("o_xd", (D, TC), BF16)})
    return _fin(p)


def build_LA8():
    p = _new_prog("LA8")
    E = p.ext
    x4 = E("x4", (D, TC), F32, percore("x4"))
    x5 = p.dout("o_x5", (D, TC), F32)
    _ph(p, "a", phase_l1a_att, {
        "i_q": E("i_q", (D, TC), BF16, percore("xd")),
        "kvAG": E("kvAG", (4 * D, TC), BF16, percore("kvAG")),
        "tkT": E("tkT", (64, 1024), BF16,
                 lambda env: [_tt_heads(env["L1_attn_a_tokeys"]).astype(bf)] * NCORE),
        "tqT": E("tqT", (64, 1024), BF16,
                 lambda env: [_tt_heads(env["L1_attn_a_toqueries"]).astype(bf)] * NCORE),
        "tvT": E("tvT", (64, 1024), BF16,
                 lambda env: [_tt_heads(env["L1_attn_a_tovalues"]).astype(bf)] * NCORE),
        "W_unify": E("W_unify", (D, D), BF16,
                     lambda env: [env["L1_attn_a_W_unify"].astype(bf)] * NCORE),
        "bu": E("bu", (128, 8), F32,
                lambda env: [col128(env["L1_attn_a_b_unify"])] * NCORE),
        "xT": x4, "o_x": x5})
    _ph(p, "n", phase_ln_dump, {
        "xT": x5,
        "o_xn": p.dout("o_xn", (D, TC), BF16),
        "o_edge": p.dout("o_edge", (D, 1), BF16)})
    return _fin(p)


def _l1f_io(p, xin, xnkey, edgekey, xout):
    E = p.ext
    return {
        "xT": xin,
        "i_xn_body": E("i_xnb", (D, TC), BF16, percore(xnkey)),
        "edgeAG": E("edgeAG1", (4 * D, 1), BF16, percore(edgekey)),
        "hmask": E("hmask", (128, 4), F32, lambda env: mk_mask(-1)),
        "W_rT": E("l1f_WrT", (D, D), BF16,
                  lambda env: [np.ascontiguousarray(
                      env["L1_ffn_W_r"].T).astype(bf)] * NCORE),
        "W_kT": E("l1f_WkT", (D, D), BF16,
                  lambda env: [np.ascontiguousarray(
                      env["L1_ffn_W_k"].T).astype(bf)] * NCORE),
        "W_vT": E("l1f_WvT", (D, D), BF16,
                  lambda env: [np.ascontiguousarray(
                      env["L1_ffn_W_v"].T).astype(bf)] * NCORE),
        "mu2": E("l1f_mu2", (128, 16), F32,
                 lambda env: [col128(env["L1_ffn_mu_r"],
                                     env["L1_ffn_mu_k"])] * NCORE),
        "o_x": xout}


def build_LA9():
    p = _new_prog("LA9")
    E = p.ext
    x5 = E("x5", (D, TC), F32, percore("x5"))
    x6 = p.dout("o_x6", (D, TC), F32)
    _ph(p, "f", phase_l1f, _l1f_io(p, x5, "xn1", "edge1_AG", x6))
    io = {
        "xT": x6,
        "Wqkvg": E("W4", (D, 512), F32,
                   lambda env: [_cat(env["L1_attn_b_Wq"], env["L1_attn_b_Wk"],
                                     env["L1_attn_b_Wv"],
                                     env["L1_attn_b_Wg"])] * NCORE),
        "bqkvg": E("b4", (128, 4), F32,
                   lambda env: [col128(env["L1_attn_b_bq"], env["L1_attn_b_bk"],
                                       env["L1_attn_b_bv"],
                                       env["L1_attn_b_bg"])] * NCORE)}
    for n in ("q", "k", "v", "g"):
        io[f"o_{n}"] = p.dout(f"o_{n}", (128, TC), F32)
    _ph(p, "t", phase_l1b_tok, io)
    return _fin(p)


def build_LA10():
    p = _new_prog("LA10")
    E = p.ext
    _ph(p, "s", phase_l1b_scan, {
        "qAG": E("qAG", (1024, TC), F32, percore("qAG")),
        "kAG": E("kAG", (1024, TC), F32, percore("kAG")),
        "gAG": E("gAG", (1024, TC), F32, percore("gAG")),
        "vBC": E("vBC", (128 * 128, TC), F32, percore("vBC")),
        "yIn2": p.dout("o_y2", (128, TC), F32)})
    return _fin(p)


def build_LA11():
    p = _new_prog("LA11")
    E = p.ext
    x6 = E("x6", (D, TC), F32, percore("x6"))
    x7 = p.dout("o_x7", (D, TC), F32)
    _ph(p, "o", phase_l1b_out, {
        "xT": x6,
        "i_y": E("i_y", (128, TC), F32, percore("yOut2")),
        "Wo": E("Wo", (128, D), F32,
                lambda env: [env["L1_attn_b_Wo"]] * NCORE),
        "bgb": E("bgb", (128, 24), F32,
                 lambda env: [col128(env["L1_attn_b_bo"],
                                     env["L1_attn_b_gamma"],
                                     env["L1_attn_b_beta"])] * NCORE),
        "o_x": x7})
    _ph(p, "n", phase_ln_dump, {
        "xT": x7,
        "o_xn": p.dout("o_xn", (D, TC), BF16),
        "o_edge": p.dout("o_edge", (D, 1), BF16)})
    return _fin(p)


def build_LA12():
    p = _new_prog("LA12")
    E = p.ext
    x7 = E("x7", (D, TC), F32, percore("x7"))
    _ph(p, "f", phase_l1f,
        _l1f_io(p, x7, "xn2", "edge2_AG", p.dout("o_x", (D, TC), F32)))
    return _fin(p)


_LP = {}


def _get(name, builder):
    if name not in _LP:
        _LP[name] = builder()
    return _LP[name]


def _launch(p, env):
    maps = [dict() for _ in range(NCORE)]
    for name, fn in p.prep.items():
        arrs = fn(env)
        for c in range(NCORE):
            maps[c][name] = arrs[c]
    res = run_bass_kernel_spmd(p.nc, maps, core_ids=PCORES)
    return res.results


def kernel(**ins):
    env = dict(ins)
    g = lambda r, k: [r[c][k] for c in range(NCORE)]

    r = _launch(_get("LA1", build_LA1), env)
    env["xbrA"], env["szA"] = g(r, "o_xbr"), g(r, "o_sz")
    env["edgeA_AG"] = ag4(g(r, "o_edge"))

    r = _launch(_get("LA2", build_LA2), env)
    env["xbA"] = g(r, "o_xb")
    env["delA_o"] = a2a8(g(r, "o_delta"), 256)
    env["xbA_o"] = a2a8(g(r, "o_xb"), 256)
    env["bcA_AG"] = ag8(g(r, "o_bc"))

    r = _launch(_get("LA3", build_LA3), env)
    env["yA_out"] = a2a8(g(r, "o_y"), 256)

    r = _launch(_get("LA4", build_LA4), env)
    env["x2"] = g(r, "o_x2")
    env["xbrB"], env["szB"] = g(r, "o_xbr"), g(r, "o_sz")
    env["edgeB_AG"] = ag4(g(r, "o_edge"))

    r = _launch(_get("LA5", build_LA5), env)
    env["xbF"], env["xbR"] = g(r, "o_xb_f"), g(r, "o_xb_r")
    env["delF_o"] = a2a8(g(r, "o_delta_f"), 256)
    env["xbF_o"] = a2a8(g(r, "o_xb_f"), 256)
    env["bcF_AG"] = ag8(g(r, "o_bc_f"))
    env["delR_o"] = a2a8(g(r, "o_delta_r"), 256)
    env["xbR_o"] = a2a8(g(r, "o_xb_r"), 256)
    env["bcR_AG"] = ag8(g(r, "o_bc_r"))

    r = _launch(_get("LA6", build_LA6), env)
    env["yF_out"] = a2a8(g(r, "o_y_f"), 256)
    env["yR_out"] = a2a8(g(r, "o_y_r"), 256)

    r = _launch(_get("LA7", build_LA7), env)
    env["x4"], env["xd"] = g(r, "o_x4"), g(r, "o_xd")
    env["kvAG"] = ag4(env["xd"])

    r = _launch(_get("LA8", build_LA8), env)
    env["x5"], env["xn1"] = g(r, "o_x5"), g(r, "o_xn")
    env["edge1_AG"] = ag4(g(r, "o_edge"))

    r = _launch(_get("LA9", build_LA9), env)
    env["x6"] = g(r, "o_x6")
    env["qAG"] = ag8(g(r, "o_q"))
    env["kAG"] = ag8(g(r, "o_k"))
    env["gAG"] = ag8(g(r, "o_g"))
    v8 = a2a8(g(r, "o_v"), 16)
    env["vBC"] = [np.ascontiguousarray(
        np.broadcast_to(a[:, None, :], (128, 128, TC)).reshape(128 * 128, TC))
        for a in v8]

    r = _launch(_get("LA10", build_LA10), env)
    env["yOut2"] = a2a8(g(r, "o_y2"), 16)

    r = _launch(_get("LA11", build_LA11), env)
    env["x7"], env["xn2"] = g(r, "o_x7"), g(r, "o_xn")
    env["edge2_AG"] = ag4(g(r, "o_edge"))

    r = _launch(_get("LA12", build_LA12), env)
    out = np.empty((B, L, D), np.float32)
    for c in range(NCORE):
        b_, t0 = c // GROUP, (c % GROUP) * TC
        out[b_, t0:t0 + TC, :] = r[c]["o_x"].T
    return out
